# version 10
# baseline (speedup 1.0000x reference)
import math

import numpy as np
import ml_dtypes

import concourse.bass as bass
import concourse.mybir as mybir
from concourse.bass_utils import run_bass_kernel_spmd

F32 = mybir.dt.float32
BF16 = mybir.dt.bfloat16
AF = mybir.ActivationFunctionType
Alu = mybir.AluOpType

BF = ml_dtypes.bfloat16

D = 1024
HD = 64
NH = 16
NKV = 4
CH = 128
ST = 512
N_ST = 8
N_CH = 32
CPST = ST // CH
S_CORE = 4096
THETA = 10000.0
EPS = 1e-5

PAIRS = [(0, 4), (1, 5), (2, 6), (3, 7), (8, 12), (9, 13), (10, 14), (11, 15)]
HEAD_POS = {}
for _p, (_lo, _hi) in enumerate(PAIRS):
    HEAD_POS[_lo] = (_p, 0)
    HEAD_POS[_hi] = (_p, 1)
SLOPES = [2.0 ** (-8.0 / NH * (h + 1)) for h in range(NH)]


def _rope_tables():
    inv = 1.0 / THETA ** (np.arange(0, HD, 2, dtype=np.float64) / HD)
    freqs = np.outer(np.arange(8192, dtype=np.float64), inv)
    emb = np.concatenate([freqs, freqs], axis=-1)
    return np.cos(emb).astype(np.float32), np.sin(emb).astype(np.float32)


_COS64, _SIN64 = _rope_tables()


def _host_consts(half):
    c = {}
    M = np.zeros((128, 128), np.float32)
    for blk in range(2):
        b0 = 64 * blk
        for i in range(32):
            M[b0 + i, b0 + 32 + i] = -1.0
            M[b0 + 32 + i, b0 + i] = 1.0
    c["mt"] = np.ascontiguousarray(M.T).astype(BF)
    c["identbf"] = np.eye(128, dtype=np.float32).astype(BF)
    mask = np.zeros((128, 16 * 128), np.float32)
    t = np.arange(CH, dtype=np.float64)
    for p in range(8):
        for e in range(2):
            h = PAIRS[p][e]
            s = p + 8 * e
            m = np.where(t[:, None] >= t[None, :],
                         np.exp(-SLOPES[h] * (t[:, None] - t[None, :])), 0.0)
            mask[:, 128 * s:128 * s + 128] = m.T.astype(np.float32)
    c["mask"] = mask.astype(BF)
    qd = np.zeros((128, 8, 128), np.float32)
    for p in range(8):
        for e in range(2):
            h = PAIRS[p][e]
            qd[64 * e:64 * e + 64, p, :] = np.exp(-SLOPES[h] * (t + 1.0))[None, :]
    c["qdec"] = qd.astype(BF)
    kd = np.zeros((128, 8, 2, 64), np.float32)
    for p in range(8):
        for e in range(2):
            h = PAIRS[p][e]
            kd[:, p, e, :] = np.exp(-SLOPES[h] * (CH - 1.0 - t))[:, None]
    c["kdec"] = kd.reshape(128, NH * 64).astype(BF)
    bd = np.zeros((128, 8, 128), np.float32)
    for p in range(8):
        for e in range(2):
            h = PAIRS[p][e]
            bd[:, p, 64 * e:64 * e + 64] = math.exp(-SLOPES[h] * CH)
    c["bdf"] = bd.reshape(128, 1024).copy()
    c["bdb"] = c["bdf"].astype(BF)
    pos0 = half * S_CORE
    cos = np.zeros((N_ST, 128, ST), np.float32)
    sin = np.zeros((N_ST, 128, ST), np.float32)
    for st in range(N_ST):
        pp = np.arange(ST) + pos0 + st * ST
        cs = _COS64[pp, :].T
        sn = _SIN64[pp, :].T
        cos[st, 0:64] = cs
        cos[st, 64:128] = cs
        sin[st, 0:64] = sn
        sin[st, 64:128] = sn
    c["costab"] = cos
    c["sintab"] = sin
    rm = np.zeros((128, 2), np.float32)
    if half == 1:
        rm[:, 0] = 1.0
    c["rmask"] = rm
    c["onesf"] = np.ones((1, 128), np.float32)
    c["epsc"] = np.full((1, 1), EPS, np.float32)
    c["onesb"] = np.ones((1, 512), np.float32).astype(BF)
    c["onescol"] = np.ones((128, 1), np.float32).astype(BF)
    return c


def _perm_rows():
    perm = []
    for (lo, hi) in PAIRS:
        perm.extend(range(64 * lo, 64 * lo + 64))
        perm.extend(range(64 * hi, 64 * hi + 64))
    return np.array(perm, np.int64)


class Prog:
    def __init__(self, nc, sems):
        self.nc = nc
        self.ops = {e: [] for e in ("sync", "gpsimd", "tensor", "vector", "scalar")}
        self.count = {s: 0 for s in sems}
        self.sems = sems

    def op(self, eng, fn, waits=(), inc=None, inc_n=1):
        tok = None
        if inc is not None:
            self.count[inc] += inc_n
            tok = (inc, self.count[inc])
        waits = tuple(w for w in waits if w is not None)
        self.ops[eng].append((fn, waits, inc, inc_n))
        return tok

    def dma(self, eng, out, in_, waits=(), inc=None, **kw):
        return self.op(eng, lambda e: e.dma_start(out=out, in_=in_, **kw),
                       waits=waits, inc=inc, inc_n=16)

    def emit(self, block):
        def make(eng_name):
            ops = self.ops[eng_name]
            sems = self.sems

            def body(eng):
                wm = {}
                for fn, waits, inc, inc_n in ops:
                    for (sn, val) in waits:
                        if wm.get(sn, -1) >= val:
                            continue
                        eng.wait_ge(sems[sn], val)
                        wm[sn] = val
                    instr = fn(eng)
                    if inc is not None:
                        if instr is None:
                            instr = eng.nop()
                        instr.then_inc(sems[inc], inc_n)
            return body

        for name in ("sync", "gpsimd", "tensor", "vector", "scalar"):
            getattr(block, name)(make(name))


def build_nc(n_st1=N_ST, do_coll=True, n_st2=N_ST, p2stage=9):
    nc = bass.Bass("TRN2", target_bir_lowering=False)

    xt = nc.dram_tensor("xt", [N_ST, 128, 8, ST], F32, kind="ExternalInput").ap()
    out = nc.dram_tensor("out", [N_ST, 128, 8, ST], F32, kind="ExternalOutput").ap()
    wq = nc.dram_tensor("wq", [D, D], F32, kind="ExternalInput").ap()
    wk = nc.dram_tensor("wk", [256, D], F32, kind="ExternalInput").ap()
    wv = nc.dram_tensor("wv", [256, D], F32, kind="ExternalInput").ap()
    wo = nc.dram_tensor("wo", [D, D], F32, kind="ExternalInput").ap()
    u1 = nc.dram_tensor("u1", [D, 256], F32, kind="ExternalInput").ap()
    v1 = nc.dram_tensor("v1", [256, 4096], F32, kind="ExternalInput").ap()
    u2 = nc.dram_tensor("u2", [4096, 256], F32, kind="ExternalInput").ap()
    v2 = nc.dram_tensor("v2", [256, D], F32, kind="ExternalInput").ap()
    b1c_d = nc.dram_tensor("b1c", [128, 32], F32, kind="ExternalInput").ap()
    b2c_d = nc.dram_tensor("b2c", [128, 8], F32, kind="ExternalInput").ap()
    mt_d = nc.dram_tensor("mt", [128, 128], BF16, kind="ExternalInput").ap()
    identbf_d = nc.dram_tensor("identbf", [128, 128], BF16, kind="ExternalInput").ap()
    mask_d = nc.dram_tensor("mask", [128, 2048], BF16, kind="ExternalInput").ap()
    qdec_d = nc.dram_tensor("qdec", [128, 8, 128], BF16, kind="ExternalInput").ap()
    kdec_d = nc.dram_tensor("kdec", [128, 1024], BF16, kind="ExternalInput").ap()
    bdb_d = nc.dram_tensor("bdb", [128, 1024], BF16, kind="ExternalInput").ap()
    costab_d = nc.dram_tensor("costab", [N_ST, 128, ST], F32, kind="ExternalInput").ap()
    sintab_d = nc.dram_tensor("sintab", [N_ST, 128, ST], F32, kind="ExternalInput").ap()
    rmask_d = nc.dram_tensor("rmask", [128, 2], F32, kind="ExternalInput").ap()
    onesf_d = nc.dram_tensor("onesf", [1, 128], F32, kind="ExternalInput").ap()
    epsc_d = nc.dram_tensor("epsc", [1, 1], F32, kind="ExternalInput").ap()
    onesb_d = nc.dram_tensor("onesb", [1, 512], BF16, kind="ExternalInput").ap()
    onescol_d = nc.dram_tensor("onescol", [128, 1], BF16, kind="ExternalInput").ap()

    wqbf_h = nc.dram_tensor("wqbf_h", [D, D], BF16, kind="Internal").ap()
    wobf_h = nc.dram_tensor("wobf_h", [D, D], BF16, kind="Internal").ap()
    wkbf_h = nc.dram_tensor("wkbf_h", [256, D], BF16, kind="Internal").ap()
    wvbf_h = nc.dram_tensor("wvbf_h", [256, D], BF16, kind="Internal").ap()
    st_h = nc.dram_tensor("st_h", [N_CH, 128, 1024], BF16, kind="Internal").ap()
    kt_h = nc.dram_tensor("kt_h", [N_ST, 128, 2, ST], BF16, kind="Internal").ap()
    v_h = nc.dram_tensor("v_h", [N_ST, 128, CPST, 256], BF16, kind="Internal").ap()
    rows_h = nc.dram_tensor("rows_h", [N_ST, 2, 512], F32, kind="Internal").ap()
    src_h = nc.dram_tensor("src_h", [128, 1024], F32, kind="Internal").ap()
    dst_h = nc.dram_tensor("dst_h", [256, 1024], F32, kind="Internal").ap()

    import contextlib
    stack = contextlib.ExitStack()
    sb = lambda name, shape, dt: stack.enter_context(nc.sbuf_tensor(name, shape, dt))
    psb = lambda name, shape, dt: stack.enter_context(nc.psum_tensor(name, shape, dt))
    sem = lambda name: stack.enter_context(nc.semaphore(name))

    wqT = sb("wqT", [128, 8, 1024], BF16)
    woT = sb("woT", [128, 8, 1024], BF16)
    wkT = sb("wkT", [128, 8, 256], BF16)
    wvT = sb("wvT", [128, 8, 256], BF16)
    u1T = sb("u1T", [128, 8, 256], BF16)
    v1T = sb("v1T", [128, 2, 4096], BF16)
    u2T = sb("u2T", [128, 32, 256], BF16)
    v2T = sb("v2T", [128, 2, 1024], BF16)
    b1c = sb("b1c_s", [128, 32], F32)
    b2c = sb("b2c_s", [128, 8], F32)
    mt = sb("mt_s", [128, 128], BF16)
    identbf = sb("identbf_s", [128, 128], BF16)
    maskT = sb("maskT", [128, 2048], BF16)
    qdec = sb("qdec_s", [128, 8, 128], BF16)
    kdec = sb("kdec_s", [128, 1024], BF16)
    bdb = sb("bdb_s", [128, 1024], BF16)
    ctab = sb("ctab", [128, ST], F32)
    stab = sb("stab", [128, ST], F32)
    rmask = sb("rmask_s", [128, 2], F32)
    onesf = sb("onesf_s", [1, 128], F32)
    epsc = sb("epsc_s", [1, 1], F32)
    nc.const_aps.register(F32, EPS, epsc[0:1, 0:1])
    onesb = sb("onesb_s", [1, 512], BF16)
    onescol = sb("onescol_s", [128, 1], BF16)
    rsum_q = sb("rsum_q", [1, 1024], BF16)
    rsum_k = sb("rsum_k", [1, 256], BF16)
    rsum_v = sb("rsum_v", [1, 256], BF16)
    rsum_u1 = sb("rsum_u1", [1, 256], BF16)

    xb = [sb(f"xb{i}", [128, 8, ST], F32) for i in range(2)]
    x_bf = sb("x_bf", [128, 8, ST], BF16)
    rows_cur = sb("rows_cur", [1, 1, 2, 512], F32)
    negmb_row = sb("negmb_row", [1, 512], BF16)
    srow = sb("srow", [1, 2, 512], F32)
    rs2_row = sb("rs2_row", [1, 512], F32)
    negm2_row = sb("negm2_row", [1, 512], F32)
    negm2b_row = sb("negm2b_row", [1, 512], BF16)
    rs_col = sb("rs_col", [128, 4], F32)

    kraw_bf = sb("kraw_bf", [128, 2, 512], BF16)
    tmp1 = sb("tmp1", [128, 512], F32)
    tmp2 = sb("tmp2", [128, 512], F32)
    rs2_sb = tmp2
    kT_sb = sb("kT_sb", [128, 2, ST], BF16)
    kq_scr = sb("kq_scr", [128, 1024], BF16)
    ks_scr = sb("ks_scr", [128, 1024], BF16)
    ktok = kq_scr
    k16 = ks_scr
    st_true = ks_scr
    qdq3 = None
    v_bf = sb("v_bf", [128, CPST, 256], BF16)
    state_r = sb("state_r", [128, 2, 1024], BF16)
    fbf = sb("fbf", [128, 1024], BF16)
    rrun = sb("rrun", [128, 1024], BF16)
    st_ld = sb("st_ld", [128, 2, 1024], BF16)
    qp = sb("qp", [128, 8, ST], BF16)
    sh_bf = sb("sh_bf", [128, 2048], BF16)
    S_bf = sh_bf
    H_bf = sh_bf
    attn_bf = sb("attn_bf", [128, 8, 256], BF16)
    pq_bf = sb("pq_bf", [128, 2, 512], BF16)
    P_bf = pq_bf
    Q_bf = pq_bf

    ps_a = psb("ps_a", [128, 512], F32)
    ps_k = [psb("ps_k0", [128, 512], F32), psb("ps_k1", [128, 512], F32)]
    ps_s = [psb(f"ps_s{i}", [128, 512], F32) for i in range(4)]
    ps_wt = psb("ps_wt", [128, 1024], BF16)

    sems = {}
    for s in ("s_const", "s_wkT", "s_wvT", "s_wqT", "s_woT", "s_x0", "s_x1",
              "s_tabc", "s_tabs", "s_rows", "s_out0", "s_out1",
              "s_w1", "s_w2", "s_w3", "s_kt", "s_vv", "s_ktl", "s_vvl", "s_st0", "s_st1",
              "s_sp0", "s_sp1", "s_g", "s_q", "s_t", "s_v", "s_s", "s_c"):
        sems[s] = sem(s)
    P = Prog(nc, sems)

    T, V, S = "s_t", "s_v", "s_s"

    tok_const = None
    for dst, src in ((mt, mt_d), (identbf, identbf_d), (maskT, mask_d),
                     (qdec, qdec_d), (kdec, kdec_d), (bdb, bdb_d),
                     (rmask, rmask_d), (onesf, onesf_d), (epsc, epsc_d),
                     (onesb, onesb_d),
                     (onescol, onescol_d), (b2c, b2c_d)):
        tok_const = P.dma("sync", dst[:], src[:, :] if len(src.shape) == 2 else src[:],
                          inc="s_const")

    P.dma("gpsimd", wkbf_h[:, :], wk[:, :], inc="s_w1")
    tok_wkv_h = P.dma("gpsimd", wvbf_h[:, :], wv[:, :], inc="s_w1")
    P.dma("gpsimd", u1T[:], u1.rearrange("(k p) r -> p k r", p=128), inc="s_w2")
    P.dma("gpsimd", v1T[:], v1.rearrange("(k p) f -> p k f", p=128), inc="s_w2")
    for g in range(4):
        P.dma("gpsimd", u2T[:, 8 * g:8 * (g + 1), :],
              u2.rearrange("(k p) r -> p k r", p=128)[:, 8 * g:8 * (g + 1), :],
              inc="s_w2")
    P.dma("gpsimd", v2T[:], v2.rearrange("(k p) d -> p k d", p=128), inc="s_w2")
    tok_w2 = P.dma("gpsimd", b1c[:], b1c_d[:, :], inc="s_w2")
    tok_u1 = tok_v1 = tok_v2 = tok_b1 = tok_w2
    tok_u2 = [tok_w2] * 4
    P.dma("gpsimd", wqbf_h[:, :], wq[:, :], inc="s_w3")
    tok_w3 = P.dma("gpsimd", wobf_h[:, :], wo[:, :], inc="s_w3")
    tok_wq_h = tok_wo_h = tok_w3

    tok_wkT = tok_wvT = tok_wqT = tok_woT = None
    for k in range(8):
        tok_wkT = P.op("sync", lambda e, k=k: e.dma_start_transpose(
            wkT[:, k, :], wkbf_h[:, 128 * k:128 * (k + 1)]),
            waits=[tok_wkv_h], inc="s_wkT", inc_n=16)
    for k in range(8):
        tok_wvT = P.op("sync", lambda e, k=k: e.dma_start_transpose(
            wvT[:, k, :], wvbf_h[:, 128 * k:128 * (k + 1)]),
            waits=[tok_wkv_h], inc="s_wvT", inc_n=16)

    def rowsum(dst_row, srcT, nsl, waits):
        prev_evac = None
        for (a, b) in nsl:
            w0 = list(waits) + ([prev_evac] if prev_evac else [])
            for k in range(8):
                tk = P.op("tensor", lambda e, k=k, a=a, b=b: e.matmul(
                    ps_a[0:1, 0:b - a], onescol[:, 0:1], srcT[:, k, a:b],
                    start=(k == 0), stop=(k == 7)),
                    waits=w0 if k == 0 else (), inc=T if k == 7 else None)
            prev_evac = P.op("scalar", lambda e, a=a, b=b: e.copy(
                out=dst_row[0:1, a:b], in_=ps_a[0:1, 0:b - a]),
                waits=[tk], inc=S)
        return prev_evac

    tok_rsum_k = rowsum(rsum_k, wkT, [(0, 256)], [tok_wkT, tok_const])
    tok_rsum_v = rowsum(rsum_v, wvT, [(0, 256)], [tok_wvT, tok_rsum_k])

    tok_x = [None] * N_ST
    tok_x[0] = P.dma("sync", xb[0][:], xt[0], inc="s_x0")
    tok_state0 = P.op("vector", lambda e: e.memset(state_r[:], 0.0), inc=V)

    tok_xbf_done = [None] * (N_ST + 1)
    tok_xb_free = [None] * (N_ST + 1)
    prev_sp2 = None
    tok_rows_free = [None] * (N_ST + 1)
    prev_ts_nm = None
    prev_ts_rc = None
    prev_tv_sin = None
    prev_kt_spill = None
    prev_v_spill = None
    tok_ctab_free = None
    prev_state_tt = tok_state0
    prev_ts_v_last = tok_rsum_v
    prev_ts_tok = None

    def pre_sq(stv, sq_waits):
        xv1 = xb[stv % 2][:, 0:4, :].rearrange("p a b -> p (a b)")
        xv2 = xb[stv % 2][:, 4:8, :].rearrange("p a b -> p (a b)")
        P.op("gpsimd", lambda e, xv1=xv1: e.tensor_mul(
            attn_bf.rearrange("p a b -> p (a b)"), xv1, xv1),
            waits=sq_waits, inc="s_q")
        return P.op("gpsimd", lambda e, xv2=xv2: e.tensor_mul(
            sh_bf[:, :], xv2, xv2), inc="s_q")

    def pre_mm(mm_waits):
        for k in range(8):
            srcb = attn_bf if k < 4 else sh_bf
            off = 512 * (k % 4)
            tt_stat = P.op("tensor", lambda e, srcb=srcb, off=off, k=k: e.matmul(
                ps_s[0][0:1, :], onescol[:, 0:1],
                bass.AP(srcb, off, [[2048, 128], [1, 512]]),
                start=(k == 0), stop=(k == 7)),
                waits=mm_waits if k == 0 else (),
                inc=T if k == 7 else None)
        return tt_stat

    tsq_cur = pre_sq(0, [tok_x[0]])
    prev_tt_stat = pre_mm([tsq_cur, tok_const])
    prev_kraw_rd = [None, None]
    prev_addsA = None
    prev_addsB = None

    for st in range(n_st1):
        if st + 1 < n_st1:
            w = [tok_xb_free[st - 1]] if st >= 1 else []
            tok_x[st + 1] = P.dma("sync", xb[(st + 1) % 2][:], xt[st + 1],
                                  waits=w, inc=f"s_x{(st + 1) % 2}")
            tsq_next = pre_sq(st + 1, [tok_x[st + 1], prev_tt_stat])
        if st == 3:
            for k in range(8):
                tok_wqT = P.op("sync", lambda e, k=k: e.dma_start_transpose(
                    wqT[:, k, :], wqbf_h[:, 128 * k:128 * (k + 1)]),
                    waits=[tok_w3], inc="s_wqT", inc_n=16)
            for k in range(8):
                tok_woT = P.op("sync", lambda e, k=k: e.dma_start_transpose(
                    woT[:, k, :], wobf_h[:, 128 * k:128 * (k + 1)]),
                    waits=[tok_w3], inc="s_woT", inc_n=16)
        xcur = xb[st % 2]
        tv_xbf1 = P.op("scalar", lambda e, xcur=xcur: e.activation(
            out=x_bf[:, 0:4, :].rearrange("p a b -> p (a b)"),
            in_=xcur[:, 0:4, :].rearrange("p a b -> p (a b)"), func=AF.Copy),
            waits=[tok_x[st], tok_xbf_done[st - 1] if st >= 1 else None],
            inc=S)
        tv_xbf = P.op("scalar", lambda e, xcur=xcur: e.activation(
            out=x_bf[:, 4:8, :].rearrange("p a b -> p (a b)"),
            in_=xcur[:, 4:8, :].rearrange("p a b -> p (a b)"), func=AF.Copy),
            waits=[tv_xbf1], inc=S)
        tt_stat = prev_tt_stat
        stat_waits = [tv_xbf1, prev_ts_v_last]
        for k in range(8):
            tt_sum = P.op("tensor", lambda e, k=k: e.matmul(
                ps_a[0:1, :], onescol[:, 0:1], x_bf[:, k, :],
                start=(k == 0), stop=(k == 7)),
                waits=stat_waits if k == 0 else ([tv_xbf] if k == 4 else ()),
                inc=T if k == 7 else None)
        s_ring = 0
        ts1 = P.op("scalar", lambda e, s_ring=s_ring: e.activation(
            out=rows_cur[0:1, s_ring, 0, :], in_=ps_a[0:1, :], func=AF.Copy,
            scale=-1.0 / D),
            waits=[tt_sum, tok_rows_free[st - 1] if st >= 1 else None,
                   ("s_rows", 16 * st) if st >= 1 else None,
                   prev_ts_nm], inc=S)
        tv_m2 = P.op("scalar", lambda e, s_ring=s_ring: e.activation(
            out=srow[0:1, 1, :], in_=rows_cur[0:1, s_ring, 0, :], func=AF.Square),
            waits=[ts1], inc=S)
        tv_var = P.op("vector", lambda e: e.scalar_tensor_tensor(
            srow[0:1, 0, :], ps_s[0][0:1, :], 1.0 / D, srow[0:1, 1, :],
            Alu.mult, Alu.subtract),
            waits=[tt_stat, tv_m2], inc=V)
        ts3 = P.op("scalar", lambda e: e.activation(
            out=srow[0:1, 1, :], in_=srow[0:1, 0, :], func=AF.Ln, bias=EPS),
            waits=[tv_var, tok_const], inc=S)
        ts_rs = P.op("scalar", lambda e, s_ring=s_ring: e.activation(
            out=rows_cur[0:1, s_ring, 1, :], in_=srow[0:1, 1, :], func=AF.Exp,
            scale=-0.5), waits=[ts3], inc=S)
        ts_nm = P.op("scalar", lambda e, s_ring=s_ring: e.copy(
            out=negmb_row[0:1, :], in_=rows_cur[0:1, s_ring, 0, :]),
            waits=[ts1], inc=S)
        prev_ts_nm = ts_nm
        tok_rows_sp = P.dma("sync",
                            rows_h[st].rearrange("a b -> (a b)").unsqueeze(0),
                            rows_cur[0:1, s_ring].rearrange("p a b -> p (a b)"),
                            waits=[ts_rs], inc="s_rows")
        tt_rsb = P.op("tensor", lambda e, s_ring=s_ring: e.matmul(
            ps_s[3][:, :], onesf[0:1, 0:128], rows_cur[0:1, s_ring, 1, :],
            start=True, stop=True),
            waits=[ts_rs, prev_tv_sin, prev_addsB], inc=T)
        for i in range(4):
            tt_rsc = P.op("tensor", lambda e, i=i, s_ring=s_ring: e.matmul(
                ps_s[2][:, i:i + 1], rows_cur[0:1, s_ring, 1, 128 * i:128 * i + 128],
                onesf[0:1, 0:1], start=True, stop=True),
                waits=[prev_ts_rc, prev_addsB] if i == 0 else (),
                inc=T if i == 3 else None)
        tok_tab = P.dma("sync", ctab[:], costab_d[st],
                        waits=[tok_ctab_free], inc="s_tabc")
        tok_tab2 = P.dma("sync", stab[:], sintab_d[st],
                         waits=[tok_ctab_free], inc="s_tabs")
        tv_cos = P.op("vector", lambda e: e.tensor_mul(ctab[:], ctab[:], ps_s[3][:, :]),
                      waits=[tt_rsb, tok_tab, tok_tab2], inc=V)
        tv_sin = P.op("vector", lambda e: e.tensor_mul(stab[:], stab[:], ps_s[3][:, :]),
                      waits=[tt_rsb, tok_tab2], inc=V)
        prev_tv_sin = tv_sin
        ts_rc = P.op("scalar", lambda e: e.copy(out=rs_col[:], in_=ps_s[2][:, 0:4]),
                     waits=[tt_rsc], inc=S)
        prev_ts_rc = ts_rc
        tv_k = [None, None]
        ts_k_prev = None
        tv1_prev = None
        tv2_prev = None
        tt_rot_prev = None
        for r in range(2):
            kps = ps_a if r == 0 else ps_s[0]
            kw = [ts1 if r == 0 else tv_var, tv_xbf, tok_wkT]
            for k in range(8):
                P.op("tensor", lambda e, k=k, r=r, kps=kps: e.matmul(
                    kps[:, :], wkT[:, k, 128 * r:128 * r + 128], x_bf[:, k, :],
                    start=(k == 0), stop=False), waits=kw if k == 0 else ())
            tt_k = P.op("tensor", lambda e, r=r, kps=kps: e.matmul(
                kps[:, :], rsum_k[0:1, 128 * r:128 * r + 128], negmb_row[0:1, :],
                start=False, stop=True), waits=[ts_nm], inc=T)
            skw = [tt_k]
            if prev_kraw_rd[r] is not None:
                skw += prev_kraw_rd[r]
            ts_k = P.op("scalar", lambda e, r=r, kps=kps: e.copy(
                out=kraw_bf[:, r, :], in_=kps[:, :]), waits=skw, inc=S)
            tt_rot = P.op("tensor", lambda e, r=r: e.matmul(
                ps_s[1][:, :], mt[:, :], kraw_bf[:, r, :], start=True, stop=True),
                waits=[ts_k, tv2_prev], inc=T)
            tv1 = P.op("vector", lambda e, r=r: e.tensor_mul(
                tmp1[:], kraw_bf[:, r, :], ctab[:]),
                waits=[ts_k, tv_cos, tv_k[0] if r == 1 else None], inc=V)
            tv2 = P.op("vector", lambda e: e.tensor_mul(tmp2[:], ps_s[1][:, :], stab[:]),
                       waits=[tt_rot, tv_sin], inc=V)
            tv_k[r] = P.op("vector", lambda e, r=r: e.tensor_add(
                kT_sb[:, r, :], tmp1[:], tmp2[:]),
                waits=[tv1, tv2, prev_kt_spill], inc=V)
            prev_kraw_rd[r] = [tv1, tt_rot]
            ts_k_prev, tv1_prev, tv2_prev, tt_rot_prev = ts_k, tv1, tv2, tt_rot
        prev_kt_spill = P.dma("gpsimd", kt_h[st], kT_sb[:],
                              waits=[tv_k[0], tv_k[1]], inc="s_kt")
        trw = [tv_k[0], tv_k[1], tok_const]
        if prev_ts_tok is not None:
            trw.append(prev_ts_tok)
        first_tr = True
        for r in range(2):
            for i in range(4):
                tt_tr = P.op("tensor", lambda e, r=r, i=i: e.transpose(
                    ps_wt[:, 256 * i + 128 * r: 256 * i + 128 * r + 128],
                    kT_sb[:, r, 128 * i:128 * i + 128], identbf[:, :]),
                    waits=trw if first_tr else (),
                    inc=T if (r == 1 and i == 3) else None)
                first_tr = False
        prev_ts_tok = P.op("scalar", lambda e: e.copy(
            out=ktok[:, :], in_=ps_wt[:, :]),
            waits=[tt_tr], inc=S)
        ts_tok = prev_ts_tok
        ts_v = [None] * CPST
        for i in range(CPST):
            vps = ps_a if i % 2 == 0 else ps_s[1]
            vw = [tok_wvT]
            if i > 1:
                vw.append(ts_v[i - 2])
            elif i == 0:
                vw.append(ts_k_prev)
            else:
                vw.append(tv2_prev)
            for k in range(8):
                P.op("tensor", lambda e, k=k, i=i, vps=vps: e.matmul(
                    vps[:, 0:256], x_bf[:, k, 128 * i:128 * i + 128], wvT[:, k, :],
                    start=(k == 0), stop=False), waits=vw if k == 0 else ())
            tt_v = P.op("tensor", lambda e, i=i, vps=vps: e.matmul(
                vps[:, 0:256], negmb_row[0:1, 128 * i:128 * i + 128],
                rsum_v[0:1, :], start=False, stop=True), inc=T)
            ts_v[i] = P.op("scalar", lambda e, i=i, vps=vps: e.activation(
                out=v_bf[:, i, :], in_=vps[:, 0:256], func=AF.Copy,
                scale=rs_col[:, i:i + 1]),
                waits=[tt_v, ts_rc, prev_v_spill], inc=S)
        prev_v_spill = P.dma("gpsimd", v_h[st], v_bf[:], waits=[ts_v[CPST - 1]],
                             inc="s_vv")
        tok_xbf_done[st] = tt_v
        prev_ts_v_last = ts_v[CPST - 1]
        if st + 1 < n_st1:
            prev_tt_stat = pre_mm([tsq_next, tv_var, ts_k_prev])
        if st == 0:
            P.op("vector", lambda e: e.memset(ps_k[0][:, :], 0.0),
                 waits=[prev_state_tt], inc=V)
            tv_z = P.op("vector", lambda e: e.memset(ps_k[1][:, :], 0.0), inc=V)
        for ci in range(CPST):
            c = CPST * st + ci
            pair = [ps_k[0], ps_k[1]] if ci % 2 == 0 else [ps_s[2], ps_s[3]]
            tok_sp = P.dma("sync", st_h[c], state_r[:, c % 2, :],
                           waits=[prev_state_tt], inc=f"s_sp{c % 2}")
            tv_k16 = P.op("vector", lambda e, ci=ci: e.tensor_tensor(
                bass.AP(k16, 0, [[1024, 128], [512, 2], [128, 4], [64, 2], [1, 64]]),
                bass.AP(ktok, 256 * ci,
                        [[1024, 128], [128, 2], [0, 4], [64, 2], [1, 64]]),
                bass.AP(kdec, 0,
                        [[1024, 128], [512, 2], [128, 4], [64, 2], [1, 64]]),
                Alu.mult), waits=[ts_tok], inc=V)
            if ci % 2 == 0:
                bank_free = [prev_addsA] if prev_addsA is not None else [tv_z]
            else:
                bank_free = ([prev_addsB] if prev_addsB is not None else []) + \
                            [ts_rc, tv_cos, tv_sin]
            uw = [tv_k16, ts_v[ci]] + bank_free
            for p in range(8):
                tt_u = P.op("tensor", lambda e, p=p, ci=ci, pair=pair: e.matmul(
                    pair[p // 4][:, 128 * (p % 4):128 * (p % 4) + 128],
                    k16[:, 128 * p:128 * p + 128],
                    v_bf[:, ci, 128 * (p // 4):128 * (p // 4) + 128],
                    start=(p % 4 == 0), stop=(p % 4 == 3)),
                    waits=uw if p == 0 else (), inc=T if p == 7 else None)
            tvm = P.op("vector", lambda e, c=c: e.tensor_mul(
                fbf[:], state_r[:, c % 2, :], bdb[:]),
                waits=[prev_state_tt], inc=V)
            ns = (c + 1) % 2
            prev_sp = tok_sp
            for g in range(2):
                for e_par in range(2):
                    sap = bass.AP(state_r, 2048 * 64 * e_par + 1024 * ns
                                  + 512 * g + 64 * e_par,
                                  [[2048, 64], [128, 4], [1, 64]])
                    fap = bass.AP(fbf, 64 * 1024 * e_par + 512 * g + 64 * e_par,
                                  [[1024, 64], [128, 4], [1, 64]])
                    pap = bass.AP(pair[g], 64 * 512 * e_par + 64 * e_par,
                                  [[512, 64], [128, 4], [1, 64]])
                    tok_last = P.op("vector",
                                    lambda e, sap=sap, fap=fap, pap=pap: e.tensor_add(
                        sap, fap, pap),
                        waits=[tt_u, tvm, prev_sp2] if (g == 0 and e_par == 0) else (),
                        inc=V)
            prev_sp2 = prev_sp
            prev_state_tt = tok_last
            if ci % 2 == 0:
                prev_addsA = tok_last
            else:
                prev_addsB = tok_last
        tok_ctab_free = tv_k[1]
        tok_xb_free[st] = tt_v
        tok_rows_free[st] = tt_rsc

    if not do_coll:
        P.op("gpsimd", lambda e: e.nop(),
             waits=[("s_sp0", P.count["s_sp0"]), ("s_sp1", P.count["s_sp1"]),
                    prev_state_tt])
        with nc.Block() as block:
            P.emit(block)
        stack.close()
        return nc
    tok_src = P.dma("gpsimd", src_h[:, :], state_r[:, N_CH % 2, :],
                    waits=[prev_state_tt], inc="s_g")
    P.op("gpsimd", lambda e: e.collective_compute(
        "AllGather", Alu.bypass,
        ins=[src_h[:, :]], outs=[dst_h[:, :]],
        replica_groups=[[0, 1], [2, 3], [4, 5], [6, 7]]),
        waits=[tok_src], inc="s_c")
    tok_coll = ("s_c", P.count["s_c"])
    tok_g0 = P.dma("gpsimd", fbf[:], dst_h[0:128, :], waits=[tok_coll], inc="s_g")
    tv_sel0 = P.op("vector", lambda e: e.tensor_scalar_mul(
        rrun[:], fbf[:], rmask[:, 0:1]),
        waits=[tok_g0, tok_const, prev_state_tt], inc=V)
    tok_g1 = P.dma("gpsimd", fbf[:], dst_h[128:256, :], waits=[tv_sel0], inc="s_g")
    tv_rrun = P.op("vector", lambda e: e.scalar_tensor_tensor(
        rrun[:], fbf[:], rmask[:, 1:2], rrun[:], Alu.mult, Alu.add),
        waits=[tok_g1], inc=V)

    tok_rsum_q = rowsum(rsum_q, wqT, [(0, 512), (512, 1024)],
                        [tok_wqT, prev_ts_v_last])
    tok_rsum_u1 = rowsum(rsum_u1, u1T, [(0, 256)], [tok_u1, tok_rsum_q])

    tok_x2 = [None] * N_ST
    tok_x2[0] = P.dma("sync", xb[0][:], xt[0], waits=[tok_xb_free[N_ST - 1]],
                      inc="s_x0")
    tok_kt_ld = [None] * N_ST
    tok_v_ld = [None] * N_ST
    tok_st_ld = [None] * N_ST
    tok_kt_ld[0] = P.dma("sync", kT_sb[:], kt_h[0],
                         waits=[("s_kt", 16 * N_ST), prev_state_tt], inc="s_ktl")
    tok_v_ld[0] = P.dma("sync", v_bf[:], v_h[0],
                        waits=[("s_vv", 16 * N_ST), prev_state_tt], inc="s_vvl")
    tok_rows_ld = [None] * N_ST
    tok_rows_ld[0] = P.dma("sync",
                           rows_cur[0:1, 0].rearrange("p a b -> p (a b)"),
                           rows_h[0].rearrange("a b -> (a b)").unsqueeze(0),
                           waits=[tok_rows_free[N_ST - 1], ("s_rows", 16 * N_ST)],
                           inc="s_rows")
    tok_stch = [None] * (N_CH + 2)
    all_sp = [("s_sp0", P.count["s_sp0"]), ("s_sp1", P.count["s_sp1"])]
    tok_stch[0] = P.dma("sync", st_ld[:, 0, :], st_h[0],
                        waits=all_sp, inc="s_st0")
    tok_stch[1] = P.dma("sync", st_ld[:, 1, :], st_h[1],
                        waits=all_sp, inc="s_st1")
    prev_tv_st = [None] * (N_CH + 2)

    tok_out = [None] * N_ST
    tv2_prev = [None]
    tv1_prev2 = [None, None]
    prev_tv_P1 = prev_tv_sin
    prev_S_evac_g = [None, None]
    prev_tt_o = None
    prev_ts_ot1 = None
    tok_rows_used = [None] * (N_ST + 1)
    prev_ts_attn = prev_ts_tok
    prev_negmb_user = None
    prev_tv_xbf2 = None
    prev_tv_rr = None
    tok_qp_free = None
    prev_o_evac = None
    prev_S_evac = None
    tok_xb2_free = [None] * (N_ST + 1)
    tok_ctab2_free = tok_ctab_free
    prev_psa_reader = tok_rsum_u1
    prev_tv_res2_last = None
    prev_tt_P = None
    prev_tsQ = None

    if n_st2 == 0:
        P.op("gpsimd", lambda e: e.nop(), waits=[tok_coll])
        with nc.Block() as block:
            P.emit(block)
        stack.close()
        return nc
    for st in range(n_st2):
        if st + 1 < n_st2:
            w = [tok_xb2_free[st - 1]] if st >= 1 else []
            tok_x2[st + 1] = P.dma("sync", xb[(st + 1) % 2][:], xt[st + 1], waits=w,
                                   inc=f"s_x{(st + 1) % 2}")
        xcur = xb[st % 2]
        s_ring = 0
        tv_xbf = P.op("vector", lambda e, xcur=xcur: e.tensor_copy(x_bf[:], xcur[:]),
                      waits=[tok_x2[st], prev_tt_P, prev_tv_xbf2], inc=V)
        ts_nm = P.op("scalar", lambda e, s_ring=s_ring: e.copy(
            out=negmb_row[0:1, :], in_=rows_cur[0:1, s_ring, 0, :]),
            waits=[tok_rows_ld[st], prev_negmb_user], inc=S)
        tt_rsb = P.op("tensor", lambda e, s_ring=s_ring: e.matmul(
            ps_a[:, :], onesf[0:1, 0:128], rows_cur[0:1, s_ring, 1, :],
            start=True, stop=True),
            waits=[prev_psa_reader, prev_tv_res2_last, tok_rows_ld[st]], inc=T)
        if st == 0:
            tok_tab_n = P.dma("sync", ctab[:], costab_d[0], waits=[tok_ctab2_free],
                              inc="s_tabc")
            tok_tab2_n = P.dma("sync", stab[:], sintab_d[0], waits=[tok_ctab2_free],
                               inc="s_tabs")
        tok_tab, tok_tab2 = tok_tab_n, tok_tab2_n
        tv_cos = P.op("vector", lambda e: e.tensor_mul(ctab[:], ctab[:], ps_a[:, :]),
                      waits=[tt_rsb, tok_tab, tok_tab2], inc=V)
        tv_sin = P.op("vector", lambda e: e.tensor_mul(stab[:], stab[:], ps_a[:, :]),
                      waits=[tt_rsb, tok_tab2], inc=V)
        tv_q = [None] * 8
        ts_q_prev = [None, None]
        pend = []
        def emit_rope(p_, ts_q_, rw_extra=()):
            rw = [ts_q_, tv2_prev[0]] + list(rw_extra)
            if prev_tsQ is not None:
                rw.append(prev_tsQ)
            tt_rot = P.op("tensor", lambda e: e.matmul(
                ps_k[0][:, :], mt[:, :], kraw_bf[:, p_ % 2, :], start=True, stop=True),
                waits=rw, inc=T)
            tv1 = P.op("vector", lambda e, p_=p_: e.tensor_mul(
                tmp1[:], kraw_bf[:, p_ % 2, :], ctab[:]),
                waits=[ts_q_, tv_cos, tv_q[p_ - 1] if p_ > 0 else None], inc=V)
            tv2 = P.op("vector", lambda e: e.tensor_mul(tmp2[:], ps_k[0][:, :], stab[:]),
                       waits=[tt_rot, tv_sin], inc=V)
            tv_q[p_] = P.op("vector", lambda e, p_=p_: e.tensor_add(
                qp[:, p_, :], tmp1[:], tmp2[:]), waits=[tv1, tv2], inc=V)
            tv2_prev[0] = tv2
        for p in range(8):
            qps = ps_a if p % 2 == 0 else ps_s[3]
            qw = [tv_cos, tv_sin, tv_xbf, tok_wqT, tok_rsum_q, ts_nm]
            if p == 0 and tok_qp_free is not None:
                qw.append(tok_qp_free)
            if p == 1:
                qw.append(prev_tv_P1)
            if ts_q_prev[p % 2] is not None:
                qw.append(ts_q_prev[p % 2])
            for k in range(8):
                P.op("tensor", lambda e, k=k, p=p, qps=qps: e.matmul(
                    qps[:, :], wqT[:, k, 128 * p:128 * p + 128], x_bf[:, k, :],
                    start=(k == 0), stop=False), waits=qw if k == 0 else ())
            tt_q = P.op("tensor", lambda e, p=p, qps=qps: e.matmul(
                qps[:, :], rsum_q[0:1, 128 * p:128 * p + 128], negmb_row[0:1, :],
                start=False, stop=True), inc=T)
            skw = [tt_q]
            if p >= 2:
                skw.append(tv1_prev2[p % 2])
            ts_q = P.op("scalar", lambda e, p=p, qps=qps: e.copy(
                out=kraw_bf[:, p % 2, :], in_=qps[:, :]), waits=skw, inc=S)
            ts_q_prev[p % 2] = ts_q
            pend.append((p, ts_q))
            if len(pend) >= 2:
                p_, t_ = pend.pop(0)
                emit_rope(p_, t_)
                tv1_prev2[p_ % 2] = tv_q[p_]
        while pend:
            p_, t_ = pend.pop(0)
            emit_rope(p_, t_)
            tv1_prev2[p_ % 2] = tv_q[p_]
        if st + 1 < n_st2:
            tok_rows_ld[st + 1] = P.dma(
                "sync", rows_cur[0:1, 0].rearrange("p a b -> p (a b)"),
                rows_h[st + 1].rearrange("a b -> (a b)").unsqueeze(0),
                waits=[tt_rsb, ts_nm], inc="s_rows")
            tok_tab_n = P.dma("sync", ctab[:], costab_d[st + 1],
                              waits=[tv_q[7]], inc="s_tabc")
            tok_tab2_n = P.dma("sync", stab[:], sintab_d[st + 1],
                               waits=[tv_q[7]], inc="s_tabs")
        tv_res1 = [None] * 8
        for ci in range(CPST):
            c = CPST * st + ci
            dqw = [tv_q[7]]
            if prev_tt_o is not None:
                dqw.append(prev_tt_o)
            tv_qdq = P.op("vector", lambda e, ci=ci: e.tensor_mul(
                kq_scr[:].rearrange("p (a b) -> p a b", a=8),
                qp[:, :, 128 * ci:128 * ci + 128], qdec[:]),
                waits=dqw, inc=V)
            tv_st = P.op("vector", lambda e, c=c: e.tensor_add(
                st_true[:], st_ld[:, c % 2, :], rrun[:]),
                waits=[tv_rrun, tok_stch[c], prev_tv_rr], inc=V)
            prev_tv_st[c] = tv_st
            if c + 2 <= N_CH - 1:
                tok_stch[c + 2] = P.dma("sync", st_ld[:, c % 2, :], st_h[c + 2],
                                        waits=[tv_st], inc=f"s_st{c % 2}")
            prev_tv_rr = P.op("vector", lambda e: e.tensor_mul(
                rrun[:], rrun[:], bdb[:]), waits=[tv_st], inc=V)
            tv_mask = [None, None]
            for e_par in range(2):
                sw0 = [tv_q[7], tok_kt_ld[st], prev_S_evac_g[e_par]]
                if e_par == 1:
                    sw0.append(prev_tv_P1)
                if prev_tsQ is not None:
                    sw0.append(prev_tsQ)
                first_s = True
                for p in range(8):
                    h = PAIRS[p][e_par]
                    kv = h // 4
                    s = p + 8 * e_par
                    tt_sc = P.op("tensor", lambda e, p=p, e_par=e_par, kv=kv, s=s, ci=ci: e.matmul(
                        ps_s[s // 4][:, 128 * (s % 4):128 * (s % 4) + 128],
                        kT_sb[64 * (kv % 2):64 * (kv % 2) + 64, kv // 2,
                              128 * ci:128 * ci + 128],
                        qp[64 * e_par:64 * e_par + 64, p, 128 * ci:128 * ci + 128],
                        start=True, stop=True),
                        waits=sw0 if first_s else (), inc=T if p == 7 else None)
                    first_s = False
                P.op("vector", lambda e, e_par=e_par: e.tensor_mul(
                    S_bf[:, 1024 * e_par:1024 * e_par + 512], ps_s[2 * e_par][:, :],
                    maskT[:, 1024 * e_par:1024 * e_par + 512]), waits=[tt_sc], inc=V)
                tv_mask[e_par] = P.op("vector", lambda e, e_par=e_par: e.tensor_mul(
                    S_bf[:, 1024 * e_par + 512:1024 * (e_par + 1)],
                    ps_s[2 * e_par + 1][:, :],
                    maskT[:, 1024 * e_par + 512:1024 * (e_par + 1)]), inc=V)
                if e_par == 0:
                    o2w = [tv_qdq, tv_st]
                    if prev_ts_ot1 is not None:
                        o2w.append(prev_ts_ot1)
                    for p in range(8):
                        P.op("tensor", lambda e, p=p: e.matmul(
                            ps_k[p // 4][:, 128 * (p % 4):128 * (p % 4) + 128],
                            kq_scr[:, 128 * p:128 * p + 128],
                            st_true[:, 128 * p:128 * p + 128],
                            start=(p % 4 == 0), stop=False),
                            waits=o2w if p == 0 else ())
            prev_S_evac = tv_mask[1]
            prev_S_evac_g = [tv_mask[0], tv_mask[1]]
            tt_o = None
            for e_par in range(2):
                ow = [tv_mask[e_par], tok_v_ld[st]]
                first_o = True
                for p in range(8):
                    h = PAIRS[p][e_par]
                    kv = h // 4
                    s = p + 8 * e_par
                    tt_o = P.op("tensor", lambda e, p=p, e_par=e_par, kv=kv, s=s, ci=ci: e.matmul(
                        ps_k[p // 4][:, 128 * (p % 4) + 64 * e_par:
                                     128 * (p % 4) + 64 * e_par + 64],
                        S_bf[:, 128 * s:128 * s + 128],
                        v_bf[:, ci, 64 * kv:64 * kv + 64],
                        start=False, stop=(e_par == 1 and p % 4 == 3)),
                        waits=ow if first_o else (),
                        inc=T if (e_par == 1 and p == 7) else None)
                    first_o = False
            ts_ot0 = P.op("scalar", lambda e: e.copy(
                out=S_bf[:, 0:512], in_=ps_k[0][:, :]),
                waits=[tt_o], inc=S)
            ts_ot1 = P.op("scalar", lambda e: e.copy(
                out=S_bf[:, 512:1024], in_=ps_k[1][:, :]), inc=S)
            prev_ts_ot1 = ts_ot1
            prev_tt_o = tt_o
            for j in range(8):
                tt_tra = P.op("tensor", lambda e, j=j: e.transpose(
                    ps_wt[:, 128 * j:128 * j + 128],
                    S_bf[:, 128 * j:128 * j + 128], identbf[:, :]),
                    waits=([ts_ot0, ts_ot1, prev_ts_attn] if j == 0 else ()),
                    inc=T if j == 7 else None)
            prev_o_evac = P.op("scalar", lambda e, ci=ci: e.copy(
                out=attn_bf[:, :, 128 * (ci % 2):128 * (ci % 2) + 128],
                in_=ps_wt[:].rearrange("p (a b) -> p a b", a=8)),
                waits=[tt_tra], inc=S)
            prev_ts_attn = prev_o_evac
            if ci % 2 == 1:
                half = ci // 2
                for dch in range(8):
                    yps = ps_a if (half == 0 or dch % 2 == 0) else ps_s[3]
                    ww = [prev_o_evac, tok_woT]
                    if half == 0:
                        if dch > 0:
                            ww.append(tv_res1[dch - 1])
                        else:
                            ww.append(tv_sin)
                    else:
                        ww.append(prev_S_evac)
                        if dch > 1:
                            ww.append(tv_res1[dch - 2])
                        else:
                            ww.append(tv_res1[7])
                    for p in range(8):
                        tt_y = P.op("tensor", lambda e, p=p, dch=dch, yps=yps: e.matmul(
                            yps[:, 0:256], woT[:, p, 128 * dch:128 * dch + 128],
                            attn_bf[:, p, :],
                            start=(p == 0), stop=(p == 7)),
                            waits=ww if p == 0 else (), inc=T if p == 7 else None)
                    tv_res1[dch] = P.op("vector",
                                        lambda e, dch=dch, half=half, xcur=xcur,
                                        yps=yps: e.tensor_add(
                        xcur[:, dch, 256 * half:256 * half + 256],
                        xcur[:, dch, 256 * half:256 * half + 256],
                        yps[:, 0:256]), waits=[tt_y], inc=V)
        if st + 1 < N_ST:
            tok_kt_ld[st + 1] = P.dma("sync", kT_sb[:], kt_h[st + 1],
                                      waits=[prev_o_evac], inc="s_ktl")
            tok_v_ld[st + 1] = P.dma("sync", v_bf[:], v_h[st + 1],
                                     waits=[prev_o_evac], inc="s_vvl")
        if p2stage <= 3:
            P.op("gpsimd", lambda e: e.nop(), waits=[tv_res1[7]])
            with nc.Block() as block:
                P.emit(block)
            stack.close()
            return nc
        tv_c2 = [None, None]
        for h2 in range(2):
            tv_c2[h2] = P.op("vector", lambda e, h2=h2, xcur=xcur: e.tensor_copy(
                x_bf[:, 4 * h2:4 * h2 + 4, :].rearrange("p a b -> p (a b)"),
                xcur[:, 4 * h2:4 * h2 + 4, :].rearrange("p a b -> p (a b)")),
                waits=[tv_res1[4 * h2 + 3]], inc=V)
        gq = [None, None]
        for h2 in range(2):
            xv = xcur[:, 4 * h2:4 * h2 + 4, :].rearrange("p a b -> p (a b)")
            gq[h2] = P.op("gpsimd", lambda e, h2=h2, xv=xv: e.tensor_mul(
                qp[:, 4 * h2:4 * h2 + 4, :].rearrange("p a b -> p (a b)"), xv, xv),
                waits=[tv_res1[4 * h2 + 3], tt_sc], inc="s_q")
        tv_xbf2 = tv_c2[1]
        prev_tv_xbf2 = tv_xbf2
        for k in range(8):
            P.op("tensor", lambda e, k=k: e.matmul(
                ps_a[0:1, :], onescol[:, 0:1], x_bf[:, k, :],
                start=(k == 0), stop=(k == 7)),
                waits=[tv_c2[k // 4]] if k % 4 == 0 else ())
        for k in range(8):
            tt_stat = P.op("tensor", lambda e, k=k: e.matmul(
                ps_k[0][0:1, :], onescol[:, 0:1], qp[:, k, :],
                start=(k == 0), stop=(k == 7)),
                waits=[gq[0], gq[1]] if k == 0 else (),
                inc=T if k == 7 else None)
        tok_qp_free = tt_stat
        ts1 = P.op("scalar", lambda e: e.activation(
            out=negm2_row[0:1, :], in_=ps_a[0:1, :], func=AF.Copy, scale=-1.0 / D),
            waits=[tt_stat], inc=S)
        tv_m2 = P.op("scalar", lambda e: e.activation(
            out=srow[0:1, 1, :], in_=negm2_row[0:1, :], func=AF.Square),
            waits=[ts1], inc=S)
        tv_var = P.op("vector", lambda e: e.scalar_tensor_tensor(
            srow[0:1, 0, :], ps_k[0][0:1, :], 1.0 / D, srow[0:1, 1, :],
            Alu.mult, Alu.subtract),
            waits=[tt_stat, tv_m2], inc=V)
        ts3 = P.op("scalar", lambda e: e.activation(
            out=srow[0:1, 1, :], in_=srow[0:1, 0, :], func=AF.Ln, bias=EPS),
            waits=[tv_var, tok_const], inc=S)
        ts_rs2 = P.op("scalar", lambda e: e.activation(
            out=rs2_row[0:1, :], in_=srow[0:1, 1, :], func=AF.Exp, scale=-0.5),
            waits=[ts3], inc=S)
        ts_nm2 = P.op("scalar", lambda e: e.copy(
            out=negm2b_row[0:1, :], in_=negm2_row[0:1, :]), waits=[ts1], inc=S)
        tt_rsb2 = P.op("tensor", lambda e: e.matmul(
            ps_s[2][:, :], onesf[0:1, 0:128], rs2_row[0:1, :], start=True, stop=True),
            waits=[ts_rs2, prev_S_evac], inc=T)
        ts_rs2sb = P.op("scalar", lambda e: e.copy(out=rs2_sb[:], in_=ps_s[2][:, :]),
                        waits=[tt_rsb2, tv_q[7]], inc=S)
        if p2stage <= 4:
            P.op("gpsimd", lambda e: e.nop(), waits=[ts_rs2sb])
            with nc.Block() as block:
                P.emit(block)
            stack.close()
            return nc
        tv_P = [None, None]
        tt_P = [None, None]
        for r in range(2):
            pps = ps_a if r == 0 else ps_s[3]
            pw = [tv_xbf2, tok_rsum_u1]
            if r == 0:
                pw.append(tv_res1[6])
            else:
                pw.append(tv_res1[7])
            for k in range(8):
                P.op("tensor", lambda e, k=k, r=r, pps=pps: e.matmul(
                    pps[:, :], u1T[:, k, 128 * r:128 * r + 128], x_bf[:, k, :],
                    start=(k == 0), stop=False), waits=pw if k == 0 else ())
        for r in range(2):
            pps = ps_a if r == 0 else ps_s[3]
            tt_P[r] = P.op("tensor", lambda e, r=r, pps=pps: e.matmul(
                pps[:, :], rsum_u1[0:1, 128 * r:128 * r + 128], negm2b_row[0:1, :],
                start=False, stop=True), waits=[ts_nm2] if r == 0 else (), inc=T)
            tv_P[r] = P.op("vector", lambda e, r=r, pps=pps: e.tensor_mul(
                P_bf[:, r, :], pps[:, :], rs2_sb[:]),
                waits=[tt_P[r], ts_rs2sb], inc=V)
        prev_tt_P = tt_P[1]
        ts_H = [None] * 16
        first_q_bank = [True, True]

        def emit_v1(g):
            buf = g % 2
            gw = [tv_P[1], tok_v1]
            if g >= 2:
                gw.append(ts_H[g - 2])
            elif g < 2:
                gw.append(prev_S_evac)
            for j in range(2):
                ff = 2 * g + j
                for r in range(2):
                    tt = P.op("tensor", lambda e, ff=ff, r=r, buf=buf, j=j: e.matmul(
                        ps_s[2 * buf + j][:, :],
                        v1T[:, r, 128 * ff:128 * ff + 128], P_bf[:, r, :],
                        start=(r == 0), stop=(r == 1)),
                        waits=gw if (j == 0 and r == 0) else (),
                        inc=T if (j == 1 and r == 1) else None)
            return tt

        def emit_gelu(g):
            buf = g % 2
            P.op("scalar", lambda e, g=g, buf=buf: e.activation(
                out=H_bf[:, 1024 * buf:1024 * buf + 512],
                in_=ps_s[2 * buf][:, :],
                func=AF.Gelu_apprx_tanh, bias=b1c[:, 2 * g:2 * g + 1]),
                waits=[tt_H_tok[g], tok_b1], inc=S)
            ts_H[g] = P.op("scalar", lambda e, g=g, buf=buf: e.activation(
                out=H_bf[:, 1024 * buf + 512:1024 * (buf + 1)],
                in_=ps_s[2 * buf + 1][:, :],
                func=AF.Gelu_apprx_tanh, bias=b1c[:, 2 * g + 1:2 * g + 2]), inc=S)

        def emit_u2(g):
            nonlocal_last = None
            buf = g % 2
            u2w = [ts_H[g], tok_u2[g // 4]]
            if g == 0:
                u2w.append(prev_o_evac)
            for j in range(2):
                ff = 2 * g + j
                for m in range(2):
                    stt = first_q_bank[m]
                    first_q_bank[m] = False
                    nonlocal_last = P.op("tensor", lambda e, ff=ff, m=m, buf=buf, j=j,
                                         stt=stt, stp=(g == 15 and j == 1): e.matmul(
                        ps_k[m][:, :],
                        u2T[:, ff, 128 * m:128 * m + 128],
                        H_bf[:, 1024 * buf + 512 * j:1024 * buf + 512 * j + 512],
                        start=stt, stop=stp),
                        waits=u2w if (j == 0 and m == 0) else (),
                        inc=T if (g == 15 and j == 1 and m == 1) else None)
            return nonlocal_last

        tt_H_tok = [None] * 16
        tt_u2last = None
        for g in range(16):
            tt_H_tok[g] = emit_v1(g)
            emit_gelu(g)
            if g >= 1:
                tt_u2last = emit_u2(g - 1)
        tt_u2last = emit_u2(15)
        tt_Q = tt_u2last
        P.op("scalar", lambda e: e.copy(
            out=Q_bf[:, 0, :], in_=ps_k[0][:, :]), waits=[tt_Q], inc=S)
        ts_Q = P.op("scalar", lambda e: e.copy(
            out=Q_bf[:, 1, :], in_=ps_k[1][:, :]), inc=S)
        if p2stage <= 5:
            P.op("gpsimd", lambda e: e.nop(), waits=[ts_Q])
            with nc.Block() as block:
                P.emit(block)
            stack.close()
            return nc
        tv_res2 = [None] * 8
        for dch in range(8):
            vps = ps_a if dch % 2 == 0 else ps_s[3]
            vw = [ts_Q, tok_v2]
            if dch > 1:
                vw.append(tv_res2[dch - 2])
            else:
                vw.append(tv_P[1])
                vw.append(tv_res1[7])
            for r in range(2):
                tt_v2 = P.op("tensor", lambda e, r=r, dch=dch, vps=vps: e.matmul(
                    vps[:, :], v2T[:, r, 128 * dch:128 * dch + 128], Q_bf[:, r, :],
                    start=(r == 0), stop=(r == 1)),
                    waits=vw if r == 0 else (), inc=T if r == 1 else None)
            tv_res2[dch] = P.op("vector",
                                lambda e, dch=dch, xcur=xcur,
                                vps=vps: e.scalar_tensor_tensor(
                xcur[:, dch, :], vps[:, :], b2c[:, dch:dch + 1], xcur[:, dch, :],
                Alu.add, Alu.add), waits=[tt_v2], inc=V)
        tok_out[st] = P.dma("sync", out[st], xcur[:], waits=[tv_res2[7]],
                            inc=f"s_out{st % 2}")
        tok_xb2_free[st] = tok_out[st]
        tok_ctab2_free = tv_q[7]
        prev_psa_reader = ts_Q
        prev_tv_res2_last = tv_res2[7]
        prev_tsQ = ts_Q
        prev_tv_P1 = tv_P[1]
        tok_rows_used[st] = tt_rsb
        prev_negmb_user = tt_rsb

    P.op("sync", lambda e: e.nop(),
         waits=[("s_out0", P.count["s_out0"]), ("s_out1", P.count["s_out1"])])
    prev_kt_spill = prev_v_spill

    with nc.Block() as block:
        P.emit(block)
    stack.close()
    return nc


_NC_CACHE = None


def kernel(L0_attn_x, L0_attn_Wq, L0_attn_Wk, L0_attn_Wv, L0_attn_Wo,
           L0_ffn_U1, L0_ffn_V1, L0_ffn_b1, L0_ffn_U2, L0_ffn_V2, L0_ffn_b2):
    global _NC_CACHE
    x = np.asarray(L0_attn_x, np.float32)
    B = x.shape[0]
    perm = _perm_rows()
    wq_p = np.ascontiguousarray(np.asarray(L0_attn_Wq, np.float32)[perm, :])
    wo_p = np.ascontiguousarray(np.asarray(L0_attn_Wo, np.float32)[:, perm])
    wk = np.ascontiguousarray(np.asarray(L0_attn_Wk, np.float32))
    wv = np.ascontiguousarray(np.asarray(L0_attn_Wv, np.float32))
    u1 = np.ascontiguousarray(np.asarray(L0_ffn_U1, np.float32))
    v1 = np.ascontiguousarray(np.asarray(L0_ffn_V1, np.float32))
    u2 = np.ascontiguousarray(np.asarray(L0_ffn_U2, np.float32))
    v2 = np.ascontiguousarray(np.asarray(L0_ffn_V2, np.float32))
    b1c = np.ascontiguousarray(np.asarray(L0_ffn_b1, np.float32).reshape(32, 128).T)
    b2c = np.ascontiguousarray(np.asarray(L0_ffn_b2, np.float32).reshape(8, 128).T)

    consts = [_host_consts(0), _host_consts(1)]
    if _NC_CACHE is None:
        _NC_CACHE = build_nc()
    nc = _NC_CACHE

    in_maps = []
    for c in range(8):
        b, h = c // 2, c % 2
        xs = np.ascontiguousarray(x[b, h * S_CORE:(h + 1) * S_CORE, :].T)
        xtl = np.ascontiguousarray(
            xs.reshape(8, 128, N_ST, ST).transpose(2, 1, 0, 3))
        cc = consts[h]
        in_maps.append(dict(
            xt=xtl, wq=wq_p, wk=wk, wv=wv, wo=wo_p, u1=u1, v1=v1, u2=u2, v2=v2,
            b1c=b1c, b2c=b2c, mt=cc["mt"], identbf=cc["identbf"], mask=cc["mask"],
            qdec=cc["qdec"], kdec=cc["kdec"], bdb=cc["bdb"],
            costab=cc["costab"], sintab=cc["sintab"], rmask=cc["rmask"],
            onesf=cc["onesf"], onesb=cc["onesb"], onescol=cc["onescol"],
            epsc=cc["epsc"],
        ))

    res = run_bass_kernel_spmd(nc, in_maps, core_ids=list(range(8)))

    outp = np.empty((B, 8192, D), np.float32)
    for c in range(8):
        b, h = c // 2, c % 2
        arr = np.asarray(res.results[c]["out"])
        sl = arr.transpose(0, 3, 2, 1).reshape(S_CORE, D)
        outp[b, h * S_CORE:(h + 1) * S_CORE, :] = sl
    return outp


# version 11
# speedup vs baseline: 1.0310x; 1.0310x over previous
import math

import numpy as np
import ml_dtypes

import concourse.bass as bass
import concourse.mybir as mybir
from concourse.bass_utils import run_bass_kernel_spmd

F32 = mybir.dt.float32
BF16 = mybir.dt.bfloat16
AF = mybir.ActivationFunctionType
Alu = mybir.AluOpType

BF = ml_dtypes.bfloat16

D = 1024
HD = 64
NH = 16
NKV = 4
CH = 128
ST = 512
N_ST = 8
N_CH = 32
CPST = ST // CH
S_CORE = 4096
THETA = 10000.0
EPS = 1e-5

PAIRS = [(0, 4), (1, 5), (2, 6), (3, 7), (8, 12), (9, 13), (10, 14), (11, 15)]
HEAD_POS = {}
for _p, (_lo, _hi) in enumerate(PAIRS):
    HEAD_POS[_lo] = (_p, 0)
    HEAD_POS[_hi] = (_p, 1)
SLOPES = [2.0 ** (-8.0 / NH * (h + 1)) for h in range(NH)]


def _rope_tables():
    inv = 1.0 / THETA ** (np.arange(0, HD, 2, dtype=np.float64) / HD)
    freqs = np.outer(np.arange(8192, dtype=np.float64), inv)
    emb = np.concatenate([freqs, freqs], axis=-1)
    return np.cos(emb).astype(np.float32), np.sin(emb).astype(np.float32)


_COS64, _SIN64 = _rope_tables()


def _host_consts(half):
    c = {}
    M = np.zeros((128, 128), np.float32)
    for blk in range(2):
        b0 = 64 * blk
        for i in range(32):
            M[b0 + i, b0 + 32 + i] = -1.0
            M[b0 + 32 + i, b0 + i] = 1.0
    c["mt"] = np.ascontiguousarray(M.T).astype(BF)
    c["identbf"] = np.eye(128, dtype=np.float32).astype(BF)
    mask = np.zeros((128, 16 * 128), np.float32)
    t = np.arange(CH, dtype=np.float64)
    for p in range(8):
        for e in range(2):
            h = PAIRS[p][e]
            s = p + 8 * e
            m = np.where(t[:, None] >= t[None, :],
                         np.exp(-SLOPES[h] * (t[:, None] - t[None, :])), 0.0)
            mask[:, 128 * s:128 * s + 128] = m.T.astype(np.float32)
    c["mask"] = mask.astype(BF)
    qd = np.zeros((128, 8, 128), np.float32)
    for p in range(8):
        for e in range(2):
            h = PAIRS[p][e]
            qd[64 * e:64 * e + 64, p, :] = np.exp(-SLOPES[h] * (t + 1.0))[None, :]
    c["qdec"] = qd.astype(BF)
    kd = np.zeros((128, 8, 2, 64), np.float32)
    for p in range(8):
        for e in range(2):
            h = PAIRS[p][e]
            kd[:, p, e, :] = np.exp(-SLOPES[h] * (CH - 1.0 - t))[:, None]
    c["kdec"] = kd.reshape(128, NH * 64).astype(BF)
    bd = np.zeros((128, 8, 128), np.float32)
    for p in range(8):
        for e in range(2):
            h = PAIRS[p][e]
            bd[:, p, 64 * e:64 * e + 64] = math.exp(-SLOPES[h] * CH)
    c["bdf"] = bd.reshape(128, 1024).copy()
    c["bdb"] = c["bdf"].astype(BF)
    pos0 = half * S_CORE
    cos = np.zeros((N_ST, 128, ST), np.float32)
    sin = np.zeros((N_ST, 128, ST), np.float32)
    for st in range(N_ST):
        pp = np.arange(ST) + pos0 + st * ST
        cs = _COS64[pp, :].T
        sn = _SIN64[pp, :].T
        cos[st, 0:64] = cs
        cos[st, 64:128] = cs
        sin[st, 0:64] = sn
        sin[st, 64:128] = sn
    c["costab"] = cos
    c["sintab"] = sin
    rm = np.zeros((128, 2), np.float32)
    if half == 1:
        rm[:, 0] = 1.0
    c["rmask"] = rm
    c["onesf"] = np.ones((1, 128), np.float32)
    c["epsc"] = np.full((1, 1), EPS, np.float32)
    c["onesb"] = np.ones((1, 512), np.float32).astype(BF)
    c["onescol"] = np.ones((128, 1), np.float32).astype(BF)
    return c


def _perm_rows():
    perm = []
    for (lo, hi) in PAIRS:
        perm.extend(range(64 * lo, 64 * lo + 64))
        perm.extend(range(64 * hi, 64 * hi + 64))
    return np.array(perm, np.int64)


class Prog:
    def __init__(self, nc, sems):
        self.nc = nc
        self.ops = {e: [] for e in ("sync", "gpsimd", "tensor", "vector", "scalar")}
        self.count = {s: 0 for s in sems}
        self.sems = sems

    def op(self, eng, fn, waits=(), inc=None, inc_n=1):
        tok = None
        if inc is not None:
            self.count[inc] += inc_n
            tok = (inc, self.count[inc])
        waits = tuple(w for w in waits if w is not None)
        self.ops[eng].append((fn, waits, inc, inc_n))
        return tok

    def dma(self, eng, out, in_, waits=(), inc=None, **kw):
        return self.op(eng, lambda e: e.dma_start(out=out, in_=in_, **kw),
                       waits=waits, inc=inc, inc_n=16)

    def emit(self, block):
        def make(eng_name):
            ops = self.ops[eng_name]
            sems = self.sems

            def body(eng):
                wm = {}
                for fn, waits, inc, inc_n in ops:
                    for (sn, val) in waits:
                        if wm.get(sn, -1) >= val:
                            continue
                        eng.wait_ge(sems[sn], val)
                        wm[sn] = val
                    instr = fn(eng)
                    if inc is not None:
                        if instr is None:
                            instr = eng.nop()
                        instr.then_inc(sems[inc], inc_n)
            return body

        for name in ("sync", "gpsimd", "tensor", "vector", "scalar"):
            getattr(block, name)(make(name))


def build_nc(n_st1=N_ST, do_coll=True, n_st2=N_ST, p2stage=9):
    nc = bass.Bass("TRN2", target_bir_lowering=False)

    xt = nc.dram_tensor("xt", [N_ST, 128, 8, ST], F32, kind="ExternalInput").ap()
    out = nc.dram_tensor("out", [N_ST, 128, 8, ST], F32, kind="ExternalOutput").ap()
    wq = nc.dram_tensor("wq", [D, D], F32, kind="ExternalInput").ap()
    wk = nc.dram_tensor("wk", [256, D], F32, kind="ExternalInput").ap()
    wv = nc.dram_tensor("wv", [256, D], F32, kind="ExternalInput").ap()
    wo = nc.dram_tensor("wo", [D, D], F32, kind="ExternalInput").ap()
    u1 = nc.dram_tensor("u1", [D, 256], F32, kind="ExternalInput").ap()
    v1 = nc.dram_tensor("v1", [256, 4096], F32, kind="ExternalInput").ap()
    u2 = nc.dram_tensor("u2", [4096, 256], F32, kind="ExternalInput").ap()
    v2 = nc.dram_tensor("v2", [256, D], F32, kind="ExternalInput").ap()
    b1c_d = nc.dram_tensor("b1c", [128, 32], F32, kind="ExternalInput").ap()
    b2c_d = nc.dram_tensor("b2c", [128, 8], F32, kind="ExternalInput").ap()
    mt_d = nc.dram_tensor("mt", [128, 128], BF16, kind="ExternalInput").ap()
    identbf_d = nc.dram_tensor("identbf", [128, 128], BF16, kind="ExternalInput").ap()
    mask_d = nc.dram_tensor("mask", [128, 2048], BF16, kind="ExternalInput").ap()
    qdec_d = nc.dram_tensor("qdec", [128, 8, 128], BF16, kind="ExternalInput").ap()
    kdec_d = nc.dram_tensor("kdec", [128, 1024], BF16, kind="ExternalInput").ap()
    bdb_d = nc.dram_tensor("bdb", [128, 1024], BF16, kind="ExternalInput").ap()
    costab_d = nc.dram_tensor("costab", [N_ST, 128, ST], F32, kind="ExternalInput").ap()
    sintab_d = nc.dram_tensor("sintab", [N_ST, 128, ST], F32, kind="ExternalInput").ap()
    rmask_d = nc.dram_tensor("rmask", [128, 2], F32, kind="ExternalInput").ap()
    onesf_d = nc.dram_tensor("onesf", [1, 128], F32, kind="ExternalInput").ap()
    epsc_d = nc.dram_tensor("epsc", [1, 1], F32, kind="ExternalInput").ap()
    onesb_d = nc.dram_tensor("onesb", [1, 512], BF16, kind="ExternalInput").ap()
    onescol_d = nc.dram_tensor("onescol", [128, 1], BF16, kind="ExternalInput").ap()

    wqbf_h = nc.dram_tensor("wqbf_h", [D, D], BF16, kind="Internal").ap()
    wobf_h = nc.dram_tensor("wobf_h", [D, D], BF16, kind="Internal").ap()
    wkbf_h = nc.dram_tensor("wkbf_h", [256, D], BF16, kind="Internal").ap()
    wvbf_h = nc.dram_tensor("wvbf_h", [256, D], BF16, kind="Internal").ap()
    st_h = nc.dram_tensor("st_h", [N_CH, 128, 1024], BF16, kind="Internal").ap()
    kt_h = nc.dram_tensor("kt_h", [N_ST, 128, 2, ST], BF16, kind="Internal").ap()
    v_h = nc.dram_tensor("v_h", [N_ST, 128, CPST, 256], BF16, kind="Internal").ap()
    rows_h = nc.dram_tensor("rows_h", [N_ST, 2, 512], F32, kind="Internal").ap()
    src_h = nc.dram_tensor("src_h", [128, 1024], F32, kind="Internal").ap()
    dst_h = nc.dram_tensor("dst_h", [256, 1024], F32, kind="Internal").ap()

    import contextlib
    stack = contextlib.ExitStack()
    sb = lambda name, shape, dt: stack.enter_context(nc.sbuf_tensor(name, shape, dt))
    psb = lambda name, shape, dt: stack.enter_context(nc.psum_tensor(name, shape, dt))
    sem = lambda name: stack.enter_context(nc.semaphore(name))

    wqT = sb("wqT", [128, 8, 1024], BF16)
    woT = sb("woT", [128, 8, 1024], BF16)
    wkT = sb("wkT", [128, 8, 256], BF16)
    wvT = sb("wvT", [128, 8, 256], BF16)
    u1T = sb("u1T", [128, 8, 256], BF16)
    v1T = sb("v1T", [128, 2, 4096], BF16)
    u2T = sb("u2T", [128, 32, 256], BF16)
    v2T = sb("v2T", [128, 2, 1024], BF16)
    b1c = sb("b1c_s", [128, 32], F32)
    b2c = sb("b2c_s", [128, 8], F32)
    mt = sb("mt_s", [128, 128], BF16)
    identbf = sb("identbf_s", [128, 128], BF16)
    maskT = sb("maskT", [128, 2048], BF16)
    qdec = sb("qdec_s", [128, 8, 128], BF16)
    kdec = sb("kdec_s", [128, 1024], BF16)
    bdb = sb("bdb_s", [128, 1024], BF16)
    ctab = sb("ctab", [128, ST], F32)
    stab = sb("stab", [128, ST], F32)
    rmask = sb("rmask_s", [128, 2], F32)
    onesf = sb("onesf_s", [1, 128], F32)
    epsc = sb("epsc_s", [1, 1], F32)
    nc.const_aps.register(F32, EPS, epsc[0:1, 0:1])
    onesb = sb("onesb_s", [1, 512], BF16)
    onescol = sb("onescol_s", [128, 1], BF16)
    rsum_q = sb("rsum_q", [1, 1024], BF16)
    rsum_k = sb("rsum_k", [1, 256], BF16)
    rsum_v = sb("rsum_v", [1, 256], BF16)
    rsum_u1 = sb("rsum_u1", [1, 256], BF16)

    xb = [sb(f"xb{i}", [128, 8, ST], F32) for i in range(2)]
    x_bf = sb("x_bf", [128, 8, ST], BF16)
    rows_cur = sb("rows_cur", [1, 1, 2, 512], F32)
    negmb_row = sb("negmb_row", [1, 512], BF16)
    srow = sb("srow", [1, 2, 512], F32)
    rs2_row = sb("rs2_row", [1, 512], F32)
    negm2_row = sb("negm2_row", [1, 512], F32)
    negm2b_row = sb("negm2b_row", [1, 512], BF16)
    rs_col = sb("rs_col", [128, 4], F32)

    kraw_bf = sb("kraw_bf", [128, 2, 512], BF16)
    tmp1 = sb("tmp1", [128, 512], F32)
    tmp2 = sb("tmp2", [128, 512], F32)
    rs2_sb = tmp2
    kT_sb = sb("kT_sb", [128, 2, ST], BF16)
    kq_scr = sb("kq_scr", [128, 1024], BF16)
    ks_scr = sb("ks_scr", [128, 1024], BF16)
    ktok = kq_scr
    k16 = ks_scr
    st_true = ks_scr
    qdq3 = None
    v_bf = sb("v_bf", [128, CPST, 256], BF16)
    state_r = sb("state_r", [128, 2, 1024], BF16)
    fbf = sb("fbf", [128, 1024], BF16)
    rrun = sb("rrun", [128, 1024], BF16)
    st_ld = sb("st_ld", [128, 2, 1024], BF16)
    qp = sb("qp", [128, 8, ST], BF16)
    sh_bf = sb("sh_bf", [128, 2048], BF16)
    S_bf = sh_bf
    H_bf = sh_bf
    attn_bf = sb("attn_bf", [128, 8, 256], BF16)
    pq_bf = sb("pq_bf", [128, 2, 512], BF16)
    P_bf = pq_bf
    Q_bf = pq_bf

    ps_a = psb("ps_a", [128, 512], F32)
    ps_k = [psb("ps_k0", [128, 512], F32), psb("ps_k1", [128, 512], F32)]
    ps_s = [psb(f"ps_s{i}", [128, 512], F32) for i in range(4)]
    ps_wt = psb("ps_wt", [128, 1024], BF16)

    sems = {}
    for s in ("s_const", "s_wkT", "s_wvT", "s_wqT", "s_woT", "s_x0", "s_x1",
              "s_tabc", "s_tabs", "s_rows", "s_out0", "s_out1",
              "s_w1", "s_w2", "s_w3", "s_kt", "s_vv", "s_ktl", "s_vvl", "s_st0", "s_st1",
              "s_sp0", "s_sp1", "s_g", "s_q", "s_t", "s_v", "s_s", "s_c"):
        sems[s] = sem(s)
    P = Prog(nc, sems)

    T, V, S = "s_t", "s_v", "s_s"

    tok_const = None
    for dst, src in ((mt, mt_d), (identbf, identbf_d), (maskT, mask_d),
                     (qdec, qdec_d), (kdec, kdec_d), (bdb, bdb_d),
                     (rmask, rmask_d), (onesf, onesf_d), (epsc, epsc_d),
                     (onesb, onesb_d),
                     (onescol, onescol_d), (b2c, b2c_d)):
        tok_const = P.dma("sync", dst[:], src[:, :] if len(src.shape) == 2 else src[:],
                          inc="s_const")

    P.dma("gpsimd", wkbf_h[:, :], wk[:, :], inc="s_w1")
    tok_wkv_h = P.dma("gpsimd", wvbf_h[:, :], wv[:, :], inc="s_w1")
    P.dma("gpsimd", u1T[:], u1.rearrange("(k p) r -> p k r", p=128), inc="s_w2")
    P.dma("gpsimd", v1T[:], v1.rearrange("(k p) f -> p k f", p=128), inc="s_w2")
    for g in range(4):
        P.dma("gpsimd", u2T[:, 8 * g:8 * (g + 1), :],
              u2.rearrange("(k p) r -> p k r", p=128)[:, 8 * g:8 * (g + 1), :],
              inc="s_w2")
    P.dma("gpsimd", v2T[:], v2.rearrange("(k p) d -> p k d", p=128), inc="s_w2")
    tok_w2 = P.dma("gpsimd", b1c[:], b1c_d[:, :], inc="s_w2")
    tok_u1 = tok_v1 = tok_v2 = tok_b1 = tok_w2
    tok_u2 = [tok_w2] * 4
    P.dma("gpsimd", wqbf_h[:, :], wq[:, :], inc="s_w3")
    tok_w3 = P.dma("gpsimd", wobf_h[:, :], wo[:, :], inc="s_w3")
    tok_wq_h = tok_wo_h = tok_w3

    tok_wkT = tok_wvT = tok_wqT = tok_woT = None
    for k in range(8):
        tok_wkT = P.op("sync", lambda e, k=k: e.dma_start_transpose(
            wkT[:, k, :], wkbf_h[:, 128 * k:128 * (k + 1)]),
            waits=[tok_wkv_h], inc="s_wkT", inc_n=16)
    for k in range(8):
        tok_wvT = P.op("sync", lambda e, k=k: e.dma_start_transpose(
            wvT[:, k, :], wvbf_h[:, 128 * k:128 * (k + 1)]),
            waits=[tok_wkv_h], inc="s_wvT", inc_n=16)

    def rowsum(dst_row, srcT, nsl, waits):
        prev_evac = None
        for (a, b) in nsl:
            w0 = list(waits) + ([prev_evac] if prev_evac else [])
            for k in range(8):
                tk = P.op("tensor", lambda e, k=k, a=a, b=b: e.matmul(
                    ps_a[0:1, 0:b - a], onescol[:, 0:1], srcT[:, k, a:b],
                    start=(k == 0), stop=(k == 7)),
                    waits=w0 if k == 0 else (), inc=T if k == 7 else None)
            prev_evac = P.op("scalar", lambda e, a=a, b=b: e.copy(
                out=dst_row[0:1, a:b], in_=ps_a[0:1, 0:b - a]),
                waits=[tk], inc=S)
        return prev_evac

    tok_rsum_k = rowsum(rsum_k, wkT, [(0, 256)], [tok_wkT, tok_const])
    tok_rsum_v = rowsum(rsum_v, wvT, [(0, 256)], [tok_wvT, tok_rsum_k])

    tok_x = [None] * N_ST
    tok_x[0] = P.dma("sync", xb[0][:], xt[0], inc="s_x0")
    tok_state0 = P.op("vector", lambda e: e.memset(state_r[:], 0.0), inc=V)

    tok_xbf_done = [None] * (N_ST + 1)
    tok_xb_free = [None] * (N_ST + 1)
    prev_sp2 = None
    tok_rows_free = [None] * (N_ST + 1)
    prev_ts_nm = None
    prev_ts_rc = None
    prev_tv_sin = None
    prev_kt_spill = None
    prev_v_spill = None
    tok_ctab_free = None
    prev_state_tt = tok_state0
    prev_ts_v_last = tok_rsum_v
    prev_ts_tok = None

    def pre_sq(stv, sq_waits):
        xv1 = xb[stv % 2][:, 0:4, :].rearrange("p a b -> p (a b)")
        xv2 = xb[stv % 2][:, 4:8, :].rearrange("p a b -> p (a b)")
        P.op("gpsimd", lambda e, xv1=xv1: e.tensor_mul(
            attn_bf.rearrange("p a b -> p (a b)"), xv1, xv1),
            waits=sq_waits, inc="s_q")
        return P.op("gpsimd", lambda e, xv2=xv2: e.tensor_mul(
            sh_bf[:, :], xv2, xv2), inc="s_q")

    def pre_mm(mm_waits):
        for k in range(8):
            srcb = attn_bf if k < 4 else sh_bf
            off = 512 * (k % 4)
            tt_stat = P.op("tensor", lambda e, srcb=srcb, off=off, k=k: e.matmul(
                ps_s[0][0:1, :], onescol[:, 0:1],
                bass.AP(srcb, off, [[2048, 128], [1, 512]]),
                start=(k == 0), stop=(k == 7)),
                waits=mm_waits if k == 0 else (),
                inc=T if k == 7 else None)
        return tt_stat

    tsq_cur = pre_sq(0, [tok_x[0]])
    prev_tt_stat = pre_mm([tsq_cur, tok_const])
    prev_kraw_rd = [None, None]
    prev_addsA = None
    prev_addsB = None

    for st in range(n_st1):
        if st + 1 < n_st1:
            w = [tok_xb_free[st - 1]] if st >= 1 else []
            tok_x[st + 1] = P.dma("sync", xb[(st + 1) % 2][:], xt[st + 1],
                                  waits=w, inc=f"s_x{(st + 1) % 2}")
            tsq_next = pre_sq(st + 1, [tok_x[st + 1], prev_tt_stat])
        if st == 3:
            for k in range(8):
                tok_wqT = P.op("sync", lambda e, k=k: e.dma_start_transpose(
                    wqT[:, k, :], wqbf_h[:, 128 * k:128 * (k + 1)]),
                    waits=[tok_w3], inc="s_wqT", inc_n=16)
            for k in range(8):
                tok_woT = P.op("sync", lambda e, k=k: e.dma_start_transpose(
                    woT[:, k, :], wobf_h[:, 128 * k:128 * (k + 1)]),
                    waits=[tok_w3], inc="s_woT", inc_n=16)
        xcur = xb[st % 2]
        tv_xbf1 = P.op("scalar", lambda e, xcur=xcur: e.activation(
            out=x_bf[:, 0:4, :].rearrange("p a b -> p (a b)"),
            in_=xcur[:, 0:4, :].rearrange("p a b -> p (a b)"), func=AF.Copy),
            waits=[tok_x[st], tok_xbf_done[st - 1] if st >= 1 else None],
            inc=S)
        tv_xbf = P.op("scalar", lambda e, xcur=xcur: e.activation(
            out=x_bf[:, 4:8, :].rearrange("p a b -> p (a b)"),
            in_=xcur[:, 4:8, :].rearrange("p a b -> p (a b)"), func=AF.Copy),
            waits=[tv_xbf1], inc=S)
        tt_stat = prev_tt_stat
        stat_waits = [tv_xbf1, prev_ts_v_last]
        for k in range(8):
            tt_sum = P.op("tensor", lambda e, k=k: e.matmul(
                ps_a[0:1, :], onescol[:, 0:1], x_bf[:, k, :],
                start=(k == 0), stop=(k == 7)),
                waits=stat_waits if k == 0 else ([tv_xbf] if k == 4 else ()),
                inc=T if k == 7 else None)
        s_ring = 0
        ts1 = P.op("scalar", lambda e, s_ring=s_ring: e.activation(
            out=rows_cur[0:1, s_ring, 0, :], in_=ps_a[0:1, :], func=AF.Copy,
            scale=-1.0 / D),
            waits=[tt_sum, tok_rows_free[st - 1] if st >= 1 else None,
                   ("s_rows", 16 * st) if st >= 1 else None,
                   prev_ts_nm], inc=S)
        tv_m2 = P.op("scalar", lambda e, s_ring=s_ring: e.activation(
            out=srow[0:1, 1, :], in_=rows_cur[0:1, s_ring, 0, :], func=AF.Square),
            waits=[ts1], inc=S)
        tv_var = P.op("vector", lambda e: e.scalar_tensor_tensor(
            srow[0:1, 0, :], ps_s[0][0:1, :], 1.0 / D, srow[0:1, 1, :],
            Alu.mult, Alu.subtract),
            waits=[tt_stat, tv_m2], inc=V)
        ts3 = P.op("scalar", lambda e: e.activation(
            out=srow[0:1, 1, :], in_=srow[0:1, 0, :], func=AF.Ln, bias=EPS),
            waits=[tv_var, tok_const], inc=S)
        ts_rs = P.op("scalar", lambda e, s_ring=s_ring: e.activation(
            out=rows_cur[0:1, s_ring, 1, :], in_=srow[0:1, 1, :], func=AF.Exp,
            scale=-0.5), waits=[ts3], inc=S)
        ts_nm = P.op("scalar", lambda e, s_ring=s_ring: e.copy(
            out=negmb_row[0:1, :], in_=rows_cur[0:1, s_ring, 0, :]),
            waits=[ts1], inc=S)
        prev_ts_nm = ts_nm
        tok_rows_sp = P.dma("sync",
                            rows_h[st].rearrange("a b -> (a b)").unsqueeze(0),
                            rows_cur[0:1, s_ring].rearrange("p a b -> p (a b)"),
                            waits=[ts_rs], inc="s_rows")
        tt_rsb = P.op("tensor", lambda e, s_ring=s_ring: e.matmul(
            ps_s[3][:, :], onesf[0:1, 0:128], rows_cur[0:1, s_ring, 1, :],
            start=True, stop=True),
            waits=[ts_rs, prev_tv_sin, prev_addsB], inc=T)
        for i in range(4):
            tt_rsc = P.op("tensor", lambda e, i=i, s_ring=s_ring: e.matmul(
                ps_s[2][:, i:i + 1], rows_cur[0:1, s_ring, 1, 128 * i:128 * i + 128],
                onesf[0:1, 0:1], start=True, stop=True),
                waits=[prev_ts_rc, prev_addsB] if i == 0 else (),
                inc=T if i == 3 else None)
        tok_tab = P.dma("sync", ctab[:], costab_d[st],
                        waits=[tok_ctab_free], inc="s_tabc")
        tok_tab2 = P.dma("sync", stab[:], sintab_d[st],
                         waits=[tok_ctab_free], inc="s_tabs")
        tv_cos = P.op("vector", lambda e: e.tensor_mul(ctab[:], ctab[:], ps_s[3][:, :]),
                      waits=[tt_rsb, tok_tab, tok_tab2], inc=V)
        tv_sin = P.op("vector", lambda e: e.tensor_mul(stab[:], stab[:], ps_s[3][:, :]),
                      waits=[tt_rsb, tok_tab2], inc=V)
        prev_tv_sin = tv_sin
        ts_rc = P.op("scalar", lambda e: e.copy(out=rs_col[:], in_=ps_s[2][:, 0:4]),
                     waits=[tt_rsc], inc=S)
        prev_ts_rc = ts_rc
        tv_k = [None, None]
        ts_k_prev = None
        tv1_prev = None
        tv2_prev = None
        tt_rot_prev = None
        for r in range(2):
            kps = ps_a if r == 0 else ps_s[0]
            kw = [ts1 if r == 0 else tv_var, tv_xbf, tok_wkT]
            for k in range(8):
                P.op("tensor", lambda e, k=k, r=r, kps=kps: e.matmul(
                    kps[:, :], wkT[:, k, 128 * r:128 * r + 128], x_bf[:, k, :],
                    start=(k == 0), stop=False), waits=kw if k == 0 else ())
            tt_k = P.op("tensor", lambda e, r=r, kps=kps: e.matmul(
                kps[:, :], rsum_k[0:1, 128 * r:128 * r + 128], negmb_row[0:1, :],
                start=False, stop=True), waits=[ts_nm], inc=T)
            skw = [tt_k]
            if prev_kraw_rd[r] is not None:
                skw += prev_kraw_rd[r]
            ts_k = P.op("scalar", lambda e, r=r, kps=kps: e.copy(
                out=kraw_bf[:, r, :], in_=kps[:, :]), waits=skw, inc=S)
            tt_rot = P.op("tensor", lambda e, r=r: e.matmul(
                ps_s[1][:, :], mt[:, :], kraw_bf[:, r, :], start=True, stop=True),
                waits=[ts_k, tv2_prev], inc=T)
            tv1 = P.op("vector", lambda e, r=r: e.tensor_mul(
                tmp1[:], kraw_bf[:, r, :], ctab[:]),
                waits=[ts_k, tv_cos, tv_k[0] if r == 1 else None], inc=V)
            tv2 = P.op("vector", lambda e: e.tensor_mul(tmp2[:], ps_s[1][:, :], stab[:]),
                       waits=[tt_rot, tv_sin], inc=V)
            tv_k[r] = P.op("vector", lambda e, r=r: e.tensor_add(
                kT_sb[:, r, :], tmp1[:], tmp2[:]),
                waits=[tv1, tv2, prev_kt_spill], inc=V)
            prev_kraw_rd[r] = [tv1, tt_rot]
            ts_k_prev, tv1_prev, tv2_prev, tt_rot_prev = ts_k, tv1, tv2, tt_rot
        prev_kt_spill = P.dma("gpsimd", kt_h[st], kT_sb[:],
                              waits=[tv_k[0], tv_k[1]], inc="s_kt")
        trw = [tv_k[0], tv_k[1], tok_const]
        if prev_ts_tok is not None:
            trw.append(prev_ts_tok)
        first_tr = True
        for r in range(2):
            for i in range(4):
                tt_tr = P.op("tensor", lambda e, r=r, i=i: e.transpose(
                    ps_wt[:, 256 * i + 128 * r: 256 * i + 128 * r + 128],
                    kT_sb[:, r, 128 * i:128 * i + 128], identbf[:, :]),
                    waits=trw if first_tr else (),
                    inc=T if (r == 1 and i == 3) else None)
                first_tr = False
        prev_ts_tok = P.op("scalar", lambda e: e.copy(
            out=ktok[:, :], in_=ps_wt[:, :]),
            waits=[tt_tr], inc=S)
        ts_tok = prev_ts_tok
        ts_v = [None] * CPST
        for i in range(CPST):
            vps = ps_a if i % 2 == 0 else ps_s[1]
            vw = [tok_wvT]
            if i > 1:
                vw.append(ts_v[i - 2])
            elif i == 0:
                vw.append(ts_k_prev)
            else:
                vw.append(tv2_prev)
            for k in range(8):
                P.op("tensor", lambda e, k=k, i=i, vps=vps: e.matmul(
                    vps[:, 0:256], x_bf[:, k, 128 * i:128 * i + 128], wvT[:, k, :],
                    start=(k == 0), stop=False), waits=vw if k == 0 else ())
            tt_v = P.op("tensor", lambda e, i=i, vps=vps: e.matmul(
                vps[:, 0:256], negmb_row[0:1, 128 * i:128 * i + 128],
                rsum_v[0:1, :], start=False, stop=True), inc=T)
            ts_v[i] = P.op("scalar", lambda e, i=i, vps=vps: e.activation(
                out=v_bf[:, i, :], in_=vps[:, 0:256], func=AF.Copy,
                scale=rs_col[:, i:i + 1]),
                waits=[tt_v, ts_rc, prev_v_spill], inc=S)
        prev_v_spill = P.dma("gpsimd", v_h[st], v_bf[:], waits=[ts_v[CPST - 1]],
                             inc="s_vv")
        tok_xbf_done[st] = tt_v
        prev_ts_v_last = ts_v[CPST - 1]
        if st + 1 < n_st1:
            prev_tt_stat = pre_mm([tsq_next, tv_var, ts_k_prev])
        if st == 0:
            P.op("vector", lambda e: e.memset(ps_k[0][:, :], 0.0),
                 waits=[prev_state_tt], inc=V)
            tv_z = P.op("vector", lambda e: e.memset(ps_k[1][:, :], 0.0), inc=V)
        for ci in range(CPST):
            c = CPST * st + ci
            pair = [ps_k[0], ps_k[1]] if ci % 2 == 0 else [ps_s[2], ps_s[3]]
            tok_sp = P.dma("sync", st_h[c], state_r[:, c % 2, :],
                           waits=[prev_state_tt], inc=f"s_sp{c % 2}")
            tv_k16 = P.op("vector", lambda e, ci=ci: e.tensor_tensor(
                bass.AP(k16, 0, [[1024, 128], [512, 2], [128, 4], [64, 2], [1, 64]]),
                bass.AP(ktok, 256 * ci,
                        [[1024, 128], [128, 2], [0, 4], [64, 2], [1, 64]]),
                bass.AP(kdec, 0,
                        [[1024, 128], [512, 2], [128, 4], [64, 2], [1, 64]]),
                Alu.mult), waits=[ts_tok], inc=V)
            if ci % 2 == 0:
                bank_free = [prev_addsA] if prev_addsA is not None else [tv_z]
            else:
                bank_free = ([prev_addsB] if prev_addsB is not None else []) + \
                            [ts_rc, tv_cos, tv_sin]
            uw = [tv_k16, ts_v[ci]] + bank_free
            for p in range(8):
                tt_u = P.op("tensor", lambda e, p=p, ci=ci, pair=pair: e.matmul(
                    pair[p // 4][:, 128 * (p % 4):128 * (p % 4) + 128],
                    k16[:, 128 * p:128 * p + 128],
                    v_bf[:, ci, 128 * (p // 4):128 * (p // 4) + 128],
                    start=(p % 4 == 0), stop=(p % 4 == 3)),
                    waits=uw if p == 0 else (), inc=T if p == 7 else None)
            tvm = P.op("vector", lambda e, c=c: e.tensor_mul(
                fbf[:], state_r[:, c % 2, :], bdb[:]),
                waits=[prev_state_tt], inc=V)
            ns = (c + 1) % 2
            prev_sp = tok_sp
            for g in range(2):
                for e_par in range(2):
                    sap = bass.AP(state_r, 2048 * 64 * e_par + 1024 * ns
                                  + 512 * g + 64 * e_par,
                                  [[2048, 64], [128, 4], [1, 64]])
                    fap = bass.AP(fbf, 64 * 1024 * e_par + 512 * g + 64 * e_par,
                                  [[1024, 64], [128, 4], [1, 64]])
                    pap = bass.AP(pair[g], 64 * 512 * e_par + 64 * e_par,
                                  [[512, 64], [128, 4], [1, 64]])
                    tok_last = P.op("vector",
                                    lambda e, sap=sap, fap=fap, pap=pap: e.tensor_add(
                        sap, fap, pap),
                        waits=[tt_u, tvm, prev_sp2] if (g == 0 and e_par == 0) else (),
                        inc=V)
            prev_sp2 = prev_sp
            prev_state_tt = tok_last
            if ci % 2 == 0:
                prev_addsA = tok_last
            else:
                prev_addsB = tok_last
        tok_ctab_free = tv_k[1]
        tok_xb_free[st] = tt_v
        tok_rows_free[st] = tt_rsc

    if not do_coll:
        P.op("gpsimd", lambda e: e.nop(),
             waits=[("s_sp0", P.count["s_sp0"]), ("s_sp1", P.count["s_sp1"]),
                    prev_state_tt])
        with nc.Block() as block:
            P.emit(block)
        stack.close()
        return nc
    tok_src = P.dma("gpsimd", src_h[:, :], state_r[:, N_CH % 2, :],
                    waits=[prev_state_tt], inc="s_g")
    P.op("gpsimd", lambda e: e.collective_compute(
        "AllGather", Alu.bypass,
        ins=[src_h[:, :]], outs=[dst_h[:, :]],
        replica_groups=[[0, 1], [2, 3], [4, 5], [6, 7]]),
        waits=[tok_src], inc="s_c")
    tok_coll = ("s_c", P.count["s_c"])
    tok_g0 = P.dma("gpsimd", fbf[:], dst_h[0:128, :], waits=[tok_coll], inc="s_g")
    tv_sel0 = P.op("vector", lambda e: e.tensor_scalar_mul(
        rrun[:], fbf[:], rmask[:, 0:1]),
        waits=[tok_g0, tok_const, prev_state_tt], inc=V)
    tok_g1 = P.dma("gpsimd", fbf[:], dst_h[128:256, :], waits=[tv_sel0], inc="s_g")
    tv_rrun = P.op("vector", lambda e: e.scalar_tensor_tensor(
        rrun[:], fbf[:], rmask[:, 1:2], rrun[:], Alu.mult, Alu.add),
        waits=[tok_g1], inc=V)

    tok_rsum_q = rowsum(rsum_q, wqT, [(0, 512), (512, 1024)],
                        [tok_wqT, prev_ts_v_last])
    tok_rsum_u1 = rowsum(rsum_u1, u1T, [(0, 256)], [tok_u1, tok_rsum_q])

    tok_x2 = [None] * N_ST
    tok_x2[0] = P.dma("sync", xb[0][:], xt[0], waits=[tok_xb_free[N_ST - 1]],
                      inc="s_x0")
    tok_kt_ld = [None] * N_ST
    tok_v_ld = [None] * N_ST
    tok_st_ld = [None] * N_ST
    tok_kt_ld[0] = P.dma("sync", kT_sb[:], kt_h[0],
                         waits=[("s_kt", 16 * N_ST), prev_state_tt], inc="s_ktl")
    tok_v_ld[0] = P.dma("sync", v_bf[:], v_h[0],
                        waits=[("s_vv", 16 * N_ST), prev_state_tt], inc="s_vvl")
    tok_rows_ld = [None] * N_ST
    tok_rows_ld[0] = P.dma("sync",
                           rows_cur[0:1, 0].rearrange("p a b -> p (a b)"),
                           rows_h[0].rearrange("a b -> (a b)").unsqueeze(0),
                           waits=[tok_rows_free[N_ST - 1], ("s_rows", 16 * N_ST)],
                           inc="s_rows")
    tok_stch = [None] * (N_CH + 2)
    all_sp = [("s_sp0", P.count["s_sp0"]), ("s_sp1", P.count["s_sp1"])]
    tok_stch[0] = P.dma("sync", st_ld[:, 0, :], st_h[0],
                        waits=all_sp, inc="s_st0")
    tok_stch[1] = P.dma("sync", st_ld[:, 1, :], st_h[1],
                        waits=all_sp, inc="s_st1")
    prev_tv_st = [None] * (N_CH + 2)

    tok_out = [None] * N_ST
    tv2_prev = [None]
    tv1_prev2 = [None, None]
    prev_tv_P1 = prev_tv_sin
    prev_S_evac_g = [None, None]
    prev_tt_o = None
    prev_ts_ot1 = None
    tok_rows_used = [None] * (N_ST + 1)
    prev_ts_attn = prev_ts_tok
    prev_negmb_user = None
    prev_tv_xbf2 = None
    prev_tv_rr = None
    tok_qp_free = None
    prev_o_evac = None
    prev_S_evac = None
    tok_xb2_free = [None] * (N_ST + 1)
    tok_ctab2_free = tok_ctab_free
    prev_psa_reader = tok_rsum_u1
    prev_tv_res2_last = None
    prev_tt_P = None
    prev_tsQ = None

    if n_st2 == 0:
        P.op("gpsimd", lambda e: e.nop(), waits=[tok_coll])
        with nc.Block() as block:
            P.emit(block)
        stack.close()
        return nc
    for st in range(n_st2):
        if st + 1 < n_st2:
            w = [tok_xb2_free[st - 1]] if st >= 1 else []
            tok_x2[st + 1] = P.dma("sync", xb[(st + 1) % 2][:], xt[st + 1], waits=w,
                                   inc=f"s_x{(st + 1) % 2}")
        xcur = xb[st % 2]
        s_ring = 0
        tv_xbf = P.op("vector", lambda e, xcur=xcur: e.tensor_copy(x_bf[:], xcur[:]),
                      waits=[tok_x2[st], prev_tt_P, prev_tv_xbf2], inc=V)
        ts_nm = P.op("scalar", lambda e, s_ring=s_ring: e.copy(
            out=negmb_row[0:1, :], in_=rows_cur[0:1, s_ring, 0, :]),
            waits=[tok_rows_ld[st], prev_negmb_user], inc=S)
        tt_rsb = P.op("tensor", lambda e, s_ring=s_ring: e.matmul(
            ps_a[:, :], onesf[0:1, 0:128], rows_cur[0:1, s_ring, 1, :],
            start=True, stop=True),
            waits=[prev_psa_reader, prev_tv_res2_last, tok_rows_ld[st]], inc=T)
        if st == 0:
            tok_tab_n = P.dma("sync", ctab[:], costab_d[0], waits=[tok_ctab2_free],
                              inc="s_tabc")
            tok_tab2_n = P.dma("sync", stab[:], sintab_d[0], waits=[tok_ctab2_free],
                               inc="s_tabs")
        tok_tab, tok_tab2 = tok_tab_n, tok_tab2_n
        tv_cos = P.op("vector", lambda e: e.tensor_mul(ctab[:], ctab[:], ps_a[:, :]),
                      waits=[tt_rsb, tok_tab, tok_tab2], inc=V)
        tv_sin = P.op("vector", lambda e: e.tensor_mul(stab[:], stab[:], ps_a[:, :]),
                      waits=[tt_rsb, tok_tab2], inc=V)
        tv_q = [None] * 8
        ts_q_prev = [None, None]
        pend = []
        def emit_rope(p_, ts_q_, rw_extra=()):
            rw = [ts_q_, tv2_prev[0]] + list(rw_extra)
            if prev_tsQ is not None:
                rw.append(prev_tsQ)
            tt_rot = P.op("tensor", lambda e: e.matmul(
                ps_k[0][:, :], mt[:, :], kraw_bf[:, p_ % 2, :], start=True, stop=True),
                waits=rw, inc=T)
            tv1 = P.op("vector", lambda e, p_=p_: e.tensor_mul(
                tmp1[:], kraw_bf[:, p_ % 2, :], ctab[:]),
                waits=[ts_q_, tv_cos, tv_q[p_ - 1] if p_ > 0 else None], inc=V)
            tv2 = P.op("vector", lambda e: e.tensor_mul(tmp2[:], ps_k[0][:, :], stab[:]),
                       waits=[tt_rot, tv_sin], inc=V)
            tv_q[p_] = P.op("vector", lambda e, p_=p_: e.tensor_add(
                qp[:, p_, :], tmp1[:], tmp2[:]), waits=[tv1, tv2], inc=V)
            tv2_prev[0] = tv2
        for p in range(8):
            qps = ps_a if p % 2 == 0 else ps_s[3]
            qw = [tv_cos, tv_sin, tv_xbf, tok_wqT, tok_rsum_q, ts_nm]
            if p == 0 and tok_qp_free is not None:
                qw.append(tok_qp_free)
            if p == 1:
                qw.append(prev_tv_P1)
            if ts_q_prev[p % 2] is not None:
                qw.append(ts_q_prev[p % 2])
            for k in range(8):
                P.op("tensor", lambda e, k=k, p=p, qps=qps: e.matmul(
                    qps[:, :], wqT[:, k, 128 * p:128 * p + 128], x_bf[:, k, :],
                    start=(k == 0), stop=False), waits=qw if k == 0 else ())
            tt_q = P.op("tensor", lambda e, p=p, qps=qps: e.matmul(
                qps[:, :], rsum_q[0:1, 128 * p:128 * p + 128], negmb_row[0:1, :],
                start=False, stop=True), inc=T)
            skw = [tt_q]
            if p >= 2:
                skw.append(tv1_prev2[p % 2])
            ts_q = P.op("scalar", lambda e, p=p, qps=qps: e.copy(
                out=kraw_bf[:, p % 2, :], in_=qps[:, :]), waits=skw, inc=S)
            ts_q_prev[p % 2] = ts_q
            pend.append((p, ts_q))
            if len(pend) >= 2:
                p_, t_ = pend.pop(0)
                emit_rope(p_, t_)
                tv1_prev2[p_ % 2] = tv_q[p_]
        while pend:
            p_, t_ = pend.pop(0)
            emit_rope(p_, t_)
            tv1_prev2[p_ % 2] = tv_q[p_]
        if st + 1 < n_st2:
            tok_rows_ld[st + 1] = P.dma(
                "sync", rows_cur[0:1, 0].rearrange("p a b -> p (a b)"),
                rows_h[st + 1].rearrange("a b -> (a b)").unsqueeze(0),
                waits=[tt_rsb, ts_nm], inc="s_rows")
            tok_tab_n = P.dma("sync", ctab[:], costab_d[st + 1],
                              waits=[tv_q[7]], inc="s_tabc")
            tok_tab2_n = P.dma("sync", stab[:], sintab_d[st + 1],
                               waits=[tv_q[7]], inc="s_tabs")
        tv_res1 = [None] * 8
        for ci in range(CPST):
            c = CPST * st + ci
            dqw = [tv_q[7]]
            if prev_tt_o is not None:
                dqw.append(prev_tt_o)
            tv_qdq = P.op("vector", lambda e, ci=ci: e.tensor_mul(
                kq_scr[:].rearrange("p (a b) -> p a b", a=8),
                qp[:, :, 128 * ci:128 * ci + 128], qdec[:]),
                waits=dqw, inc=V)
            tv_st = P.op("vector", lambda e, c=c: e.tensor_add(
                st_true[:], st_ld[:, c % 2, :], rrun[:]),
                waits=[tv_rrun, tok_stch[c], prev_tv_rr], inc=V)
            prev_tv_st[c] = tv_st
            if c + 2 <= N_CH - 1:
                tok_stch[c + 2] = P.dma("sync", st_ld[:, c % 2, :], st_h[c + 2],
                                        waits=[tv_st], inc=f"s_st{c % 2}")
            prev_tv_rr = P.op("vector", lambda e: e.tensor_mul(
                rrun[:], rrun[:], bdb[:]), waits=[tv_st], inc=V)
            tv_mask = [None, None]
            for e_par in range(2):
                sw0 = [tv_q[7], tok_kt_ld[st], prev_S_evac_g[e_par]]
                if e_par == 1:
                    sw0.append(prev_tv_P1)
                if prev_tsQ is not None:
                    sw0.append(prev_tsQ)
                first_s = True
                for p in range(8):
                    h = PAIRS[p][e_par]
                    kv = h // 4
                    s = p + 8 * e_par
                    tt_sc = P.op("tensor", lambda e, p=p, e_par=e_par, kv=kv, s=s, ci=ci: e.matmul(
                        ps_s[s // 4][:, 128 * (s % 4):128 * (s % 4) + 128],
                        kT_sb[64 * (kv % 2):64 * (kv % 2) + 64, kv // 2,
                              128 * ci:128 * ci + 128],
                        qp[64 * e_par:64 * e_par + 64, p, 128 * ci:128 * ci + 128],
                        start=True, stop=True),
                        waits=sw0 if first_s else (), inc=T if p == 7 else None)
                    first_s = False
                P.op("vector", lambda e, e_par=e_par: e.tensor_mul(
                    S_bf[:, 1024 * e_par:1024 * e_par + 512], ps_s[2 * e_par][:, :],
                    maskT[:, 1024 * e_par:1024 * e_par + 512]), waits=[tt_sc], inc=V)
                tv_mask[e_par] = P.op("vector", lambda e, e_par=e_par: e.tensor_mul(
                    S_bf[:, 1024 * e_par + 512:1024 * (e_par + 1)],
                    ps_s[2 * e_par + 1][:, :],
                    maskT[:, 1024 * e_par + 512:1024 * (e_par + 1)]), inc=V)
                if e_par == 0:
                    o2w = [tv_qdq, tv_st]
                    if prev_ts_ot1 is not None:
                        o2w.append(prev_ts_ot1)
                    for p in range(8):
                        P.op("tensor", lambda e, p=p: e.matmul(
                            ps_k[p // 4][:, 128 * (p % 4):128 * (p % 4) + 128],
                            kq_scr[:, 128 * p:128 * p + 128],
                            st_true[:, 128 * p:128 * p + 128],
                            start=(p % 4 == 0), stop=False),
                            waits=o2w if p == 0 else ())
            prev_S_evac = tv_mask[1]
            prev_S_evac_g = [tv_mask[0], tv_mask[1]]
            tt_o = None
            for e_par in range(2):
                ow = [tv_mask[e_par], tok_v_ld[st]]
                first_o = True
                for p in range(8):
                    h = PAIRS[p][e_par]
                    kv = h // 4
                    s = p + 8 * e_par
                    tt_o = P.op("tensor", lambda e, p=p, e_par=e_par, kv=kv, s=s, ci=ci: e.matmul(
                        ps_k[p // 4][:, 128 * (p % 4) + 64 * e_par:
                                     128 * (p % 4) + 64 * e_par + 64],
                        S_bf[:, 128 * s:128 * s + 128],
                        v_bf[:, ci, 64 * kv:64 * kv + 64],
                        start=False, stop=(e_par == 1 and p % 4 == 3)),
                        waits=ow if first_o else (),
                        inc=T if (e_par == 1 and p == 7) else None)
                    first_o = False
            ts_ot0 = P.op("scalar", lambda e: e.copy(
                out=S_bf[:, 0:512], in_=ps_k[0][:, :]),
                waits=[tt_o], inc=S)
            ts_ot1 = P.op("scalar", lambda e: e.copy(
                out=S_bf[:, 512:1024], in_=ps_k[1][:, :]), inc=S)
            prev_ts_ot1 = ts_ot1
            prev_tt_o = tt_o
            for j in range(8):
                tt_tra = P.op("tensor", lambda e, j=j: e.transpose(
                    ps_wt[:, 128 * j:128 * j + 128],
                    S_bf[:, 128 * j:128 * j + 128], identbf[:, :]),
                    waits=([ts_ot0, ts_ot1, prev_ts_attn] if j == 0 else ()),
                    inc=T if j == 7 else None)
            prev_o_evac = P.op("scalar", lambda e, ci=ci: e.copy(
                out=attn_bf[:, :, 128 * (ci % 2):128 * (ci % 2) + 128],
                in_=ps_wt[:].rearrange("p (a b) -> p a b", a=8)),
                waits=[tt_tra], inc=S)
            prev_ts_attn = prev_o_evac
            if ci % 2 == 1:
                half = ci // 2
                for dch in range(8):
                    yps = ps_a if (half == 0 or dch % 2 == 0) else ps_s[3]
                    ww = [prev_o_evac, tok_woT]
                    if half == 0:
                        if dch > 0:
                            ww.append(tv_res1[dch - 1])
                        else:
                            ww.append(tv_sin)
                    else:
                        ww.append(prev_S_evac)
                        if dch > 1:
                            ww.append(tv_res1[dch - 2])
                        else:
                            ww.append(tv_res1[7])
                    for p in range(8):
                        tt_y = P.op("tensor", lambda e, p=p, dch=dch, yps=yps: e.matmul(
                            yps[:, 0:256], woT[:, p, 128 * dch:128 * dch + 128],
                            attn_bf[:, p, :],
                            start=(p == 0), stop=(p == 7)),
                            waits=ww if p == 0 else (), inc=T if p == 7 else None)
                    tv_res1[dch] = P.op("vector",
                                        lambda e, dch=dch, half=half, xcur=xcur,
                                        yps=yps: e.tensor_add(
                        xcur[:, dch, 256 * half:256 * half + 256],
                        xcur[:, dch, 256 * half:256 * half + 256],
                        yps[:, 0:256]), waits=[tt_y], inc=V)
        if st + 1 < N_ST:
            tok_kt_ld[st + 1] = P.dma("sync", kT_sb[:], kt_h[st + 1],
                                      waits=[prev_o_evac], inc="s_ktl")
            tok_v_ld[st + 1] = P.dma("sync", v_bf[:], v_h[st + 1],
                                     waits=[prev_o_evac], inc="s_vvl")
        if p2stage <= 3:
            P.op("gpsimd", lambda e: e.nop(), waits=[tv_res1[7]])
            with nc.Block() as block:
                P.emit(block)
            stack.close()
            return nc
        tv_c2 = [None, None]
        for h2 in range(2):
            tv_c2[h2] = P.op("vector", lambda e, h2=h2, xcur=xcur: e.tensor_copy(
                x_bf[:, 4 * h2:4 * h2 + 4, :].rearrange("p a b -> p (a b)"),
                xcur[:, 4 * h2:4 * h2 + 4, :].rearrange("p a b -> p (a b)")),
                waits=[tv_res1[4 * h2 + 3]], inc=V)
        gq = [None] * 8
        for k in range(8):
            gq[k] = P.op("scalar", lambda e, k=k, xcur=xcur: e.activation(
                out=qp[:, k, :], in_=xcur[:, k, :], func=AF.Square),
                waits=[tv_res1[k], tt_sc] if k == 0 else [tv_res1[k]], inc=S)
        tv_xbf2 = tv_c2[1]
        prev_tv_xbf2 = tv_xbf2
        for k in range(8):
            P.op("tensor", lambda e, k=k: e.matmul(
                ps_a[0:1, :], onescol[:, 0:1], x_bf[:, k, :],
                start=(k == 0), stop=(k == 7)),
                waits=[tv_c2[k // 4]] if k % 4 == 0 else ())
        for k in range(8):
            tt_stat = P.op("tensor", lambda e, k=k: e.matmul(
                ps_k[0][0:1, :], onescol[:, 0:1], qp[:, k, :],
                start=(k == 0), stop=(k == 7)),
                waits=[gq[k]],
                inc=T if k == 7 else None)
        tok_qp_free = tt_stat
        ts1 = P.op("scalar", lambda e: e.activation(
            out=negm2_row[0:1, :], in_=ps_a[0:1, :], func=AF.Copy, scale=-1.0 / D),
            waits=[tt_stat], inc=S)
        tv_m2 = P.op("scalar", lambda e: e.activation(
            out=srow[0:1, 1, :], in_=negm2_row[0:1, :], func=AF.Square),
            waits=[ts1], inc=S)
        tv_var = P.op("vector", lambda e: e.scalar_tensor_tensor(
            srow[0:1, 0, :], ps_k[0][0:1, :], 1.0 / D, srow[0:1, 1, :],
            Alu.mult, Alu.subtract),
            waits=[tt_stat, tv_m2], inc=V)
        ts3 = P.op("scalar", lambda e: e.activation(
            out=srow[0:1, 1, :], in_=srow[0:1, 0, :], func=AF.Ln, bias=EPS),
            waits=[tv_var, tok_const], inc=S)
        ts_rs2 = P.op("scalar", lambda e: e.activation(
            out=rs2_row[0:1, :], in_=srow[0:1, 1, :], func=AF.Exp, scale=-0.5),
            waits=[ts3], inc=S)
        ts_nm2 = P.op("scalar", lambda e: e.copy(
            out=negm2b_row[0:1, :], in_=negm2_row[0:1, :]), waits=[ts1], inc=S)
        tt_rsb2 = P.op("tensor", lambda e: e.matmul(
            ps_s[2][:, :], onesf[0:1, 0:128], rs2_row[0:1, :], start=True, stop=True),
            waits=[ts_rs2, prev_S_evac], inc=T)
        ts_rs2sb = P.op("scalar", lambda e: e.copy(out=rs2_sb[:], in_=ps_s[2][:, :]),
                        waits=[tt_rsb2, tv_q[7]], inc=S)
        if p2stage <= 4:
            P.op("gpsimd", lambda e: e.nop(), waits=[ts_rs2sb])
            with nc.Block() as block:
                P.emit(block)
            stack.close()
            return nc
        tv_P = [None, None]
        tt_P = [None, None]
        for r in range(2):
            pps = ps_a if r == 0 else ps_s[3]
            pw = [tv_xbf2, tok_rsum_u1]
            if r == 0:
                pw.append(tv_res1[6])
            else:
                pw.append(tv_res1[7])
            for k in range(8):
                P.op("tensor", lambda e, k=k, r=r, pps=pps: e.matmul(
                    pps[:, :], u1T[:, k, 128 * r:128 * r + 128], x_bf[:, k, :],
                    start=(k == 0), stop=False), waits=pw if k == 0 else ())
        for r in range(2):
            pps = ps_a if r == 0 else ps_s[3]
            tt_P[r] = P.op("tensor", lambda e, r=r, pps=pps: e.matmul(
                pps[:, :], rsum_u1[0:1, 128 * r:128 * r + 128], negm2b_row[0:1, :],
                start=False, stop=True), waits=[ts_nm2] if r == 0 else (), inc=T)
            tv_P[r] = P.op("vector", lambda e, r=r, pps=pps: e.tensor_mul(
                P_bf[:, r, :], pps[:, :], rs2_sb[:]),
                waits=[tt_P[r], ts_rs2sb], inc=V)
        prev_tt_P = tt_P[1]
        ts_H = [None] * 16
        first_q_bank = [True, True]

        def emit_v1(g):
            buf = g % 2
            gw = [tv_P[1], tok_v1]
            if g >= 2:
                gw.append(ts_H[g - 2])
            elif g < 2:
                gw.append(prev_S_evac)
            for j in range(2):
                ff = 2 * g + j
                for r in range(2):
                    tt = P.op("tensor", lambda e, ff=ff, r=r, buf=buf, j=j: e.matmul(
                        ps_s[2 * buf + j][:, :],
                        v1T[:, r, 128 * ff:128 * ff + 128], P_bf[:, r, :],
                        start=(r == 0), stop=(r == 1)),
                        waits=gw if (j == 0 and r == 0) else (),
                        inc=T if (j == 1 and r == 1) else None)
            return tt

        def emit_gelu(g):
            buf = g % 2
            P.op("scalar", lambda e, g=g, buf=buf: e.activation(
                out=H_bf[:, 1024 * buf:1024 * buf + 512],
                in_=ps_s[2 * buf][:, :],
                func=AF.Gelu_apprx_tanh, bias=b1c[:, 2 * g:2 * g + 1]),
                waits=[tt_H_tok[g], tok_b1], inc=S)
            ts_H[g] = P.op("scalar", lambda e, g=g, buf=buf: e.activation(
                out=H_bf[:, 1024 * buf + 512:1024 * (buf + 1)],
                in_=ps_s[2 * buf + 1][:, :],
                func=AF.Gelu_apprx_tanh, bias=b1c[:, 2 * g + 1:2 * g + 2]), inc=S)

        def emit_u2(g):
            nonlocal_last = None
            buf = g % 2
            u2w = [ts_H[g], tok_u2[g // 4]]
            if g == 0:
                u2w.append(prev_o_evac)
            for j in range(2):
                ff = 2 * g + j
                for m in range(2):
                    stt = first_q_bank[m]
                    first_q_bank[m] = False
                    nonlocal_last = P.op("tensor", lambda e, ff=ff, m=m, buf=buf, j=j,
                                         stt=stt, stp=(g == 15 and j == 1): e.matmul(
                        ps_k[m][:, :],
                        u2T[:, ff, 128 * m:128 * m + 128],
                        H_bf[:, 1024 * buf + 512 * j:1024 * buf + 512 * j + 512],
                        start=stt, stop=stp),
                        waits=u2w if (j == 0 and m == 0) else (),
                        inc=T if (g == 15 and j == 1 and m == 1) else None)
            return nonlocal_last

        tt_H_tok = [None] * 16
        tt_u2last = None
        for g in range(16):
            tt_H_tok[g] = emit_v1(g)
            emit_gelu(g)
            if g >= 1:
                tt_u2last = emit_u2(g - 1)
        tt_u2last = emit_u2(15)
        tt_Q = tt_u2last
        P.op("scalar", lambda e: e.copy(
            out=Q_bf[:, 0, :], in_=ps_k[0][:, :]), waits=[tt_Q], inc=S)
        ts_Q = P.op("scalar", lambda e: e.copy(
            out=Q_bf[:, 1, :], in_=ps_k[1][:, :]), inc=S)
        if p2stage <= 5:
            P.op("gpsimd", lambda e: e.nop(), waits=[ts_Q])
            with nc.Block() as block:
                P.emit(block)
            stack.close()
            return nc
        tv_res2 = [None] * 8
        for dch in range(8):
            vps = ps_a if dch % 2 == 0 else ps_s[3]
            vw = [ts_Q, tok_v2]
            if dch > 1:
                vw.append(tv_res2[dch - 2])
            else:
                vw.append(tv_P[1])
                vw.append(tv_res1[7])
            for r in range(2):
                tt_v2 = P.op("tensor", lambda e, r=r, dch=dch, vps=vps: e.matmul(
                    vps[:, :], v2T[:, r, 128 * dch:128 * dch + 128], Q_bf[:, r, :],
                    start=(r == 0), stop=(r == 1)),
                    waits=vw if r == 0 else (), inc=T if r == 1 else None)
            tv_res2[dch] = P.op("vector",
                                lambda e, dch=dch, xcur=xcur,
                                vps=vps: e.scalar_tensor_tensor(
                xcur[:, dch, :], vps[:, :], b2c[:, dch:dch + 1], xcur[:, dch, :],
                Alu.add, Alu.add), waits=[tt_v2], inc=V)
        tok_out[st] = P.dma("sync", out[st], xcur[:], waits=[tv_res2[7]],
                            inc=f"s_out{st % 2}")
        tok_xb2_free[st] = tok_out[st]
        tok_ctab2_free = tv_q[7]
        prev_psa_reader = ts_Q
        prev_tv_res2_last = tv_res2[7]
        prev_tsQ = ts_Q
        prev_tv_P1 = tv_P[1]
        tok_rows_used[st] = tt_rsb
        prev_negmb_user = tt_rsb

    P.op("sync", lambda e: e.nop(),
         waits=[("s_out0", P.count["s_out0"]), ("s_out1", P.count["s_out1"])])
    prev_kt_spill = prev_v_spill

    with nc.Block() as block:
        P.emit(block)
    stack.close()
    return nc


_NC_CACHE = None


def kernel(L0_attn_x, L0_attn_Wq, L0_attn_Wk, L0_attn_Wv, L0_attn_Wo,
           L0_ffn_U1, L0_ffn_V1, L0_ffn_b1, L0_ffn_U2, L0_ffn_V2, L0_ffn_b2):
    global _NC_CACHE
    x = np.asarray(L0_attn_x, np.float32)
    B = x.shape[0]
    perm = _perm_rows()
    wq_p = np.ascontiguousarray(np.asarray(L0_attn_Wq, np.float32)[perm, :])
    wo_p = np.ascontiguousarray(np.asarray(L0_attn_Wo, np.float32)[:, perm])
    wk = np.ascontiguousarray(np.asarray(L0_attn_Wk, np.float32))
    wv = np.ascontiguousarray(np.asarray(L0_attn_Wv, np.float32))
    u1 = np.ascontiguousarray(np.asarray(L0_ffn_U1, np.float32))
    v1 = np.ascontiguousarray(np.asarray(L0_ffn_V1, np.float32))
    u2 = np.ascontiguousarray(np.asarray(L0_ffn_U2, np.float32))
    v2 = np.ascontiguousarray(np.asarray(L0_ffn_V2, np.float32))
    b1c = np.ascontiguousarray(np.asarray(L0_ffn_b1, np.float32).reshape(32, 128).T)
    b2c = np.ascontiguousarray(np.asarray(L0_ffn_b2, np.float32).reshape(8, 128).T)

    consts = [_host_consts(0), _host_consts(1)]
    if _NC_CACHE is None:
        _NC_CACHE = build_nc()
    nc = _NC_CACHE

    in_maps = []
    for c in range(8):
        b, h = c // 2, c % 2
        xs = np.ascontiguousarray(x[b, h * S_CORE:(h + 1) * S_CORE, :].T)
        xtl = np.ascontiguousarray(
            xs.reshape(8, 128, N_ST, ST).transpose(2, 1, 0, 3))
        cc = consts[h]
        in_maps.append(dict(
            xt=xtl, wq=wq_p, wk=wk, wv=wv, wo=wo_p, u1=u1, v1=v1, u2=u2, v2=v2,
            b1c=b1c, b2c=b2c, mt=cc["mt"], identbf=cc["identbf"], mask=cc["mask"],
            qdec=cc["qdec"], kdec=cc["kdec"], bdb=cc["bdb"],
            costab=cc["costab"], sintab=cc["sintab"], rmask=cc["rmask"],
            onesf=cc["onesf"], onesb=cc["onesb"], onescol=cc["onescol"],
            epsc=cc["epsc"],
        ))

    res = run_bass_kernel_spmd(nc, in_maps, core_ids=list(range(8)))

    outp = np.empty((B, 8192, D), np.float32)
    for c in range(8):
        b, h = c // 2, c % 2
        arr = np.asarray(res.results[c]["out"])
        sl = arr.transpose(0, 3, 2, 1).reshape(S_CORE, D)
        outp[b, h * S_CORE:(h + 1) * S_CORE, :] = sl
    return outp


# version 12
# speedup vs baseline: 1.0400x; 1.0087x over previous
import math

import numpy as np
import ml_dtypes

import concourse.bass as bass
import concourse.mybir as mybir
from concourse.bass_utils import run_bass_kernel_spmd

F32 = mybir.dt.float32
BF16 = mybir.dt.bfloat16
AF = mybir.ActivationFunctionType
Alu = mybir.AluOpType

BF = ml_dtypes.bfloat16

D = 1024
HD = 64
NH = 16
NKV = 4
CH = 128
ST = 512
N_ST = 8
N_CH = 32
CPST = ST // CH
S_CORE = 4096
THETA = 10000.0
EPS = 1e-5

PAIRS = [(0, 4), (1, 5), (2, 6), (3, 7), (8, 12), (9, 13), (10, 14), (11, 15)]
HEAD_POS = {}
for _p, (_lo, _hi) in enumerate(PAIRS):
    HEAD_POS[_lo] = (_p, 0)
    HEAD_POS[_hi] = (_p, 1)
SLOPES = [2.0 ** (-8.0 / NH * (h + 1)) for h in range(NH)]


def _rope_tables():
    inv = 1.0 / THETA ** (np.arange(0, HD, 2, dtype=np.float64) / HD)
    freqs = np.outer(np.arange(8192, dtype=np.float64), inv)
    emb = np.concatenate([freqs, freqs], axis=-1)
    return np.cos(emb).astype(np.float32), np.sin(emb).astype(np.float32)


_COS64, _SIN64 = _rope_tables()


def _host_consts(half):
    c = {}
    M = np.zeros((128, 128), np.float32)
    for blk in range(2):
        b0 = 64 * blk
        for i in range(32):
            M[b0 + i, b0 + 32 + i] = -1.0
            M[b0 + 32 + i, b0 + i] = 1.0
    c["mt"] = np.ascontiguousarray(M.T).astype(BF)
    c["identbf"] = np.eye(128, dtype=np.float32).astype(BF)
    mask = np.zeros((128, 16 * 128), np.float32)
    t = np.arange(CH, dtype=np.float64)
    for p in range(8):
        for e in range(2):
            h = PAIRS[p][e]
            s = p + 8 * e
            m = np.where(t[:, None] >= t[None, :],
                         np.exp(-SLOPES[h] * (t[:, None] - t[None, :])), 0.0)
            mask[:, 128 * s:128 * s + 128] = m.T.astype(np.float32)
    c["mask"] = mask.astype(BF)
    qd = np.zeros((128, 8, 128), np.float32)
    for p in range(8):
        for e in range(2):
            h = PAIRS[p][e]
            qd[64 * e:64 * e + 64, p, :] = np.exp(-SLOPES[h] * (t + 1.0))[None, :]
    c["qdec"] = qd.astype(BF)
    kd = np.zeros((128, 8, 2, 64), np.float32)
    for p in range(8):
        for e in range(2):
            h = PAIRS[p][e]
            kd[:, p, e, :] = np.exp(-SLOPES[h] * (CH - 1.0 - t))[:, None]
    c["kdec"] = kd.reshape(128, NH * 64).astype(BF)
    bd = np.zeros((128, 8, 128), np.float32)
    for p in range(8):
        for e in range(2):
            h = PAIRS[p][e]
            bd[:, p, 64 * e:64 * e + 64] = math.exp(-SLOPES[h] * CH)
    c["bdf"] = bd.reshape(128, 1024).copy()
    c["bdb"] = c["bdf"].astype(BF)
    pos0 = half * S_CORE
    cos = np.zeros((N_ST, 128, ST), np.float32)
    sin = np.zeros((N_ST, 128, ST), np.float32)
    for st in range(N_ST):
        pp = np.arange(ST) + pos0 + st * ST
        cs = _COS64[pp, :].T
        sn = _SIN64[pp, :].T
        cos[st, 0:64] = cs
        cos[st, 64:128] = cs
        sin[st, 0:64] = sn
        sin[st, 64:128] = sn
    c["costab"] = cos
    c["sintab"] = sin
    rm = np.zeros((128, 2), np.float32)
    if half == 1:
        rm[:, 0] = 1.0
    c["rmask"] = rm
    c["onesf"] = np.ones((1, 128), np.float32)
    c["epsc"] = np.full((1, 1), EPS, np.float32)
    c["onesb"] = np.ones((1, 512), np.float32).astype(BF)
    c["onescol"] = np.ones((128, 1), np.float32).astype(BF)
    return c


def _perm_rows():
    perm = []
    for (lo, hi) in PAIRS:
        perm.extend(range(64 * lo, 64 * lo + 64))
        perm.extend(range(64 * hi, 64 * hi + 64))
    return np.array(perm, np.int64)


class Prog:
    def __init__(self, nc, sems):
        self.nc = nc
        self.ops = {e: [] for e in ("sync", "gpsimd", "tensor", "vector", "scalar")}
        self.count = {s: 0 for s in sems}
        self.sems = sems

    def op(self, eng, fn, waits=(), inc=None, inc_n=1):
        tok = None
        if inc is not None:
            self.count[inc] += inc_n
            tok = (inc, self.count[inc])
        waits = tuple(w for w in waits if w is not None)
        self.ops[eng].append((fn, waits, inc, inc_n))
        return tok

    def dma(self, eng, out, in_, waits=(), inc=None, **kw):
        return self.op(eng, lambda e: e.dma_start(out=out, in_=in_, **kw),
                       waits=waits, inc=inc, inc_n=16)

    def emit(self, block):
        def make(eng_name):
            ops = self.ops[eng_name]
            sems = self.sems

            def body(eng):
                wm = {}
                for fn, waits, inc, inc_n in ops:
                    for (sn, val) in waits:
                        if wm.get(sn, -1) >= val:
                            continue
                        eng.wait_ge(sems[sn], val)
                        wm[sn] = val
                    instr = fn(eng)
                    if inc is not None:
                        if instr is None:
                            instr = eng.nop()
                        instr.then_inc(sems[inc], inc_n)
            return body

        for name in ("sync", "gpsimd", "tensor", "vector", "scalar"):
            getattr(block, name)(make(name))


def build_nc(n_st1=N_ST, do_coll=True, n_st2=N_ST, p2stage=9):
    nc = bass.Bass("TRN2", target_bir_lowering=False)

    xt = nc.dram_tensor("xt", [N_ST, 128, 8, ST], F32, kind="ExternalInput").ap()
    out = nc.dram_tensor("out", [N_ST, 128, 8, ST], F32, kind="ExternalOutput").ap()
    wq = nc.dram_tensor("wq", [D, D], F32, kind="ExternalInput").ap()
    wk = nc.dram_tensor("wk", [256, D], F32, kind="ExternalInput").ap()
    wv = nc.dram_tensor("wv", [256, D], F32, kind="ExternalInput").ap()
    wo = nc.dram_tensor("wo", [D, D], F32, kind="ExternalInput").ap()
    u1 = nc.dram_tensor("u1", [D, 256], F32, kind="ExternalInput").ap()
    v1 = nc.dram_tensor("v1", [256, 4096], F32, kind="ExternalInput").ap()
    u2 = nc.dram_tensor("u2", [4096, 256], F32, kind="ExternalInput").ap()
    v2 = nc.dram_tensor("v2", [256, D], F32, kind="ExternalInput").ap()
    b1c_d = nc.dram_tensor("b1c", [128, 32], F32, kind="ExternalInput").ap()
    b2c_d = nc.dram_tensor("b2c", [128, 8], F32, kind="ExternalInput").ap()
    mt_d = nc.dram_tensor("mt", [128, 128], BF16, kind="ExternalInput").ap()
    identbf_d = nc.dram_tensor("identbf", [128, 128], BF16, kind="ExternalInput").ap()
    mask_d = nc.dram_tensor("mask", [128, 2048], BF16, kind="ExternalInput").ap()
    qdec_d = nc.dram_tensor("qdec", [128, 8, 128], BF16, kind="ExternalInput").ap()
    kdec_d = nc.dram_tensor("kdec", [128, 1024], BF16, kind="ExternalInput").ap()
    bdb_d = nc.dram_tensor("bdb", [128, 1024], BF16, kind="ExternalInput").ap()
    costab_d = nc.dram_tensor("costab", [N_ST, 128, ST], F32, kind="ExternalInput").ap()
    sintab_d = nc.dram_tensor("sintab", [N_ST, 128, ST], F32, kind="ExternalInput").ap()
    rmask_d = nc.dram_tensor("rmask", [128, 2], F32, kind="ExternalInput").ap()
    onesf_d = nc.dram_tensor("onesf", [1, 128], F32, kind="ExternalInput").ap()
    epsc_d = nc.dram_tensor("epsc", [1, 1], F32, kind="ExternalInput").ap()
    onesb_d = nc.dram_tensor("onesb", [1, 512], BF16, kind="ExternalInput").ap()
    onescol_d = nc.dram_tensor("onescol", [128, 1], BF16, kind="ExternalInput").ap()

    wqbf_h = nc.dram_tensor("wqbf_h", [D, D], BF16, kind="Internal").ap()
    wobf_h = nc.dram_tensor("wobf_h", [D, D], BF16, kind="Internal").ap()
    wkbf_h = nc.dram_tensor("wkbf_h", [256, D], BF16, kind="Internal").ap()
    wvbf_h = nc.dram_tensor("wvbf_h", [256, D], BF16, kind="Internal").ap()
    st_h = nc.dram_tensor("st_h", [N_CH, 128, 1024], BF16, kind="Internal").ap()
    kt_h = nc.dram_tensor("kt_h", [N_ST, 128, 2, ST], BF16, kind="Internal").ap()
    v_h = nc.dram_tensor("v_h", [N_ST, 128, CPST, 256], BF16, kind="Internal").ap()
    rows_h = nc.dram_tensor("rows_h", [N_ST, 2, 512], F32, kind="Internal").ap()
    src_h = nc.dram_tensor("src_h", [128, 1024], F32, kind="Internal").ap()
    dst_h = nc.dram_tensor("dst_h", [256, 1024], F32, kind="Internal").ap()

    import contextlib
    stack = contextlib.ExitStack()
    sb = lambda name, shape, dt: stack.enter_context(nc.sbuf_tensor(name, shape, dt))
    psb = lambda name, shape, dt: stack.enter_context(nc.psum_tensor(name, shape, dt))
    sem = lambda name: stack.enter_context(nc.semaphore(name))

    wqT = sb("wqT", [128, 8, 1024], BF16)
    woT = sb("woT", [128, 8, 1024], BF16)
    wkT = sb("wkT", [128, 8, 256], BF16)
    wvT = sb("wvT", [128, 8, 256], BF16)
    u1T = sb("u1T", [128, 8, 256], BF16)
    v1T = sb("v1T", [128, 2, 4096], BF16)
    u2T = sb("u2T", [128, 32, 256], BF16)
    v2T = sb("v2T", [128, 2, 1024], BF16)
    b1c = sb("b1c_s", [128, 32], F32)
    b2c = sb("b2c_s", [128, 8], F32)
    mt = sb("mt_s", [128, 128], BF16)
    identbf = sb("identbf_s", [128, 128], BF16)
    maskT = sb("maskT", [128, 2048], BF16)
    qdec = sb("qdec_s", [128, 8, 128], BF16)
    kdec = sb("kdec_s", [128, 1024], BF16)
    bdb = sb("bdb_s", [128, 1024], BF16)
    ctab = sb("ctab", [128, ST], F32)
    stab = sb("stab", [128, ST], F32)
    rmask = sb("rmask_s", [128, 2], F32)
    onesf = sb("onesf_s", [1, 128], F32)
    epsc = sb("epsc_s", [1, 1], F32)
    nc.const_aps.register(F32, EPS, epsc[0:1, 0:1])
    onesb = sb("onesb_s", [1, 512], BF16)
    onescol = sb("onescol_s", [128, 1], BF16)
    rsum_q = sb("rsum_q", [1, 1024], BF16)
    rsum_k = sb("rsum_k", [1, 256], BF16)
    rsum_v = sb("rsum_v", [1, 256], BF16)
    rsum_u1 = sb("rsum_u1", [1, 256], BF16)

    xb = [sb(f"xb{i}", [128, 8, ST], F32) for i in range(2)]
    x_bf = sb("x_bf", [128, 8, ST], BF16)
    rows_cur = sb("rows_cur", [1, 1, 2, 512], F32)
    negmb_row = sb("negmb_row", [1, 512], BF16)
    srow = sb("srow", [1, 2, 512], F32)
    rs2_row = sb("rs2_row", [1, 512], F32)
    negm2_row = sb("negm2_row", [1, 512], F32)
    negm2b_row = sb("negm2b_row", [1, 512], BF16)
    rs_col = sb("rs_col", [128, 4], F32)

    kraw_bf = sb("kraw_bf", [128, 2, 512], BF16)
    tmp1 = sb("tmp1", [128, 512], F32)
    tmp2 = sb("tmp2", [128, 512], F32)
    rs2_sb = tmp2
    kT_sb = sb("kT_sb", [128, 2, ST], BF16)
    kq_scr = sb("kq_scr", [128, 1024], BF16)
    ks_scr = sb("ks_scr", [128, 1024], BF16)
    ktok = kq_scr
    k16 = ks_scr
    st_true = ks_scr
    qdq3 = None
    v_bf = sb("v_bf", [128, CPST, 256], BF16)
    state_r = sb("state_r", [128, 2, 1024], BF16)
    fbf = sb("fbf", [128, 1024], BF16)
    rrun = sb("rrun", [128, 1024], BF16)
    st_ld = sb("st_ld", [128, 2, 1024], BF16)
    qp = sb("qp", [128, 8, ST], BF16)
    sh_bf = sb("sh_bf", [128, 2048], BF16)
    S_bf = sh_bf
    H_bf = sh_bf
    attn_bf = sb("attn_bf", [128, 8, 256], BF16)
    pq_bf = sb("pq_bf", [128, 2, 512], BF16)
    P_bf = pq_bf
    Q_bf = pq_bf

    ps_a = psb("ps_a", [128, 512], F32)
    ps_k = [psb("ps_k0", [128, 512], F32), psb("ps_k1", [128, 512], F32)]
    ps_s = [psb(f"ps_s{i}", [128, 512], F32) for i in range(4)]
    ps_wt = psb("ps_wt", [128, 1024], BF16)

    sems = {}
    for s in ("s_const", "s_wkT", "s_wvT", "s_wqT", "s_woT", "s_x0", "s_x1",
              "s_tabc", "s_tabs", "s_rows", "s_out0", "s_out1",
              "s_w1", "s_w2", "s_w3", "s_kt", "s_vv", "s_ktl", "s_vvl", "s_st0", "s_st1",
              "s_sp0", "s_sp1", "s_g", "s_q", "s_t", "s_v", "s_s", "s_c"):
        sems[s] = sem(s)
    P = Prog(nc, sems)

    T, V, S = "s_t", "s_v", "s_s"

    tok_x = [None] * N_ST
    tok_x[0] = P.dma("sync", xb[0][:], xt[0], inc="s_x0")
    tok_state0 = P.op("vector", lambda e: e.memset(state_r[:], 0.0), inc=V)
    tok_const = None
    for dst, src in ((mt, mt_d), (identbf, identbf_d), (maskT, mask_d),
                     (qdec, qdec_d), (kdec, kdec_d), (bdb, bdb_d),
                     (rmask, rmask_d), (onesf, onesf_d), (epsc, epsc_d),
                     (onesb, onesb_d),
                     (onescol, onescol_d), (b2c, b2c_d)):
        tok_const = P.dma("sync", dst[:], src[:, :] if len(src.shape) == 2 else src[:],
                          inc="s_const")

    P.dma("gpsimd", wkbf_h[:, :], wk[:, :], inc="s_w1")
    tok_wkv_h = P.dma("gpsimd", wvbf_h[:, :], wv[:, :], inc="s_w1")
    P.dma("gpsimd", u1T[:], u1.rearrange("(k p) r -> p k r", p=128), inc="s_w2")
    P.dma("gpsimd", v1T[:], v1.rearrange("(k p) f -> p k f", p=128), inc="s_w2")
    for g in range(4):
        P.dma("gpsimd", u2T[:, 8 * g:8 * (g + 1), :],
              u2.rearrange("(k p) r -> p k r", p=128)[:, 8 * g:8 * (g + 1), :],
              inc="s_w2")
    P.dma("gpsimd", v2T[:], v2.rearrange("(k p) d -> p k d", p=128), inc="s_w2")
    tok_w2 = P.dma("gpsimd", b1c[:], b1c_d[:, :], inc="s_w2")
    tok_u1 = tok_v1 = tok_v2 = tok_b1 = tok_w2
    tok_u2 = [tok_w2] * 4
    P.dma("gpsimd", wqbf_h[:, :], wq[:, :], inc="s_w3")
    tok_w3 = P.dma("gpsimd", wobf_h[:, :], wo[:, :], inc="s_w3")
    tok_wq_h = tok_wo_h = tok_w3

    tok_wkT = tok_wvT = tok_wqT = tok_woT = None
    for k in range(8):
        tok_wkT = P.op("sync", lambda e, k=k: e.dma_start_transpose(
            wkT[:, k, :], wkbf_h[:, 128 * k:128 * (k + 1)]),
            waits=[tok_wkv_h], inc="s_wkT", inc_n=16)
    for k in range(8):
        tok_wvT = P.op("sync", lambda e, k=k: e.dma_start_transpose(
            wvT[:, k, :], wvbf_h[:, 128 * k:128 * (k + 1)]),
            waits=[tok_wkv_h], inc="s_wvT", inc_n=16)

    def rowsum(dst_row, srcT, nsl, waits):
        prev_evac = None
        for (a, b) in nsl:
            w0 = list(waits) + ([prev_evac] if prev_evac else [])
            for k in range(8):
                tk = P.op("tensor", lambda e, k=k, a=a, b=b: e.matmul(
                    ps_a[0:1, 0:b - a], onescol[:, 0:1], srcT[:, k, a:b],
                    start=(k == 0), stop=(k == 7)),
                    waits=w0 if k == 0 else (), inc=T if k == 7 else None)
            prev_evac = P.op("scalar", lambda e, a=a, b=b: e.copy(
                out=dst_row[0:1, a:b], in_=ps_a[0:1, 0:b - a]),
                waits=[tk], inc=S)
        return prev_evac

    tok_rsum_k = rowsum(rsum_k, wkT, [(0, 256)], [tok_wkT, tok_const])
    tok_rsum_v = rowsum(rsum_v, wvT, [(0, 256)], [tok_wvT, tok_rsum_k])

    tok_xbf_done = [None] * (N_ST + 1)
    tok_xb_free = [None] * (N_ST + 1)
    prev_sp2 = None
    tok_rows_free = [None] * (N_ST + 1)
    prev_ts_nm = None
    prev_ts_rc = None
    prev_tv_sin = None
    prev_kt_spill = None
    prev_v_spill = None
    tok_ctab_free = None
    prev_state_tt = tok_state0
    prev_ts_v_last = tok_rsum_v
    prev_ts_tok = None

    def pre_sq(stv, sq_waits):
        xv1 = xb[stv % 2][:, 0:4, :].rearrange("p a b -> p (a b)")
        xv2 = xb[stv % 2][:, 4:8, :].rearrange("p a b -> p (a b)")
        P.op("gpsimd", lambda e, xv1=xv1: e.tensor_mul(
            attn_bf.rearrange("p a b -> p (a b)"), xv1, xv1),
            waits=sq_waits, inc="s_q")
        return P.op("gpsimd", lambda e, xv2=xv2: e.tensor_mul(
            sh_bf[:, :], xv2, xv2), inc="s_q")

    def pre_mm(mm_waits):
        for k in range(8):
            srcb = attn_bf if k < 4 else sh_bf
            off = 512 * (k % 4)
            tt_stat = P.op("tensor", lambda e, srcb=srcb, off=off, k=k: e.matmul(
                ps_s[0][0:1, :], onescol[:, 0:1],
                bass.AP(srcb, off, [[2048, 128], [1, 512]]),
                start=(k == 0), stop=(k == 7)),
                waits=mm_waits if k == 0 else (),
                inc=T if k == 7 else None)
        return tt_stat

    tsq_cur = pre_sq(0, [tok_x[0]])
    prev_tt_stat = pre_mm([tsq_cur, tok_const])
    prev_kraw_rd = [None, None]
    prev_addsA = None
    prev_addsB = None

    for st in range(n_st1):
        if st + 1 < n_st1:
            w = [tok_xb_free[st - 1]] if st >= 1 else []
            tok_x[st + 1] = P.dma("sync", xb[(st + 1) % 2][:], xt[st + 1],
                                  waits=w, inc=f"s_x{(st + 1) % 2}")
            tsq_next = pre_sq(st + 1, [tok_x[st + 1], prev_tt_stat])
        if st == 3:
            for k in range(8):
                tok_wqT = P.op("sync", lambda e, k=k: e.dma_start_transpose(
                    wqT[:, k, :], wqbf_h[:, 128 * k:128 * (k + 1)]),
                    waits=[tok_w3], inc="s_wqT", inc_n=16)
            for k in range(8):
                tok_woT = P.op("sync", lambda e, k=k: e.dma_start_transpose(
                    woT[:, k, :], wobf_h[:, 128 * k:128 * (k + 1)]),
                    waits=[tok_w3], inc="s_woT", inc_n=16)
        xcur = xb[st % 2]
        tv_xbf1 = P.op("scalar", lambda e, xcur=xcur: e.activation(
            out=x_bf[:, 0:4, :].rearrange("p a b -> p (a b)"),
            in_=xcur[:, 0:4, :].rearrange("p a b -> p (a b)"), func=AF.Copy),
            waits=[tok_x[st], tok_xbf_done[st - 1] if st >= 1 else None],
            inc=S)
        tv_xbf = P.op("scalar", lambda e, xcur=xcur: e.activation(
            out=x_bf[:, 4:8, :].rearrange("p a b -> p (a b)"),
            in_=xcur[:, 4:8, :].rearrange("p a b -> p (a b)"), func=AF.Copy),
            waits=[tv_xbf1], inc=S)
        tt_stat = prev_tt_stat
        stat_waits = [tv_xbf1, prev_ts_v_last]
        for k in range(8):
            tt_sum = P.op("tensor", lambda e, k=k: e.matmul(
                ps_a[0:1, :], onescol[:, 0:1], x_bf[:, k, :],
                start=(k == 0), stop=(k == 7)),
                waits=stat_waits if k == 0 else ([tv_xbf] if k == 4 else ()),
                inc=T if k == 7 else None)
        s_ring = 0
        ts1 = P.op("scalar", lambda e, s_ring=s_ring: e.activation(
            out=rows_cur[0:1, s_ring, 0, :], in_=ps_a[0:1, :], func=AF.Copy,
            scale=-1.0 / D),
            waits=[tt_sum, tok_rows_free[st - 1] if st >= 1 else None,
                   ("s_rows", 16 * st) if st >= 1 else None,
                   prev_ts_nm], inc=S)
        tv_m2 = P.op("scalar", lambda e, s_ring=s_ring: e.activation(
            out=srow[0:1, 1, :], in_=rows_cur[0:1, s_ring, 0, :], func=AF.Square),
            waits=[ts1], inc=S)
        tv_var = P.op("vector", lambda e: e.scalar_tensor_tensor(
            srow[0:1, 0, :], ps_s[0][0:1, :], 1.0 / D, srow[0:1, 1, :],
            Alu.mult, Alu.subtract),
            waits=[tt_stat, tv_m2], inc=V)
        ts3 = P.op("scalar", lambda e: e.activation(
            out=srow[0:1, 1, :], in_=srow[0:1, 0, :], func=AF.Ln, bias=EPS),
            waits=[tv_var, tok_const], inc=S)
        ts_rs = P.op("scalar", lambda e, s_ring=s_ring: e.activation(
            out=rows_cur[0:1, s_ring, 1, :], in_=srow[0:1, 1, :], func=AF.Exp,
            scale=-0.5), waits=[ts3], inc=S)
        ts_nm = P.op("scalar", lambda e, s_ring=s_ring: e.copy(
            out=negmb_row[0:1, :], in_=rows_cur[0:1, s_ring, 0, :]),
            waits=[ts1], inc=S)
        prev_ts_nm = ts_nm
        tok_rows_sp = P.dma("sync",
                            rows_h[st].rearrange("a b -> (a b)").unsqueeze(0),
                            rows_cur[0:1, s_ring].rearrange("p a b -> p (a b)"),
                            waits=[ts_rs], inc="s_rows")
        tt_rsb = P.op("tensor", lambda e, s_ring=s_ring: e.matmul(
            ps_s[3][:, :], onesf[0:1, 0:128], rows_cur[0:1, s_ring, 1, :],
            start=True, stop=True),
            waits=[ts_rs, prev_tv_sin, prev_addsB], inc=T)
        for i in range(4):
            tt_rsc = P.op("tensor", lambda e, i=i, s_ring=s_ring: e.matmul(
                ps_s[2][:, i:i + 1], rows_cur[0:1, s_ring, 1, 128 * i:128 * i + 128],
                onesf[0:1, 0:1], start=True, stop=True),
                waits=[prev_ts_rc, prev_addsB] if i == 0 else (),
                inc=T if i == 3 else None)
        tok_tab = P.dma("sync", ctab[:], costab_d[st],
                        waits=[tok_ctab_free], inc="s_tabc")
        tok_tab2 = P.dma("sync", stab[:], sintab_d[st],
                         waits=[tok_ctab_free], inc="s_tabs")
        tv_cos = P.op("vector", lambda e: e.tensor_mul(ctab[:], ctab[:], ps_s[3][:, :]),
                      waits=[tt_rsb, tok_tab, tok_tab2], inc=V)
        tv_sin = P.op("vector", lambda e: e.tensor_mul(stab[:], stab[:], ps_s[3][:, :]),
                      waits=[tt_rsb, tok_tab2], inc=V)
        prev_tv_sin = tv_sin
        ts_rc = P.op("scalar", lambda e: e.copy(out=rs_col[:], in_=ps_s[2][:, 0:4]),
                     waits=[tt_rsc], inc=S)
        prev_ts_rc = ts_rc
        tv_k = [None, None]
        ts_k_prev = None
        tv1_prev = None
        tv2_prev = None
        tt_rot_prev = None
        for r in range(2):
            kps = ps_a if r == 0 else ps_s[0]
            kw = [ts1 if r == 0 else tv_var, tv_xbf, tok_wkT]
            for k in range(8):
                P.op("tensor", lambda e, k=k, r=r, kps=kps: e.matmul(
                    kps[:, :], wkT[:, k, 128 * r:128 * r + 128], x_bf[:, k, :],
                    start=(k == 0), stop=False), waits=kw if k == 0 else ())
            tt_k = P.op("tensor", lambda e, r=r, kps=kps: e.matmul(
                kps[:, :], rsum_k[0:1, 128 * r:128 * r + 128], negmb_row[0:1, :],
                start=False, stop=True), waits=[ts_nm], inc=T)
            skw = [tt_k]
            if prev_kraw_rd[r] is not None:
                skw += prev_kraw_rd[r]
            ts_k = P.op("scalar", lambda e, r=r, kps=kps: e.copy(
                out=kraw_bf[:, r, :], in_=kps[:, :]), waits=skw, inc=S)
            tt_rot = P.op("tensor", lambda e, r=r: e.matmul(
                ps_s[1][:, :], mt[:, :], kraw_bf[:, r, :], start=True, stop=True),
                waits=[ts_k, tv2_prev], inc=T)
            tv1 = P.op("vector", lambda e, r=r: e.tensor_mul(
                tmp1[:], kraw_bf[:, r, :], ctab[:]),
                waits=[ts_k, tv_cos, tv_k[0] if r == 1 else None], inc=V)
            tv2 = P.op("vector", lambda e: e.tensor_mul(tmp2[:], ps_s[1][:, :], stab[:]),
                       waits=[tt_rot, tv_sin], inc=V)
            tv_k[r] = P.op("vector", lambda e, r=r: e.tensor_add(
                kT_sb[:, r, :], tmp1[:], tmp2[:]),
                waits=[tv1, tv2, prev_kt_spill], inc=V)
            prev_kraw_rd[r] = [tv1, tt_rot]
            ts_k_prev, tv1_prev, tv2_prev, tt_rot_prev = ts_k, tv1, tv2, tt_rot
        prev_kt_spill = P.dma("gpsimd", kt_h[st], kT_sb[:],
                              waits=[tv_k[0], tv_k[1]], inc="s_kt")
        trw = [tv_k[0], tv_k[1], tok_const]
        if prev_ts_tok is not None:
            trw.append(prev_ts_tok)
        first_tr = True
        for r in range(2):
            for i in range(4):
                tt_tr = P.op("tensor", lambda e, r=r, i=i: e.transpose(
                    ps_wt[:, 256 * i + 128 * r: 256 * i + 128 * r + 128],
                    kT_sb[:, r, 128 * i:128 * i + 128], identbf[:, :]),
                    waits=trw if first_tr else (),
                    inc=T if (r == 1 and i == 3) else None)
                first_tr = False
        prev_ts_tok = P.op("scalar", lambda e: e.copy(
            out=ktok[:, :], in_=ps_wt[:, :]),
            waits=[tt_tr], inc=S)
        ts_tok = prev_ts_tok
        ts_v = [None] * CPST
        for i in range(CPST):
            vps = ps_a if i % 2 == 0 else ps_s[1]
            vw = [tok_wvT]
            if i > 1:
                vw.append(ts_v[i - 2])
            elif i == 0:
                vw.append(ts_k_prev)
            else:
                vw.append(tv2_prev)
            for k in range(8):
                P.op("tensor", lambda e, k=k, i=i, vps=vps: e.matmul(
                    vps[:, 0:256], x_bf[:, k, 128 * i:128 * i + 128], wvT[:, k, :],
                    start=(k == 0), stop=False), waits=vw if k == 0 else ())
            tt_v = P.op("tensor", lambda e, i=i, vps=vps: e.matmul(
                vps[:, 0:256], negmb_row[0:1, 128 * i:128 * i + 128],
                rsum_v[0:1, :], start=False, stop=True), inc=T)
            ts_v[i] = P.op("scalar", lambda e, i=i, vps=vps: e.activation(
                out=v_bf[:, i, :], in_=vps[:, 0:256], func=AF.Copy,
                scale=rs_col[:, i:i + 1]),
                waits=[tt_v, ts_rc, prev_v_spill], inc=S)
        prev_v_spill = P.dma("gpsimd", v_h[st], v_bf[:], waits=[ts_v[CPST - 1]],
                             inc="s_vv")
        tok_xbf_done[st] = tt_v
        prev_ts_v_last = ts_v[CPST - 1]
        if st + 1 < n_st1:
            prev_tt_stat = pre_mm([tsq_next, tv_var, ts_k_prev])
        if st == 0:
            P.op("vector", lambda e: e.memset(ps_k[0][:, :], 0.0),
                 waits=[prev_state_tt], inc=V)
            tv_z = P.op("vector", lambda e: e.memset(ps_k[1][:, :], 0.0), inc=V)
        for ci in range(CPST):
            c = CPST * st + ci
            pair = [ps_k[0], ps_k[1]] if ci % 2 == 0 else [ps_s[2], ps_s[3]]
            tok_sp = P.dma("sync", st_h[c], state_r[:, c % 2, :],
                           waits=[prev_state_tt], inc=f"s_sp{c % 2}")
            tv_k16 = P.op("vector", lambda e, ci=ci: e.tensor_tensor(
                bass.AP(k16, 0, [[1024, 128], [512, 2], [128, 4], [64, 2], [1, 64]]),
                bass.AP(ktok, 256 * ci,
                        [[1024, 128], [128, 2], [0, 4], [64, 2], [1, 64]]),
                bass.AP(kdec, 0,
                        [[1024, 128], [512, 2], [128, 4], [64, 2], [1, 64]]),
                Alu.mult), waits=[ts_tok], inc=V)
            if ci % 2 == 0:
                bank_free = [prev_addsA] if prev_addsA is not None else [tv_z]
            else:
                bank_free = ([prev_addsB] if prev_addsB is not None else []) + \
                            [ts_rc, tv_cos, tv_sin]
            uw = [tv_k16, ts_v[ci]] + bank_free
            for p in range(8):
                tt_u = P.op("tensor", lambda e, p=p, ci=ci, pair=pair: e.matmul(
                    pair[p // 4][:, 128 * (p % 4):128 * (p % 4) + 128],
                    k16[:, 128 * p:128 * p + 128],
                    v_bf[:, ci, 128 * (p // 4):128 * (p // 4) + 128],
                    start=(p % 4 == 0), stop=(p % 4 == 3)),
                    waits=uw if p == 0 else (), inc=T if p == 7 else None)
            tvm = P.op("vector", lambda e, c=c: e.tensor_mul(
                fbf[:], state_r[:, c % 2, :], bdb[:]),
                waits=[prev_state_tt], inc=V)
            ns = (c + 1) % 2
            prev_sp = tok_sp
            for g in range(2):
                for e_par in range(2):
                    sap = bass.AP(state_r, 2048 * 64 * e_par + 1024 * ns
                                  + 512 * g + 64 * e_par,
                                  [[2048, 64], [128, 4], [1, 64]])
                    fap = bass.AP(fbf, 64 * 1024 * e_par + 512 * g + 64 * e_par,
                                  [[1024, 64], [128, 4], [1, 64]])
                    pap = bass.AP(pair[g], 64 * 512 * e_par + 64 * e_par,
                                  [[512, 64], [128, 4], [1, 64]])
                    tok_last = P.op("vector",
                                    lambda e, sap=sap, fap=fap, pap=pap: e.tensor_add(
                        sap, fap, pap),
                        waits=[tt_u, tvm, prev_sp2] if (g == 0 and e_par == 0) else (),
                        inc=V)
            prev_sp2 = prev_sp
            prev_state_tt = tok_last
            if ci % 2 == 0:
                prev_addsA = tok_last
            else:
                prev_addsB = tok_last
        tok_ctab_free = tv_k[1]
        tok_xb_free[st] = tt_v
        tok_rows_free[st] = tt_rsc

    if not do_coll:
        P.op("gpsimd", lambda e: e.nop(),
             waits=[("s_sp0", P.count["s_sp0"]), ("s_sp1", P.count["s_sp1"]),
                    prev_state_tt])
        with nc.Block() as block:
            P.emit(block)
        stack.close()
        return nc
    tok_src = P.dma("gpsimd", src_h[:, :], state_r[:, N_CH % 2, :],
                    waits=[prev_state_tt], inc="s_g")
    P.op("gpsimd", lambda e: e.collective_compute(
        "AllGather", Alu.bypass,
        ins=[src_h[:, :]], outs=[dst_h[:, :]],
        replica_groups=[[0, 1], [2, 3], [4, 5], [6, 7]]),
        waits=[tok_src], inc="s_c")
    tok_coll = ("s_c", P.count["s_c"])
    tok_g0 = P.dma("gpsimd", fbf[:], dst_h[0:128, :], waits=[tok_coll], inc="s_g")
    tv_sel0 = P.op("vector", lambda e: e.tensor_scalar_mul(
        rrun[:], fbf[:], rmask[:, 0:1]),
        waits=[tok_g0, tok_const, prev_state_tt], inc=V)
    tok_g1 = P.dma("gpsimd", fbf[:], dst_h[128:256, :], waits=[tv_sel0], inc="s_g")
    tv_rrun = P.op("vector", lambda e: e.scalar_tensor_tensor(
        rrun[:], fbf[:], rmask[:, 1:2], rrun[:], Alu.mult, Alu.add),
        waits=[tok_g1], inc=V)

    tok_rsum_q = rowsum(rsum_q, wqT, [(0, 512), (512, 1024)],
                        [tok_wqT, prev_ts_v_last])
    tok_rsum_u1 = rowsum(rsum_u1, u1T, [(0, 256)], [tok_u1, tok_rsum_q])

    tok_x2 = [None] * N_ST
    tok_x2[0] = P.dma("sync", xb[0][:], xt[0], waits=[tok_xb_free[N_ST - 1]],
                      inc="s_x0")
    tok_kt_ld = [None] * N_ST
    tok_v_ld = [None] * N_ST
    tok_st_ld = [None] * N_ST
    tok_kt_ld[0] = P.dma("sync", kT_sb[:], kt_h[0],
                         waits=[("s_kt", 16 * N_ST), prev_state_tt], inc="s_ktl")
    tok_v_ld[0] = P.dma("sync", v_bf[:], v_h[0],
                        waits=[("s_vv", 16 * N_ST), prev_state_tt], inc="s_vvl")
    tok_rows_ld = [None] * N_ST
    tok_rows_ld[0] = P.dma("sync",
                           rows_cur[0:1, 0].rearrange("p a b -> p (a b)"),
                           rows_h[0].rearrange("a b -> (a b)").unsqueeze(0),
                           waits=[tok_rows_free[N_ST - 1], ("s_rows", 16 * N_ST)],
                           inc="s_rows")
    tok_stch = [None] * (N_CH + 2)
    all_sp = [("s_sp0", P.count["s_sp0"]), ("s_sp1", P.count["s_sp1"])]
    tok_stch[0] = P.dma("sync", st_ld[:, 0, :], st_h[0],
                        waits=all_sp, inc="s_st0")
    tok_stch[1] = P.dma("sync", st_ld[:, 1, :], st_h[1],
                        waits=all_sp, inc="s_st1")
    prev_tv_st = [None] * (N_CH + 2)

    tok_out = [None] * N_ST
    tv2_prev = [None]
    tv1_prev2 = [None, None]
    prev_tv_P1 = prev_tv_sin
    prev_S_evac_g = [None, None]
    prev_tt_o = None
    prev_ts_ot1 = None
    tok_rows_used = [None] * (N_ST + 1)
    prev_ts_attn = prev_ts_tok
    prev_negmb_user = None
    prev_tv_xbf2 = None
    prev_tv_rr = None
    tok_qp_free = None
    prev_o_evac = None
    prev_S_evac = None
    tok_xb2_free = [None] * (N_ST + 1)
    tok_ctab2_free = tok_ctab_free
    prev_psa_reader = tok_rsum_u1
    prev_tv_res2_last = None
    prev_tt_P = None
    prev_tsQ = None

    if n_st2 == 0:
        P.op("gpsimd", lambda e: e.nop(), waits=[tok_coll])
        with nc.Block() as block:
            P.emit(block)
        stack.close()
        return nc
    for st in range(n_st2):
        if st + 1 < n_st2:
            w = [tok_xb2_free[st - 1]] if st >= 1 else []
            tok_x2[st + 1] = P.dma("sync", xb[(st + 1) % 2][:], xt[st + 1], waits=w,
                                   inc=f"s_x{(st + 1) % 2}")
        xcur = xb[st % 2]
        s_ring = 0
        tv_xbf = P.op("vector", lambda e, xcur=xcur: e.tensor_copy(x_bf[:], xcur[:]),
                      waits=[tok_x2[st], prev_tt_P, prev_tv_xbf2], inc=V)
        ts_nm = P.op("scalar", lambda e, s_ring=s_ring: e.copy(
            out=negmb_row[0:1, :], in_=rows_cur[0:1, s_ring, 0, :]),
            waits=[tok_rows_ld[st], prev_negmb_user], inc=S)
        tt_rsb = P.op("tensor", lambda e, s_ring=s_ring: e.matmul(
            ps_a[:, :], onesf[0:1, 0:128], rows_cur[0:1, s_ring, 1, :],
            start=True, stop=True),
            waits=[prev_psa_reader, prev_tv_res2_last, tok_rows_ld[st]], inc=T)
        if st == 0:
            tok_tab_n = P.dma("sync", ctab[:], costab_d[0], waits=[tok_ctab2_free],
                              inc="s_tabc")
            tok_tab2_n = P.dma("sync", stab[:], sintab_d[0], waits=[tok_ctab2_free],
                               inc="s_tabs")
        tok_tab, tok_tab2 = tok_tab_n, tok_tab2_n
        tv_cos = P.op("vector", lambda e: e.tensor_mul(ctab[:], ctab[:], ps_a[:, :]),
                      waits=[tt_rsb, tok_tab, tok_tab2], inc=V)
        tv_sin = P.op("vector", lambda e: e.tensor_mul(stab[:], stab[:], ps_a[:, :]),
                      waits=[tt_rsb, tok_tab2], inc=V)
        tv_q = [None] * 8
        ts_q_prev = [None, None]
        pend = []
        def emit_rope(p_, ts_q_, rw_extra=()):
            rw = [ts_q_, tv2_prev[0]] + list(rw_extra)
            if prev_tsQ is not None:
                rw.append(prev_tsQ)
            tt_rot = P.op("tensor", lambda e: e.matmul(
                ps_k[0][:, :], mt[:, :], kraw_bf[:, p_ % 2, :], start=True, stop=True),
                waits=rw, inc=T)
            tv1 = P.op("vector", lambda e, p_=p_: e.tensor_mul(
                tmp1[:], kraw_bf[:, p_ % 2, :], ctab[:]),
                waits=[ts_q_, tv_cos, tv_q[p_ - 1] if p_ > 0 else None], inc=V)
            tv2 = P.op("vector", lambda e: e.tensor_mul(tmp2[:], ps_k[0][:, :], stab[:]),
                       waits=[tt_rot, tv_sin], inc=V)
            tv_q[p_] = P.op("vector", lambda e, p_=p_: e.tensor_add(
                qp[:, p_, :], tmp1[:], tmp2[:]), waits=[tv1, tv2], inc=V)
            tv2_prev[0] = tv2
        for p in range(8):
            qps = ps_a if p % 2 == 0 else ps_s[3]
            qw = [tv_cos, tv_sin, tv_xbf, tok_wqT, tok_rsum_q, ts_nm]
            if p == 0 and tok_qp_free is not None:
                qw.append(tok_qp_free)
            if p == 1:
                qw.append(prev_tv_P1)
            if ts_q_prev[p % 2] is not None:
                qw.append(ts_q_prev[p % 2])
            for k in range(8):
                P.op("tensor", lambda e, k=k, p=p, qps=qps: e.matmul(
                    qps[:, :], wqT[:, k, 128 * p:128 * p + 128], x_bf[:, k, :],
                    start=(k == 0), stop=False), waits=qw if k == 0 else ())
            tt_q = P.op("tensor", lambda e, p=p, qps=qps: e.matmul(
                qps[:, :], rsum_q[0:1, 128 * p:128 * p + 128], negmb_row[0:1, :],
                start=False, stop=True), inc=T)
            skw = [tt_q]
            if p >= 2:
                skw.append(tv1_prev2[p % 2])
            ts_q = P.op("scalar", lambda e, p=p, qps=qps: e.copy(
                out=kraw_bf[:, p % 2, :], in_=qps[:, :]), waits=skw, inc=S)
            ts_q_prev[p % 2] = ts_q
            pend.append((p, ts_q))
            if len(pend) >= 2:
                p_, t_ = pend.pop(0)
                emit_rope(p_, t_)
                tv1_prev2[p_ % 2] = tv_q[p_]
        while pend:
            p_, t_ = pend.pop(0)
            emit_rope(p_, t_)
            tv1_prev2[p_ % 2] = tv_q[p_]
        if st + 1 < n_st2:
            tok_rows_ld[st + 1] = P.dma(
                "sync", rows_cur[0:1, 0].rearrange("p a b -> p (a b)"),
                rows_h[st + 1].rearrange("a b -> (a b)").unsqueeze(0),
                waits=[tt_rsb, ts_nm], inc="s_rows")
            tok_tab_n = P.dma("sync", ctab[:], costab_d[st + 1],
                              waits=[tv_q[7]], inc="s_tabc")
            tok_tab2_n = P.dma("sync", stab[:], sintab_d[st + 1],
                               waits=[tv_q[7]], inc="s_tabs")
        tv_res1 = [None] * 8
        for ci in range(CPST):
            c = CPST * st + ci
            dqw = [tv_q[7]]
            if prev_tt_o is not None:
                dqw.append(prev_tt_o)
            tv_qdq = P.op("vector", lambda e, ci=ci: e.tensor_mul(
                kq_scr[:].rearrange("p (a b) -> p a b", a=8),
                qp[:, :, 128 * ci:128 * ci + 128], qdec[:]),
                waits=dqw, inc=V)
            tv_st = P.op("vector", lambda e, c=c: e.tensor_add(
                st_true[:], st_ld[:, c % 2, :], rrun[:]),
                waits=[tv_rrun, tok_stch[c], prev_tv_rr], inc=V)
            prev_tv_st[c] = tv_st
            if c + 2 <= N_CH - 1:
                tok_stch[c + 2] = P.dma("sync", st_ld[:, c % 2, :], st_h[c + 2],
                                        waits=[tv_st], inc=f"s_st{c % 2}")
            prev_tv_rr = P.op("vector", lambda e: e.tensor_mul(
                rrun[:], rrun[:], bdb[:]), waits=[tv_st], inc=V)
            tv_mask = [None, None]
            for e_par in range(2):
                sw0 = [tv_q[7], tok_kt_ld[st], prev_S_evac_g[e_par]]
                if e_par == 1:
                    sw0.append(prev_tv_P1)
                if prev_tsQ is not None:
                    sw0.append(prev_tsQ)
                first_s = True
                for p in range(8):
                    h = PAIRS[p][e_par]
                    kv = h // 4
                    s = p + 8 * e_par
                    tt_sc = P.op("tensor", lambda e, p=p, e_par=e_par, kv=kv, s=s, ci=ci: e.matmul(
                        ps_s[s // 4][:, 128 * (s % 4):128 * (s % 4) + 128],
                        kT_sb[64 * (kv % 2):64 * (kv % 2) + 64, kv // 2,
                              128 * ci:128 * ci + 128],
                        qp[64 * e_par:64 * e_par + 64, p, 128 * ci:128 * ci + 128],
                        start=True, stop=True),
                        waits=sw0 if first_s else (), inc=T if p == 7 else None)
                    first_s = False
                P.op("vector", lambda e, e_par=e_par: e.tensor_mul(
                    S_bf[:, 1024 * e_par:1024 * e_par + 512], ps_s[2 * e_par][:, :],
                    maskT[:, 1024 * e_par:1024 * e_par + 512]), waits=[tt_sc], inc=V)
                tv_mask[e_par] = P.op("vector", lambda e, e_par=e_par: e.tensor_mul(
                    S_bf[:, 1024 * e_par + 512:1024 * (e_par + 1)],
                    ps_s[2 * e_par + 1][:, :],
                    maskT[:, 1024 * e_par + 512:1024 * (e_par + 1)]), inc=V)
                if e_par == 0:
                    o2w = [tv_qdq, tv_st]
                    if prev_ts_ot1 is not None:
                        o2w.append(prev_ts_ot1)
                    for p in range(8):
                        P.op("tensor", lambda e, p=p: e.matmul(
                            ps_k[p // 4][:, 128 * (p % 4):128 * (p % 4) + 128],
                            kq_scr[:, 128 * p:128 * p + 128],
                            st_true[:, 128 * p:128 * p + 128],
                            start=(p % 4 == 0), stop=False),
                            waits=o2w if p == 0 else ())
            prev_S_evac = tv_mask[1]
            prev_S_evac_g = [tv_mask[0], tv_mask[1]]
            tt_o = None
            for e_par in range(2):
                ow = [tv_mask[e_par], tok_v_ld[st]]
                first_o = True
                for p in range(8):
                    h = PAIRS[p][e_par]
                    kv = h // 4
                    s = p + 8 * e_par
                    tt_o = P.op("tensor", lambda e, p=p, e_par=e_par, kv=kv, s=s, ci=ci: e.matmul(
                        ps_k[p // 4][:, 128 * (p % 4) + 64 * e_par:
                                     128 * (p % 4) + 64 * e_par + 64],
                        S_bf[:, 128 * s:128 * s + 128],
                        v_bf[:, ci, 64 * kv:64 * kv + 64],
                        start=False, stop=(e_par == 1 and p % 4 == 3)),
                        waits=ow if first_o else (),
                        inc=T if (e_par == 1 and p == 7) else None)
                    first_o = False
            ts_ot0 = P.op("scalar", lambda e: e.copy(
                out=S_bf[:, 0:512], in_=ps_k[0][:, :]),
                waits=[tt_o], inc=S)
            ts_ot1 = P.op("scalar", lambda e: e.copy(
                out=S_bf[:, 512:1024], in_=ps_k[1][:, :]), inc=S)
            prev_ts_ot1 = ts_ot1
            prev_tt_o = tt_o
            for j in range(8):
                tt_tra = P.op("tensor", lambda e, j=j: e.transpose(
                    ps_wt[:, 128 * j:128 * j + 128],
                    S_bf[:, 128 * j:128 * j + 128], identbf[:, :]),
                    waits=([ts_ot0, ts_ot1, prev_ts_attn] if j == 0 else ()),
                    inc=T if j == 7 else None)
            prev_o_evac = P.op("scalar", lambda e, ci=ci: e.copy(
                out=attn_bf[:, :, 128 * (ci % 2):128 * (ci % 2) + 128],
                in_=ps_wt[:].rearrange("p (a b) -> p a b", a=8)),
                waits=[tt_tra], inc=S)
            prev_ts_attn = prev_o_evac
            if ci % 2 == 1:
                half = ci // 2
                for dch in range(8):
                    yps = ps_a if (half == 0 or dch % 2 == 0) else ps_s[3]
                    ww = [prev_o_evac, tok_woT]
                    if half == 0:
                        if dch > 0:
                            ww.append(tv_res1[dch - 1])
                        else:
                            ww.append(tv_sin)
                    else:
                        ww.append(prev_S_evac)
                        if dch > 1:
                            ww.append(tv_res1[dch - 2])
                        else:
                            ww.append(tv_res1[7])
                    for p in range(8):
                        tt_y = P.op("tensor", lambda e, p=p, dch=dch, yps=yps: e.matmul(
                            yps[:, 0:256], woT[:, p, 128 * dch:128 * dch + 128],
                            attn_bf[:, p, :],
                            start=(p == 0), stop=(p == 7)),
                            waits=ww if p == 0 else (), inc=T if p == 7 else None)
                    tv_res1[dch] = P.op("vector",
                                        lambda e, dch=dch, half=half, xcur=xcur,
                                        yps=yps: e.tensor_add(
                        xcur[:, dch, 256 * half:256 * half + 256],
                        xcur[:, dch, 256 * half:256 * half + 256],
                        yps[:, 0:256]), waits=[tt_y], inc=V)
        if st + 1 < N_ST:
            tok_kt_ld[st + 1] = P.dma("sync", kT_sb[:], kt_h[st + 1],
                                      waits=[prev_o_evac], inc="s_ktl")
            tok_v_ld[st + 1] = P.dma("sync", v_bf[:], v_h[st + 1],
                                     waits=[prev_o_evac], inc="s_vvl")
        if p2stage <= 3:
            P.op("gpsimd", lambda e: e.nop(), waits=[tv_res1[7]])
            with nc.Block() as block:
                P.emit(block)
            stack.close()
            return nc
        tv_c2 = [None, None]
        for h2 in range(2):
            tv_c2[h2] = P.op("vector", lambda e, h2=h2, xcur=xcur: e.tensor_copy(
                x_bf[:, 4 * h2:4 * h2 + 4, :].rearrange("p a b -> p (a b)"),
                xcur[:, 4 * h2:4 * h2 + 4, :].rearrange("p a b -> p (a b)")),
                waits=[tv_res1[4 * h2 + 3]], inc=V)
        gq = [None] * 8
        for k in range(8):
            gq[k] = P.op("scalar", lambda e, k=k, xcur=xcur: e.activation(
                out=qp[:, k, :], in_=xcur[:, k, :], func=AF.Square),
                waits=[tv_res1[k], tt_sc] if k == 0 else [tv_res1[k]], inc=S)
        tv_xbf2 = tv_c2[1]
        prev_tv_xbf2 = tv_xbf2
        for k in range(8):
            P.op("tensor", lambda e, k=k: e.matmul(
                ps_a[0:1, :], onescol[:, 0:1], x_bf[:, k, :],
                start=(k == 0), stop=(k == 7)),
                waits=[tv_c2[k // 4]] if k % 4 == 0 else ())
        for k in range(8):
            tt_stat = P.op("tensor", lambda e, k=k: e.matmul(
                ps_k[0][0:1, :], onescol[:, 0:1], qp[:, k, :],
                start=(k == 0), stop=(k == 7)),
                waits=[gq[k]],
                inc=T if k == 7 else None)
        tok_qp_free = tt_stat
        ts1 = P.op("scalar", lambda e: e.activation(
            out=negm2_row[0:1, :], in_=ps_a[0:1, :], func=AF.Copy, scale=-1.0 / D),
            waits=[tt_stat], inc=S)
        tv_m2 = P.op("scalar", lambda e: e.activation(
            out=srow[0:1, 1, :], in_=negm2_row[0:1, :], func=AF.Square),
            waits=[ts1], inc=S)
        tv_var = P.op("vector", lambda e: e.scalar_tensor_tensor(
            srow[0:1, 0, :], ps_k[0][0:1, :], 1.0 / D, srow[0:1, 1, :],
            Alu.mult, Alu.subtract),
            waits=[tt_stat, tv_m2], inc=V)
        ts3 = P.op("scalar", lambda e: e.activation(
            out=srow[0:1, 1, :], in_=srow[0:1, 0, :], func=AF.Ln, bias=EPS),
            waits=[tv_var, tok_const], inc=S)
        ts_rs2 = P.op("scalar", lambda e: e.activation(
            out=rs2_row[0:1, :], in_=srow[0:1, 1, :], func=AF.Exp, scale=-0.5),
            waits=[ts3], inc=S)
        ts_nm2 = P.op("scalar", lambda e: e.copy(
            out=negm2b_row[0:1, :], in_=negm2_row[0:1, :]), waits=[ts1], inc=S)
        tt_rsb2 = P.op("tensor", lambda e: e.matmul(
            ps_s[2][:, :], onesf[0:1, 0:128], rs2_row[0:1, :], start=True, stop=True),
            waits=[ts_rs2, prev_S_evac], inc=T)
        ts_rs2sb = P.op("scalar", lambda e: e.copy(out=rs2_sb[:], in_=ps_s[2][:, :]),
                        waits=[tt_rsb2, tv_q[7]], inc=S)
        if p2stage <= 4:
            P.op("gpsimd", lambda e: e.nop(), waits=[ts_rs2sb])
            with nc.Block() as block:
                P.emit(block)
            stack.close()
            return nc
        tv_P = [None, None]
        tt_P = [None, None]
        for r in range(2):
            pps = ps_a if r == 0 else ps_s[3]
            pw = [tv_xbf2, tok_rsum_u1]
            if r == 0:
                pw.append(tv_res1[6])
            else:
                pw.append(tv_res1[7])
            for k in range(8):
                P.op("tensor", lambda e, k=k, r=r, pps=pps: e.matmul(
                    pps[:, :], u1T[:, k, 128 * r:128 * r + 128], x_bf[:, k, :],
                    start=(k == 0), stop=False), waits=pw if k == 0 else ())
        for r in range(2):
            pps = ps_a if r == 0 else ps_s[3]
            tt_P[r] = P.op("tensor", lambda e, r=r, pps=pps: e.matmul(
                pps[:, :], rsum_u1[0:1, 128 * r:128 * r + 128], negm2b_row[0:1, :],
                start=False, stop=True), waits=[ts_nm2] if r == 0 else (), inc=T)
            tv_P[r] = P.op("vector", lambda e, r=r, pps=pps: e.tensor_mul(
                P_bf[:, r, :], pps[:, :], rs2_sb[:]),
                waits=[tt_P[r], ts_rs2sb], inc=V)
        prev_tt_P = tt_P[1]
        ts_H = [None] * 16
        first_q_bank = [True, True]

        def emit_v1(g):
            buf = g % 2
            gw = [tv_P[1], tok_v1]
            if g >= 2:
                gw.append(ts_H[g - 2])
            elif g < 2:
                gw.append(prev_S_evac)
            for j in range(2):
                ff = 2 * g + j
                for r in range(2):
                    tt = P.op("tensor", lambda e, ff=ff, r=r, buf=buf, j=j: e.matmul(
                        ps_s[2 * buf + j][:, :],
                        v1T[:, r, 128 * ff:128 * ff + 128], P_bf[:, r, :],
                        start=(r == 0), stop=(r == 1)),
                        waits=gw if (j == 0 and r == 0) else (),
                        inc=T if (j == 1 and r == 1) else None)
            return tt

        def emit_gelu(g):
            buf = g % 2
            P.op("scalar", lambda e, g=g, buf=buf: e.activation(
                out=H_bf[:, 1024 * buf:1024 * buf + 512],
                in_=ps_s[2 * buf][:, :],
                func=AF.Gelu_apprx_tanh, bias=b1c[:, 2 * g:2 * g + 1]),
                waits=[tt_H_tok[g], tok_b1], inc=S)
            ts_H[g] = P.op("scalar", lambda e, g=g, buf=buf: e.activation(
                out=H_bf[:, 1024 * buf + 512:1024 * (buf + 1)],
                in_=ps_s[2 * buf + 1][:, :],
                func=AF.Gelu_apprx_tanh, bias=b1c[:, 2 * g + 1:2 * g + 2]), inc=S)

        def emit_u2(g):
            nonlocal_last = None
            buf = g % 2
            u2w = [ts_H[g], tok_u2[g // 4]]
            if g == 0:
                u2w.append(prev_o_evac)
            for j in range(2):
                ff = 2 * g + j
                for m in range(2):
                    stt = first_q_bank[m]
                    first_q_bank[m] = False
                    nonlocal_last = P.op("tensor", lambda e, ff=ff, m=m, buf=buf, j=j,
                                         stt=stt, stp=(g == 15 and j == 1): e.matmul(
                        ps_k[m][:, :],
                        u2T[:, ff, 128 * m:128 * m + 128],
                        H_bf[:, 1024 * buf + 512 * j:1024 * buf + 512 * j + 512],
                        start=stt, stop=stp),
                        waits=u2w if (j == 0 and m == 0) else (),
                        inc=T if (g == 15 and j == 1 and m == 1) else None)
            return nonlocal_last

        tt_H_tok = [None] * 16
        tt_u2last = None
        for g in range(16):
            tt_H_tok[g] = emit_v1(g)
            emit_gelu(g)
            if g >= 1:
                tt_u2last = emit_u2(g - 1)
        tt_u2last = emit_u2(15)
        tt_Q = tt_u2last
        P.op("scalar", lambda e: e.copy(
            out=Q_bf[:, 0, :], in_=ps_k[0][:, :]), waits=[tt_Q], inc=S)
        ts_Q = P.op("scalar", lambda e: e.copy(
            out=Q_bf[:, 1, :], in_=ps_k[1][:, :]), inc=S)
        if p2stage <= 5:
            P.op("gpsimd", lambda e: e.nop(), waits=[ts_Q])
            with nc.Block() as block:
                P.emit(block)
            stack.close()
            return nc
        tv_res2 = [None] * 8
        for dch in range(8):
            vps = ps_a if dch % 2 == 0 else ps_s[3]
            vw = [ts_Q, tok_v2]
            if dch > 1:
                vw.append(tv_res2[dch - 2])
            else:
                vw.append(tv_P[1])
                vw.append(tv_res1[7])
            for r in range(2):
                tt_v2 = P.op("tensor", lambda e, r=r, dch=dch, vps=vps: e.matmul(
                    vps[:, :], v2T[:, r, 128 * dch:128 * dch + 128], Q_bf[:, r, :],
                    start=(r == 0), stop=(r == 1)),
                    waits=vw if r == 0 else (), inc=T if r == 1 else None)
            tv_res2[dch] = P.op("vector",
                                lambda e, dch=dch, xcur=xcur,
                                vps=vps: e.scalar_tensor_tensor(
                xcur[:, dch, :], vps[:, :], b2c[:, dch:dch + 1], xcur[:, dch, :],
                Alu.add, Alu.add), waits=[tt_v2], inc=V)
        tok_out[st] = P.dma("sync", out[st], xcur[:], waits=[tv_res2[7]],
                            inc=f"s_out{st % 2}")
        tok_xb2_free[st] = tok_out[st]
        tok_ctab2_free = tv_q[7]
        prev_psa_reader = ts_Q
        prev_tv_res2_last = tv_res2[7]
        prev_tsQ = ts_Q
        prev_tv_P1 = tv_P[1]
        tok_rows_used[st] = tt_rsb
        prev_negmb_user = tt_rsb

    P.op("sync", lambda e: e.nop(),
         waits=[("s_out0", P.count["s_out0"]), ("s_out1", P.count["s_out1"])])
    prev_kt_spill = prev_v_spill

    with nc.Block() as block:
        P.emit(block)
    stack.close()
    return nc


_NC_CACHE = None


def kernel(L0_attn_x, L0_attn_Wq, L0_attn_Wk, L0_attn_Wv, L0_attn_Wo,
           L0_ffn_U1, L0_ffn_V1, L0_ffn_b1, L0_ffn_U2, L0_ffn_V2, L0_ffn_b2):
    global _NC_CACHE
    x = np.asarray(L0_attn_x, np.float32)
    B = x.shape[0]
    perm = _perm_rows()
    wq_p = np.ascontiguousarray(np.asarray(L0_attn_Wq, np.float32)[perm, :])
    wo_p = np.ascontiguousarray(np.asarray(L0_attn_Wo, np.float32)[:, perm])
    wk = np.ascontiguousarray(np.asarray(L0_attn_Wk, np.float32))
    wv = np.ascontiguousarray(np.asarray(L0_attn_Wv, np.float32))
    u1 = np.ascontiguousarray(np.asarray(L0_ffn_U1, np.float32))
    v1 = np.ascontiguousarray(np.asarray(L0_ffn_V1, np.float32))
    u2 = np.ascontiguousarray(np.asarray(L0_ffn_U2, np.float32))
    v2 = np.ascontiguousarray(np.asarray(L0_ffn_V2, np.float32))
    b1c = np.ascontiguousarray(np.asarray(L0_ffn_b1, np.float32).reshape(32, 128).T)
    b2c = np.ascontiguousarray(np.asarray(L0_ffn_b2, np.float32).reshape(8, 128).T)

    consts = [_host_consts(0), _host_consts(1)]
    if _NC_CACHE is None:
        _NC_CACHE = build_nc()
    nc = _NC_CACHE

    in_maps = []
    for c in range(8):
        b, h = c // 2, c % 2
        xs = np.ascontiguousarray(x[b, h * S_CORE:(h + 1) * S_CORE, :].T)
        xtl = np.ascontiguousarray(
            xs.reshape(8, 128, N_ST, ST).transpose(2, 1, 0, 3))
        cc = consts[h]
        in_maps.append(dict(
            xt=xtl, wq=wq_p, wk=wk, wv=wv, wo=wo_p, u1=u1, v1=v1, u2=u2, v2=v2,
            b1c=b1c, b2c=b2c, mt=cc["mt"], identbf=cc["identbf"], mask=cc["mask"],
            qdec=cc["qdec"], kdec=cc["kdec"], bdb=cc["bdb"],
            costab=cc["costab"], sintab=cc["sintab"], rmask=cc["rmask"],
            onesf=cc["onesf"], onesb=cc["onesb"], onescol=cc["onescol"],
            epsc=cc["epsc"],
        ))

    res = run_bass_kernel_spmd(nc, in_maps, core_ids=list(range(8)))

    outp = np.empty((B, 8192, D), np.float32)
    for c in range(8):
        b, h = c // 2, c % 2
        arr = np.asarray(res.results[c]["out"])
        sl = arr.transpose(0, 3, 2, 1).reshape(S_CORE, D)
        outp[b, h * S_CORE:(h + 1) * S_CORE, :] = sl
    return outp


# version 13
# speedup vs baseline: 1.0536x; 1.0131x over previous
import math

import numpy as np
import ml_dtypes

import concourse.bass as bass
import concourse.mybir as mybir
from concourse.bass_utils import run_bass_kernel_spmd

F32 = mybir.dt.float32
BF16 = mybir.dt.bfloat16
AF = mybir.ActivationFunctionType
Alu = mybir.AluOpType

BF = ml_dtypes.bfloat16

D = 1024
HD = 64
NH = 16
NKV = 4
CH = 128
ST = 512
N_ST = 8
N_CH = 32
CPST = ST // CH
S_CORE = 4096
THETA = 10000.0
EPS = 1e-5

PAIRS = [(0, 4), (1, 5), (2, 6), (3, 7), (8, 12), (9, 13), (10, 14), (11, 15)]
HEAD_POS = {}
for _p, (_lo, _hi) in enumerate(PAIRS):
    HEAD_POS[_lo] = (_p, 0)
    HEAD_POS[_hi] = (_p, 1)
SLOPES = [2.0 ** (-8.0 / NH * (h + 1)) for h in range(NH)]


def _rope_tables():
    inv = 1.0 / THETA ** (np.arange(0, HD, 2, dtype=np.float64) / HD)
    freqs = np.outer(np.arange(8192, dtype=np.float64), inv)
    emb = np.concatenate([freqs, freqs], axis=-1)
    return np.cos(emb).astype(np.float32), np.sin(emb).astype(np.float32)


_COS64, _SIN64 = _rope_tables()


def _host_consts(half):
    c = {}
    M = np.zeros((128, 128), np.float32)
    for blk in range(2):
        b0 = 64 * blk
        for i in range(32):
            M[b0 + i, b0 + 32 + i] = -1.0
            M[b0 + 32 + i, b0 + i] = 1.0
    c["mt"] = np.ascontiguousarray(M.T).astype(BF)
    c["identbf"] = np.eye(128, dtype=np.float32).astype(BF)
    mask = np.zeros((128, 16 * 128), np.float32)
    t = np.arange(CH, dtype=np.float64)
    for p in range(8):
        for e in range(2):
            h = PAIRS[p][e]
            s = p + 8 * e
            m = np.where(t[:, None] >= t[None, :],
                         np.exp(-SLOPES[h] * (t[:, None] - t[None, :])), 0.0)
            mask[:, 128 * s:128 * s + 128] = m.T.astype(np.float32)
    c["mask"] = mask.astype(BF)
    qd = np.zeros((128, 8, 128), np.float32)
    for p in range(8):
        for e in range(2):
            h = PAIRS[p][e]
            qd[64 * e:64 * e + 64, p, :] = np.exp(-SLOPES[h] * (t + 1.0))[None, :]
    c["qdec"] = qd.astype(BF)
    kd = np.zeros((128, 8, 2, 64), np.float32)
    for p in range(8):
        for e in range(2):
            h = PAIRS[p][e]
            kd[:, p, e, :] = np.exp(-SLOPES[h] * (CH - 1.0 - t))[:, None]
    c["kdec"] = kd.reshape(128, NH * 64).astype(BF)
    bd = np.zeros((128, 8, 128), np.float32)
    for p in range(8):
        for e in range(2):
            h = PAIRS[p][e]
            bd[:, p, 64 * e:64 * e + 64] = math.exp(-SLOPES[h] * CH)
    c["bdf"] = bd.reshape(128, 1024).copy()
    c["bdb"] = c["bdf"].astype(BF)
    pos0 = half * S_CORE
    cos = np.zeros((N_ST, 128, ST), np.float32)
    sin = np.zeros((N_ST, 128, ST), np.float32)
    for st in range(N_ST):
        pp = np.arange(ST) + pos0 + st * ST
        cs = _COS64[pp, :].T
        sn = _SIN64[pp, :].T
        cos[st, 0:64] = cs
        cos[st, 64:128] = cs
        sin[st, 0:64] = sn
        sin[st, 64:128] = sn
    c["costab"] = cos
    c["sintab"] = sin
    rm = np.zeros((128, 2), np.float32)
    if half == 1:
        rm[:, 0] = 1.0
    c["rmask"] = rm
    c["onesf"] = np.ones((1, 128), np.float32)
    c["epsc"] = np.full((1, 1), EPS, np.float32)
    c["onesb"] = np.ones((1, 512), np.float32).astype(BF)
    c["onescol"] = np.ones((128, 1), np.float32).astype(BF)
    return c


def _perm_rows():
    perm = []
    for (lo, hi) in PAIRS:
        perm.extend(range(64 * lo, 64 * lo + 64))
        perm.extend(range(64 * hi, 64 * hi + 64))
    return np.array(perm, np.int64)


class Prog:
    def __init__(self, nc, sems):
        self.nc = nc
        self.ops = {e: [] for e in ("sync", "gpsimd", "tensor", "vector", "scalar")}
        self.count = {s: 0 for s in sems}
        self.sems = sems

    def op(self, eng, fn, waits=(), inc=None, inc_n=1):
        tok = None
        if inc is not None:
            self.count[inc] += inc_n
            tok = (inc, self.count[inc])
        waits = tuple(w for w in waits if w is not None)
        self.ops[eng].append((fn, waits, inc, inc_n))
        return tok

    def dma(self, eng, out, in_, waits=(), inc=None, **kw):
        return self.op(eng, lambda e: e.dma_start(out=out, in_=in_, **kw),
                       waits=waits, inc=inc, inc_n=16)

    def emit(self, block):
        def make(eng_name):
            ops = self.ops[eng_name]
            sems = self.sems

            def body(eng):
                wm = {}
                for fn, waits, inc, inc_n in ops:
                    for (sn, val) in waits:
                        if wm.get(sn, -1) >= val:
                            continue
                        eng.wait_ge(sems[sn], val)
                        wm[sn] = val
                    instr = fn(eng)
                    if inc is not None:
                        if instr is None:
                            instr = eng.nop()
                        instr.then_inc(sems[inc], inc_n)
            return body

        for name in ("sync", "gpsimd", "tensor", "vector", "scalar"):
            getattr(block, name)(make(name))


def build_nc(n_st1=N_ST, do_coll=True, n_st2=N_ST, p2stage=9):
    nc = bass.Bass("TRN2", target_bir_lowering=False)

    xt = nc.dram_tensor("xt", [N_ST, 128, 8, ST], F32, kind="ExternalInput").ap()
    out = nc.dram_tensor("out", [N_ST, 128, 8, ST], F32, kind="ExternalOutput").ap()
    wq = nc.dram_tensor("wq", [D, D], F32, kind="ExternalInput").ap()
    wk = nc.dram_tensor("wk", [256, D], F32, kind="ExternalInput").ap()
    wv = nc.dram_tensor("wv", [256, D], F32, kind="ExternalInput").ap()
    wo = nc.dram_tensor("wo", [D, D], F32, kind="ExternalInput").ap()
    u1 = nc.dram_tensor("u1", [D, 256], F32, kind="ExternalInput").ap()
    v1 = nc.dram_tensor("v1", [256, 4096], F32, kind="ExternalInput").ap()
    u2 = nc.dram_tensor("u2", [4096, 256], F32, kind="ExternalInput").ap()
    v2 = nc.dram_tensor("v2", [256, D], F32, kind="ExternalInput").ap()
    b1c_d = nc.dram_tensor("b1c", [128, 32], F32, kind="ExternalInput").ap()
    b2c_d = nc.dram_tensor("b2c", [128, 8], F32, kind="ExternalInput").ap()
    mt_d = nc.dram_tensor("mt", [128, 128], BF16, kind="ExternalInput").ap()
    identbf_d = nc.dram_tensor("identbf", [128, 128], BF16, kind="ExternalInput").ap()
    mask_d = nc.dram_tensor("mask", [128, 2048], BF16, kind="ExternalInput").ap()
    qdec_d = nc.dram_tensor("qdec", [128, 8, 128], BF16, kind="ExternalInput").ap()
    kdec_d = nc.dram_tensor("kdec", [128, 1024], BF16, kind="ExternalInput").ap()
    bdb_d = nc.dram_tensor("bdb", [128, 1024], BF16, kind="ExternalInput").ap()
    costab_d = nc.dram_tensor("costab", [N_ST, 128, ST], F32, kind="ExternalInput").ap()
    sintab_d = nc.dram_tensor("sintab", [N_ST, 128, ST], F32, kind="ExternalInput").ap()
    rmask_d = nc.dram_tensor("rmask", [128, 2], F32, kind="ExternalInput").ap()
    onesf_d = nc.dram_tensor("onesf", [1, 128], F32, kind="ExternalInput").ap()
    epsc_d = nc.dram_tensor("epsc", [1, 1], F32, kind="ExternalInput").ap()
    onesb_d = nc.dram_tensor("onesb", [1, 512], BF16, kind="ExternalInput").ap()
    onescol_d = nc.dram_tensor("onescol", [128, 1], BF16, kind="ExternalInput").ap()

    wqbf_h = nc.dram_tensor("wqbf_h", [D, D], BF16, kind="Internal").ap()
    wobf_h = nc.dram_tensor("wobf_h", [D, D], BF16, kind="Internal").ap()
    wkbf_h = nc.dram_tensor("wkbf_h", [256, D], BF16, kind="Internal").ap()
    wvbf_h = nc.dram_tensor("wvbf_h", [256, D], BF16, kind="Internal").ap()
    st_h = nc.dram_tensor("st_h", [N_CH, 128, 1024], BF16, kind="Internal").ap()
    kt_h = nc.dram_tensor("kt_h", [N_ST, 128, 2, ST], BF16, kind="Internal").ap()
    v_h = nc.dram_tensor("v_h", [N_ST, 128, CPST, 256], BF16, kind="Internal").ap()
    rows_h = nc.dram_tensor("rows_h", [N_ST, 2, 512], F32, kind="Internal").ap()
    src_h = nc.dram_tensor("src_h", [128, 1024], F32, kind="Internal").ap()
    dst_h = nc.dram_tensor("dst_h", [256, 1024], F32, kind="Internal").ap()

    import contextlib
    stack = contextlib.ExitStack()
    sb = lambda name, shape, dt: stack.enter_context(nc.sbuf_tensor(name, shape, dt))
    psb = lambda name, shape, dt: stack.enter_context(nc.psum_tensor(name, shape, dt))
    sem = lambda name: stack.enter_context(nc.semaphore(name))

    wqT = sb("wqT", [128, 8, 1024], BF16)
    woT = sb("woT", [128, 8, 1024], BF16)
    wkT = sb("wkT", [128, 8, 256], BF16)
    wvT = sb("wvT", [128, 8, 256], BF16)
    u1T = sb("u1T", [128, 8, 256], BF16)
    v1T = sb("v1T", [128, 2, 4096], BF16)
    u2T = sb("u2T", [128, 32, 256], BF16)
    v2T = sb("v2T", [128, 2, 1024], BF16)
    b1c = sb("b1c_s", [128, 32], F32)
    b2c = sb("b2c_s", [128, 8], F32)
    mt = sb("mt_s", [128, 128], BF16)
    identbf = sb("identbf_s", [128, 128], BF16)
    maskT = sb("maskT", [128, 2048], BF16)
    qdec = sb("qdec_s", [128, 8, 128], BF16)
    kdec = sb("kdec_s", [128, 1024], BF16)
    bdb = sb("bdb_s", [128, 1024], BF16)
    ctab = sb("ctab", [128, ST], F32)
    stab = sb("stab", [128, ST], F32)
    rmask = sb("rmask_s", [128, 2], F32)
    onesf = sb("onesf_s", [1, 128], F32)
    epsc = sb("epsc_s", [1, 1], F32)
    nc.const_aps.register(F32, EPS, epsc[0:1, 0:1])
    onesb = sb("onesb_s", [1, 512], BF16)
    onescol = sb("onescol_s", [128, 1], BF16)
    rsum_q = sb("rsum_q", [1, 1024], BF16)
    rsum_k = sb("rsum_k", [1, 256], BF16)
    rsum_v = sb("rsum_v", [1, 256], BF16)
    rsum_u1 = sb("rsum_u1", [1, 256], BF16)

    xb = [sb(f"xb{i}", [128, 8, ST], F32) for i in range(2)]
    x_bf = sb("x_bf", [128, 8, ST], BF16)
    rows_cur = sb("rows_cur", [1, 1, 2, 512], F32)
    negmb_row = sb("negmb_row", [1, 512], BF16)
    srow = sb("srow", [1, 2, 512], F32)
    rs2_row = sb("rs2_row", [1, 512], F32)
    negm2_row = sb("negm2_row", [1, 512], F32)
    negm2b_row = sb("negm2b_row", [1, 512], BF16)
    rs_col = sb("rs_col", [128, 4], F32)

    kraw_bf = sb("kraw_bf", [128, 2, 512], BF16)
    tmp1 = sb("tmp1", [128, 512], F32)
    tmp2 = sb("tmp2", [128, 512], F32)
    rs2_sb = tmp2
    kT_sb = sb("kT_sb", [128, 2, ST], BF16)
    kq_scr = sb("kq_scr", [128, 1024], BF16)
    ks_scr = sb("ks_scr", [128, 1024], BF16)
    ktok = kq_scr
    k16 = ks_scr
    st_true = ks_scr
    qdq3 = None
    v_bf = sb("v_bf", [128, CPST, 256], BF16)
    state_r = sb("state_r", [128, 2, 1024], BF16)
    fbf = sb("fbf", [128, 1024], BF16)
    rrun = sb("rrun", [128, 1024], BF16)
    st_ld = sb("st_ld", [128, 2, 1024], BF16)
    qp = sb("qp", [128, 8, ST], BF16)
    sh_bf = sb("sh_bf", [128, 2048], BF16)
    S_bf = sh_bf
    H_bf = sh_bf
    attn_bf = sb("attn_bf", [128, 8, 256], BF16)
    pq_bf = sb("pq_bf", [128, 2, 512], BF16)
    P_bf = pq_bf
    Q_bf = pq_bf

    ps_a = psb("ps_a", [128, 512], F32)
    ps_k = [psb("ps_k0", [128, 512], F32), psb("ps_k1", [128, 512], F32)]
    ps_s = [psb(f"ps_s{i}", [128, 512], F32) for i in range(4)]
    ps_wt = psb("ps_wt", [128, 1024], BF16)

    sems = {}
    for s in ("s_const", "s_wkT", "s_wvT", "s_wqT", "s_woT", "s_x0", "s_x1",
              "s_tabc", "s_tabs", "s_rows", "s_out0", "s_out1",
              "s_w1", "s_w2", "s_w3", "s_kt", "s_vv", "s_ktl", "s_vvl", "s_st0", "s_st1",
              "s_sp0", "s_sp1", "s_g", "s_q", "s_t", "s_v", "s_s", "s_c"):
        sems[s] = sem(s)
    P = Prog(nc, sems)

    T, V, S = "s_t", "s_v", "s_s"

    tok_x = [None] * N_ST
    tok_x[0] = P.dma("sync", xb[0][:], xt[0], inc="s_x0")
    tok_state0 = P.op("vector", lambda e: e.memset(state_r[:], 0.0), inc=V)
    tok_const = None
    for dst, src in ((mt, mt_d), (identbf, identbf_d), (maskT, mask_d),
                     (qdec, qdec_d), (kdec, kdec_d), (bdb, bdb_d),
                     (rmask, rmask_d), (onesf, onesf_d), (epsc, epsc_d),
                     (onesb, onesb_d),
                     (onescol, onescol_d), (b2c, b2c_d)):
        tok_const = P.dma("sync", dst[:], src[:, :] if len(src.shape) == 2 else src[:],
                          inc="s_const")

    P.dma("gpsimd", wkbf_h[:, :], wk[:, :], inc="s_w1")
    tok_wkv_h = P.dma("gpsimd", wvbf_h[:, :], wv[:, :], inc="s_w1")
    P.dma("gpsimd", u1T[:], u1.rearrange("(k p) r -> p k r", p=128), inc="s_w2")
    P.dma("gpsimd", v1T[:], v1.rearrange("(k p) f -> p k f", p=128), inc="s_w2")
    for g in range(4):
        P.dma("gpsimd", u2T[:, 8 * g:8 * (g + 1), :],
              u2.rearrange("(k p) r -> p k r", p=128)[:, 8 * g:8 * (g + 1), :],
              inc="s_w2")
    P.dma("gpsimd", v2T[:], v2.rearrange("(k p) d -> p k d", p=128), inc="s_w2")
    tok_w2 = P.dma("gpsimd", b1c[:], b1c_d[:, :], inc="s_w2")
    tok_u1 = tok_v1 = tok_v2 = tok_b1 = tok_w2
    tok_u2 = [tok_w2] * 4
    P.dma("gpsimd", wqbf_h[:, :], wq[:, :], inc="s_w3")
    tok_w3 = P.dma("gpsimd", wobf_h[:, :], wo[:, :], inc="s_w3")
    tok_wq_h = tok_wo_h = tok_w3

    tok_wkT = tok_wvT = tok_wqT = tok_woT = None
    for k in range(8):
        tok_wkT = P.op("sync", lambda e, k=k: e.dma_start_transpose(
            wkT[:, k, :], wkbf_h[:, 128 * k:128 * (k + 1)]),
            waits=[tok_wkv_h], inc="s_wkT", inc_n=16)
    for k in range(8):
        tok_wvT = P.op("sync", lambda e, k=k: e.dma_start_transpose(
            wvT[:, k, :], wvbf_h[:, 128 * k:128 * (k + 1)]),
            waits=[tok_wkv_h], inc="s_wvT", inc_n=16)

    def rowsum(dst_row, srcT, nsl, waits):
        prev_evac = None
        for (a, b) in nsl:
            w0 = list(waits) + ([prev_evac] if prev_evac else [])
            for k in range(8):
                tk = P.op("tensor", lambda e, k=k, a=a, b=b: e.matmul(
                    ps_a[0:1, 0:b - a], onescol[:, 0:1], srcT[:, k, a:b],
                    start=(k == 0), stop=(k == 7)),
                    waits=w0 if k == 0 else (), inc=T if k == 7 else None)
            prev_evac = P.op("scalar", lambda e, a=a, b=b: e.copy(
                out=dst_row[0:1, a:b], in_=ps_a[0:1, 0:b - a]),
                waits=[tk], inc=S)
        return prev_evac

    tok_rsum_k = rowsum(rsum_k, wkT, [(0, 256)], [tok_wkT, tok_const])
    tok_rsum_v = rowsum(rsum_v, wvT, [(0, 256)], [tok_wvT, tok_rsum_k])

    tok_xbf_done = [None] * (N_ST + 1)
    tok_xb_free = [None] * (N_ST + 1)
    prev_sp2 = None
    tok_rows_free = [None] * (N_ST + 1)
    prev_ts_nm = None
    prev_ts_rc = None
    prev_tv_sin = None
    prev_kt_spill = None
    prev_v_spill = None
    tok_ctab_free = None
    prev_state_tt = tok_state0
    prev_ts_v_last = tok_rsum_v
    prev_ts_tok = None

    def pre_sq(stv, sq_waits):
        xv1 = xb[stv % 2][:, 0:4, :].rearrange("p a b -> p (a b)")
        xv2 = xb[stv % 2][:, 4:8, :].rearrange("p a b -> p (a b)")
        P.op("gpsimd", lambda e, xv1=xv1: e.tensor_mul(
            attn_bf.rearrange("p a b -> p (a b)"), xv1, xv1),
            waits=sq_waits, inc="s_q")
        return P.op("gpsimd", lambda e, xv2=xv2: e.tensor_mul(
            sh_bf[:, :], xv2, xv2), inc="s_q")

    def pre_mm(mm_waits):
        for k in range(8):
            srcb = attn_bf if k < 4 else sh_bf
            off = 512 * (k % 4)
            tt_stat = P.op("tensor", lambda e, srcb=srcb, off=off, k=k: e.matmul(
                ps_s[0][0:1, :], onescol[:, 0:1],
                bass.AP(srcb, off, [[2048, 128], [1, 512]]),
                start=(k == 0), stop=(k == 7)),
                waits=mm_waits if k == 0 else (),
                inc=T if k == 7 else None)
        return tt_stat

    tsq_cur = pre_sq(0, [tok_x[0]])
    prev_tt_stat = pre_mm([tsq_cur, tok_const])
    prev_kraw_rd = [None, None]
    prev_addsA = None
    prev_addsB = None
    prev_ttu = [None, None]

    for st in range(n_st1):
        if st + 1 < n_st1:
            w = [tok_xb_free[st - 1]] if st >= 1 else []
            tok_x[st + 1] = P.dma("sync", xb[(st + 1) % 2][:], xt[st + 1],
                                  waits=w, inc=f"s_x{(st + 1) % 2}")
            tsq_next = pre_sq(st + 1, [tok_x[st + 1], prev_tt_stat])
        if st == 3:
            for k in range(8):
                tok_wqT = P.op("sync", lambda e, k=k: e.dma_start_transpose(
                    wqT[:, k, :], wqbf_h[:, 128 * k:128 * (k + 1)]),
                    waits=[tok_w3], inc="s_wqT", inc_n=16)
            for k in range(8):
                tok_woT = P.op("sync", lambda e, k=k: e.dma_start_transpose(
                    woT[:, k, :], wobf_h[:, 128 * k:128 * (k + 1)]),
                    waits=[tok_w3], inc="s_woT", inc_n=16)
        xcur = xb[st % 2]
        tv_xbf1 = P.op("scalar", lambda e, xcur=xcur: e.activation(
            out=x_bf[:, 0:4, :].rearrange("p a b -> p (a b)"),
            in_=xcur[:, 0:4, :].rearrange("p a b -> p (a b)"), func=AF.Copy),
            waits=[tok_x[st], tok_xbf_done[st - 1] if st >= 1 else None],
            inc=S)
        tv_xbf = P.op("scalar", lambda e, xcur=xcur: e.activation(
            out=x_bf[:, 4:8, :].rearrange("p a b -> p (a b)"),
            in_=xcur[:, 4:8, :].rearrange("p a b -> p (a b)"), func=AF.Copy),
            waits=[tv_xbf1], inc=S)
        tt_stat = prev_tt_stat
        stat_waits = [tv_xbf1, prev_ts_v_last]
        for k in range(8):
            tt_sum = P.op("tensor", lambda e, k=k: e.matmul(
                ps_a[0:1, :], onescol[:, 0:1], x_bf[:, k, :],
                start=(k == 0), stop=(k == 7)),
                waits=stat_waits if k == 0 else ([tv_xbf] if k == 4 else ()),
                inc=T if k == 7 else None)
        s_ring = 0
        ts1 = P.op("scalar", lambda e, s_ring=s_ring: e.activation(
            out=rows_cur[0:1, s_ring, 0, :], in_=ps_a[0:1, :], func=AF.Copy,
            scale=-1.0 / D),
            waits=[tt_sum, tok_rows_free[st - 1] if st >= 1 else None,
                   ("s_rows", 16 * st) if st >= 1 else None,
                   prev_ts_nm], inc=S)
        tv_m2 = P.op("scalar", lambda e, s_ring=s_ring: e.activation(
            out=srow[0:1, 1, :], in_=rows_cur[0:1, s_ring, 0, :], func=AF.Square),
            waits=[ts1], inc=S)
        tv_var = P.op("vector", lambda e: e.scalar_tensor_tensor(
            srow[0:1, 0, :], ps_s[0][0:1, :], 1.0 / D, srow[0:1, 1, :],
            Alu.mult, Alu.subtract),
            waits=[tt_stat, tv_m2], inc=V)
        ts3 = P.op("scalar", lambda e: e.activation(
            out=srow[0:1, 1, :], in_=srow[0:1, 0, :], func=AF.Ln, bias=EPS),
            waits=[tv_var, tok_const], inc=S)
        ts_rs = P.op("scalar", lambda e, s_ring=s_ring: e.activation(
            out=rows_cur[0:1, s_ring, 1, :], in_=srow[0:1, 1, :], func=AF.Exp,
            scale=-0.5), waits=[ts3], inc=S)
        ts_nm = P.op("scalar", lambda e, s_ring=s_ring: e.copy(
            out=negmb_row[0:1, :], in_=rows_cur[0:1, s_ring, 0, :]),
            waits=[ts1], inc=S)
        prev_ts_nm = ts_nm
        tok_rows_sp = P.dma("sync",
                            rows_h[st].rearrange("a b -> (a b)").unsqueeze(0),
                            rows_cur[0:1, s_ring].rearrange("p a b -> p (a b)"),
                            waits=[ts_rs], inc="s_rows")
        tt_rsb = P.op("tensor", lambda e, s_ring=s_ring: e.matmul(
            ps_s[3][:, :], onesf[0:1, 0:128], rows_cur[0:1, s_ring, 1, :],
            start=True, stop=True),
            waits=[ts_rs, prev_tv_sin, prev_addsB], inc=T)
        for i in range(4):
            tt_rsc = P.op("tensor", lambda e, i=i, s_ring=s_ring: e.matmul(
                ps_s[2][:, i:i + 1], rows_cur[0:1, s_ring, 1, 128 * i:128 * i + 128],
                onesf[0:1, 0:1], start=True, stop=True),
                waits=[prev_ts_rc, prev_addsB] if i == 0 else (),
                inc=T if i == 3 else None)
        tok_tab = P.dma("sync", ctab[:], costab_d[st],
                        waits=[tok_ctab_free], inc="s_tabc")
        tok_tab2 = P.dma("sync", stab[:], sintab_d[st],
                         waits=[tok_ctab_free], inc="s_tabs")
        tv_cos = P.op("vector", lambda e: e.tensor_mul(ctab[:], ctab[:], ps_s[3][:, :]),
                      waits=[tt_rsb, tok_tab, tok_tab2], inc=V)
        tv_sin = P.op("vector", lambda e: e.tensor_mul(stab[:], stab[:], ps_s[3][:, :]),
                      waits=[tt_rsb, tok_tab2], inc=V)
        prev_tv_sin = tv_sin
        ts_rc = P.op("scalar", lambda e: e.copy(out=rs_col[:], in_=ps_s[2][:, 0:4]),
                     waits=[tt_rsc], inc=S)
        prev_ts_rc = ts_rc
        tv_k = [None, None]
        ts_k_prev = None
        tv1_prev = None
        tv2_prev = None
        tt_rot_prev = None
        for r in range(2):
            kps = ps_a if r == 0 else ps_s[0]
            kw = [ts1 if r == 0 else tv_var, tv_xbf, tok_wkT]
            for k in range(8):
                P.op("tensor", lambda e, k=k, r=r, kps=kps: e.matmul(
                    kps[:, :], wkT[:, k, 128 * r:128 * r + 128], x_bf[:, k, :],
                    start=(k == 0), stop=False), waits=kw if k == 0 else ())
            tt_k = P.op("tensor", lambda e, r=r, kps=kps: e.matmul(
                kps[:, :], rsum_k[0:1, 128 * r:128 * r + 128], negmb_row[0:1, :],
                start=False, stop=True), waits=[ts_nm], inc=T)
            skw = [tt_k]
            if prev_kraw_rd[r] is not None:
                skw += prev_kraw_rd[r]
            ts_k = P.op("scalar", lambda e, r=r, kps=kps: e.copy(
                out=kraw_bf[:, r, :], in_=kps[:, :]), waits=skw, inc=S)
            tt_rot = P.op("tensor", lambda e, r=r: e.matmul(
                ps_s[1][:, :], mt[:, :], kraw_bf[:, r, :], start=True, stop=True),
                waits=[ts_k, tv2_prev], inc=T)
            tv1 = P.op("vector", lambda e, r=r: e.tensor_mul(
                tmp1[:], kraw_bf[:, r, :], ctab[:]),
                waits=[ts_k, tv_cos, tv_k[0] if r == 1 else None], inc=V)
            tv2 = P.op("vector", lambda e: e.tensor_mul(tmp2[:], ps_s[1][:, :], stab[:]),
                       waits=[tt_rot, tv_sin], inc=V)
            tv_k[r] = P.op("vector", lambda e, r=r: e.tensor_add(
                kT_sb[:, r, :], tmp1[:], tmp2[:]),
                waits=[tv1, tv2, prev_kt_spill], inc=V)
            prev_kraw_rd[r] = [tv1, tt_rot]
            ts_k_prev, tv1_prev, tv2_prev, tt_rot_prev = ts_k, tv1, tv2, tt_rot
        prev_kt_spill = P.dma("gpsimd", kt_h[st], kT_sb[:],
                              waits=[tv_k[0], tv_k[1]], inc="s_kt")
        trw = [tv_k[0], tv_k[1], tok_const]
        if prev_ts_tok is not None:
            trw.append(prev_ts_tok)
        first_tr = True
        for r in range(2):
            for i in range(4):
                tt_tr = P.op("tensor", lambda e, r=r, i=i: e.transpose(
                    ps_wt[:, 256 * i + 128 * r: 256 * i + 128 * r + 128],
                    kT_sb[:, r, 128 * i:128 * i + 128], identbf[:, :]),
                    waits=trw if first_tr else (),
                    inc=T if (r == 1 and i == 3) else None)
                first_tr = False
        prev_ts_tok = P.op("scalar", lambda e: e.copy(
            out=ktok[:, :], in_=ps_wt[:, :]),
            waits=[tt_tr], inc=S)
        ts_tok = prev_ts_tok
        ts_v = [None] * CPST
        for i in range(CPST):
            vps = ps_a if i % 2 == 0 else ps_s[1]
            vw = [tok_wvT]
            if i > 1:
                vw.append(ts_v[i - 2])
            elif i == 0:
                vw.append(ts_k_prev)
            else:
                vw.append(tv2_prev)
            for k in range(8):
                P.op("tensor", lambda e, k=k, i=i, vps=vps: e.matmul(
                    vps[:, 0:256], x_bf[:, k, 128 * i:128 * i + 128], wvT[:, k, :],
                    start=(k == 0), stop=False), waits=vw if k == 0 else ())
            tt_v = P.op("tensor", lambda e, i=i, vps=vps: e.matmul(
                vps[:, 0:256], negmb_row[0:1, 128 * i:128 * i + 128],
                rsum_v[0:1, :], start=False, stop=True), inc=T)
            ts_v[i] = P.op("scalar", lambda e, i=i, vps=vps: e.activation(
                out=v_bf[:, i, :], in_=vps[:, 0:256], func=AF.Copy,
                scale=rs_col[:, i:i + 1]),
                waits=[tt_v, ts_rc, prev_v_spill], inc=S)
        prev_v_spill = P.dma("gpsimd", v_h[st], v_bf[:], waits=[ts_v[CPST - 1]],
                             inc="s_vv")
        tok_xbf_done[st] = tt_v
        prev_ts_v_last = ts_v[CPST - 1]
        if st + 1 < n_st1:
            prev_tt_stat = pre_mm([tsq_next, tv_var, ts_k_prev])
        if st == 0:
            P.op("vector", lambda e: e.memset(ps_k[0][:, :], 0.0),
                 waits=[prev_state_tt], inc=V)
            tv_z = P.op("vector", lambda e: e.memset(ps_k[1][:, :], 0.0), inc=V)
        def emit_k16(ci):
            kb = k16 if ci % 2 == 0 else pq_bf
            return P.op("vector", lambda e, ci=ci, kb=kb: e.tensor_tensor(
                bass.AP(kb, 0, [[1024, 128], [512, 2], [128, 4], [64, 2], [1, 64]]),
                bass.AP(ktok, 256 * ci,
                        [[1024, 128], [128, 2], [0, 4], [64, 2], [1, 64]]),
                bass.AP(kdec, 0,
                        [[1024, 128], [512, 2], [128, 4], [64, 2], [1, 64]]),
                Alu.mult),
                waits=[ts_tok, prev_ttu[ci % 2]], inc=V)

        k16tok = [None] * CPST
        k16tok[0] = emit_k16(0)
        k16tok[1] = emit_k16(1)
        for ci in range(CPST):
            c = CPST * st + ci
            pair = [ps_k[0], ps_k[1]] if ci % 2 == 0 else [ps_s[2], ps_s[3]]
            kb = k16 if ci % 2 == 0 else pq_bf
            tok_sp = P.dma("sync", st_h[c], state_r[:, c % 2, :],
                           waits=[prev_state_tt], inc=f"s_sp{c % 2}")
            if ci % 2 == 0:
                bank_free = [prev_addsA] if prev_addsA is not None else [tv_z]
            else:
                bank_free = ([prev_addsB] if prev_addsB is not None else []) + \
                            [ts_rc, tv_cos, tv_sin]
            uw = [k16tok[ci], ts_v[ci]] + bank_free
            for p in range(8):
                tt_u = P.op("tensor", lambda e, p=p, ci=ci, pair=pair, kb=kb: e.matmul(
                    pair[p // 4][:, 128 * (p % 4):128 * (p % 4) + 128],
                    kb[:, p // 4, 128 * (p % 4):128 * (p % 4) + 128]
                    if ci % 2 == 1 else kb[:, 128 * p:128 * p + 128],
                    v_bf[:, ci, 128 * (p // 4):128 * (p // 4) + 128],
                    start=(p % 4 == 0), stop=(p % 4 == 3)),
                    waits=uw if p == 0 else (), inc=T if p == 7 else None)
            prev_ttu[ci % 2] = tt_u
            if ci + 2 < CPST:
                k16tok[ci + 2] = emit_k16(ci + 2)
            tvm = P.op("vector", lambda e, c=c: e.tensor_mul(
                fbf[:], state_r[:, c % 2, :], bdb[:]),
                waits=[prev_state_tt], inc=V)
            ns = (c + 1) % 2
            prev_sp = tok_sp
            for g in range(2):
                for e_par in range(2):
                    sap = bass.AP(state_r, 2048 * 64 * e_par + 1024 * ns
                                  + 512 * g + 64 * e_par,
                                  [[2048, 64], [128, 4], [1, 64]])
                    fap = bass.AP(fbf, 64 * 1024 * e_par + 512 * g + 64 * e_par,
                                  [[1024, 64], [128, 4], [1, 64]])
                    pap = bass.AP(pair[g], 64 * 512 * e_par + 64 * e_par,
                                  [[512, 64], [128, 4], [1, 64]])
                    tok_last = P.op("vector",
                                    lambda e, sap=sap, fap=fap, pap=pap: e.tensor_add(
                        sap, fap, pap),
                        waits=[tt_u, tvm, prev_sp2] if (g == 0 and e_par == 0) else (),
                        inc=V)
            prev_sp2 = prev_sp
            prev_state_tt = tok_last
            if ci % 2 == 0:
                prev_addsA = tok_last
            else:
                prev_addsB = tok_last
        tok_ctab_free = tv_k[1]
        tok_xb_free[st] = tt_v
        tok_rows_free[st] = tt_rsc

    if not do_coll:
        P.op("gpsimd", lambda e: e.nop(),
             waits=[("s_sp0", P.count["s_sp0"]), ("s_sp1", P.count["s_sp1"]),
                    prev_state_tt])
        with nc.Block() as block:
            P.emit(block)
        stack.close()
        return nc
    tok_src = P.dma("gpsimd", src_h[:, :], state_r[:, N_CH % 2, :],
                    waits=[prev_state_tt], inc="s_g")
    P.op("gpsimd", lambda e: e.collective_compute(
        "AllGather", Alu.bypass,
        ins=[src_h[:, :]], outs=[dst_h[:, :]],
        replica_groups=[[0, 1], [2, 3], [4, 5], [6, 7]]),
        waits=[tok_src], inc="s_c")
    tok_coll = ("s_c", P.count["s_c"])
    tok_g0 = P.dma("gpsimd", fbf[:], dst_h[0:128, :], waits=[tok_coll], inc="s_g")
    tv_sel0 = P.op("vector", lambda e: e.tensor_scalar_mul(
        rrun[:], fbf[:], rmask[:, 0:1]),
        waits=[tok_g0, tok_const, prev_state_tt], inc=V)
    tok_g1 = P.dma("gpsimd", fbf[:], dst_h[128:256, :], waits=[tv_sel0], inc="s_g")
    tv_rrun = P.op("vector", lambda e: e.scalar_tensor_tensor(
        rrun[:], fbf[:], rmask[:, 1:2], rrun[:], Alu.mult, Alu.add),
        waits=[tok_g1], inc=V)

    tok_rsum_q = rowsum(rsum_q, wqT, [(0, 512), (512, 1024)],
                        [tok_wqT, prev_ts_v_last])
    tok_rsum_u1 = rowsum(rsum_u1, u1T, [(0, 256)], [tok_u1, tok_rsum_q])

    tok_x2 = [None] * N_ST
    tok_x2[0] = P.dma("sync", xb[0][:], xt[0], waits=[tok_xb_free[N_ST - 1]],
                      inc="s_x0")
    tok_kt_ld = [None] * N_ST
    tok_v_ld = [None] * N_ST
    tok_st_ld = [None] * N_ST
    tok_kt_ld[0] = P.dma("sync", kT_sb[:], kt_h[0],
                         waits=[("s_kt", 16 * N_ST), prev_state_tt], inc="s_ktl")
    tok_v_ld[0] = P.dma("sync", v_bf[:], v_h[0],
                        waits=[("s_vv", 16 * N_ST), prev_state_tt], inc="s_vvl")
    tok_rows_ld = [None] * N_ST
    tok_rows_ld[0] = P.dma("sync",
                           rows_cur[0:1, 0].rearrange("p a b -> p (a b)"),
                           rows_h[0].rearrange("a b -> (a b)").unsqueeze(0),
                           waits=[tok_rows_free[N_ST - 1], ("s_rows", 16 * N_ST)],
                           inc="s_rows")
    tok_stch = [None] * (N_CH + 2)
    all_sp = [("s_sp0", P.count["s_sp0"]), ("s_sp1", P.count["s_sp1"])]
    tok_stch[0] = P.dma("sync", st_ld[:, 0, :], st_h[0],
                        waits=all_sp, inc="s_st0")
    tok_stch[1] = P.dma("sync", st_ld[:, 1, :], st_h[1],
                        waits=all_sp, inc="s_st1")
    prev_tv_st = [None] * (N_CH + 2)

    tok_out = [None] * N_ST
    tv2_prev = [None]
    tv1_prev2 = [None, None]
    prev_tv_P1 = prev_tv_sin
    prev_S_evac_g = [None, None]
    prev_tt_o = None
    prev_ts_ot1 = None
    tok_rows_used = [None] * (N_ST + 1)
    prev_ts_attn = prev_ts_tok
    prev_negmb_user = None
    prev_tv_xbf2 = None
    prev_tv_rr = None
    tok_qp_free = None
    prev_o_evac = None
    prev_S_evac = None
    tok_xb2_free = [None] * (N_ST + 1)
    tok_ctab2_free = tok_ctab_free
    prev_psa_reader = tok_rsum_u1
    prev_tv_res2_last = None
    prev_tt_P = None
    prev_tsQ = None

    if n_st2 == 0:
        P.op("gpsimd", lambda e: e.nop(), waits=[tok_coll])
        with nc.Block() as block:
            P.emit(block)
        stack.close()
        return nc
    for st in range(n_st2):
        if st + 1 < n_st2:
            w = [tok_xb2_free[st - 1]] if st >= 1 else []
            tok_x2[st + 1] = P.dma("sync", xb[(st + 1) % 2][:], xt[st + 1], waits=w,
                                   inc=f"s_x{(st + 1) % 2}")
        xcur = xb[st % 2]
        s_ring = 0
        tv_xbf = P.op("vector", lambda e, xcur=xcur: e.tensor_copy(x_bf[:], xcur[:]),
                      waits=[tok_x2[st], prev_tt_P, prev_tv_xbf2], inc=V)
        ts_nm = P.op("scalar", lambda e, s_ring=s_ring: e.copy(
            out=negmb_row[0:1, :], in_=rows_cur[0:1, s_ring, 0, :]),
            waits=[tok_rows_ld[st], prev_negmb_user], inc=S)
        tt_rsb = P.op("tensor", lambda e, s_ring=s_ring: e.matmul(
            ps_a[:, :], onesf[0:1, 0:128], rows_cur[0:1, s_ring, 1, :],
            start=True, stop=True),
            waits=[prev_psa_reader, prev_tv_res2_last, tok_rows_ld[st]], inc=T)
        if st == 0:
            tok_tab_n = P.dma("sync", ctab[:], costab_d[0], waits=[tok_ctab2_free],
                              inc="s_tabc")
            tok_tab2_n = P.dma("sync", stab[:], sintab_d[0], waits=[tok_ctab2_free],
                               inc="s_tabs")
        tok_tab, tok_tab2 = tok_tab_n, tok_tab2_n
        tv_cos = P.op("vector", lambda e: e.tensor_mul(ctab[:], ctab[:], ps_a[:, :]),
                      waits=[tt_rsb, tok_tab, tok_tab2], inc=V)
        tv_sin = P.op("vector", lambda e: e.tensor_mul(stab[:], stab[:], ps_a[:, :]),
                      waits=[tt_rsb, tok_tab2], inc=V)
        tv_q = [None] * 8
        ts_q_prev = [None, None]
        pend = []
        def emit_rope(p_, ts_q_, rw_extra=()):
            rw = [ts_q_, tv2_prev[0]] + list(rw_extra)
            if prev_tsQ is not None:
                rw.append(prev_tsQ)
            tt_rot = P.op("tensor", lambda e: e.matmul(
                ps_k[0][:, :], mt[:, :], kraw_bf[:, p_ % 2, :], start=True, stop=True),
                waits=rw, inc=T)
            tv1 = P.op("vector", lambda e, p_=p_: e.tensor_mul(
                tmp1[:], kraw_bf[:, p_ % 2, :], ctab[:]),
                waits=[ts_q_, tv_cos, tv_q[p_ - 1] if p_ > 0 else None], inc=V)
            tv2 = P.op("vector", lambda e: e.tensor_mul(tmp2[:], ps_k[0][:, :], stab[:]),
                       waits=[tt_rot, tv_sin], inc=V)
            tv_q[p_] = P.op("vector", lambda e, p_=p_: e.tensor_add(
                qp[:, p_, :], tmp1[:], tmp2[:]), waits=[tv1, tv2], inc=V)
            tv2_prev[0] = tv2
        for p in range(8):
            qps = ps_a if p % 2 == 0 else ps_s[3]
            qw = [tv_cos, tv_sin, tv_xbf, tok_wqT, tok_rsum_q, ts_nm]
            if p == 0 and tok_qp_free is not None:
                qw.append(tok_qp_free)
            if p == 1:
                qw.append(prev_tv_P1)
            if ts_q_prev[p % 2] is not None:
                qw.append(ts_q_prev[p % 2])
            for k in range(8):
                P.op("tensor", lambda e, k=k, p=p, qps=qps: e.matmul(
                    qps[:, :], wqT[:, k, 128 * p:128 * p + 128], x_bf[:, k, :],
                    start=(k == 0), stop=False), waits=qw if k == 0 else ())
            tt_q = P.op("tensor", lambda e, p=p, qps=qps: e.matmul(
                qps[:, :], rsum_q[0:1, 128 * p:128 * p + 128], negmb_row[0:1, :],
                start=False, stop=True), inc=T)
            skw = [tt_q]
            if p >= 2:
                skw.append(tv1_prev2[p % 2])
            ts_q = P.op("scalar", lambda e, p=p, qps=qps: e.copy(
                out=kraw_bf[:, p % 2, :], in_=qps[:, :]), waits=skw, inc=S)
            ts_q_prev[p % 2] = ts_q
            pend.append((p, ts_q))
            if len(pend) >= 2:
                p_, t_ = pend.pop(0)
                emit_rope(p_, t_)
                tv1_prev2[p_ % 2] = tv_q[p_]
        while pend:
            p_, t_ = pend.pop(0)
            emit_rope(p_, t_)
            tv1_prev2[p_ % 2] = tv_q[p_]
        if st + 1 < n_st2:
            tok_rows_ld[st + 1] = P.dma(
                "sync", rows_cur[0:1, 0].rearrange("p a b -> p (a b)"),
                rows_h[st + 1].rearrange("a b -> (a b)").unsqueeze(0),
                waits=[tt_rsb, ts_nm], inc="s_rows")
            tok_tab_n = P.dma("sync", ctab[:], costab_d[st + 1],
                              waits=[tv_q[7]], inc="s_tabc")
            tok_tab2_n = P.dma("sync", stab[:], sintab_d[st + 1],
                               waits=[tv_q[7]], inc="s_tabs")
        tv_res1 = [None] * 8
        for ci in range(CPST):
            c = CPST * st + ci
            dqw = [tv_q[7]]
            if prev_tt_o is not None:
                dqw.append(prev_tt_o)
            tv_qdq = P.op("vector", lambda e, ci=ci: e.tensor_mul(
                kq_scr[:].rearrange("p (a b) -> p a b", a=8),
                qp[:, :, 128 * ci:128 * ci + 128], qdec[:]),
                waits=dqw, inc=V)
            tv_st = P.op("vector", lambda e, c=c: e.tensor_add(
                st_true[:], st_ld[:, c % 2, :], rrun[:]),
                waits=[tv_rrun, tok_stch[c], prev_tv_rr], inc=V)
            prev_tv_st[c] = tv_st
            if c + 2 <= N_CH - 1:
                tok_stch[c + 2] = P.dma("sync", st_ld[:, c % 2, :], st_h[c + 2],
                                        waits=[tv_st], inc=f"s_st{c % 2}")
            prev_tv_rr = P.op("vector", lambda e: e.tensor_mul(
                rrun[:], rrun[:], bdb[:]), waits=[tv_st], inc=V)
            tv_mask = [None, None]
            for e_par in range(2):
                sw0 = [tv_q[7], tok_kt_ld[st], prev_S_evac_g[e_par]]
                if e_par == 1:
                    sw0.append(prev_tv_P1)
                if prev_tsQ is not None:
                    sw0.append(prev_tsQ)
                first_s = True
                for p in range(8):
                    h = PAIRS[p][e_par]
                    kv = h // 4
                    s = p + 8 * e_par
                    tt_sc = P.op("tensor", lambda e, p=p, e_par=e_par, kv=kv, s=s, ci=ci: e.matmul(
                        ps_s[s // 4][:, 128 * (s % 4):128 * (s % 4) + 128],
                        kT_sb[64 * (kv % 2):64 * (kv % 2) + 64, kv // 2,
                              128 * ci:128 * ci + 128],
                        qp[64 * e_par:64 * e_par + 64, p, 128 * ci:128 * ci + 128],
                        start=True, stop=True),
                        waits=sw0 if first_s else (), inc=T if p == 7 else None)
                    first_s = False
                P.op("vector", lambda e, e_par=e_par: e.tensor_mul(
                    S_bf[:, 1024 * e_par:1024 * e_par + 512], ps_s[2 * e_par][:, :],
                    maskT[:, 1024 * e_par:1024 * e_par + 512]), waits=[tt_sc], inc=V)
                tv_mask[e_par] = P.op("vector", lambda e, e_par=e_par: e.tensor_mul(
                    S_bf[:, 1024 * e_par + 512:1024 * (e_par + 1)],
                    ps_s[2 * e_par + 1][:, :],
                    maskT[:, 1024 * e_par + 512:1024 * (e_par + 1)]), inc=V)
                if e_par == 0:
                    o2w = [tv_qdq, tv_st]
                    if prev_ts_ot1 is not None:
                        o2w.append(prev_ts_ot1)
                    for p in range(8):
                        P.op("tensor", lambda e, p=p: e.matmul(
                            ps_k[p // 4][:, 128 * (p % 4):128 * (p % 4) + 128],
                            kq_scr[:, 128 * p:128 * p + 128],
                            st_true[:, 128 * p:128 * p + 128],
                            start=(p % 4 == 0), stop=False),
                            waits=o2w if p == 0 else ())
            prev_S_evac = tv_mask[1]
            prev_S_evac_g = [tv_mask[0], tv_mask[1]]
            tt_o = None
            for e_par in range(2):
                ow = [tv_mask[e_par], tok_v_ld[st]]
                first_o = True
                for p in range(8):
                    h = PAIRS[p][e_par]
                    kv = h // 4
                    s = p + 8 * e_par
                    tt_o = P.op("tensor", lambda e, p=p, e_par=e_par, kv=kv, s=s, ci=ci: e.matmul(
                        ps_k[p // 4][:, 128 * (p % 4) + 64 * e_par:
                                     128 * (p % 4) + 64 * e_par + 64],
                        S_bf[:, 128 * s:128 * s + 128],
                        v_bf[:, ci, 64 * kv:64 * kv + 64],
                        start=False, stop=(e_par == 1 and p % 4 == 3)),
                        waits=ow if first_o else (),
                        inc=T if (e_par == 1 and p == 7) else None)
                    first_o = False
            ts_ot0 = P.op("scalar", lambda e: e.copy(
                out=S_bf[:, 0:512], in_=ps_k[0][:, :]),
                waits=[tt_o], inc=S)
            ts_ot1 = P.op("scalar", lambda e: e.copy(
                out=S_bf[:, 512:1024], in_=ps_k[1][:, :]), inc=S)
            prev_ts_ot1 = ts_ot1
            prev_tt_o = tt_o
            for j in range(8):
                tt_tra = P.op("tensor", lambda e, j=j: e.transpose(
                    ps_wt[:, 128 * j:128 * j + 128],
                    S_bf[:, 128 * j:128 * j + 128], identbf[:, :]),
                    waits=([ts_ot0, ts_ot1, prev_ts_attn] if j == 0 else ()),
                    inc=T if j == 7 else None)
            prev_o_evac = P.op("scalar", lambda e, ci=ci: e.copy(
                out=attn_bf[:, :, 128 * (ci % 2):128 * (ci % 2) + 128],
                in_=ps_wt[:].rearrange("p (a b) -> p a b", a=8)),
                waits=[tt_tra], inc=S)
            prev_ts_attn = prev_o_evac
            if ci % 2 == 1:
                half = ci // 2
                for dch in range(8):
                    yps = ps_a if (half == 0 or dch % 2 == 0) else ps_s[3]
                    ww = [prev_o_evac, tok_woT]
                    if half == 0:
                        if dch > 0:
                            ww.append(tv_res1[dch - 1])
                        else:
                            ww.append(tv_sin)
                    else:
                        ww.append(prev_S_evac)
                        if dch > 1:
                            ww.append(tv_res1[dch - 2])
                        else:
                            ww.append(tv_res1[7])
                    for p in range(8):
                        tt_y = P.op("tensor", lambda e, p=p, dch=dch, yps=yps: e.matmul(
                            yps[:, 0:256], woT[:, p, 128 * dch:128 * dch + 128],
                            attn_bf[:, p, :],
                            start=(p == 0), stop=(p == 7)),
                            waits=ww if p == 0 else (), inc=T if p == 7 else None)
                    tv_res1[dch] = P.op("vector",
                                        lambda e, dch=dch, half=half, xcur=xcur,
                                        yps=yps: e.tensor_add(
                        xcur[:, dch, 256 * half:256 * half + 256],
                        xcur[:, dch, 256 * half:256 * half + 256],
                        yps[:, 0:256]), waits=[tt_y], inc=V)
        if st + 1 < N_ST:
            tok_kt_ld[st + 1] = P.dma("sync", kT_sb[:], kt_h[st + 1],
                                      waits=[prev_o_evac], inc="s_ktl")
            tok_v_ld[st + 1] = P.dma("sync", v_bf[:], v_h[st + 1],
                                     waits=[prev_o_evac], inc="s_vvl")
        if p2stage <= 3:
            P.op("gpsimd", lambda e: e.nop(), waits=[tv_res1[7]])
            with nc.Block() as block:
                P.emit(block)
            stack.close()
            return nc
        tv_c2 = [None, None]
        for h2 in range(2):
            tv_c2[h2] = P.op("vector", lambda e, h2=h2, xcur=xcur: e.tensor_copy(
                x_bf[:, 4 * h2:4 * h2 + 4, :].rearrange("p a b -> p (a b)"),
                xcur[:, 4 * h2:4 * h2 + 4, :].rearrange("p a b -> p (a b)")),
                waits=[tv_res1[4 * h2 + 3]], inc=V)
        gq = [None] * 8
        for k in range(8):
            gq[k] = P.op("scalar", lambda e, k=k, xcur=xcur: e.activation(
                out=qp[:, k, :], in_=xcur[:, k, :], func=AF.Square),
                waits=[tv_res1[k], tt_sc] if k == 0 else [tv_res1[k]], inc=S)
        tv_xbf2 = tv_c2[1]
        prev_tv_xbf2 = tv_xbf2
        for k in range(8):
            P.op("tensor", lambda e, k=k: e.matmul(
                ps_a[0:1, :], onescol[:, 0:1], x_bf[:, k, :],
                start=(k == 0), stop=(k == 7)),
                waits=[tv_c2[k // 4]] if k % 4 == 0 else ())
        for k in range(8):
            tt_stat = P.op("tensor", lambda e, k=k: e.matmul(
                ps_k[0][0:1, :], onescol[:, 0:1], qp[:, k, :],
                start=(k == 0), stop=(k == 7)),
                waits=[gq[k]],
                inc=T if k == 7 else None)
        tok_qp_free = tt_stat
        ts1 = P.op("scalar", lambda e: e.activation(
            out=negm2_row[0:1, :], in_=ps_a[0:1, :], func=AF.Copy, scale=-1.0 / D),
            waits=[tt_stat], inc=S)
        tv_m2 = P.op("scalar", lambda e: e.activation(
            out=srow[0:1, 1, :], in_=negm2_row[0:1, :], func=AF.Square),
            waits=[ts1], inc=S)
        tv_var = P.op("vector", lambda e: e.scalar_tensor_tensor(
            srow[0:1, 0, :], ps_k[0][0:1, :], 1.0 / D, srow[0:1, 1, :],
            Alu.mult, Alu.subtract),
            waits=[tt_stat, tv_m2], inc=V)
        ts3 = P.op("scalar", lambda e: e.activation(
            out=srow[0:1, 1, :], in_=srow[0:1, 0, :], func=AF.Ln, bias=EPS),
            waits=[tv_var, tok_const], inc=S)
        ts_rs2 = P.op("scalar", lambda e: e.activation(
            out=rs2_row[0:1, :], in_=srow[0:1, 1, :], func=AF.Exp, scale=-0.5),
            waits=[ts3], inc=S)
        ts_nm2 = P.op("scalar", lambda e: e.copy(
            out=negm2b_row[0:1, :], in_=negm2_row[0:1, :]), waits=[ts1], inc=S)
        tt_rsb2 = P.op("tensor", lambda e: e.matmul(
            ps_s[2][:, :], onesf[0:1, 0:128], rs2_row[0:1, :], start=True, stop=True),
            waits=[ts_rs2, prev_S_evac], inc=T)
        ts_rs2sb = P.op("scalar", lambda e: e.copy(out=rs2_sb[:], in_=ps_s[2][:, :]),
                        waits=[tt_rsb2, tv_q[7]], inc=S)
        if p2stage <= 4:
            P.op("gpsimd", lambda e: e.nop(), waits=[ts_rs2sb])
            with nc.Block() as block:
                P.emit(block)
            stack.close()
            return nc
        tv_P = [None, None]
        tt_P = [None, None]
        for r in range(2):
            pps = ps_a if r == 0 else ps_s[3]
            pw = [tv_xbf2, tok_rsum_u1]
            if r == 0:
                pw.append(tv_res1[6])
            else:
                pw.append(tv_res1[7])
            for k in range(8):
                P.op("tensor", lambda e, k=k, r=r, pps=pps: e.matmul(
                    pps[:, :], u1T[:, k, 128 * r:128 * r + 128], x_bf[:, k, :],
                    start=(k == 0), stop=False), waits=pw if k == 0 else ())
        for r in range(2):
            pps = ps_a if r == 0 else ps_s[3]
            tt_P[r] = P.op("tensor", lambda e, r=r, pps=pps: e.matmul(
                pps[:, :], rsum_u1[0:1, 128 * r:128 * r + 128], negm2b_row[0:1, :],
                start=False, stop=True), waits=[ts_nm2] if r == 0 else (), inc=T)
            tv_P[r] = P.op("vector", lambda e, r=r, pps=pps: e.tensor_mul(
                P_bf[:, r, :], pps[:, :], rs2_sb[:]),
                waits=[tt_P[r], ts_rs2sb], inc=V)
        prev_tt_P = tt_P[1]
        ts_H = [None] * 16
        first_q_bank = [True, True]

        def emit_v1(g):
            buf = g % 2
            gw = [tv_P[1], tok_v1]
            if g >= 2:
                gw.append(ts_H[g - 2])
            elif g < 2:
                gw.append(prev_S_evac)
            for j in range(2):
                ff = 2 * g + j
                for r in range(2):
                    tt = P.op("tensor", lambda e, ff=ff, r=r, buf=buf, j=j: e.matmul(
                        ps_s[2 * buf + j][:, :],
                        v1T[:, r, 128 * ff:128 * ff + 128], P_bf[:, r, :],
                        start=(r == 0), stop=(r == 1)),
                        waits=gw if (j == 0 and r == 0) else (),
                        inc=T if (j == 1 and r == 1) else None)
            return tt

        def emit_gelu(g):
            buf = g % 2
            P.op("scalar", lambda e, g=g, buf=buf: e.activation(
                out=H_bf[:, 1024 * buf:1024 * buf + 512],
                in_=ps_s[2 * buf][:, :],
                func=AF.Gelu_apprx_tanh, bias=b1c[:, 2 * g:2 * g + 1]),
                waits=[tt_H_tok[g], tok_b1], inc=S)
            ts_H[g] = P.op("scalar", lambda e, g=g, buf=buf: e.activation(
                out=H_bf[:, 1024 * buf + 512:1024 * (buf + 1)],
                in_=ps_s[2 * buf + 1][:, :],
                func=AF.Gelu_apprx_tanh, bias=b1c[:, 2 * g + 1:2 * g + 2]), inc=S)

        def emit_u2(g):
            nonlocal_last = None
            buf = g % 2
            u2w = [ts_H[g], tok_u2[g // 4]]
            if g == 0:
                u2w.append(prev_o_evac)
            for j in range(2):
                ff = 2 * g + j
                for m in range(2):
                    stt = first_q_bank[m]
                    first_q_bank[m] = False
                    nonlocal_last = P.op("tensor", lambda e, ff=ff, m=m, buf=buf, j=j,
                                         stt=stt, stp=(g == 15 and j == 1): e.matmul(
                        ps_k[m][:, :],
                        u2T[:, ff, 128 * m:128 * m + 128],
                        H_bf[:, 1024 * buf + 512 * j:1024 * buf + 512 * j + 512],
                        start=stt, stop=stp),
                        waits=u2w if (j == 0 and m == 0) else (),
                        inc=T if (g == 15 and j == 1 and m == 1) else None)
            return nonlocal_last

        tt_H_tok = [None] * 16
        tt_u2last = None
        for g in range(16):
            tt_H_tok[g] = emit_v1(g)
            emit_gelu(g)
            if g >= 1:
                tt_u2last = emit_u2(g - 1)
        tt_u2last = emit_u2(15)
        tt_Q = tt_u2last
        P.op("scalar", lambda e: e.copy(
            out=Q_bf[:, 0, :], in_=ps_k[0][:, :]), waits=[tt_Q], inc=S)
        ts_Q = P.op("scalar", lambda e: e.copy(
            out=Q_bf[:, 1, :], in_=ps_k[1][:, :]), inc=S)
        if p2stage <= 5:
            P.op("gpsimd", lambda e: e.nop(), waits=[ts_Q])
            with nc.Block() as block:
                P.emit(block)
            stack.close()
            return nc
        tv_res2 = [None] * 8
        for dch in range(8):
            vps = ps_a if dch % 2 == 0 else ps_s[3]
            vw = [ts_Q, tok_v2]
            if dch > 1:
                vw.append(tv_res2[dch - 2])
            else:
                vw.append(tv_P[1])
                vw.append(tv_res1[7])
            for r in range(2):
                tt_v2 = P.op("tensor", lambda e, r=r, dch=dch, vps=vps: e.matmul(
                    vps[:, :], v2T[:, r, 128 * dch:128 * dch + 128], Q_bf[:, r, :],
                    start=(r == 0), stop=(r == 1)),
                    waits=vw if r == 0 else (), inc=T if r == 1 else None)
            tv_res2[dch] = P.op("vector",
                                lambda e, dch=dch, xcur=xcur,
                                vps=vps: e.scalar_tensor_tensor(
                xcur[:, dch, :], vps[:, :], b2c[:, dch:dch + 1], xcur[:, dch, :],
                Alu.add, Alu.add), waits=[tt_v2], inc=V)
        tok_out[st] = P.dma("sync", out[st], xcur[:], waits=[tv_res2[7]],
                            inc=f"s_out{st % 2}")
        tok_xb2_free[st] = tok_out[st]
        tok_ctab2_free = tv_q[7]
        prev_psa_reader = ts_Q
        prev_tv_res2_last = tv_res2[7]
        prev_tsQ = ts_Q
        prev_tv_P1 = tv_P[1]
        tok_rows_used[st] = tt_rsb
        prev_negmb_user = tt_rsb

    P.op("sync", lambda e: e.nop(),
         waits=[("s_out0", P.count["s_out0"]), ("s_out1", P.count["s_out1"])])
    prev_kt_spill = prev_v_spill

    with nc.Block() as block:
        P.emit(block)
    stack.close()
    return nc


_NC_CACHE = None


def kernel(L0_attn_x, L0_attn_Wq, L0_attn_Wk, L0_attn_Wv, L0_attn_Wo,
           L0_ffn_U1, L0_ffn_V1, L0_ffn_b1, L0_ffn_U2, L0_ffn_V2, L0_ffn_b2):
    global _NC_CACHE
    x = np.asarray(L0_attn_x, np.float32)
    B = x.shape[0]
    perm = _perm_rows()
    wq_p = np.ascontiguousarray(np.asarray(L0_attn_Wq, np.float32)[perm, :])
    wo_p = np.ascontiguousarray(np.asarray(L0_attn_Wo, np.float32)[:, perm])
    wk = np.ascontiguousarray(np.asarray(L0_attn_Wk, np.float32))
    wv = np.ascontiguousarray(np.asarray(L0_attn_Wv, np.float32))
    u1 = np.ascontiguousarray(np.asarray(L0_ffn_U1, np.float32))
    v1 = np.ascontiguousarray(np.asarray(L0_ffn_V1, np.float32))
    u2 = np.ascontiguousarray(np.asarray(L0_ffn_U2, np.float32))
    v2 = np.ascontiguousarray(np.asarray(L0_ffn_V2, np.float32))
    b1c = np.ascontiguousarray(np.asarray(L0_ffn_b1, np.float32).reshape(32, 128).T)
    b2c = np.ascontiguousarray(np.asarray(L0_ffn_b2, np.float32).reshape(8, 128).T)

    consts = [_host_consts(0), _host_consts(1)]
    if _NC_CACHE is None:
        _NC_CACHE = build_nc()
    nc = _NC_CACHE

    in_maps = []
    for c in range(8):
        b, h = c // 2, c % 2
        xs = np.ascontiguousarray(x[b, h * S_CORE:(h + 1) * S_CORE, :].T)
        xtl = np.ascontiguousarray(
            xs.reshape(8, 128, N_ST, ST).transpose(2, 1, 0, 3))
        cc = consts[h]
        in_maps.append(dict(
            xt=xtl, wq=wq_p, wk=wk, wv=wv, wo=wo_p, u1=u1, v1=v1, u2=u2, v2=v2,
            b1c=b1c, b2c=b2c, mt=cc["mt"], identbf=cc["identbf"], mask=cc["mask"],
            qdec=cc["qdec"], kdec=cc["kdec"], bdb=cc["bdb"],
            costab=cc["costab"], sintab=cc["sintab"], rmask=cc["rmask"],
            onesf=cc["onesf"], onesb=cc["onesb"], onescol=cc["onescol"],
            epsc=cc["epsc"],
        ))

    res = run_bass_kernel_spmd(nc, in_maps, core_ids=list(range(8)))

    outp = np.empty((B, 8192, D), np.float32)
    for c in range(8):
        b, h = c // 2, c % 2
        arr = np.asarray(res.results[c]["out"])
        sl = arr.transpose(0, 3, 2, 1).reshape(S_CORE, D)
        outp[b, h * S_CORE:(h + 1) * S_CORE, :] = sl
    return outp


# version 14
# speedup vs baseline: 1.1025x; 1.0464x over previous
import math

import numpy as np
import ml_dtypes

import concourse.bass as bass
import concourse.mybir as mybir
from concourse.bass_utils import run_bass_kernel_spmd

F32 = mybir.dt.float32
BF16 = mybir.dt.bfloat16
AF = mybir.ActivationFunctionType
Alu = mybir.AluOpType

BF = ml_dtypes.bfloat16

D = 1024
HD = 64
NH = 16
NKV = 4
CH = 128
ST = 512
N_ST = 8
N_CH = 32
CPST = ST // CH
S_CORE = 4096
THETA = 10000.0
EPS = 1e-5

PAIRS = [(0, 4), (1, 5), (2, 6), (3, 7), (8, 12), (9, 13), (10, 14), (11, 15)]
HEAD_POS = {}
for _p, (_lo, _hi) in enumerate(PAIRS):
    HEAD_POS[_lo] = (_p, 0)
    HEAD_POS[_hi] = (_p, 1)
SLOPES = [2.0 ** (-8.0 / NH * (h + 1)) for h in range(NH)]


def _rope_tables():
    inv = 1.0 / THETA ** (np.arange(0, HD, 2, dtype=np.float64) / HD)
    freqs = np.outer(np.arange(8192, dtype=np.float64), inv)
    emb = np.concatenate([freqs, freqs], axis=-1)
    return np.cos(emb).astype(np.float32), np.sin(emb).astype(np.float32)


_COS64, _SIN64 = _rope_tables()


def _host_consts(half):
    c = {}
    M = np.zeros((128, 128), np.float32)
    for blk in range(2):
        b0 = 64 * blk
        for i in range(32):
            M[b0 + i, b0 + 32 + i] = -1.0
            M[b0 + 32 + i, b0 + i] = 1.0
    c["mt"] = np.ascontiguousarray(M.T).astype(BF)
    c["identbf"] = np.eye(128, dtype=np.float32).astype(BF)
    mask = np.zeros((128, 16 * 128), np.float32)
    t = np.arange(CH, dtype=np.float64)
    for p in range(8):
        for e in range(2):
            h = PAIRS[p][e]
            s = p + 8 * e
            m = np.where(t[:, None] >= t[None, :],
                         np.exp(-SLOPES[h] * (t[:, None] - t[None, :])), 0.0)
            mask[:, 128 * s:128 * s + 128] = m.T.astype(np.float32)
    c["mask"] = mask.astype(BF)
    qd = np.zeros((128, 8, 128), np.float32)
    for p in range(8):
        for e in range(2):
            h = PAIRS[p][e]
            qd[64 * e:64 * e + 64, p, :] = np.exp(-SLOPES[h] * (t + 1.0))[None, :]
    c["qdec"] = qd.astype(BF)
    kd = np.zeros((128, 8, 2, 64), np.float32)
    for p in range(8):
        for e in range(2):
            h = PAIRS[p][e]
            kd[:, p, e, :] = np.exp(-SLOPES[h] * (CH - 1.0 - t))[:, None]
    c["kdec"] = kd.reshape(128, NH * 64).astype(BF)
    bd = np.zeros((128, 8, 128), np.float32)
    for p in range(8):
        for e in range(2):
            h = PAIRS[p][e]
            bd[:, p, 64 * e:64 * e + 64] = math.exp(-SLOPES[h] * CH)
    c["bdf"] = bd.reshape(128, 1024).copy()
    c["bdb"] = c["bdf"].astype(BF)
    pos0 = half * S_CORE
    cos = np.zeros((N_ST, 128, ST), np.float32)
    sin = np.zeros((N_ST, 128, ST), np.float32)
    for st in range(N_ST):
        pp = np.arange(ST) + pos0 + st * ST
        cs = _COS64[pp, :].T
        sn = _SIN64[pp, :].T
        cos[st, 0:64] = cs
        cos[st, 64:128] = cs
        sin[st, 0:64] = sn
        sin[st, 64:128] = sn
    c["costab"] = cos
    c["sintab"] = sin
    rm = np.zeros((128, 2), np.float32)
    if half == 1:
        rm[:, 0] = 1.0
    c["rmask"] = rm
    c["onesf"] = np.ones((1, 128), np.float32)
    c["epsc"] = np.full((1, 1), EPS, np.float32)
    c["onesb"] = np.ones((1, 512), np.float32).astype(BF)
    c["onescol"] = np.ones((128, 1), np.float32).astype(BF)
    return c


def _perm_rows():
    perm = []
    for (lo, hi) in PAIRS:
        perm.extend(range(64 * lo, 64 * lo + 64))
        perm.extend(range(64 * hi, 64 * hi + 64))
    return np.array(perm, np.int64)


class Prog:
    def __init__(self, nc, sems):
        self.nc = nc
        self.ops = {e: [] for e in ("sync", "gpsimd", "tensor", "vector", "scalar")}
        self.count = {s: 0 for s in sems}
        self.sems = sems

    def op(self, eng, fn, waits=(), inc=None, inc_n=1):
        tok = None
        if inc is not None:
            self.count[inc] += inc_n
            tok = (inc, self.count[inc])
        waits = tuple(w for w in waits if w is not None)
        self.ops[eng].append((fn, waits, inc, inc_n))
        return tok

    def dma(self, eng, out, in_, waits=(), inc=None, **kw):
        return self.op(eng, lambda e: e.dma_start(out=out, in_=in_, **kw),
                       waits=waits, inc=inc, inc_n=16)

    def emit(self, block):
        def make(eng_name):
            ops = self.ops[eng_name]
            sems = self.sems

            def body(eng):
                wm = {}
                for fn, waits, inc, inc_n in ops:
                    for (sn, val) in waits:
                        if wm.get(sn, -1) >= val:
                            continue
                        eng.wait_ge(sems[sn], val)
                        wm[sn] = val
                    instr = fn(eng)
                    if inc is not None:
                        if instr is None:
                            instr = eng.nop()
                        instr.then_inc(sems[inc], inc_n)
            return body

        for name in ("sync", "gpsimd", "tensor", "vector", "scalar"):
            getattr(block, name)(make(name))


def build_nc(n_st1=N_ST, do_coll=True, n_st2=N_ST, p2stage=9):
    nc = bass.Bass("TRN2", target_bir_lowering=False)

    xt = nc.dram_tensor("xt", [N_ST, 128, 8, ST], F32, kind="ExternalInput").ap()
    out = nc.dram_tensor("out", [N_ST, 128, 8, ST], F32, kind="ExternalOutput").ap()
    wq = nc.dram_tensor("wq", [D, D], F32, kind="ExternalInput").ap()
    wk = nc.dram_tensor("wk", [256, D], F32, kind="ExternalInput").ap()
    wv = nc.dram_tensor("wv", [256, D], F32, kind="ExternalInput").ap()
    wo = nc.dram_tensor("wo", [D, D], F32, kind="ExternalInput").ap()
    u1 = nc.dram_tensor("u1", [D, 256], F32, kind="ExternalInput").ap()
    v1 = nc.dram_tensor("v1", [256, 4096], F32, kind="ExternalInput").ap()
    u2 = nc.dram_tensor("u2", [4096, 256], F32, kind="ExternalInput").ap()
    v2 = nc.dram_tensor("v2", [256, D], F32, kind="ExternalInput").ap()
    b1c_d = nc.dram_tensor("b1c", [128, 32], F32, kind="ExternalInput").ap()
    b2c_d = nc.dram_tensor("b2c", [128, 8], F32, kind="ExternalInput").ap()
    mt_d = nc.dram_tensor("mt", [128, 128], BF16, kind="ExternalInput").ap()
    identbf_d = nc.dram_tensor("identbf", [128, 128], BF16, kind="ExternalInput").ap()
    mask_d = nc.dram_tensor("mask", [128, 2048], BF16, kind="ExternalInput").ap()
    qdec_d = nc.dram_tensor("qdec", [128, 8, 128], BF16, kind="ExternalInput").ap()
    kdec_d = nc.dram_tensor("kdec", [128, 1024], BF16, kind="ExternalInput").ap()
    bdb_d = nc.dram_tensor("bdb", [128, 1024], BF16, kind="ExternalInput").ap()
    costab_d = nc.dram_tensor("costab", [N_ST, 128, ST], F32, kind="ExternalInput").ap()
    sintab_d = nc.dram_tensor("sintab", [N_ST, 128, ST], F32, kind="ExternalInput").ap()
    rmask_d = nc.dram_tensor("rmask", [128, 2], F32, kind="ExternalInput").ap()
    onesf_d = nc.dram_tensor("onesf", [1, 128], F32, kind="ExternalInput").ap()
    epsc_d = nc.dram_tensor("epsc", [1, 1], F32, kind="ExternalInput").ap()
    onesb_d = nc.dram_tensor("onesb", [1, 512], BF16, kind="ExternalInput").ap()
    onescol_d = nc.dram_tensor("onescol", [128, 1], BF16, kind="ExternalInput").ap()

    wqbf_h = nc.dram_tensor("wqbf_h", [D, D], BF16, kind="Internal").ap()
    wobf_h = nc.dram_tensor("wobf_h", [D, D], BF16, kind="Internal").ap()
    wkbf_h = nc.dram_tensor("wkbf_h", [256, D], BF16, kind="Internal").ap()
    wvbf_h = nc.dram_tensor("wvbf_h", [256, D], BF16, kind="Internal").ap()
    st_h = nc.dram_tensor("st_h", [N_CH, 128, 1024], BF16, kind="Internal").ap()
    kt_h = nc.dram_tensor("kt_h", [N_ST, 128, 2, ST], BF16, kind="Internal").ap()
    v_h = nc.dram_tensor("v_h", [N_ST, 128, CPST, 256], BF16, kind="Internal").ap()
    rows_h = nc.dram_tensor("rows_h", [N_ST, 2, 512], F32, kind="Internal").ap()
    src_h = nc.dram_tensor("src_h", [128, 1024], F32, kind="Internal").ap()
    dst_h = nc.dram_tensor("dst_h", [256, 1024], F32, kind="Internal").ap()

    import contextlib
    stack = contextlib.ExitStack()
    sb = lambda name, shape, dt: stack.enter_context(nc.sbuf_tensor(name, shape, dt))
    psb = lambda name, shape, dt: stack.enter_context(nc.psum_tensor(name, shape, dt))
    sem = lambda name: stack.enter_context(nc.semaphore(name))

    wqT = sb("wqT", [128, 8, 1024], BF16)
    woT = sb("woT", [128, 8, 1024], BF16)
    wkT = sb("wkT", [128, 8, 256], BF16)
    wvT = sb("wvT", [128, 8, 256], BF16)
    u1T = sb("u1T", [128, 8, 256], BF16)
    v1T = sb("v1T", [128, 2, 4096], BF16)
    u2T = sb("u2T", [128, 32, 256], BF16)
    v2T = sb("v2T", [128, 2, 1024], BF16)
    b1c = sb("b1c_s", [128, 32], F32)
    b2c = sb("b2c_s", [128, 8], F32)
    mt = sb("mt_s", [128, 128], BF16)
    identbf = sb("identbf_s", [128, 128], BF16)
    maskT = sb("maskT", [128, 2048], BF16)
    qdec = sb("qdec_s", [128, 8, 128], BF16)
    kdec = sb("kdec_s", [128, 1024], BF16)
    bdb = sb("bdb_s", [128, 1024], BF16)
    ctab = sb("ctab", [128, ST], F32)
    stab = sb("stab", [128, ST], F32)
    rmask = sb("rmask_s", [128, 2], F32)
    onesf = sb("onesf_s", [1, 128], F32)
    epsc = sb("epsc_s", [1, 1], F32)
    nc.const_aps.register(F32, EPS, epsc[0:1, 0:1])
    onesb = sb("onesb_s", [1, 512], BF16)
    onescol = sb("onescol_s", [128, 1], BF16)
    rsum_q = sb("rsum_q", [1, 1024], BF16)
    rsum_k = sb("rsum_k", [1, 256], BF16)
    rsum_v = sb("rsum_v", [1, 256], BF16)
    rsum_u1 = sb("rsum_u1", [1, 256], BF16)

    xb = [sb(f"xb{i}", [128, 8, ST], F32) for i in range(2)]
    x_bf = sb("x_bf", [128, 8, ST], BF16)
    rows_cur = sb("rows_cur", [1, 1, 2, 512], F32)
    negmb_row = sb("negmb_row", [1, 512], BF16)
    srow = sb("srow", [1, 2, 512], F32)
    rs2_row = sb("rs2_row", [1, 512], F32)
    negm2_row = sb("negm2_row", [1, 512], F32)
    negm2b_row = sb("negm2b_row", [1, 512], BF16)
    rs_col = sb("rs_col", [128, 4], F32)

    kraw_bf = sb("kraw_bf", [128, 2, 512], BF16)
    tmp1 = sb("tmp1", [128, 512], F32)
    tmp2 = sb("tmp2", [128, 512], F32)
    rs2_sb = tmp2
    kT_sb = sb("kT_sb", [128, 2, ST], BF16)
    kq_scr = sb("kq_scr", [128, 1024], BF16)
    ks_scr = sb("ks_scr", [128, 1024], BF16)
    ktok = kq_scr
    k16 = ks_scr
    st_true = ks_scr
    qdq3 = None
    v_bf = sb("v_bf", [128, CPST, 256], BF16)
    state_r = sb("state_r", [128, 2, 1024], BF16)
    fbf = sb("fbf", [128, 1024], BF16)
    rrun = sb("rrun", [128, 1024], BF16)
    st_ld = sb("st_ld", [128, 2, 1024], BF16)
    qp = sb("qp", [128, 8, ST], BF16)
    sh_bf = sb("sh_bf", [128, 2048], BF16)
    S_bf = sh_bf
    H_bf = sh_bf
    attn_bf = sb("attn_bf", [128, 8, 256], BF16)
    pq_bf = sb("pq_bf", [128, 2, 512], BF16)
    P_bf = pq_bf
    Q_bf = pq_bf

    ps_a = psb("ps_a", [128, 512], F32)
    ps_k = [psb("ps_k0", [128, 512], F32), psb("ps_k1", [128, 512], F32)]
    ps_s = [psb(f"ps_s{i}", [128, 512], F32) for i in range(4)]
    ps_wt = psb("ps_wt", [128, 1024], BF16)

    sems = {}
    for s in ("s_const", "s_wkT", "s_wvT", "s_wqT", "s_woT", "s_x0", "s_x1",
              "s_tabc", "s_tabs", "s_rows", "s_out0", "s_out1",
              "s_w1", "s_w2", "s_w3", "s_kt", "s_vv", "s_ktl", "s_vvl", "s_st0", "s_st1",
              "s_sp0", "s_sp1", "s_g", "s_q", "s_t", "s_v", "s_s", "s_c"):
        sems[s] = sem(s)
    P = Prog(nc, sems)

    T, V, S = "s_t", "s_v", "s_s"

    tok_x = [None] * N_ST
    tok_x[0] = P.dma("sync", xb[0][:], xt[0], inc="s_x0")
    tok_state0 = P.op("vector", lambda e: e.memset(state_r[:], 0.0), inc=V)
    tok_const = None
    for dst, src in ((mt, mt_d), (identbf, identbf_d), (maskT, mask_d),
                     (qdec, qdec_d), (kdec, kdec_d), (bdb, bdb_d),
                     (rmask, rmask_d), (onesf, onesf_d), (epsc, epsc_d),
                     (onesb, onesb_d),
                     (onescol, onescol_d), (b2c, b2c_d)):
        tok_const = P.dma("sync", dst[:], src[:, :] if len(src.shape) == 2 else src[:],
                          inc="s_const")

    P.dma("gpsimd", wkbf_h[:, :], wk[:, :], inc="s_w1")
    tok_wkv_h = P.dma("gpsimd", wvbf_h[:, :], wv[:, :], inc="s_w1")
    P.dma("gpsimd", u1T[:], u1.rearrange("(k p) r -> p k r", p=128), inc="s_w2")
    P.dma("gpsimd", v1T[:], v1.rearrange("(k p) f -> p k f", p=128), inc="s_w2")
    for g in range(4):
        P.dma("gpsimd", u2T[:, 8 * g:8 * (g + 1), :],
              u2.rearrange("(k p) r -> p k r", p=128)[:, 8 * g:8 * (g + 1), :],
              inc="s_w2")
    P.dma("gpsimd", v2T[:], v2.rearrange("(k p) d -> p k d", p=128), inc="s_w2")
    tok_w2 = P.dma("gpsimd", b1c[:], b1c_d[:, :], inc="s_w2")
    tok_u1 = tok_v1 = tok_v2 = tok_b1 = tok_w2
    tok_u2 = [tok_w2] * 4
    P.dma("gpsimd", wqbf_h[:, :], wq[:, :], inc="s_w3")
    tok_w3 = P.dma("gpsimd", wobf_h[:, :], wo[:, :], inc="s_w3")
    tok_wq_h = tok_wo_h = tok_w3

    tok_wkT = tok_wvT = tok_wqT = tok_woT = None
    for k in range(8):
        tok_wkT = P.op("sync", lambda e, k=k: e.dma_start_transpose(
            wkT[:, k, :], wkbf_h[:, 128 * k:128 * (k + 1)]),
            waits=[tok_wkv_h], inc="s_wkT", inc_n=16)
    for k in range(8):
        tok_wvT = P.op("sync", lambda e, k=k: e.dma_start_transpose(
            wvT[:, k, :], wvbf_h[:, 128 * k:128 * (k + 1)]),
            waits=[tok_wkv_h], inc="s_wvT", inc_n=16)

    def rowsum(dst_row, srcT, nsl, waits):
        prev_evac = None
        for (a, b) in nsl:
            w0 = list(waits) + ([prev_evac] if prev_evac else [])
            for k in range(8):
                tk = P.op("tensor", lambda e, k=k, a=a, b=b: e.matmul(
                    ps_a[0:1, 0:b - a], onescol[:, 0:1], srcT[:, k, a:b],
                    start=(k == 0), stop=(k == 7)),
                    waits=w0 if k == 0 else (), inc=T if k == 7 else None)
            prev_evac = P.op("scalar", lambda e, a=a, b=b: e.copy(
                out=dst_row[0:1, a:b], in_=ps_a[0:1, 0:b - a]),
                waits=[tk], inc=S)
        return prev_evac

    tok_rsum_k = rowsum(rsum_k, wkT, [(0, 256)], [tok_wkT, tok_const])
    tok_rsum_v = rowsum(rsum_v, wvT, [(0, 256)], [tok_wvT, tok_rsum_k])

    tok_xbf_done = [None] * (N_ST + 1)
    tok_xb_free = [None] * (N_ST + 1)
    prev_sp2 = None
    tok_rows_free = [None] * (N_ST + 1)
    prev_ts_nm = None
    prev_ts_rc = None
    prev_tv_sin = None
    prev_kt_spill = None
    prev_v_spill = None
    tok_ctab_free = None
    prev_state_tt = tok_state0
    prev_ts_v_last = tok_rsum_v
    prev_ts_tok = None

    def pre_sq(stv, sq_waits):
        xv1 = xb[stv % 2][:, 0:4, :].rearrange("p a b -> p (a b)")
        xv2 = xb[stv % 2][:, 4:8, :].rearrange("p a b -> p (a b)")
        P.op("gpsimd", lambda e, xv1=xv1: e.tensor_mul(
            attn_bf.rearrange("p a b -> p (a b)"), xv1, xv1),
            waits=sq_waits, inc="s_q")
        return P.op("gpsimd", lambda e, xv2=xv2: e.tensor_mul(
            sh_bf[:, :], xv2, xv2), inc="s_q")

    def pre_mm(mm_waits):
        for k in range(8):
            srcb = attn_bf if k < 4 else sh_bf
            off = 512 * (k % 4)
            tt_stat = P.op("tensor", lambda e, srcb=srcb, off=off, k=k: e.matmul(
                ps_s[0][0:1, :], onescol[:, 0:1],
                bass.AP(srcb, off, [[2048, 128], [1, 512]]),
                start=(k == 0), stop=(k == 7)),
                waits=mm_waits if k == 0 else (),
                inc=T if k == 7 else None)
        return tt_stat

    tsq_cur = pre_sq(0, [tok_x[0]])
    prev_tt_stat = pre_mm([tsq_cur, tok_const])
    prev_kraw_rd = [None, None]
    prev_addsA = None
    prev_addsB = None
    prev_ttu = [None, None]

    for st in range(n_st1):
        if st + 1 < n_st1:
            w = [tok_xb_free[st - 1]] if st >= 1 else []
            tok_x[st + 1] = P.dma("sync", xb[(st + 1) % 2][:], xt[st + 1],
                                  waits=w, inc=f"s_x{(st + 1) % 2}")
            tsq_next = pre_sq(st + 1, [tok_x[st + 1], prev_tt_stat])
        if st == 3:
            for k in range(8):
                tok_wqT = P.op("sync", lambda e, k=k: e.dma_start_transpose(
                    wqT[:, k, :], wqbf_h[:, 128 * k:128 * (k + 1)]),
                    waits=[tok_w3], inc="s_wqT", inc_n=16)
            for k in range(8):
                tok_woT = P.op("sync", lambda e, k=k: e.dma_start_transpose(
                    woT[:, k, :], wobf_h[:, 128 * k:128 * (k + 1)]),
                    waits=[tok_w3], inc="s_woT", inc_n=16)
        xcur = xb[st % 2]
        tv_xbf1 = P.op("scalar", lambda e, xcur=xcur: e.activation(
            out=x_bf[:, 0:4, :].rearrange("p a b -> p (a b)"),
            in_=xcur[:, 0:4, :].rearrange("p a b -> p (a b)"), func=AF.Copy),
            waits=[tok_x[st], tok_xbf_done[st - 1] if st >= 1 else None],
            inc=S)
        tv_xbf = P.op("scalar", lambda e, xcur=xcur: e.activation(
            out=x_bf[:, 4:8, :].rearrange("p a b -> p (a b)"),
            in_=xcur[:, 4:8, :].rearrange("p a b -> p (a b)"), func=AF.Copy),
            waits=[tv_xbf1], inc=S)
        tt_stat = prev_tt_stat
        stat_waits = [tv_xbf1, prev_ts_v_last]
        for k in range(8):
            tt_sum = P.op("tensor", lambda e, k=k: e.matmul(
                ps_a[0:1, :], onescol[:, 0:1], x_bf[:, k, :],
                start=(k == 0), stop=(k == 7)),
                waits=stat_waits if k == 0 else ([tv_xbf] if k == 4 else ()),
                inc=T if k == 7 else None)
        s_ring = 0
        ts1 = P.op("scalar", lambda e, s_ring=s_ring: e.activation(
            out=rows_cur[0:1, s_ring, 0, :], in_=ps_a[0:1, :], func=AF.Copy,
            scale=-1.0 / D),
            waits=[tt_sum, tok_rows_free[st - 1] if st >= 1 else None,
                   ("s_rows", 16 * st) if st >= 1 else None,
                   prev_ts_nm], inc=S)
        tv_m2 = P.op("scalar", lambda e, s_ring=s_ring: e.activation(
            out=srow[0:1, 1, :], in_=rows_cur[0:1, s_ring, 0, :], func=AF.Square),
            waits=[ts1], inc=S)
        tv_var = P.op("vector", lambda e: e.scalar_tensor_tensor(
            srow[0:1, 0, :], ps_s[0][0:1, :], 1.0 / D, srow[0:1, 1, :],
            Alu.mult, Alu.subtract),
            waits=[tt_stat, tv_m2], inc=V)
        ts3 = P.op("scalar", lambda e: e.activation(
            out=srow[0:1, 1, :], in_=srow[0:1, 0, :], func=AF.Ln, bias=EPS),
            waits=[tv_var, tok_const], inc=S)
        ts_rs = P.op("scalar", lambda e, s_ring=s_ring: e.activation(
            out=rows_cur[0:1, s_ring, 1, :], in_=srow[0:1, 1, :], func=AF.Exp,
            scale=-0.5), waits=[ts3], inc=S)
        ts_nm = P.op("scalar", lambda e, s_ring=s_ring: e.copy(
            out=negmb_row[0:1, :], in_=rows_cur[0:1, s_ring, 0, :]),
            waits=[ts1], inc=S)
        prev_ts_nm = ts_nm
        tok_rows_sp = P.dma("sync",
                            rows_h[st].rearrange("a b -> (a b)").unsqueeze(0),
                            rows_cur[0:1, s_ring].rearrange("p a b -> p (a b)"),
                            waits=[ts_rs], inc="s_rows")
        tt_rsb = P.op("tensor", lambda e, s_ring=s_ring: e.matmul(
            ps_s[3][:, :], onesf[0:1, 0:128], rows_cur[0:1, s_ring, 1, :],
            start=True, stop=True),
            waits=[ts_rs, prev_tv_sin, prev_addsB], inc=T)
        for i in range(4):
            tt_rsc = P.op("tensor", lambda e, i=i, s_ring=s_ring: e.matmul(
                ps_s[2][:, i:i + 1], rows_cur[0:1, s_ring, 1, 128 * i:128 * i + 128],
                onesf[0:1, 0:1], start=True, stop=True),
                waits=[prev_ts_rc, prev_addsB] if i == 0 else (),
                inc=T if i == 3 else None)
        tok_tab = P.dma("sync", ctab[:], costab_d[st],
                        waits=[tok_ctab_free], inc="s_tabc")
        tok_tab2 = P.dma("sync", stab[:], sintab_d[st],
                         waits=[tok_ctab_free], inc="s_tabs")
        tv_cos = P.op("vector", lambda e: e.tensor_mul(ctab[:], ctab[:], ps_s[3][:, :]),
                      waits=[tt_rsb, tok_tab, tok_tab2], inc=V)
        tv_sin = P.op("vector", lambda e: e.tensor_mul(stab[:], stab[:], ps_s[3][:, :]),
                      waits=[tt_rsb, tok_tab2], inc=V)
        prev_tv_sin = tv_sin
        ts_rc = P.op("scalar", lambda e: e.copy(out=rs_col[:], in_=ps_s[2][:, 0:4]),
                     waits=[tt_rsc], inc=S)
        prev_ts_rc = ts_rc
        tv_k = [None, None]
        ts_k_prev = None
        tv1_prev = None
        tv2_prev = None
        tt_rot_prev = None
        for r in range(2):
            kps = ps_a if r == 0 else ps_s[0]
            kw = [ts1 if r == 0 else tv_var, tv_xbf, tok_wkT]
            for k in range(8):
                P.op("tensor", lambda e, k=k, r=r, kps=kps: e.matmul(
                    kps[:, :], wkT[:, k, 128 * r:128 * r + 128], x_bf[:, k, :],
                    start=(k == 0), stop=False), waits=kw if k == 0 else ())
            tt_k = P.op("tensor", lambda e, r=r, kps=kps: e.matmul(
                kps[:, :], rsum_k[0:1, 128 * r:128 * r + 128], negmb_row[0:1, :],
                start=False, stop=True), waits=[ts_nm], inc=T)
            skw = [tt_k]
            if prev_kraw_rd[r] is not None:
                skw += prev_kraw_rd[r]
            ts_k = P.op("scalar", lambda e, r=r, kps=kps: e.copy(
                out=kraw_bf[:, r, :], in_=kps[:, :]), waits=skw, inc=S)
            tt_rot = P.op("tensor", lambda e, r=r: e.matmul(
                ps_s[1][:, :], mt[:, :], kraw_bf[:, r, :], start=True, stop=True),
                waits=[ts_k, tv2_prev], inc=T)
            tv1 = P.op("vector", lambda e, r=r: e.tensor_mul(
                tmp1[:], kraw_bf[:, r, :], ctab[:]),
                waits=[ts_k, tv_cos, tv_k[0] if r == 1 else None], inc=V)
            tv2 = P.op("vector", lambda e: e.tensor_mul(tmp2[:], ps_s[1][:, :], stab[:]),
                       waits=[tt_rot, tv_sin], inc=V)
            tv_k[r] = P.op("vector", lambda e, r=r: e.tensor_add(
                kT_sb[:, r, :], tmp1[:], tmp2[:]),
                waits=[tv1, tv2, prev_kt_spill], inc=V)
            prev_kraw_rd[r] = [tv1, tt_rot]
            ts_k_prev, tv1_prev, tv2_prev, tt_rot_prev = ts_k, tv1, tv2, tt_rot
        prev_kt_spill = P.dma("gpsimd", kt_h[st], kT_sb[:],
                              waits=[tv_k[0], tv_k[1]], inc="s_kt")
        trw = [tv_k[0], tv_k[1], tok_const]
        if prev_ts_tok is not None:
            trw.append(prev_ts_tok)
        first_tr = True
        for r in range(2):
            for i in range(4):
                tt_tr = P.op("tensor", lambda e, r=r, i=i: e.transpose(
                    ps_wt[:, 256 * i + 128 * r: 256 * i + 128 * r + 128],
                    kT_sb[:, r, 128 * i:128 * i + 128], identbf[:, :]),
                    waits=trw if first_tr else (),
                    inc=T if (r == 1 and i == 3) else None)
                first_tr = False
        prev_ts_tok = P.op("scalar", lambda e: e.copy(
            out=ktok[:, :], in_=ps_wt[:, :]),
            waits=[tt_tr], inc=S)
        ts_tok = prev_ts_tok
        ts_v = [None] * CPST
        for i in range(CPST):
            vps = ps_a if i % 2 == 0 else ps_s[1]
            vw = [tok_wvT]
            if i > 1:
                vw.append(ts_v[i - 2])
            elif i == 0:
                vw.append(ts_k_prev)
            else:
                vw.append(tv2_prev)
            for k in range(8):
                P.op("tensor", lambda e, k=k, i=i, vps=vps: e.matmul(
                    vps[:, 0:256], x_bf[:, k, 128 * i:128 * i + 128], wvT[:, k, :],
                    start=(k == 0), stop=False), waits=vw if k == 0 else ())
            tt_v = P.op("tensor", lambda e, i=i, vps=vps: e.matmul(
                vps[:, 0:256], negmb_row[0:1, 128 * i:128 * i + 128],
                rsum_v[0:1, :], start=False, stop=True), inc=T)
            ts_v[i] = P.op("scalar", lambda e, i=i, vps=vps: e.activation(
                out=v_bf[:, i, :], in_=vps[:, 0:256], func=AF.Copy,
                scale=rs_col[:, i:i + 1]),
                waits=[tt_v, ts_rc, prev_v_spill], inc=S)
        prev_v_spill = P.dma("gpsimd", v_h[st], v_bf[:], waits=[ts_v[CPST - 1]],
                             inc="s_vv")
        tok_xbf_done[st] = tt_v
        prev_ts_v_last = ts_v[CPST - 1]
        if st + 1 < n_st1:
            prev_tt_stat = pre_mm([tsq_next, tv_var, ts_k_prev])
        if st == 0:
            P.op("vector", lambda e: e.memset(ps_k[0][:, :], 0.0),
                 waits=[prev_state_tt], inc=V)
            tv_z = P.op("vector", lambda e: e.memset(ps_k[1][:, :], 0.0), inc=V)
        def emit_k16(ci):
            kb = k16 if ci % 2 == 0 else pq_bf
            return P.op("vector", lambda e, ci=ci, kb=kb: e.tensor_tensor(
                bass.AP(kb, 0, [[1024, 128], [512, 2], [128, 4], [64, 2], [1, 64]]),
                bass.AP(ktok, 256 * ci,
                        [[1024, 128], [128, 2], [0, 4], [64, 2], [1, 64]]),
                bass.AP(kdec, 0,
                        [[1024, 128], [512, 2], [128, 4], [64, 2], [1, 64]]),
                Alu.mult),
                waits=[ts_tok, prev_ttu[ci % 2]], inc=V)

        k16tok = [None] * CPST
        k16tok[0] = emit_k16(0)
        k16tok[1] = emit_k16(1)
        for ci in range(CPST):
            c = CPST * st + ci
            pair = [ps_k[0], ps_k[1]] if ci % 2 == 0 else [ps_s[2], ps_s[3]]
            kb = k16 if ci % 2 == 0 else pq_bf
            tok_sp = P.dma("sync", st_h[c], state_r[:, c % 2, :],
                           waits=[prev_state_tt], inc=f"s_sp{c % 2}")
            if ci % 2 == 0:
                bank_free = [prev_addsA] if prev_addsA is not None else [tv_z]
            else:
                bank_free = ([prev_addsB] if prev_addsB is not None else []) + \
                            [ts_rc, tv_cos, tv_sin]
            uw = [k16tok[ci], ts_v[ci]] + bank_free
            for p in range(8):
                tt_u = P.op("tensor", lambda e, p=p, ci=ci, pair=pair, kb=kb: e.matmul(
                    pair[p // 4][:, 128 * (p % 4):128 * (p % 4) + 128],
                    kb[:, p // 4, 128 * (p % 4):128 * (p % 4) + 128]
                    if ci % 2 == 1 else kb[:, 128 * p:128 * p + 128],
                    v_bf[:, ci, 128 * (p // 4):128 * (p // 4) + 128],
                    start=(p % 4 == 0), stop=(p % 4 == 3)),
                    waits=uw if p == 0 else (), inc=T if p == 7 else None)
            prev_ttu[ci % 2] = tt_u
            if ci + 2 < CPST:
                k16tok[ci + 2] = emit_k16(ci + 2)
            tvm = P.op("vector", lambda e, c=c: e.tensor_mul(
                fbf[:], state_r[:, c % 2, :], bdb[:]),
                waits=[prev_state_tt], inc=V)
            ns = (c + 1) % 2
            prev_sp = tok_sp
            for g in range(2):
                for e_par in range(2):
                    sap = bass.AP(state_r, 2048 * 64 * e_par + 1024 * ns
                                  + 512 * g + 64 * e_par,
                                  [[2048, 64], [128, 4], [1, 64]])
                    fap = bass.AP(fbf, 64 * 1024 * e_par + 512 * g + 64 * e_par,
                                  [[1024, 64], [128, 4], [1, 64]])
                    pap = bass.AP(pair[g], 64 * 512 * e_par + 64 * e_par,
                                  [[512, 64], [128, 4], [1, 64]])
                    tok_last = P.op("vector",
                                    lambda e, sap=sap, fap=fap, pap=pap: e.tensor_add(
                        sap, fap, pap),
                        waits=[tt_u, tvm, prev_sp2] if (g == 0 and e_par == 0) else (),
                        inc=V)
            prev_sp2 = prev_sp
            prev_state_tt = tok_last
            if ci % 2 == 0:
                prev_addsA = tok_last
            else:
                prev_addsB = tok_last
        tok_ctab_free = tv_k[1]
        tok_xb_free[st] = tt_v
        tok_rows_free[st] = tt_rsc

    if not do_coll:
        P.op("gpsimd", lambda e: e.nop(),
             waits=[("s_sp0", P.count["s_sp0"]), ("s_sp1", P.count["s_sp1"]),
                    prev_state_tt])
        with nc.Block() as block:
            P.emit(block)
        stack.close()
        return nc
    tok_src = P.dma("gpsimd", src_h[:, :], state_r[:, N_CH % 2, :],
                    waits=[prev_state_tt], inc="s_g")
    P.op("gpsimd", lambda e: e.collective_compute(
        "AllGather", Alu.bypass,
        ins=[src_h[:, :]], outs=[dst_h[:, :]],
        replica_groups=[[0, 1], [2, 3], [4, 5], [6, 7]]),
        waits=[tok_src], inc="s_c")
    tok_coll = ("s_c", P.count["s_c"])
    tok_g0 = P.dma("gpsimd", fbf[:], dst_h[0:128, :], waits=[tok_coll], inc="s_g")
    tv_sel0 = P.op("vector", lambda e: e.tensor_scalar_mul(
        rrun[:], fbf[:], rmask[:, 0:1]),
        waits=[tok_g0, tok_const, prev_state_tt], inc=V)
    tok_g1 = P.dma("gpsimd", fbf[:], dst_h[128:256, :], waits=[tv_sel0], inc="s_g")
    tv_rrun = P.op("vector", lambda e: e.scalar_tensor_tensor(
        rrun[:], fbf[:], rmask[:, 1:2], rrun[:], Alu.mult, Alu.add),
        waits=[tok_g1], inc=V)

    tok_rsum_q = rowsum(rsum_q, wqT, [(0, 512), (512, 1024)],
                        [tok_wqT, prev_ts_v_last])
    tok_rsum_u1 = rowsum(rsum_u1, u1T, [(0, 256)], [tok_u1, tok_rsum_q])

    tok_x2 = [None] * N_ST
    tok_x2[0] = P.dma("sync", xb[0][:], xt[0], waits=[tok_xb_free[N_ST - 1]],
                      inc="s_x0")
    tok_kt_ld = [None] * N_ST
    tok_v_ld = [None] * N_ST
    tok_st_ld = [None] * N_ST
    tok_kt_ld[0] = P.dma("sync", kT_sb[:], kt_h[0],
                         waits=[("s_kt", 16 * N_ST), prev_state_tt], inc="s_ktl")
    tok_v_ld[0] = P.dma("sync", v_bf[:], v_h[0],
                        waits=[("s_vv", 16 * N_ST), prev_state_tt], inc="s_vvl")
    tok_rows_ld = [None] * N_ST
    tok_rows_ld[0] = P.dma("sync",
                           rows_cur[0:1, 0].rearrange("p a b -> p (a b)"),
                           rows_h[0].rearrange("a b -> (a b)").unsqueeze(0),
                           waits=[tok_rows_free[N_ST - 1], ("s_rows", 16 * N_ST)],
                           inc="s_rows")
    tok_stch = [None] * (N_CH + 2)
    all_sp = [("s_sp0", P.count["s_sp0"]), ("s_sp1", P.count["s_sp1"])]
    tok_stch[0] = P.dma("sync", st_ld[:, 0, :], st_h[0],
                        waits=all_sp, inc="s_st0")
    tok_stch[1] = P.dma("sync", st_ld[:, 1, :], st_h[1],
                        waits=all_sp, inc="s_st1")
    prev_tv_st = [None] * (N_CH + 2)

    tok_out = [None] * N_ST
    tv2_prev = [None]
    tv1_prev2 = [None, None]
    prev_tv_P1 = prev_tv_sin
    prev_S_evac_g = [None, None]
    prev_tt_o = None
    prev_ts_ot1 = None
    tok_rows_used = [None] * (N_ST + 1)
    prev_ts_attn = prev_ts_tok
    prev_negmb_user = None
    prev_tv_xbf2 = None
    prev_tv_rr = None
    tok_qp_free = None
    prev_o_evac = None
    prev_S_evac = None
    tok_xb2_free = [None] * (N_ST + 1)
    tok_ctab2_free = tok_ctab_free
    prev_psa_reader = tok_rsum_u1
    prev_tv_res2_last = None
    prev_tt_P = None
    prev_tsQ = None

    if n_st2 == 0:
        P.op("gpsimd", lambda e: e.nop(), waits=[tok_coll])
        with nc.Block() as block:
            P.emit(block)
        stack.close()
        return nc
    for st in range(n_st2):
        if st + 1 < n_st2:
            w = [tok_xb2_free[st - 1]] if st >= 1 else []
            tok_x2[st + 1] = P.dma("sync", xb[(st + 1) % 2][:], xt[st + 1], waits=w,
                                   inc=f"s_x{(st + 1) % 2}")
        xcur = xb[st % 2]
        s_ring = 0
        tv_xbf = P.op("vector", lambda e, xcur=xcur: e.tensor_copy(x_bf[:], xcur[:]),
                      waits=[tok_x2[st], prev_tt_P, prev_tv_xbf2], inc=V)
        ts_nm = P.op("scalar", lambda e, s_ring=s_ring: e.copy(
            out=negmb_row[0:1, :], in_=rows_cur[0:1, s_ring, 0, :]),
            waits=[tok_rows_ld[st], prev_negmb_user], inc=S)
        tt_rsb = P.op("tensor", lambda e, s_ring=s_ring: e.matmul(
            ps_a[:, :], onesf[0:1, 0:128], rows_cur[0:1, s_ring, 1, :],
            start=True, stop=True),
            waits=[prev_psa_reader, prev_tv_res2_last, tok_rows_ld[st]], inc=T)
        if st == 0:
            tok_tab_n = P.dma("sync", ctab[:], costab_d[0], waits=[tok_ctab2_free],
                              inc="s_tabc")
            tok_tab2_n = P.dma("sync", stab[:], sintab_d[0], waits=[tok_ctab2_free],
                               inc="s_tabs")
        tok_tab, tok_tab2 = tok_tab_n, tok_tab2_n
        tv_cos = P.op("vector", lambda e: e.tensor_mul(ctab[:], ctab[:], ps_a[:, :]),
                      waits=[tt_rsb, tok_tab, tok_tab2], inc=V)
        tv_sin = P.op("vector", lambda e: e.tensor_mul(stab[:], stab[:], ps_a[:, :]),
                      waits=[tt_rsb, tok_tab2], inc=V)
        tv_q = [None] * 8
        ts_q_prev = [None, None]
        pend = []
        def emit_rope(p_, ts_q_, rw_extra=()):
            rw = [ts_q_, tv2_prev[0]] + list(rw_extra)
            if prev_tsQ is not None:
                rw.append(prev_tsQ)
            tt_rot = P.op("tensor", lambda e: e.matmul(
                ps_k[0][:, :], mt[:, :], kraw_bf[:, p_ % 2, :], start=True, stop=True),
                waits=rw, inc=T)
            tv1 = P.op("vector", lambda e, p_=p_: e.tensor_mul(
                tmp1[:], kraw_bf[:, p_ % 2, :], ctab[:]),
                waits=[ts_q_, tv_cos, tv_q[p_ - 1] if p_ > 0 else None], inc=V)
            tv2 = P.op("vector", lambda e: e.tensor_mul(tmp2[:], ps_k[0][:, :], stab[:]),
                       waits=[tt_rot, tv_sin], inc=V)
            tv_q[p_] = P.op("vector", lambda e, p_=p_: e.tensor_add(
                qp[:, p_, :], tmp1[:], tmp2[:]), waits=[tv1, tv2], inc=V)
            tv2_prev[0] = tv2
        for p in range(8):
            qps = ps_a if p % 2 == 0 else ps_s[3]
            qw = [tv_cos, tv_sin, tv_xbf, tok_wqT, tok_rsum_q, ts_nm]
            if p == 0 and tok_qp_free is not None:
                qw.append(tok_qp_free)
            if p == 1:
                qw.append(prev_tv_P1)
            if ts_q_prev[p % 2] is not None:
                qw.append(ts_q_prev[p % 2])
            for k in range(8):
                P.op("tensor", lambda e, k=k, p=p, qps=qps: e.matmul(
                    qps[:, :], wqT[:, k, 128 * p:128 * p + 128], x_bf[:, k, :],
                    start=(k == 0), stop=False), waits=qw if k == 0 else ())
            tt_q = P.op("tensor", lambda e, p=p, qps=qps: e.matmul(
                qps[:, :], rsum_q[0:1, 128 * p:128 * p + 128], negmb_row[0:1, :],
                start=False, stop=True), inc=T)
            skw = [tt_q]
            if p >= 2:
                skw.append(tv1_prev2[p % 2])
            ts_q = P.op("scalar", lambda e, p=p, qps=qps: e.copy(
                out=kraw_bf[:, p % 2, :], in_=qps[:, :]), waits=skw, inc=S)
            ts_q_prev[p % 2] = ts_q
            pend.append((p, ts_q))
            if len(pend) >= 2:
                p_, t_ = pend.pop(0)
                emit_rope(p_, t_)
                tv1_prev2[p_ % 2] = tv_q[p_]
        while pend:
            p_, t_ = pend.pop(0)
            emit_rope(p_, t_)
            tv1_prev2[p_ % 2] = tv_q[p_]
        if st + 1 < n_st2:
            tok_rows_ld[st + 1] = P.dma(
                "sync", rows_cur[0:1, 0].rearrange("p a b -> p (a b)"),
                rows_h[st + 1].rearrange("a b -> (a b)").unsqueeze(0),
                waits=[tt_rsb, ts_nm], inc="s_rows")
            tok_tab_n = P.dma("sync", ctab[:], costab_d[st + 1],
                              waits=[tv_q[7]], inc="s_tabc")
            tok_tab2_n = P.dma("sync", stab[:], sintab_d[st + 1],
                               waits=[tv_q[7]], inc="s_tabs")
        tv_res1 = [None] * 8
        for ci in range(CPST):
            c = CPST * st + ci
            dqw = [tv_q[7]]
            if prev_tt_o is not None:
                dqw.append(prev_tt_o)
            tv_qdq = P.op("vector", lambda e, ci=ci: e.tensor_mul(
                kq_scr[:].rearrange("p (a b) -> p a b", a=8),
                qp[:, :, 128 * ci:128 * ci + 128], qdec[:]),
                waits=dqw, inc=V)
            tv_st = P.op("vector", lambda e, c=c: e.tensor_add(
                st_true[:], st_ld[:, c % 2, :], rrun[:]),
                waits=[tv_rrun, tok_stch[c], prev_tv_rr], inc=V)
            prev_tv_st[c] = tv_st
            if c + 2 <= N_CH - 1:
                tok_stch[c + 2] = P.dma("sync", st_ld[:, c % 2, :], st_h[c + 2],
                                        waits=[tv_st], inc=f"s_st{c % 2}")
            prev_tv_rr = P.op("vector", lambda e: e.tensor_mul(
                rrun[:], rrun[:], bdb[:]), waits=[tv_st], inc=V)
            tv_mask = [None, None]
            for e_par in range(2):
                sw0 = [tv_q[7], tok_kt_ld[st], prev_S_evac_g[e_par]]
                if e_par == 1:
                    sw0.append(prev_tv_P1)
                    if ci >= 2:
                        sw0.append(tv_res1[7])
                if prev_tsQ is not None:
                    sw0.append(prev_tsQ)
                first_s = True
                for p in range(8):
                    h = PAIRS[p][e_par]
                    kv = h // 4
                    s = p + 8 * e_par
                    tt_sc = P.op("tensor", lambda e, p=p, e_par=e_par, kv=kv, s=s, ci=ci: e.matmul(
                        ps_s[s // 4][:, 128 * (s % 4):128 * (s % 4) + 128],
                        kT_sb[64 * (kv % 2):64 * (kv % 2) + 64, kv // 2,
                              128 * ci:128 * ci + 128],
                        qp[64 * e_par:64 * e_par + 64, p, 128 * ci:128 * ci + 128],
                        start=True, stop=True),
                        waits=sw0 if first_s else (), inc=T if p == 7 else None)
                    first_s = False
                P.op("vector", lambda e, e_par=e_par: e.tensor_mul(
                    S_bf[:, 1024 * e_par:1024 * e_par + 512], ps_s[2 * e_par][:, :],
                    maskT[:, 1024 * e_par:1024 * e_par + 512]), waits=[tt_sc], inc=V)
                tv_mask[e_par] = P.op("vector", lambda e, e_par=e_par: e.tensor_mul(
                    S_bf[:, 1024 * e_par + 512:1024 * (e_par + 1)],
                    ps_s[2 * e_par + 1][:, :],
                    maskT[:, 1024 * e_par + 512:1024 * (e_par + 1)]), inc=V)
                if e_par == 0:
                    o2w = [tv_qdq, tv_st]
                    if prev_ts_ot1 is not None:
                        o2w.append(prev_ts_ot1)
                    for p in range(8):
                        P.op("tensor", lambda e, p=p: e.matmul(
                            ps_k[p // 4][:, 128 * (p % 4):128 * (p % 4) + 128],
                            kq_scr[:, 128 * p:128 * p + 128],
                            st_true[:, 128 * p:128 * p + 128],
                            start=(p % 4 == 0), stop=False),
                            waits=o2w if p == 0 else ())
            prev_S_evac = tv_mask[1]
            prev_S_evac_g = [tv_mask[0], tv_mask[1]]
            tt_o = None
            for e_par in range(2):
                ow = [tv_mask[e_par], tok_v_ld[st]]
                first_o = True
                for p in range(8):
                    h = PAIRS[p][e_par]
                    kv = h // 4
                    s = p + 8 * e_par
                    tt_o = P.op("tensor", lambda e, p=p, e_par=e_par, kv=kv, s=s, ci=ci: e.matmul(
                        ps_k[p // 4][:, 128 * (p % 4) + 64 * e_par:
                                     128 * (p % 4) + 64 * e_par + 64],
                        S_bf[:, 128 * s:128 * s + 128],
                        v_bf[:, ci, 64 * kv:64 * kv + 64],
                        start=False, stop=(e_par == 1 and p % 4 == 3)),
                        waits=ow if first_o else (),
                        inc=T if (e_par == 1 and p == 7) else None)
                    first_o = False
            ts_ot0 = P.op("scalar", lambda e: e.copy(
                out=S_bf[:, 0:512], in_=ps_k[0][:, :]),
                waits=[tt_o], inc=S)
            ts_ot1 = P.op("scalar", lambda e: e.copy(
                out=S_bf[:, 512:1024], in_=ps_k[1][:, :]), inc=S)
            prev_ts_ot1 = ts_ot1
            prev_tt_o = tt_o
            for j in range(8):
                tt_tra = P.op("tensor", lambda e, j=j: e.transpose(
                    ps_wt[:, 128 * j:128 * j + 128],
                    S_bf[:, 128 * j:128 * j + 128], identbf[:, :]),
                    waits=([ts_ot0, ts_ot1, prev_ts_attn] if j == 0 else ()),
                    inc=T if j == 7 else None)
            prev_o_evac = P.op("scalar", lambda e, ci=ci: e.copy(
                out=attn_bf[:, :, 128 * (ci % 2):128 * (ci % 2) + 128],
                in_=ps_wt[:].rearrange("p (a b) -> p a b", a=8)),
                waits=[tt_tra], inc=S)
            prev_ts_attn = prev_o_evac
            if ci % 2 == 1:
                half = ci // 2
                for dch in range(8):
                    yps = ps_a if dch % 2 == 0 else ps_s[3]
                    ww = [prev_o_evac, tok_woT]
                    if half == 0:
                        if dch > 1:
                            ww.append(tv_res1[dch - 2])
                        elif dch == 0:
                            ww.append(tv_sin)
                        else:
                            ww.append(prev_S_evac)
                    else:
                        ww.append(prev_S_evac)
                        if dch > 1:
                            ww.append(tv_res1[dch - 2])
                        else:
                            ww.append(tv_res1[7])
                    for p in range(8):
                        tt_y = P.op("tensor", lambda e, p=p, dch=dch, yps=yps: e.matmul(
                            yps[:, 0:256], woT[:, p, 128 * dch:128 * dch + 128],
                            attn_bf[:, p, :],
                            start=(p == 0), stop=(p == 7)),
                            waits=ww if p == 0 else (), inc=T if p == 7 else None)
                    tv_res1[dch] = P.op("vector",
                                        lambda e, dch=dch, half=half, xcur=xcur,
                                        yps=yps: e.tensor_add(
                        xcur[:, dch, 256 * half:256 * half + 256],
                        xcur[:, dch, 256 * half:256 * half + 256],
                        yps[:, 0:256]), waits=[tt_y], inc=V)
        if st + 1 < N_ST:
            tok_kt_ld[st + 1] = P.dma("sync", kT_sb[:], kt_h[st + 1],
                                      waits=[prev_o_evac], inc="s_ktl")
            tok_v_ld[st + 1] = P.dma("sync", v_bf[:], v_h[st + 1],
                                     waits=[prev_o_evac], inc="s_vvl")
        if p2stage <= 3:
            P.op("gpsimd", lambda e: e.nop(), waits=[tv_res1[7]])
            with nc.Block() as block:
                P.emit(block)
            stack.close()
            return nc
        tv_c2 = [None, None]
        for h2 in range(2):
            tv_c2[h2] = P.op("vector", lambda e, h2=h2, xcur=xcur: e.tensor_copy(
                x_bf[:, 4 * h2:4 * h2 + 4, :].rearrange("p a b -> p (a b)"),
                xcur[:, 4 * h2:4 * h2 + 4, :].rearrange("p a b -> p (a b)")),
                waits=[tv_res1[4 * h2 + 3]], inc=V)
        gq = [None] * 8
        for k in range(8):
            gq[k] = P.op("scalar", lambda e, k=k, xcur=xcur: e.activation(
                out=qp[:, k, :], in_=xcur[:, k, :], func=AF.Square),
                waits=[tv_res1[k], tt_sc] if k == 0 else [tv_res1[k]], inc=S)
        tv_xbf2 = tv_c2[1]
        prev_tv_xbf2 = tv_xbf2
        for k in range(8):
            P.op("tensor", lambda e, k=k: e.matmul(
                ps_a[0:1, :], onescol[:, 0:1], x_bf[:, k, :],
                start=(k == 0), stop=(k == 7)),
                waits=[tv_c2[k // 4]] if k % 4 == 0 else ())
        for k in range(8):
            tt_stat = P.op("tensor", lambda e, k=k: e.matmul(
                ps_k[0][0:1, :], onescol[:, 0:1], qp[:, k, :],
                start=(k == 0), stop=(k == 7)),
                waits=[gq[k]],
                inc=T if k == 7 else None)
        tok_qp_free = tt_stat
        ts1 = P.op("scalar", lambda e: e.activation(
            out=negm2_row[0:1, :], in_=ps_a[0:1, :], func=AF.Copy, scale=-1.0 / D),
            waits=[tt_stat], inc=S)
        tv_m2 = P.op("scalar", lambda e: e.activation(
            out=srow[0:1, 1, :], in_=negm2_row[0:1, :], func=AF.Square),
            waits=[ts1], inc=S)
        tv_var = P.op("vector", lambda e: e.scalar_tensor_tensor(
            srow[0:1, 0, :], ps_k[0][0:1, :], 1.0 / D, srow[0:1, 1, :],
            Alu.mult, Alu.subtract),
            waits=[tt_stat, tv_m2], inc=V)
        ts3 = P.op("scalar", lambda e: e.activation(
            out=srow[0:1, 1, :], in_=srow[0:1, 0, :], func=AF.Ln, bias=EPS),
            waits=[tv_var, tok_const], inc=S)
        ts_rs2 = P.op("scalar", lambda e: e.activation(
            out=rs2_row[0:1, :], in_=srow[0:1, 1, :], func=AF.Exp, scale=-0.5),
            waits=[ts3], inc=S)
        ts_nm2 = P.op("scalar", lambda e: e.copy(
            out=negm2b_row[0:1, :], in_=negm2_row[0:1, :]), waits=[ts1], inc=S)
        tt_rsb2 = P.op("tensor", lambda e: e.matmul(
            ps_s[2][:, :], onesf[0:1, 0:128], rs2_row[0:1, :], start=True, stop=True),
            waits=[ts_rs2, prev_S_evac], inc=T)
        ts_rs2sb = P.op("scalar", lambda e: e.copy(out=rs2_sb[:], in_=ps_s[2][:, :]),
                        waits=[tt_rsb2, tv_q[7]], inc=S)
        if p2stage <= 4:
            P.op("gpsimd", lambda e: e.nop(), waits=[ts_rs2sb])
            with nc.Block() as block:
                P.emit(block)
            stack.close()
            return nc
        tv_P = [None, None]
        tt_P = [None, None]
        for r in range(2):
            pps = ps_a if r == 0 else ps_s[3]
            pw = [tv_xbf2, tok_rsum_u1]
            if r == 0:
                pw.append(tv_res1[6])
            else:
                pw.append(tv_res1[7])
            for k in range(8):
                P.op("tensor", lambda e, k=k, r=r, pps=pps: e.matmul(
                    pps[:, :], u1T[:, k, 128 * r:128 * r + 128], x_bf[:, k, :],
                    start=(k == 0), stop=False), waits=pw if k == 0 else ())
        for r in range(2):
            pps = ps_a if r == 0 else ps_s[3]
            tt_P[r] = P.op("tensor", lambda e, r=r, pps=pps: e.matmul(
                pps[:, :], rsum_u1[0:1, 128 * r:128 * r + 128], negm2b_row[0:1, :],
                start=False, stop=True), waits=[ts_nm2] if r == 0 else (), inc=T)
            tv_P[r] = P.op("vector", lambda e, r=r, pps=pps: e.tensor_mul(
                P_bf[:, r, :], pps[:, :], rs2_sb[:]),
                waits=[tt_P[r], ts_rs2sb], inc=V)
        prev_tt_P = tt_P[1]
        ts_H = [None] * 16
        first_q_bank = [True, True]

        def emit_v1(g):
            buf = g % 2
            gw = [tv_P[1], tok_v1]
            if g >= 2:
                gw.append(ts_H[g - 2])
            elif g < 2:
                gw.append(prev_S_evac)
            for j in range(2):
                ff = 2 * g + j
                for r in range(2):
                    tt = P.op("tensor", lambda e, ff=ff, r=r, buf=buf, j=j: e.matmul(
                        ps_s[2 * buf + j][:, :],
                        v1T[:, r, 128 * ff:128 * ff + 128], P_bf[:, r, :],
                        start=(r == 0), stop=(r == 1)),
                        waits=gw if (j == 0 and r == 0) else (),
                        inc=T if (j == 1 and r == 1) else None)
            return tt

        def emit_gelu(g):
            buf = g % 2
            P.op("scalar", lambda e, g=g, buf=buf: e.activation(
                out=H_bf[:, 1024 * buf:1024 * buf + 512],
                in_=ps_s[2 * buf][:, :],
                func=AF.Gelu_apprx_tanh, bias=b1c[:, 2 * g:2 * g + 1]),
                waits=[tt_H_tok[g], tok_b1], inc=S)
            ts_H[g] = P.op("scalar", lambda e, g=g, buf=buf: e.activation(
                out=H_bf[:, 1024 * buf + 512:1024 * (buf + 1)],
                in_=ps_s[2 * buf + 1][:, :],
                func=AF.Gelu_apprx_tanh, bias=b1c[:, 2 * g + 1:2 * g + 2]), inc=S)

        def emit_u2(g):
            nonlocal_last = None
            buf = g % 2
            u2w = [ts_H[g], tok_u2[g // 4]]
            if g == 0:
                u2w.append(prev_o_evac)
            for j in range(2):
                ff = 2 * g + j
                for m in range(2):
                    stt = first_q_bank[m]
                    first_q_bank[m] = False
                    nonlocal_last = P.op("tensor", lambda e, ff=ff, m=m, buf=buf, j=j,
                                         stt=stt, stp=(g == 15 and j == 1): e.matmul(
                        ps_k[m][:, :],
                        u2T[:, ff, 128 * m:128 * m + 128],
                        H_bf[:, 1024 * buf + 512 * j:1024 * buf + 512 * j + 512],
                        start=stt, stop=stp),
                        waits=u2w if (j == 0 and m == 0) else (),
                        inc=T if (g == 15 and j == 1 and m == 1) else None)
            return nonlocal_last

        tt_H_tok = [None] * 16
        tt_u2last = None
        for g in range(16):
            tt_H_tok[g] = emit_v1(g)
            emit_gelu(g)
            if g >= 1:
                tt_u2last = emit_u2(g - 1)
        tt_u2last = emit_u2(15)
        tt_Q = tt_u2last
        P.op("scalar", lambda e: e.copy(
            out=Q_bf[:, 0, :], in_=ps_k[0][:, :]), waits=[tt_Q], inc=S)
        ts_Q = P.op("scalar", lambda e: e.copy(
            out=Q_bf[:, 1, :], in_=ps_k[1][:, :]), inc=S)
        if p2stage <= 5:
            P.op("gpsimd", lambda e: e.nop(), waits=[ts_Q])
            with nc.Block() as block:
                P.emit(block)
            stack.close()
            return nc
        tv_res2 = [None] * 8
        for dch in range(8):
            vps = ps_a if dch % 2 == 0 else ps_s[3]
            vw = [ts_Q, tok_v2]
            if dch > 1:
                vw.append(tv_res2[dch - 2])
            else:
                vw.append(tv_P[1])
                vw.append(tv_res1[7])
            for r in range(2):
                tt_v2 = P.op("tensor", lambda e, r=r, dch=dch, vps=vps: e.matmul(
                    vps[:, :], v2T[:, r, 128 * dch:128 * dch + 128], Q_bf[:, r, :],
                    start=(r == 0), stop=(r == 1)),
                    waits=vw if r == 0 else (), inc=T if r == 1 else None)
            tv_res2[dch] = P.op("vector",
                                lambda e, dch=dch, xcur=xcur,
                                vps=vps: e.scalar_tensor_tensor(
                xcur[:, dch, :], vps[:, :], b2c[:, dch:dch + 1], xcur[:, dch, :],
                Alu.add, Alu.add), waits=[tt_v2], inc=V)
        tok_out[st] = P.dma("sync", out[st], xcur[:], waits=[tv_res2[7]],
                            inc=f"s_out{st % 2}")
        tok_xb2_free[st] = tok_out[st]
        tok_ctab2_free = tv_q[7]
        prev_psa_reader = ts_Q
        prev_tv_res2_last = tv_res2[7]
        prev_tsQ = ts_Q
        prev_tv_P1 = tv_P[1]
        tok_rows_used[st] = tt_rsb
        prev_negmb_user = tt_rsb

    P.op("sync", lambda e: e.nop(),
         waits=[("s_out0", P.count["s_out0"]), ("s_out1", P.count["s_out1"])])
    prev_kt_spill = prev_v_spill

    with nc.Block() as block:
        P.emit(block)
    stack.close()
    return nc


_NC_CACHE = None


def kernel(L0_attn_x, L0_attn_Wq, L0_attn_Wk, L0_attn_Wv, L0_attn_Wo,
           L0_ffn_U1, L0_ffn_V1, L0_ffn_b1, L0_ffn_U2, L0_ffn_V2, L0_ffn_b2):
    global _NC_CACHE
    x = np.asarray(L0_attn_x, np.float32)
    B = x.shape[0]
    perm = _perm_rows()
    wq_p = np.ascontiguousarray(np.asarray(L0_attn_Wq, np.float32)[perm, :])
    wo_p = np.ascontiguousarray(np.asarray(L0_attn_Wo, np.float32)[:, perm])
    wk = np.ascontiguousarray(np.asarray(L0_attn_Wk, np.float32))
    wv = np.ascontiguousarray(np.asarray(L0_attn_Wv, np.float32))
    u1 = np.ascontiguousarray(np.asarray(L0_ffn_U1, np.float32))
    v1 = np.ascontiguousarray(np.asarray(L0_ffn_V1, np.float32))
    u2 = np.ascontiguousarray(np.asarray(L0_ffn_U2, np.float32))
    v2 = np.ascontiguousarray(np.asarray(L0_ffn_V2, np.float32))
    b1c = np.ascontiguousarray(np.asarray(L0_ffn_b1, np.float32).reshape(32, 128).T)
    b2c = np.ascontiguousarray(np.asarray(L0_ffn_b2, np.float32).reshape(8, 128).T)

    consts = [_host_consts(0), _host_consts(1)]
    if _NC_CACHE is None:
        _NC_CACHE = build_nc()
    nc = _NC_CACHE

    in_maps = []
    for c in range(8):
        b, h = c // 2, c % 2
        xs = np.ascontiguousarray(x[b, h * S_CORE:(h + 1) * S_CORE, :].T)
        xtl = np.ascontiguousarray(
            xs.reshape(8, 128, N_ST, ST).transpose(2, 1, 0, 3))
        cc = consts[h]
        in_maps.append(dict(
            xt=xtl, wq=wq_p, wk=wk, wv=wv, wo=wo_p, u1=u1, v1=v1, u2=u2, v2=v2,
            b1c=b1c, b2c=b2c, mt=cc["mt"], identbf=cc["identbf"], mask=cc["mask"],
            qdec=cc["qdec"], kdec=cc["kdec"], bdb=cc["bdb"],
            costab=cc["costab"], sintab=cc["sintab"], rmask=cc["rmask"],
            onesf=cc["onesf"], onesb=cc["onesb"], onescol=cc["onescol"],
            epsc=cc["epsc"],
        ))

    res = run_bass_kernel_spmd(nc, in_maps, core_ids=list(range(8)))

    outp = np.empty((B, 8192, D), np.float32)
    for c in range(8):
        b, h = c // 2, c % 2
        arr = np.asarray(res.results[c]["out"])
        sl = arr.transpose(0, 3, 2, 1).reshape(S_CORE, D)
        outp[b, h * S_CORE:(h + 1) * S_CORE, :] = sl
    return outp


# version 15
# speedup vs baseline: 1.1185x; 1.0146x over previous
import math

import numpy as np
import ml_dtypes

import concourse.bass as bass
import concourse.mybir as mybir
from concourse.bass_utils import run_bass_kernel_spmd

F32 = mybir.dt.float32
BF16 = mybir.dt.bfloat16
AF = mybir.ActivationFunctionType
Alu = mybir.AluOpType

BF = ml_dtypes.bfloat16

D = 1024
HD = 64
NH = 16
NKV = 4
CH = 128
ST = 512
N_ST = 8
N_CH = 32
CPST = ST // CH
S_CORE = 4096
THETA = 10000.0
EPS = 1e-5

PAIRS = [(0, 4), (1, 5), (2, 6), (3, 7), (8, 12), (9, 13), (10, 14), (11, 15)]
HEAD_POS = {}
for _p, (_lo, _hi) in enumerate(PAIRS):
    HEAD_POS[_lo] = (_p, 0)
    HEAD_POS[_hi] = (_p, 1)
SLOPES = [2.0 ** (-8.0 / NH * (h + 1)) for h in range(NH)]


def _rope_tables():
    inv = 1.0 / THETA ** (np.arange(0, HD, 2, dtype=np.float64) / HD)
    freqs = np.outer(np.arange(8192, dtype=np.float64), inv)
    emb = np.concatenate([freqs, freqs], axis=-1)
    return np.cos(emb).astype(np.float32), np.sin(emb).astype(np.float32)


_COS64, _SIN64 = _rope_tables()


def _host_consts(half):
    c = {}
    M = np.zeros((128, 128), np.float32)
    for blk in range(2):
        b0 = 64 * blk
        for i in range(32):
            M[b0 + i, b0 + 32 + i] = -1.0
            M[b0 + 32 + i, b0 + i] = 1.0
    c["mt"] = np.ascontiguousarray(M.T).astype(BF)
    c["identbf"] = np.eye(128, dtype=np.float32).astype(BF)
    mask = np.zeros((128, 16 * 128), np.float32)
    t = np.arange(CH, dtype=np.float64)
    for p in range(8):
        for e in range(2):
            h = PAIRS[p][e]
            s = p + 8 * e
            m = np.where(t[:, None] >= t[None, :],
                         np.exp(-SLOPES[h] * (t[:, None] - t[None, :])), 0.0)
            mask[:, 128 * s:128 * s + 128] = m.T.astype(np.float32)
    c["mask"] = mask.astype(BF)
    qd = np.zeros((128, 8, 128), np.float32)
    for p in range(8):
        for e in range(2):
            h = PAIRS[p][e]
            qd[64 * e:64 * e + 64, p, :] = np.exp(-SLOPES[h] * (t + 1.0))[None, :]
    c["qdec"] = qd.astype(BF)
    kd = np.zeros((128, 8, 2, 64), np.float32)
    for p in range(8):
        for e in range(2):
            h = PAIRS[p][e]
            kd[:, p, e, :] = np.exp(-SLOPES[h] * (CH - 1.0 - t))[:, None]
    c["kdec"] = kd.reshape(128, NH * 64).astype(BF)
    bd = np.zeros((128, 8, 128), np.float32)
    for p in range(8):
        for e in range(2):
            h = PAIRS[p][e]
            bd[:, p, 64 * e:64 * e + 64] = math.exp(-SLOPES[h] * CH)
    c["bdf"] = bd.reshape(128, 1024).copy()
    c["bdb"] = c["bdf"].astype(BF)
    pos0 = half * S_CORE
    cos = np.zeros((N_ST, 128, ST), np.float32)
    sin = np.zeros((N_ST, 128, ST), np.float32)
    for st in range(N_ST):
        pp = np.arange(ST) + pos0 + st * ST
        cs = _COS64[pp, :].T
        sn = _SIN64[pp, :].T
        cos[st, 0:64] = cs
        cos[st, 64:128] = cs
        sin[st, 0:64] = sn
        sin[st, 64:128] = sn
    c["costab"] = cos
    c["sintab"] = sin
    rm = np.zeros((128, 2), np.float32)
    if half == 1:
        rm[:, 0] = 1.0
    c["rmask"] = rm
    c["onesf"] = np.ones((1, 128), np.float32)
    c["epsc"] = np.full((1, 1), EPS, np.float32)
    c["onesb"] = np.ones((1, 512), np.float32).astype(BF)
    c["onescol"] = np.ones((128, 1), np.float32).astype(BF)
    return c


def _perm_rows():
    perm = []
    for (lo, hi) in PAIRS:
        perm.extend(range(64 * lo, 64 * lo + 64))
        perm.extend(range(64 * hi, 64 * hi + 64))
    return np.array(perm, np.int64)


class Prog:
    def __init__(self, nc, sems):
        self.nc = nc
        self.ops = {e: [] for e in ("sync", "gpsimd", "tensor", "vector", "scalar")}
        self.count = {s: 0 for s in sems}
        self.sems = sems

    def op(self, eng, fn, waits=(), inc=None, inc_n=1):
        tok = None
        if inc is not None:
            self.count[inc] += inc_n
            tok = (inc, self.count[inc])
        waits = tuple(w for w in waits if w is not None)
        self.ops[eng].append((fn, waits, inc, inc_n))
        return tok

    def dma(self, eng, out, in_, waits=(), inc=None, **kw):
        return self.op(eng, lambda e: e.dma_start(out=out, in_=in_, **kw),
                       waits=waits, inc=inc, inc_n=16)

    def emit(self, block):
        def make(eng_name):
            ops = self.ops[eng_name]
            sems = self.sems

            def body(eng):
                wm = {}
                for fn, waits, inc, inc_n in ops:
                    for (sn, val) in waits:
                        if wm.get(sn, -1) >= val:
                            continue
                        eng.wait_ge(sems[sn], val)
                        wm[sn] = val
                    instr = fn(eng)
                    if inc is not None:
                        if instr is None:
                            instr = eng.nop()
                        instr.then_inc(sems[inc], inc_n)
            return body

        for name in ("sync", "gpsimd", "tensor", "vector", "scalar"):
            getattr(block, name)(make(name))


def build_nc(n_st1=N_ST, do_coll=True, n_st2=N_ST, p2stage=9):
    nc = bass.Bass("TRN2", target_bir_lowering=False)

    xt = nc.dram_tensor("xt", [N_ST, 128, 8, ST], F32, kind="ExternalInput").ap()
    out = nc.dram_tensor("out", [N_ST, 128, 8, ST], F32, kind="ExternalOutput").ap()
    wq = nc.dram_tensor("wq", [D, D], F32, kind="ExternalInput").ap()
    wk = nc.dram_tensor("wk", [256, D], F32, kind="ExternalInput").ap()
    wv = nc.dram_tensor("wv", [256, D], F32, kind="ExternalInput").ap()
    wo = nc.dram_tensor("wo", [D, D], F32, kind="ExternalInput").ap()
    u1 = nc.dram_tensor("u1", [D, 256], F32, kind="ExternalInput").ap()
    v1 = nc.dram_tensor("v1", [256, 4096], F32, kind="ExternalInput").ap()
    u2 = nc.dram_tensor("u2", [4096, 256], F32, kind="ExternalInput").ap()
    v2 = nc.dram_tensor("v2", [256, D], F32, kind="ExternalInput").ap()
    b1c_d = nc.dram_tensor("b1c", [128, 32], F32, kind="ExternalInput").ap()
    b2c_d = nc.dram_tensor("b2c", [128, 8], F32, kind="ExternalInput").ap()
    mt_d = nc.dram_tensor("mt", [128, 128], BF16, kind="ExternalInput").ap()
    identbf_d = nc.dram_tensor("identbf", [128, 128], BF16, kind="ExternalInput").ap()
    mask_d = nc.dram_tensor("mask", [128, 2048], BF16, kind="ExternalInput").ap()
    qdec_d = nc.dram_tensor("qdec", [128, 8, 128], BF16, kind="ExternalInput").ap()
    kdec_d = nc.dram_tensor("kdec", [128, 1024], BF16, kind="ExternalInput").ap()
    bdb_d = nc.dram_tensor("bdb", [128, 1024], BF16, kind="ExternalInput").ap()
    costab_d = nc.dram_tensor("costab", [N_ST, 128, ST], F32, kind="ExternalInput").ap()
    sintab_d = nc.dram_tensor("sintab", [N_ST, 128, ST], F32, kind="ExternalInput").ap()
    rmask_d = nc.dram_tensor("rmask", [128, 2], F32, kind="ExternalInput").ap()
    onesf_d = nc.dram_tensor("onesf", [1, 128], F32, kind="ExternalInput").ap()
    epsc_d = nc.dram_tensor("epsc", [1, 1], F32, kind="ExternalInput").ap()
    onesb_d = nc.dram_tensor("onesb", [1, 512], BF16, kind="ExternalInput").ap()
    onescol_d = nc.dram_tensor("onescol", [128, 1], BF16, kind="ExternalInput").ap()

    wqbf_h = nc.dram_tensor("wqbf_h", [D, D], BF16, kind="Internal").ap()
    wobf_h = nc.dram_tensor("wobf_h", [D, D], BF16, kind="Internal").ap()
    wkbf_h = nc.dram_tensor("wkbf_h", [256, D], BF16, kind="Internal").ap()
    wvbf_h = nc.dram_tensor("wvbf_h", [256, D], BF16, kind="Internal").ap()
    st_h = nc.dram_tensor("st_h", [N_CH, 128, 1024], BF16, kind="Internal").ap()
    kt_h = nc.dram_tensor("kt_h", [N_ST, 128, 2, ST], BF16, kind="Internal").ap()
    v_h = nc.dram_tensor("v_h", [N_ST, 128, CPST, 256], BF16, kind="Internal").ap()
    rows_h = nc.dram_tensor("rows_h", [N_ST, 2, 512], F32, kind="Internal").ap()
    src_h = nc.dram_tensor("src_h", [128, 1024], F32, kind="Internal").ap()
    dst_h = nc.dram_tensor("dst_h", [256, 1024], F32, kind="Internal").ap()

    import contextlib
    stack = contextlib.ExitStack()
    sb = lambda name, shape, dt: stack.enter_context(nc.sbuf_tensor(name, shape, dt))
    psb = lambda name, shape, dt: stack.enter_context(nc.psum_tensor(name, shape, dt))
    sem = lambda name: stack.enter_context(nc.semaphore(name))

    wqT = sb("wqT", [128, 8, 1024], BF16)
    woT = sb("woT", [128, 8, 1024], BF16)
    wkT = sb("wkT", [128, 8, 256], BF16)
    wvT = sb("wvT", [128, 8, 256], BF16)
    u1T = sb("u1T", [128, 8, 256], BF16)
    v1T = sb("v1T", [128, 2, 4096], BF16)
    u2T = sb("u2T", [128, 32, 256], BF16)
    v2T = sb("v2T", [128, 2, 1024], BF16)
    b1c = sb("b1c_s", [128, 32], F32)
    b2c = sb("b2c_s", [128, 8], F32)
    mt = sb("mt_s", [128, 128], BF16)
    identbf = sb("identbf_s", [128, 128], BF16)
    maskT = sb("maskT", [128, 2048], BF16)
    qdec = sb("qdec_s", [128, 8, 128], BF16)
    kdec = sb("kdec_s", [128, 1024], BF16)
    bdb = sb("bdb_s", [128, 1024], BF16)
    ctab = sb("ctab", [128, ST], F32)
    stab = sb("stab", [128, ST], F32)
    rmask = sb("rmask_s", [128, 2], F32)
    onesf = sb("onesf_s", [1, 128], F32)
    epsc = sb("epsc_s", [1, 1], F32)
    nc.const_aps.register(F32, EPS, epsc[0:1, 0:1])
    onesb = sb("onesb_s", [1, 512], BF16)
    onescol = sb("onescol_s", [128, 1], BF16)
    rsum_q = sb("rsum_q", [1, 1024], BF16)
    rsum_k = sb("rsum_k", [1, 256], BF16)
    rsum_v = sb("rsum_v", [1, 256], BF16)
    rsum_u1 = sb("rsum_u1", [1, 256], BF16)

    xb = [sb(f"xb{i}", [128, 8, ST], F32) for i in range(2)]
    x_bf = sb("x_bf", [128, 8, ST], BF16)
    rows_cur = sb("rows_cur", [1, 1, 2, 512], F32)
    negmb_row = sb("negmb_row", [1, 512], BF16)
    srow = sb("srow", [1, 2, 512], F32)
    rs2_row = sb("rs2_row", [1, 512], F32)
    negm2_row = sb("negm2_row", [1, 512], F32)
    negm2b_row = sb("negm2b_row", [1, 512], BF16)
    rs_col = sb("rs_col", [128, 4], F32)

    kraw_bf = sb("kraw_bf", [128, 2, 512], BF16)
    tmp1 = sb("tmp1", [128, 512], F32)
    tmp2 = sb("tmp2", [128, 512], F32)
    rs2_sb = tmp2
    kT_sb = sb("kT_sb", [128, 2, ST], BF16)
    kq_scr = sb("kq_scr", [128, 1024], BF16)
    ks_scr = sb("ks_scr", [128, 1024], BF16)
    ktok = kq_scr
    k16 = ks_scr
    st_true = ks_scr
    qdq3 = None
    v_bf = sb("v_bf", [128, CPST, 256], BF16)
    state_r = sb("state_r", [128, 2, 1024], BF16)
    fbf = sb("fbf", [128, 1024], BF16)
    rrun = sb("rrun", [128, 1024], BF16)
    st_ld = sb("st_ld", [128, 2, 1024], BF16)
    qp = sb("qp", [128, 8, ST], BF16)
    sh_bf = sb("sh_bf", [128, 2048], BF16)
    S_bf = sh_bf
    H_bf = sh_bf
    attn_bf = sb("attn_bf", [128, 8, 256], BF16)
    pq_bf = sb("pq_bf", [128, 2, 512], BF16)
    P_bf = pq_bf
    Q_bf = pq_bf

    ps_a = psb("ps_a", [128, 512], F32)
    ps_k = [psb("ps_k0", [128, 512], F32), psb("ps_k1", [128, 512], F32)]
    ps_s = [psb(f"ps_s{i}", [128, 512], F32) for i in range(4)]
    ps_wt = psb("ps_wt", [128, 1024], BF16)

    sems = {}
    for s in ("s_const", "s_wkT", "s_wvT", "s_wqT", "s_woT", "s_x0", "s_x1",
              "s_tabc", "s_tabs", "s_rows", "s_out0", "s_out1",
              "s_w1", "s_w2", "s_w3", "s_kt", "s_vv", "s_ktl", "s_vvl", "s_st0", "s_st1",
              "s_sp0", "s_sp1", "s_g", "s_q", "s_t", "s_v", "s_s", "s_c"):
        sems[s] = sem(s)
    P = Prog(nc, sems)

    T, V, S = "s_t", "s_v", "s_s"

    tok_x = [None] * N_ST
    tok_x[0] = P.dma("sync", xb[0][:], xt[0], inc="s_x0")
    tok_state0 = P.op("vector", lambda e: e.memset(state_r[:], 0.0), inc=V)
    tok_const = None
    for dst, src in ((mt, mt_d), (identbf, identbf_d), (maskT, mask_d),
                     (qdec, qdec_d), (kdec, kdec_d), (bdb, bdb_d),
                     (rmask, rmask_d), (onesf, onesf_d), (epsc, epsc_d),
                     (onesb, onesb_d),
                     (onescol, onescol_d), (b2c, b2c_d)):
        tok_const = P.dma("sync", dst[:], src[:, :] if len(src.shape) == 2 else src[:],
                          inc="s_const")

    P.dma("gpsimd", wkbf_h[:, :], wk[:, :], inc="s_w1")
    tok_wkv_h = P.dma("gpsimd", wvbf_h[:, :], wv[:, :], inc="s_w1")
    P.dma("gpsimd", u1T[:], u1.rearrange("(k p) r -> p k r", p=128), inc="s_w2")
    P.dma("gpsimd", v1T[:], v1.rearrange("(k p) f -> p k f", p=128), inc="s_w2")
    for g in range(4):
        P.dma("gpsimd", u2T[:, 8 * g:8 * (g + 1), :],
              u2.rearrange("(k p) r -> p k r", p=128)[:, 8 * g:8 * (g + 1), :],
              inc="s_w2")
    P.dma("gpsimd", v2T[:], v2.rearrange("(k p) d -> p k d", p=128), inc="s_w2")
    tok_w2 = P.dma("gpsimd", b1c[:], b1c_d[:, :], inc="s_w2")
    tok_u1 = tok_v1 = tok_v2 = tok_b1 = tok_w2
    tok_u2 = [tok_w2] * 4
    P.dma("gpsimd", wqbf_h[:, :], wq[:, :], inc="s_w3")
    tok_w3 = P.dma("gpsimd", wobf_h[:, :], wo[:, :], inc="s_w3")
    tok_wq_h = tok_wo_h = tok_w3

    tok_wkT = tok_wvT = tok_wqT = tok_woT = None
    for k in range(8):
        tok_wkT = P.op("sync", lambda e, k=k: e.dma_start_transpose(
            wkT[:, k, :], wkbf_h[:, 128 * k:128 * (k + 1)]),
            waits=[tok_wkv_h], inc="s_wkT", inc_n=16)
    for k in range(8):
        tok_wvT = P.op("sync", lambda e, k=k: e.dma_start_transpose(
            wvT[:, k, :], wvbf_h[:, 128 * k:128 * (k + 1)]),
            waits=[tok_wkv_h], inc="s_wvT", inc_n=16)

    def rowsum(dst_row, srcT, nsl, waits):
        prev_evac = None
        for (a, b) in nsl:
            w0 = list(waits) + ([prev_evac] if prev_evac else [])
            for k in range(8):
                tk = P.op("tensor", lambda e, k=k, a=a, b=b: e.matmul(
                    ps_a[0:1, 0:b - a], onescol[:, 0:1], srcT[:, k, a:b],
                    start=(k == 0), stop=(k == 7)),
                    waits=w0 if k == 0 else (), inc=T if k == 7 else None)
            prev_evac = P.op("scalar", lambda e, a=a, b=b: e.copy(
                out=dst_row[0:1, a:b], in_=ps_a[0:1, 0:b - a]),
                waits=[tk], inc=S)
        return prev_evac

    tok_rsum_k = rowsum(rsum_k, wkT, [(0, 256)], [tok_wkT, tok_const])
    tok_rsum_v = rowsum(rsum_v, wvT, [(0, 256)], [tok_wvT, tok_rsum_k])

    tok_xbf_done = [None] * (N_ST + 1)
    tok_xb_free = [None] * (N_ST + 1)
    prev_sp2 = None
    tok_rows_free = [None] * (N_ST + 1)
    prev_ts_nm = None
    prev_ts_rc = None
    prev_tv_sin = None
    prev_kt_spill = None
    prev_v_spill = None
    tok_ctab_free = None
    prev_state_tt = tok_state0
    prev_ts_v_last = tok_rsum_v
    prev_ts_tok = None

    def pre_sq(stv, sq_waits):
        xv1 = xb[stv % 2][:, 0:4, :].rearrange("p a b -> p (a b)")
        xv2 = xb[stv % 2][:, 4:8, :].rearrange("p a b -> p (a b)")
        P.op("gpsimd", lambda e, xv1=xv1: e.tensor_mul(
            attn_bf.rearrange("p a b -> p (a b)"), xv1, xv1),
            waits=sq_waits, inc="s_q")
        return P.op("gpsimd", lambda e, xv2=xv2: e.tensor_mul(
            sh_bf[:, :], xv2, xv2), inc="s_q")

    def pre_mm(mm_waits):
        for k in range(8):
            srcb = attn_bf if k < 4 else sh_bf
            off = 512 * (k % 4)
            tt_stat = P.op("tensor", lambda e, srcb=srcb, off=off, k=k: e.matmul(
                ps_s[0][0:1, :], onescol[:, 0:1],
                bass.AP(srcb, off, [[2048, 128], [1, 512]]),
                start=(k == 0), stop=(k == 7)),
                waits=mm_waits if k == 0 else (),
                inc=T if k == 7 else None)
        return tt_stat

    tsq_cur = pre_sq(0, [tok_x[0]])
    prev_tt_stat = pre_mm([tsq_cur, tok_const])
    prev_kraw_rd = [None, None]
    prev_addsA = None
    prev_addsB = None
    prev_ttu = [None, None]

    for st in range(n_st1):
        if st + 1 < n_st1:
            w = [tok_xb_free[st - 1]] if st >= 1 else []
            tok_x[st + 1] = P.dma("sync", xb[(st + 1) % 2][:], xt[st + 1],
                                  waits=w, inc=f"s_x{(st + 1) % 2}")
            tsq_next = pre_sq(st + 1, [tok_x[st + 1], prev_tt_stat])
        if st == 3:
            for k in range(8):
                tok_wqT = P.op("sync", lambda e, k=k: e.dma_start_transpose(
                    wqT[:, k, :], wqbf_h[:, 128 * k:128 * (k + 1)]),
                    waits=[tok_w3], inc="s_wqT", inc_n=16)
            for k in range(8):
                tok_woT = P.op("sync", lambda e, k=k: e.dma_start_transpose(
                    woT[:, k, :], wobf_h[:, 128 * k:128 * (k + 1)]),
                    waits=[tok_w3], inc="s_woT", inc_n=16)
        xcur = xb[st % 2]
        tv_xbf1 = P.op("scalar", lambda e, xcur=xcur: e.activation(
            out=x_bf[:, 0:4, :].rearrange("p a b -> p (a b)"),
            in_=xcur[:, 0:4, :].rearrange("p a b -> p (a b)"), func=AF.Copy),
            waits=[tok_x[st], tok_xbf_done[st - 1] if st >= 1 else None],
            inc=S)
        tv_xbf = P.op("scalar", lambda e, xcur=xcur: e.activation(
            out=x_bf[:, 4:8, :].rearrange("p a b -> p (a b)"),
            in_=xcur[:, 4:8, :].rearrange("p a b -> p (a b)"), func=AF.Copy),
            waits=[tv_xbf1], inc=S)
        tt_stat = prev_tt_stat
        stat_waits = [tv_xbf1, prev_ts_v_last]
        for k in range(8):
            tt_sum = P.op("tensor", lambda e, k=k: e.matmul(
                ps_a[0:1, :], onescol[:, 0:1], x_bf[:, k, :],
                start=(k == 0), stop=(k == 7)),
                waits=stat_waits if k == 0 else ([tv_xbf] if k == 4 else ()),
                inc=T if k == 7 else None)
        s_ring = 0
        ts1 = P.op("scalar", lambda e, s_ring=s_ring: e.activation(
            out=rows_cur[0:1, s_ring, 0, :], in_=ps_a[0:1, :], func=AF.Copy,
            scale=-1.0 / D),
            waits=[tt_sum, tok_rows_free[st - 1] if st >= 1 else None,
                   ("s_rows", 16 * st) if st >= 1 else None,
                   prev_ts_nm], inc=S)
        tv_m2 = P.op("scalar", lambda e, s_ring=s_ring: e.activation(
            out=srow[0:1, 1, :], in_=rows_cur[0:1, s_ring, 0, :], func=AF.Square),
            waits=[ts1], inc=S)
        tv_var = P.op("vector", lambda e: e.scalar_tensor_tensor(
            srow[0:1, 0, :], ps_s[0][0:1, :], 1.0 / D, srow[0:1, 1, :],
            Alu.mult, Alu.subtract),
            waits=[tt_stat, tv_m2], inc=V)
        ts3 = P.op("scalar", lambda e: e.activation(
            out=srow[0:1, 1, :], in_=srow[0:1, 0, :], func=AF.Ln, bias=EPS),
            waits=[tv_var, tok_const], inc=S)
        ts_rs = P.op("scalar", lambda e, s_ring=s_ring: e.activation(
            out=rows_cur[0:1, s_ring, 1, :], in_=srow[0:1, 1, :], func=AF.Exp,
            scale=-0.5), waits=[ts3], inc=S)
        ts_nm = P.op("scalar", lambda e, s_ring=s_ring: e.copy(
            out=negmb_row[0:1, :], in_=rows_cur[0:1, s_ring, 0, :]),
            waits=[ts1], inc=S)
        prev_ts_nm = ts_nm
        tok_rows_sp = P.dma("sync",
                            rows_h[st].rearrange("a b -> (a b)").unsqueeze(0),
                            rows_cur[0:1, s_ring].rearrange("p a b -> p (a b)"),
                            waits=[ts_rs], inc="s_rows")
        tt_rsb = P.op("tensor", lambda e, s_ring=s_ring: e.matmul(
            ps_s[3][:, :], onesf[0:1, 0:128], rows_cur[0:1, s_ring, 1, :],
            start=True, stop=True),
            waits=[ts_rs, prev_tv_sin, prev_addsB], inc=T)
        for i in range(4):
            tt_rsc = P.op("tensor", lambda e, i=i, s_ring=s_ring: e.matmul(
                ps_s[2][:, i:i + 1], rows_cur[0:1, s_ring, 1, 128 * i:128 * i + 128],
                onesf[0:1, 0:1], start=True, stop=True),
                waits=[prev_ts_rc, prev_addsB] if i == 0 else (),
                inc=T if i == 3 else None)
        tok_tab = P.dma("sync", ctab[:], costab_d[st],
                        waits=[tok_ctab_free], inc="s_tabc")
        tok_tab2 = P.dma("sync", stab[:], sintab_d[st],
                         waits=[tok_ctab_free], inc="s_tabs")
        tv_cos = P.op("vector", lambda e: e.tensor_mul(ctab[:], ctab[:], ps_s[3][:, :]),
                      waits=[tt_rsb, tok_tab, tok_tab2], inc=V)
        tv_sin = P.op("vector", lambda e: e.tensor_mul(stab[:], stab[:], ps_s[3][:, :]),
                      waits=[tt_rsb, tok_tab2], inc=V)
        prev_tv_sin = tv_sin
        ts_rc = P.op("scalar", lambda e: e.copy(out=rs_col[:], in_=ps_s[2][:, 0:4]),
                     waits=[tt_rsc], inc=S)
        prev_ts_rc = ts_rc
        tv_k = [None, None]
        ts_k_prev = None
        tv1_prev = None
        tv2_prev = None
        tt_rot_prev = None
        for r in range(2):
            kps = ps_a if r == 0 else ps_s[0]
            kw = [ts1 if r == 0 else tv_var, tv_xbf, tok_wkT]
            for k in range(8):
                P.op("tensor", lambda e, k=k, r=r, kps=kps: e.matmul(
                    kps[:, :], wkT[:, k, 128 * r:128 * r + 128], x_bf[:, k, :],
                    start=(k == 0), stop=False), waits=kw if k == 0 else ())
            tt_k = P.op("tensor", lambda e, r=r, kps=kps: e.matmul(
                kps[:, :], rsum_k[0:1, 128 * r:128 * r + 128], negmb_row[0:1, :],
                start=False, stop=True), waits=[ts_nm], inc=T)
            skw = [tt_k]
            if prev_kraw_rd[r] is not None:
                skw += prev_kraw_rd[r]
            ts_k = P.op("scalar", lambda e, r=r, kps=kps: e.copy(
                out=kraw_bf[:, r, :], in_=kps[:, :]), waits=skw, inc=S)
            tt_rot = P.op("tensor", lambda e, r=r: e.matmul(
                ps_s[1][:, :], mt[:, :], kraw_bf[:, r, :], start=True, stop=True),
                waits=[ts_k, tv2_prev], inc=T)
            tv1 = P.op("vector", lambda e, r=r: e.tensor_mul(
                tmp1[:], kraw_bf[:, r, :], ctab[:]),
                waits=[ts_k, tv_cos, tv_k[0] if r == 1 else None], inc=V)
            tv2 = P.op("vector", lambda e: e.tensor_mul(tmp2[:], ps_s[1][:, :], stab[:]),
                       waits=[tt_rot, tv_sin], inc=V)
            tv_k[r] = P.op("vector", lambda e, r=r: e.tensor_add(
                kT_sb[:, r, :], tmp1[:], tmp2[:]),
                waits=[tv1, tv2, prev_kt_spill], inc=V)
            prev_kraw_rd[r] = [tv1, tt_rot]
            ts_k_prev, tv1_prev, tv2_prev, tt_rot_prev = ts_k, tv1, tv2, tt_rot
        prev_kt_spill = P.dma("gpsimd", kt_h[st], kT_sb[:],
                              waits=[tv_k[0], tv_k[1]], inc="s_kt")
        trw = [tv_k[0], tv_k[1], tok_const]
        if prev_ts_tok is not None:
            trw.append(prev_ts_tok)
        first_tr = True
        for r in range(2):
            for i in range(4):
                tt_tr = P.op("tensor", lambda e, r=r, i=i: e.transpose(
                    ps_wt[:, 256 * i + 128 * r: 256 * i + 128 * r + 128],
                    kT_sb[:, r, 128 * i:128 * i + 128], identbf[:, :]),
                    waits=trw if first_tr else (),
                    inc=T if (r == 1 and i == 3) else None)
                first_tr = False
        prev_ts_tok = P.op("scalar", lambda e: e.copy(
            out=ktok[:, :], in_=ps_wt[:, :]),
            waits=[tt_tr], inc=S)
        ts_tok = prev_ts_tok
        ts_v = [None] * CPST
        for i in range(CPST):
            vps = ps_a if i % 2 == 0 else ps_s[1]
            vw = [tok_wvT]
            if i > 1:
                vw.append(ts_v[i - 2])
            elif i == 0:
                vw.append(ts_k_prev)
            else:
                vw.append(tv2_prev)
            for k in range(8):
                P.op("tensor", lambda e, k=k, i=i, vps=vps: e.matmul(
                    vps[:, 0:256], x_bf[:, k, 128 * i:128 * i + 128], wvT[:, k, :],
                    start=(k == 0), stop=False), waits=vw if k == 0 else ())
            tt_v = P.op("tensor", lambda e, i=i, vps=vps: e.matmul(
                vps[:, 0:256], negmb_row[0:1, 128 * i:128 * i + 128],
                rsum_v[0:1, :], start=False, stop=True), inc=T)
            ts_v[i] = P.op("scalar", lambda e, i=i, vps=vps: e.activation(
                out=v_bf[:, i, :], in_=vps[:, 0:256], func=AF.Copy,
                scale=rs_col[:, i:i + 1]),
                waits=[tt_v, ts_rc, prev_v_spill], inc=S)
        prev_v_spill = P.dma("gpsimd", v_h[st], v_bf[:], waits=[ts_v[CPST - 1]],
                             inc="s_vv")
        tok_xbf_done[st] = tt_v
        prev_ts_v_last = ts_v[CPST - 1]
        if st + 1 < n_st1:
            prev_tt_stat = pre_mm([tsq_next, tv_var, ts_k_prev])
        if st == 0:
            P.op("vector", lambda e: e.memset(ps_k[0][:, :], 0.0),
                 waits=[prev_state_tt], inc=V)
            tv_z = P.op("vector", lambda e: e.memset(ps_k[1][:, :], 0.0), inc=V)
        def emit_k16(ci):
            kb = k16 if ci % 2 == 0 else pq_bf
            return P.op("vector", lambda e, ci=ci, kb=kb: e.tensor_tensor(
                bass.AP(kb, 0, [[1024, 128], [512, 2], [128, 4], [64, 2], [1, 64]]),
                bass.AP(ktok, 256 * ci,
                        [[1024, 128], [128, 2], [0, 4], [64, 2], [1, 64]]),
                bass.AP(kdec, 0,
                        [[1024, 128], [512, 2], [128, 4], [64, 2], [1, 64]]),
                Alu.mult),
                waits=[ts_tok, prev_ttu[ci % 2]], inc=V)

        k16tok = [None] * CPST
        k16tok[0] = emit_k16(0)
        k16tok[1] = emit_k16(1)
        for ci in range(CPST):
            c = CPST * st + ci
            pair = [ps_k[0], ps_k[1]] if ci % 2 == 0 else [ps_s[2], ps_s[3]]
            kb = k16 if ci % 2 == 0 else pq_bf
            tok_sp = P.dma("sync", st_h[c], state_r[:, c % 2, :],
                           waits=[prev_state_tt], inc=f"s_sp{c % 2}")
            if ci % 2 == 0:
                bank_free = [prev_addsA] if prev_addsA is not None else [tv_z]
            else:
                bank_free = ([prev_addsB] if prev_addsB is not None else []) + \
                            [ts_rc, tv_cos, tv_sin]
            uw = [k16tok[ci], ts_v[ci]] + bank_free
            for p in range(8):
                tt_u = P.op("tensor", lambda e, p=p, ci=ci, pair=pair, kb=kb: e.matmul(
                    pair[p // 4][:, 128 * (p % 4):128 * (p % 4) + 128],
                    kb[:, p // 4, 128 * (p % 4):128 * (p % 4) + 128]
                    if ci % 2 == 1 else kb[:, 128 * p:128 * p + 128],
                    v_bf[:, ci, 128 * (p // 4):128 * (p // 4) + 128],
                    start=(p % 4 == 0), stop=(p % 4 == 3)),
                    waits=uw if p == 0 else (), inc=T if p == 7 else None)
            prev_ttu[ci % 2] = tt_u
            if ci + 2 < CPST:
                k16tok[ci + 2] = emit_k16(ci + 2)
            tvm = P.op("vector", lambda e, c=c: e.tensor_mul(
                fbf[:], state_r[:, c % 2, :], bdb[:]),
                waits=[prev_state_tt], inc=V)
            ns = (c + 1) % 2
            prev_sp = tok_sp
            for g in range(2):
                for e_par in range(2):
                    sap = bass.AP(state_r, 2048 * 64 * e_par + 1024 * ns
                                  + 512 * g + 64 * e_par,
                                  [[2048, 64], [128, 4], [1, 64]])
                    fap = bass.AP(fbf, 64 * 1024 * e_par + 512 * g + 64 * e_par,
                                  [[1024, 64], [128, 4], [1, 64]])
                    pap = bass.AP(pair[g], 64 * 512 * e_par + 64 * e_par,
                                  [[512, 64], [128, 4], [1, 64]])
                    tok_last = P.op("vector",
                                    lambda e, sap=sap, fap=fap, pap=pap: e.tensor_add(
                        sap, fap, pap),
                        waits=[tt_u, tvm, prev_sp2] if (g == 0 and e_par == 0) else (),
                        inc=V)
            prev_sp2 = prev_sp
            prev_state_tt = tok_last
            if ci % 2 == 0:
                prev_addsA = tok_last
            else:
                prev_addsB = tok_last
        tok_ctab_free = tv_k[1]
        tok_xb_free[st] = tt_v
        tok_rows_free[st] = tt_rsc

    if not do_coll:
        P.op("gpsimd", lambda e: e.nop(),
             waits=[("s_sp0", P.count["s_sp0"]), ("s_sp1", P.count["s_sp1"]),
                    prev_state_tt])
        with nc.Block() as block:
            P.emit(block)
        stack.close()
        return nc
    tok_src = P.dma("gpsimd", src_h[:, :], state_r[:, N_CH % 2, :],
                    waits=[prev_state_tt], inc="s_g")
    P.op("gpsimd", lambda e: e.collective_compute(
        "AllGather", Alu.bypass,
        ins=[src_h[:, :]], outs=[dst_h[:, :]],
        replica_groups=[[0, 1], [2, 3], [4, 5], [6, 7]]),
        waits=[tok_src], inc="s_c")
    tok_coll = ("s_c", P.count["s_c"])
    tok_g0 = P.dma("gpsimd", fbf[:], dst_h[0:128, :], waits=[tok_coll], inc="s_g")
    tv_rrun = None

    tok_rsum_q = rowsum(rsum_q, wqT, [(0, 512), (512, 1024)],
                        [tok_wqT, prev_ts_v_last])
    tok_rsum_u1 = rowsum(rsum_u1, u1T, [(0, 256)], [tok_u1, tok_rsum_q])

    tok_x2 = [None] * N_ST
    tok_x2[0] = P.dma("sync", xb[0][:], xt[0], waits=[tok_xb_free[N_ST - 1]],
                      inc="s_x0")
    tok_kt_ld = [None] * N_ST
    tok_v_ld = [None] * N_ST
    tok_st_ld = [None] * N_ST
    tok_kt_ld[0] = P.dma("sync", kT_sb[:], kt_h[0],
                         waits=[("s_kt", 16 * N_ST), prev_state_tt], inc="s_ktl")
    tok_v_ld[0] = P.dma("sync", v_bf[:], v_h[0],
                        waits=[("s_vv", 16 * N_ST), prev_state_tt], inc="s_vvl")
    tok_rows_ld = [None] * N_ST
    tok_rows_ld[0] = P.dma("sync",
                           rows_cur[0:1, 0].rearrange("p a b -> p (a b)"),
                           rows_h[0].rearrange("a b -> (a b)").unsqueeze(0),
                           waits=[tok_rows_free[N_ST - 1], ("s_rows", 16 * N_ST)],
                           inc="s_rows")
    tok_stch = [None] * (N_CH + 2)
    all_sp = [("s_sp0", P.count["s_sp0"]), ("s_sp1", P.count["s_sp1"])]
    tok_stch[0] = P.dma("sync", st_ld[:, 0, :], st_h[0],
                        waits=all_sp, inc="s_st0")
    tok_stch[1] = P.dma("sync", st_ld[:, 1, :], st_h[1],
                        waits=all_sp, inc="s_st1")
    prev_tv_st = [None] * (N_CH + 2)

    tok_out = [None] * N_ST
    tv2_prev = [None]
    tv1_prev2 = [None, None]
    prev_tv_P1 = prev_tv_sin
    prev_S_evac_g = [None, None]
    prev_tt_o = None
    prev_ts_ot1 = None
    tok_rows_used = [None] * (N_ST + 1)
    prev_ts_attn = prev_ts_tok
    prev_negmb_user = None
    prev_tv_xbf2 = None
    prev_tv_rr = None
    tok_qp_free = None
    prev_o_evac = None
    prev_S_evac = None
    tok_xb2_free = [None] * (N_ST + 1)
    tok_ctab2_free = tok_ctab_free
    prev_psa_reader = tok_rsum_u1
    prev_tv_res2_last = None
    prev_tt_P = None
    prev_tsQ = None
    prev_ts_ot0 = None

    if n_st2 == 0:
        P.op("gpsimd", lambda e: e.nop(), waits=[tok_coll])
        with nc.Block() as block:
            P.emit(block)
        stack.close()
        return nc
    for st in range(n_st2):
        if st + 1 < n_st2:
            w = [tok_xb2_free[st - 1]] if st >= 1 else []
            tok_x2[st + 1] = P.dma("sync", xb[(st + 1) % 2][:], xt[st + 1], waits=w,
                                   inc=f"s_x{(st + 1) % 2}")
        xcur = xb[st % 2]
        s_ring = 0
        tv_xbf = P.op("vector", lambda e, xcur=xcur: e.tensor_copy(x_bf[:], xcur[:]),
                      waits=[tok_x2[st], prev_tt_P, prev_tv_xbf2], inc=V)
        ts_nm = P.op("scalar", lambda e, s_ring=s_ring: e.copy(
            out=negmb_row[0:1, :], in_=rows_cur[0:1, s_ring, 0, :]),
            waits=[tok_rows_ld[st], prev_negmb_user], inc=S)
        tt_rsb = P.op("tensor", lambda e, s_ring=s_ring: e.matmul(
            ps_a[:, :], onesf[0:1, 0:128], rows_cur[0:1, s_ring, 1, :],
            start=True, stop=True),
            waits=[prev_psa_reader, prev_tv_res2_last, tok_rows_ld[st]], inc=T)
        if st == 0:
            tok_tab_n = P.dma("sync", ctab[:], costab_d[0], waits=[tok_ctab2_free],
                              inc="s_tabc")
            tok_tab2_n = P.dma("sync", stab[:], sintab_d[0], waits=[tok_ctab2_free],
                               inc="s_tabs")
        tok_tab, tok_tab2 = tok_tab_n, tok_tab2_n
        tv_cos = P.op("vector", lambda e: e.tensor_mul(ctab[:], ctab[:], ps_a[:, :]),
                      waits=[tt_rsb, tok_tab, tok_tab2], inc=V)
        tv_sin = P.op("vector", lambda e: e.tensor_mul(stab[:], stab[:], ps_a[:, :]),
                      waits=[tt_rsb, tok_tab2], inc=V)
        tv_q = [None] * 8
        ts_q_prev = [None, None]
        pend = []
        def emit_rope(p_, ts_q_, rw_extra=()):
            rw = [ts_q_, tv2_prev[0]] + list(rw_extra)
            if prev_tsQ is not None:
                rw.append(prev_tsQ)
            tt_rot = P.op("tensor", lambda e: e.matmul(
                ps_k[0][:, :], mt[:, :], kraw_bf[:, p_ % 2, :], start=True, stop=True),
                waits=rw, inc=T)
            tv1 = P.op("vector", lambda e, p_=p_: e.tensor_mul(
                tmp1[:], kraw_bf[:, p_ % 2, :], ctab[:]),
                waits=[ts_q_, tv_cos, tv_q[p_ - 1] if p_ > 0 else None], inc=V)
            tv2 = P.op("vector", lambda e: e.tensor_mul(tmp2[:], ps_k[0][:, :], stab[:]),
                       waits=[tt_rot, tv_sin], inc=V)
            tv_q[p_] = P.op("vector", lambda e, p_=p_: e.tensor_add(
                qp[:, p_, :], tmp1[:], tmp2[:]), waits=[tv1, tv2], inc=V)
            tv2_prev[0] = tv2
        for p in range(8):
            qps = ps_a if p % 2 == 0 else ps_s[3]
            qw = [tv_cos, tv_sin, tv_xbf, tok_wqT, tok_rsum_q, ts_nm]
            if p == 0 and tok_qp_free is not None:
                qw.append(tok_qp_free)
            if p == 1:
                qw.append(prev_tv_P1)
            if ts_q_prev[p % 2] is not None:
                qw.append(ts_q_prev[p % 2])
            for k in range(8):
                P.op("tensor", lambda e, k=k, p=p, qps=qps: e.matmul(
                    qps[:, :], wqT[:, k, 128 * p:128 * p + 128], x_bf[:, k, :],
                    start=(k == 0), stop=False), waits=qw if k == 0 else ())
            tt_q = P.op("tensor", lambda e, p=p, qps=qps: e.matmul(
                qps[:, :], rsum_q[0:1, 128 * p:128 * p + 128], negmb_row[0:1, :],
                start=False, stop=True), inc=T)
            skw = [tt_q]
            if p >= 2:
                skw.append(tv1_prev2[p % 2])
            ts_q = P.op("scalar", lambda e, p=p, qps=qps: e.copy(
                out=kraw_bf[:, p % 2, :], in_=qps[:, :]), waits=skw, inc=S)
            ts_q_prev[p % 2] = ts_q
            pend.append((p, ts_q))
            if len(pend) >= 2:
                p_, t_ = pend.pop(0)
                emit_rope(p_, t_)
                tv1_prev2[p_ % 2] = tv_q[p_]
        while pend:
            p_, t_ = pend.pop(0)
            emit_rope(p_, t_)
            tv1_prev2[p_ % 2] = tv_q[p_]
        if st == 0:
            tv_sel0 = P.op("vector", lambda e: e.tensor_scalar_mul(
                rrun[:], fbf[:], rmask[:, 0:1]),
                waits=[tok_g0, tok_const, prev_state_tt], inc=V)
            tok_g1 = P.dma("gpsimd", fbf[:], dst_h[128:256, :], waits=[tv_sel0],
                           inc="s_g")
            tv_rrun = P.op("vector", lambda e: e.scalar_tensor_tensor(
                rrun[:], fbf[:], rmask[:, 1:2], rrun[:], Alu.mult, Alu.add),
                waits=[tok_g1], inc=V)
        if st + 1 < n_st2:
            tok_rows_ld[st + 1] = P.dma(
                "sync", rows_cur[0:1, 0].rearrange("p a b -> p (a b)"),
                rows_h[st + 1].rearrange("a b -> (a b)").unsqueeze(0),
                waits=[tt_rsb, ts_nm], inc="s_rows")
            tok_tab_n = P.dma("sync", ctab[:], costab_d[st + 1],
                              waits=[tv_q[7]], inc="s_tabc")
            tok_tab2_n = P.dma("sync", stab[:], sintab_d[st + 1],
                               waits=[tv_q[7]], inc="s_tabs")
        tv_res1 = [None] * 8
        for ci in range(CPST):
            c = CPST * st + ci
            dqw = [tv_q[7]]
            if prev_tt_o is not None:
                dqw.append(prev_tt_o)
            tv_qdq = P.op("vector", lambda e, ci=ci: e.tensor_mul(
                kq_scr[:].rearrange("p (a b) -> p a b", a=8),
                qp[:, :, 128 * ci:128 * ci + 128], qdec[:]),
                waits=dqw, inc=V)
            tv_st = P.op("vector", lambda e, c=c: e.tensor_add(
                st_true[:], st_ld[:, c % 2, :], rrun[:]),
                waits=[tv_rrun, tok_stch[c], prev_tv_rr], inc=V)
            prev_tv_st[c] = tv_st
            if c + 2 <= N_CH - 1:
                tok_stch[c + 2] = P.dma("sync", st_ld[:, c % 2, :], st_h[c + 2],
                                        waits=[tv_st], inc=f"s_st{c % 2}")
            prev_tv_rr = P.op("vector", lambda e: e.tensor_mul(
                rrun[:], rrun[:], bdb[:]), waits=[tv_st], inc=V)
            tv_mask = [None, None]
            for e_par in range(2):
                sw0 = [tv_q[7], tok_kt_ld[st], prev_S_evac_g[e_par]]
                if e_par == 1:
                    sw0.append(prev_tv_P1)
                    if ci >= 2:
                        sw0.append(tv_res1[7])
                if prev_tsQ is not None:
                    sw0.append(prev_tsQ)
                first_s = True
                for p in range(8):
                    h = PAIRS[p][e_par]
                    kv = h // 4
                    s = p + 8 * e_par
                    tt_sc = P.op("tensor", lambda e, p=p, e_par=e_par, kv=kv, s=s, ci=ci: e.matmul(
                        ps_s[s // 4][:, 128 * (s % 4):128 * (s % 4) + 128],
                        kT_sb[64 * (kv % 2):64 * (kv % 2) + 64, kv // 2,
                              128 * ci:128 * ci + 128],
                        qp[64 * e_par:64 * e_par + 64, p, 128 * ci:128 * ci + 128],
                        start=True, stop=True),
                        waits=sw0 if first_s else (), inc=T if p == 7 else None)
                    first_s = False
                P.op("vector", lambda e, e_par=e_par: e.tensor_mul(
                    S_bf[:, 1024 * e_par:1024 * e_par + 512], ps_s[2 * e_par][:, :],
                    maskT[:, 1024 * e_par:1024 * e_par + 512]), waits=[tt_sc], inc=V)
                tv_mask[e_par] = P.op("vector", lambda e, e_par=e_par: e.tensor_mul(
                    S_bf[:, 1024 * e_par + 512:1024 * (e_par + 1)],
                    ps_s[2 * e_par + 1][:, :],
                    maskT[:, 1024 * e_par + 512:1024 * (e_par + 1)]), inc=V)
                if e_par == 0:
                    o2w = [tv_qdq, tv_st]
                    if prev_ts_ot0 is not None:
                        o2w.append(prev_ts_ot0)
                    for p in range(8):
                        pw2 = o2w if p == 0 else (
                            [prev_ts_ot1] if (p == 4 and prev_ts_ot1 is not None)
                            else ())
                        P.op("tensor", lambda e, p=p: e.matmul(
                            ps_k[p // 4][:, 128 * (p % 4):128 * (p % 4) + 128],
                            kq_scr[:, 128 * p:128 * p + 128],
                            st_true[:, 128 * p:128 * p + 128],
                            start=(p % 4 == 0), stop=False),
                            waits=pw2)
            prev_S_evac = tv_mask[1]
            prev_S_evac_g = [tv_mask[0], tv_mask[1]]
            tt_o = None
            for e_par in range(2):
                ow = [tv_mask[e_par], tok_v_ld[st]]
                first_o = True
                for p in range(8):
                    h = PAIRS[p][e_par]
                    kv = h // 4
                    s = p + 8 * e_par
                    tt_o = P.op("tensor", lambda e, p=p, e_par=e_par, kv=kv, s=s, ci=ci: e.matmul(
                        ps_k[p // 4][:, 128 * (p % 4) + 64 * e_par:
                                     128 * (p % 4) + 64 * e_par + 64],
                        S_bf[:, 128 * s:128 * s + 128],
                        v_bf[:, ci, 64 * kv:64 * kv + 64],
                        start=False, stop=(e_par == 1 and p % 4 == 3)),
                        waits=ow if first_o else (),
                        inc=T if (e_par == 1 and p == 7) else None)
                    first_o = False
            ts_ot0 = P.op("scalar", lambda e: e.copy(
                out=S_bf[:, 0:512], in_=ps_k[0][:, :]),
                waits=[tt_o], inc=S)
            ts_ot1 = P.op("scalar", lambda e: e.copy(
                out=S_bf[:, 512:1024], in_=ps_k[1][:, :]), inc=S)
            prev_ts_ot0 = ts_ot0
            prev_ts_ot1 = ts_ot1
            prev_tt_o = tt_o
            for j in range(8):
                tt_tra = P.op("tensor", lambda e, j=j: e.transpose(
                    ps_wt[:, 128 * j:128 * j + 128],
                    S_bf[:, 128 * j:128 * j + 128], identbf[:, :]),
                    waits=([ts_ot0, ts_ot1, prev_ts_attn] if j == 0 else ()),
                    inc=T if j == 7 else None)
            prev_o_evac = P.op("scalar", lambda e, ci=ci: e.copy(
                out=attn_bf[:, :, 128 * (ci % 2):128 * (ci % 2) + 128],
                in_=ps_wt[:].rearrange("p (a b) -> p a b", a=8)),
                waits=[tt_tra], inc=S)
            prev_ts_attn = prev_o_evac
            if ci % 2 == 1:
                half = ci // 2
                for dch in range(8):
                    yps = ps_a if dch % 2 == 0 else ps_s[3]
                    ww = [prev_o_evac, tok_woT]
                    if half == 0:
                        if dch > 1:
                            ww.append(tv_res1[dch - 2])
                        elif dch == 0:
                            ww.append(tv_sin)
                        else:
                            ww.append(prev_S_evac)
                    else:
                        ww.append(prev_S_evac)
                        if dch > 1:
                            ww.append(tv_res1[dch - 2])
                        else:
                            ww.append(tv_res1[7])
                    for p in range(8):
                        tt_y = P.op("tensor", lambda e, p=p, dch=dch, yps=yps: e.matmul(
                            yps[:, 0:256], woT[:, p, 128 * dch:128 * dch + 128],
                            attn_bf[:, p, :],
                            start=(p == 0), stop=(p == 7)),
                            waits=ww if p == 0 else (), inc=T if p == 7 else None)
                    tv_res1[dch] = P.op("vector",
                                        lambda e, dch=dch, half=half, xcur=xcur,
                                        yps=yps: e.tensor_add(
                        xcur[:, dch, 256 * half:256 * half + 256],
                        xcur[:, dch, 256 * half:256 * half + 256],
                        yps[:, 0:256]), waits=[tt_y], inc=V)
        if st + 1 < N_ST:
            tok_kt_ld[st + 1] = P.dma("sync", kT_sb[:], kt_h[st + 1],
                                      waits=[prev_o_evac], inc="s_ktl")
            tok_v_ld[st + 1] = P.dma("sync", v_bf[:], v_h[st + 1],
                                     waits=[prev_o_evac], inc="s_vvl")
        if p2stage <= 3:
            P.op("gpsimd", lambda e: e.nop(), waits=[tv_res1[7]])
            with nc.Block() as block:
                P.emit(block)
            stack.close()
            return nc
        tv_c2 = [None, None]
        for h2 in range(2):
            tv_c2[h2] = P.op("vector", lambda e, h2=h2, xcur=xcur: e.tensor_copy(
                x_bf[:, 4 * h2:4 * h2 + 4, :].rearrange("p a b -> p (a b)"),
                xcur[:, 4 * h2:4 * h2 + 4, :].rearrange("p a b -> p (a b)")),
                waits=[tv_res1[4 * h2 + 3]], inc=V)
        gq = [None] * 8
        for k in range(8):
            gq[k] = P.op("scalar", lambda e, k=k, xcur=xcur: e.activation(
                out=qp[:, k, :], in_=xcur[:, k, :], func=AF.Square),
                waits=[tv_res1[k], tt_sc] if k == 0 else [tv_res1[k]], inc=S)
        tv_xbf2 = tv_c2[1]
        prev_tv_xbf2 = tv_xbf2
        for k in range(8):
            P.op("tensor", lambda e, k=k: e.matmul(
                ps_a[0:1, :], onescol[:, 0:1], x_bf[:, k, :],
                start=(k == 0), stop=(k == 7)),
                waits=[tv_c2[k // 4]] if k % 4 == 0 else ())
        for k in range(8):
            tt_stat = P.op("tensor", lambda e, k=k: e.matmul(
                ps_k[0][0:1, :], onescol[:, 0:1], qp[:, k, :],
                start=(k == 0), stop=(k == 7)),
                waits=[gq[k]],
                inc=T if k == 7 else None)
        tok_qp_free = tt_stat
        ts1 = P.op("scalar", lambda e: e.activation(
            out=negm2_row[0:1, :], in_=ps_a[0:1, :], func=AF.Copy, scale=-1.0 / D),
            waits=[tt_stat], inc=S)
        tv_m2 = P.op("scalar", lambda e: e.activation(
            out=srow[0:1, 1, :], in_=negm2_row[0:1, :], func=AF.Square),
            waits=[ts1], inc=S)
        tv_var = P.op("vector", lambda e: e.scalar_tensor_tensor(
            srow[0:1, 0, :], ps_k[0][0:1, :], 1.0 / D, srow[0:1, 1, :],
            Alu.mult, Alu.subtract),
            waits=[tt_stat, tv_m2], inc=V)
        ts3 = P.op("scalar", lambda e: e.activation(
            out=srow[0:1, 1, :], in_=srow[0:1, 0, :], func=AF.Ln, bias=EPS),
            waits=[tv_var, tok_const], inc=S)
        ts_rs2 = P.op("scalar", lambda e: e.activation(
            out=rs2_row[0:1, :], in_=srow[0:1, 1, :], func=AF.Exp, scale=-0.5),
            waits=[ts3], inc=S)
        ts_nm2 = P.op("scalar", lambda e: e.copy(
            out=negm2b_row[0:1, :], in_=negm2_row[0:1, :]), waits=[ts1], inc=S)
        tt_rsb2 = P.op("tensor", lambda e: e.matmul(
            ps_s[2][:, :], onesf[0:1, 0:128], rs2_row[0:1, :], start=True, stop=True),
            waits=[ts_rs2, prev_S_evac], inc=T)
        ts_rs2sb = P.op("scalar", lambda e: e.copy(out=rs2_sb[:], in_=ps_s[2][:, :]),
                        waits=[tt_rsb2, tv_q[7]], inc=S)
        if p2stage <= 4:
            P.op("gpsimd", lambda e: e.nop(), waits=[ts_rs2sb])
            with nc.Block() as block:
                P.emit(block)
            stack.close()
            return nc
        tv_P = [None, None]
        tt_P = [None, None]
        for r in range(2):
            pps = ps_a if r == 0 else ps_s[3]
            pw = [tv_xbf2, tok_rsum_u1]
            if r == 0:
                pw.append(tv_res1[6])
            else:
                pw.append(tv_res1[7])
            for k in range(8):
                P.op("tensor", lambda e, k=k, r=r, pps=pps: e.matmul(
                    pps[:, :], u1T[:, k, 128 * r:128 * r + 128], x_bf[:, k, :],
                    start=(k == 0), stop=False), waits=pw if k == 0 else ())
        for r in range(2):
            pps = ps_a if r == 0 else ps_s[3]
            tt_P[r] = P.op("tensor", lambda e, r=r, pps=pps: e.matmul(
                pps[:, :], rsum_u1[0:1, 128 * r:128 * r + 128], negm2b_row[0:1, :],
                start=False, stop=True), waits=[ts_nm2] if r == 0 else (), inc=T)
            tv_P[r] = P.op("vector", lambda e, r=r, pps=pps: e.tensor_mul(
                P_bf[:, r, :], pps[:, :], rs2_sb[:]),
                waits=[tt_P[r], ts_rs2sb], inc=V)
        prev_tt_P = tt_P[1]
        ts_H = [None] * 16
        first_q_bank = [True, True]

        def emit_v1(g):
            buf = g % 2
            gw = [tv_P[1], tok_v1]
            if g >= 2:
                gw.append(ts_H[g - 2])
            elif g < 2:
                gw.append(prev_S_evac)
            for j in range(2):
                ff = 2 * g + j
                for r in range(2):
                    tt = P.op("tensor", lambda e, ff=ff, r=r, buf=buf, j=j: e.matmul(
                        ps_s[2 * buf + j][:, :],
                        v1T[:, r, 128 * ff:128 * ff + 128], P_bf[:, r, :],
                        start=(r == 0), stop=(r == 1)),
                        waits=gw if (j == 0 and r == 0) else (),
                        inc=T if (j == 1 and r == 1) else None)
            return tt

        def emit_gelu(g):
            buf = g % 2
            P.op("scalar", lambda e, g=g, buf=buf: e.activation(
                out=H_bf[:, 1024 * buf:1024 * buf + 512],
                in_=ps_s[2 * buf][:, :],
                func=AF.Gelu_apprx_tanh, bias=b1c[:, 2 * g:2 * g + 1]),
                waits=[tt_H_tok[g], tok_b1], inc=S)
            ts_H[g] = P.op("scalar", lambda e, g=g, buf=buf: e.activation(
                out=H_bf[:, 1024 * buf + 512:1024 * (buf + 1)],
                in_=ps_s[2 * buf + 1][:, :],
                func=AF.Gelu_apprx_tanh, bias=b1c[:, 2 * g + 1:2 * g + 2]), inc=S)

        def emit_u2(g):
            nonlocal_last = None
            buf = g % 2
            u2w = [ts_H[g], tok_u2[g // 4]]
            if g == 0:
                u2w.append(prev_o_evac)
            for j in range(2):
                ff = 2 * g + j
                for m in range(2):
                    stt = first_q_bank[m]
                    first_q_bank[m] = False
                    nonlocal_last = P.op("tensor", lambda e, ff=ff, m=m, buf=buf, j=j,
                                         stt=stt, stp=(g == 15 and j == 1): e.matmul(
                        ps_k[m][:, :],
                        u2T[:, ff, 128 * m:128 * m + 128],
                        H_bf[:, 1024 * buf + 512 * j:1024 * buf + 512 * j + 512],
                        start=stt, stop=stp),
                        waits=u2w if (j == 0 and m == 0) else (),
                        inc=T if (g == 15 and j == 1 and m == 1) else None)
            return nonlocal_last

        tt_H_tok = [None] * 16
        tt_u2last = None
        for g in range(16):
            tt_H_tok[g] = emit_v1(g)
            emit_gelu(g)
            if g >= 1:
                tt_u2last = emit_u2(g - 1)
        tt_u2last = emit_u2(15)
        tt_Q = tt_u2last
        P.op("scalar", lambda e: e.copy(
            out=Q_bf[:, 0, :], in_=ps_k[0][:, :]), waits=[tt_Q], inc=S)
        ts_Q = P.op("scalar", lambda e: e.copy(
            out=Q_bf[:, 1, :], in_=ps_k[1][:, :]), inc=S)
        if p2stage <= 5:
            P.op("gpsimd", lambda e: e.nop(), waits=[ts_Q])
            with nc.Block() as block:
                P.emit(block)
            stack.close()
            return nc
        tv_res2 = [None] * 8
        for dch in range(8):
            vps = ps_a if dch % 2 == 0 else ps_s[3]
            vw = [ts_Q, tok_v2]
            if dch > 1:
                vw.append(tv_res2[dch - 2])
            else:
                vw.append(tv_P[1])
                vw.append(tv_res1[7])
            for r in range(2):
                tt_v2 = P.op("tensor", lambda e, r=r, dch=dch, vps=vps: e.matmul(
                    vps[:, :], v2T[:, r, 128 * dch:128 * dch + 128], Q_bf[:, r, :],
                    start=(r == 0), stop=(r == 1)),
                    waits=vw if r == 0 else (), inc=T if r == 1 else None)
            tv_res2[dch] = P.op("vector",
                                lambda e, dch=dch, xcur=xcur,
                                vps=vps: e.scalar_tensor_tensor(
                xcur[:, dch, :], vps[:, :], b2c[:, dch:dch + 1], xcur[:, dch, :],
                Alu.add, Alu.add), waits=[tt_v2], inc=V)
        tok_out[st] = P.dma("sync", out[st], xcur[:], waits=[tv_res2[7]],
                            inc=f"s_out{st % 2}")
        tok_xb2_free[st] = tok_out[st]
        tok_ctab2_free = tv_q[7]
        prev_psa_reader = ts_Q
        prev_tv_res2_last = tv_res2[7]
        prev_tsQ = ts_Q
        prev_tv_P1 = tv_P[1]
        tok_rows_used[st] = tt_rsb
        prev_negmb_user = tt_rsb

    P.op("sync", lambda e: e.nop(),
         waits=[("s_out0", P.count["s_out0"]), ("s_out1", P.count["s_out1"])])
    prev_kt_spill = prev_v_spill

    with nc.Block() as block:
        P.emit(block)
    stack.close()
    return nc


_NC_CACHE = None


def kernel(L0_attn_x, L0_attn_Wq, L0_attn_Wk, L0_attn_Wv, L0_attn_Wo,
           L0_ffn_U1, L0_ffn_V1, L0_ffn_b1, L0_ffn_U2, L0_ffn_V2, L0_ffn_b2):
    global _NC_CACHE
    x = np.asarray(L0_attn_x, np.float32)
    B = x.shape[0]
    perm = _perm_rows()
    wq_p = np.ascontiguousarray(np.asarray(L0_attn_Wq, np.float32)[perm, :])
    wo_p = np.ascontiguousarray(np.asarray(L0_attn_Wo, np.float32)[:, perm])
    wk = np.ascontiguousarray(np.asarray(L0_attn_Wk, np.float32))
    wv = np.ascontiguousarray(np.asarray(L0_attn_Wv, np.float32))
    u1 = np.ascontiguousarray(np.asarray(L0_ffn_U1, np.float32))
    v1 = np.ascontiguousarray(np.asarray(L0_ffn_V1, np.float32))
    u2 = np.ascontiguousarray(np.asarray(L0_ffn_U2, np.float32))
    v2 = np.ascontiguousarray(np.asarray(L0_ffn_V2, np.float32))
    b1c = np.ascontiguousarray(np.asarray(L0_ffn_b1, np.float32).reshape(32, 128).T)
    b2c = np.ascontiguousarray(np.asarray(L0_ffn_b2, np.float32).reshape(8, 128).T)

    consts = [_host_consts(0), _host_consts(1)]
    if _NC_CACHE is None:
        _NC_CACHE = build_nc()
    nc = _NC_CACHE

    in_maps = []
    for c in range(8):
        b, h = c // 2, c % 2
        xs = np.ascontiguousarray(x[b, h * S_CORE:(h + 1) * S_CORE, :].T)
        xtl = np.ascontiguousarray(
            xs.reshape(8, 128, N_ST, ST).transpose(2, 1, 0, 3))
        cc = consts[h]
        in_maps.append(dict(
            xt=xtl, wq=wq_p, wk=wk, wv=wv, wo=wo_p, u1=u1, v1=v1, u2=u2, v2=v2,
            b1c=b1c, b2c=b2c, mt=cc["mt"], identbf=cc["identbf"], mask=cc["mask"],
            qdec=cc["qdec"], kdec=cc["kdec"], bdb=cc["bdb"],
            costab=cc["costab"], sintab=cc["sintab"], rmask=cc["rmask"],
            onesf=cc["onesf"], onesb=cc["onesb"], onescol=cc["onescol"],
            epsc=cc["epsc"],
        ))

    res = run_bass_kernel_spmd(nc, in_maps, core_ids=list(range(8)))

    outp = np.empty((B, 8192, D), np.float32)
    for c in range(8):
        b, h = c // 2, c % 2
        arr = np.asarray(res.results[c]["out"])
        sl = arr.transpose(0, 3, 2, 1).reshape(S_CORE, D)
        outp[b, h * S_CORE:(h + 1) * S_CORE, :] = sl
    return outp


# version 16
# speedup vs baseline: 1.1279x; 1.0084x over previous
import math

import numpy as np
import ml_dtypes

import concourse.bass as bass
import concourse.mybir as mybir
from concourse.bass_utils import run_bass_kernel_spmd

F32 = mybir.dt.float32
BF16 = mybir.dt.bfloat16
AF = mybir.ActivationFunctionType
Alu = mybir.AluOpType

BF = ml_dtypes.bfloat16

D = 1024
HD = 64
NH = 16
NKV = 4
CH = 128
ST = 512
N_ST = 8
N_CH = 32
CPST = ST // CH
S_CORE = 4096
THETA = 10000.0
EPS = 1e-5

PAIRS = [(0, 4), (1, 5), (2, 6), (3, 7), (8, 12), (9, 13), (10, 14), (11, 15)]
HEAD_POS = {}
for _p, (_lo, _hi) in enumerate(PAIRS):
    HEAD_POS[_lo] = (_p, 0)
    HEAD_POS[_hi] = (_p, 1)
SLOPES = [2.0 ** (-8.0 / NH * (h + 1)) for h in range(NH)]


def _rope_tables():
    inv = 1.0 / THETA ** (np.arange(0, HD, 2, dtype=np.float64) / HD)
    freqs = np.outer(np.arange(8192, dtype=np.float64), inv)
    emb = np.concatenate([freqs, freqs], axis=-1)
    return np.cos(emb).astype(np.float32), np.sin(emb).astype(np.float32)


_COS64, _SIN64 = _rope_tables()


def _host_consts(half):
    c = {}
    M = np.zeros((128, 128), np.float32)
    for blk in range(2):
        b0 = 64 * blk
        for i in range(32):
            M[b0 + i, b0 + 32 + i] = -1.0
            M[b0 + 32 + i, b0 + i] = 1.0
    c["mt"] = np.ascontiguousarray(M.T).astype(BF)
    c["identbf"] = np.eye(128, dtype=np.float32).astype(BF)
    mask = np.zeros((128, 16 * 128), np.float32)
    t = np.arange(CH, dtype=np.float64)
    for p in range(8):
        for e in range(2):
            h = PAIRS[p][e]
            s = p + 8 * e
            m = np.where(t[:, None] >= t[None, :],
                         np.exp(-SLOPES[h] * (t[:, None] - t[None, :])), 0.0)
            mask[:, 128 * s:128 * s + 128] = m.T.astype(np.float32)
    c["mask"] = mask.astype(BF)
    qd = np.zeros((128, 8, 128), np.float32)
    for p in range(8):
        for e in range(2):
            h = PAIRS[p][e]
            qd[64 * e:64 * e + 64, p, :] = np.exp(-SLOPES[h] * (t + 1.0))[None, :]
    c["qdec"] = qd.astype(BF)
    kd = np.zeros((128, 8, 2, 64), np.float32)
    for p in range(8):
        for e in range(2):
            h = PAIRS[p][e]
            kd[:, p, e, :] = np.exp(-SLOPES[h] * (CH - 1.0 - t))[:, None]
    c["kdec"] = kd.reshape(128, NH * 64).astype(BF)
    bd = np.zeros((128, 8, 128), np.float32)
    for p in range(8):
        for e in range(2):
            h = PAIRS[p][e]
            bd[:, p, 64 * e:64 * e + 64] = math.exp(-SLOPES[h] * CH)
    c["bdf"] = bd.reshape(128, 1024).copy()
    c["bdb"] = c["bdf"].astype(BF)
    pos0 = half * S_CORE
    cos = np.zeros((N_ST, 128, ST), np.float32)
    sin = np.zeros((N_ST, 128, ST), np.float32)
    for st in range(N_ST):
        pp = np.arange(ST) + pos0 + st * ST
        cs = _COS64[pp, :].T
        sn = _SIN64[pp, :].T
        cos[st, 0:64] = cs
        cos[st, 64:128] = cs
        sin[st, 0:64] = sn
        sin[st, 64:128] = sn
    c["costab"] = cos
    c["sintab"] = sin
    rm = np.zeros((128, 2), np.float32)
    if half == 1:
        rm[:, 0] = 1.0
    c["rmask"] = rm
    c["onesf"] = np.ones((1, 128), np.float32)
    c["epsc"] = np.full((1, 1), EPS, np.float32)
    c["onesb"] = np.ones((1, 512), np.float32).astype(BF)
    c["onescol"] = np.ones((128, 1), np.float32).astype(BF)
    return c


def _perm_rows():
    perm = []
    for (lo, hi) in PAIRS:
        perm.extend(range(64 * lo, 64 * lo + 64))
        perm.extend(range(64 * hi, 64 * hi + 64))
    return np.array(perm, np.int64)


class Prog:
    def __init__(self, nc, sems):
        self.nc = nc
        self.ops = {e: [] for e in ("sync", "gpsimd", "tensor", "vector", "scalar")}
        self.count = {s: 0 for s in sems}
        self.sems = sems

    def op(self, eng, fn, waits=(), inc=None, inc_n=1):
        tok = None
        if inc is not None:
            self.count[inc] += inc_n
            tok = (inc, self.count[inc])
        waits = tuple(w for w in waits if w is not None)
        self.ops[eng].append((fn, waits, inc, inc_n))
        return tok

    def dma(self, eng, out, in_, waits=(), inc=None, **kw):
        return self.op(eng, lambda e: e.dma_start(out=out, in_=in_, **kw),
                       waits=waits, inc=inc, inc_n=16)

    def emit(self, block):
        def make(eng_name):
            ops = self.ops[eng_name]
            sems = self.sems

            def body(eng):
                wm = {}
                for fn, waits, inc, inc_n in ops:
                    for (sn, val) in waits:
                        if wm.get(sn, -1) >= val:
                            continue
                        eng.wait_ge(sems[sn], val)
                        wm[sn] = val
                    instr = fn(eng)
                    if inc is not None:
                        if instr is None:
                            instr = eng.nop()
                        instr.then_inc(sems[inc], inc_n)
            return body

        for name in ("sync", "gpsimd", "tensor", "vector", "scalar"):
            getattr(block, name)(make(name))


def build_nc(n_st1=N_ST, do_coll=True, n_st2=N_ST, p2stage=9):
    nc = bass.Bass("TRN2", target_bir_lowering=False)

    xt = nc.dram_tensor("xt", [N_ST, 128, 8, ST], F32, kind="ExternalInput").ap()
    out = nc.dram_tensor("out", [N_ST, 128, 8, ST], F32, kind="ExternalOutput").ap()
    wq = nc.dram_tensor("wq", [D, D], F32, kind="ExternalInput").ap()
    wk = nc.dram_tensor("wk", [256, D], F32, kind="ExternalInput").ap()
    wv = nc.dram_tensor("wv", [256, D], F32, kind="ExternalInput").ap()
    wo = nc.dram_tensor("wo", [D, D], F32, kind="ExternalInput").ap()
    u1 = nc.dram_tensor("u1", [D, 256], F32, kind="ExternalInput").ap()
    v1 = nc.dram_tensor("v1", [256, 4096], F32, kind="ExternalInput").ap()
    u2 = nc.dram_tensor("u2", [4096, 256], F32, kind="ExternalInput").ap()
    v2 = nc.dram_tensor("v2", [256, D], F32, kind="ExternalInput").ap()
    b1c_d = nc.dram_tensor("b1c", [128, 32], F32, kind="ExternalInput").ap()
    b2c_d = nc.dram_tensor("b2c", [128, 8], F32, kind="ExternalInput").ap()
    mt_d = nc.dram_tensor("mt", [128, 128], BF16, kind="ExternalInput").ap()
    identbf_d = nc.dram_tensor("identbf", [128, 128], BF16, kind="ExternalInput").ap()
    mask_d = nc.dram_tensor("mask", [128, 2048], BF16, kind="ExternalInput").ap()
    qdec_d = nc.dram_tensor("qdec", [128, 8, 128], BF16, kind="ExternalInput").ap()
    kdec_d = nc.dram_tensor("kdec", [128, 1024], BF16, kind="ExternalInput").ap()
    bdb_d = nc.dram_tensor("bdb", [128, 1024], BF16, kind="ExternalInput").ap()
    costab_d = nc.dram_tensor("costab", [N_ST, 128, ST], F32, kind="ExternalInput").ap()
    sintab_d = nc.dram_tensor("sintab", [N_ST, 128, ST], F32, kind="ExternalInput").ap()
    rmask_d = nc.dram_tensor("rmask", [128, 2], F32, kind="ExternalInput").ap()
    onesf_d = nc.dram_tensor("onesf", [1, 128], F32, kind="ExternalInput").ap()
    epsc_d = nc.dram_tensor("epsc", [1, 1], F32, kind="ExternalInput").ap()
    onesb_d = nc.dram_tensor("onesb", [1, 512], BF16, kind="ExternalInput").ap()
    onescol_d = nc.dram_tensor("onescol", [128, 1], BF16, kind="ExternalInput").ap()

    wqbf_h = nc.dram_tensor("wqbf_h", [D, D], BF16, kind="Internal").ap()
    wobf_h = nc.dram_tensor("wobf_h", [D, D], BF16, kind="Internal").ap()
    wkbf_h = nc.dram_tensor("wkbf_h", [256, D], BF16, kind="Internal").ap()
    wvbf_h = nc.dram_tensor("wvbf_h", [256, D], BF16, kind="Internal").ap()
    st_h = nc.dram_tensor("st_h", [N_CH, 128, 1024], BF16, kind="Internal").ap()
    kt_h = nc.dram_tensor("kt_h", [N_ST, 128, 2, ST], BF16, kind="Internal").ap()
    v_h = nc.dram_tensor("v_h", [N_ST, 128, CPST, 256], BF16, kind="Internal").ap()
    rows_h = nc.dram_tensor("rows_h", [N_ST, 2, 512], F32, kind="Internal").ap()
    src_h = nc.dram_tensor("src_h", [128, 1024], F32, kind="Internal").ap()
    dst_h = nc.dram_tensor("dst_h", [256, 1024], F32, kind="Internal").ap()

    import contextlib
    stack = contextlib.ExitStack()
    sb = lambda name, shape, dt: stack.enter_context(nc.sbuf_tensor(name, shape, dt))
    psb = lambda name, shape, dt: stack.enter_context(nc.psum_tensor(name, shape, dt))
    sem = lambda name: stack.enter_context(nc.semaphore(name))

    wqT = sb("wqT", [128, 8, 1024], BF16)
    woT = sb("woT", [128, 8, 1024], BF16)
    wkT = sb("wkT", [128, 8, 256], BF16)
    wvT = sb("wvT", [128, 8, 256], BF16)
    u1T = sb("u1T", [128, 8, 256], BF16)
    v1T = sb("v1T", [128, 2, 4096], BF16)
    u2T = sb("u2T", [128, 32, 256], BF16)
    v2T = sb("v2T", [128, 2, 1024], BF16)
    b1c = sb("b1c_s", [128, 32], F32)
    b2c = sb("b2c_s", [128, 8], F32)
    mt = sb("mt_s", [128, 128], BF16)
    identbf = sb("identbf_s", [128, 128], BF16)
    maskT = sb("maskT", [128, 2048], BF16)
    qdec = sb("qdec_s", [128, 8, 128], BF16)
    kdec = sb("kdec_s", [128, 1024], BF16)
    bdb = sb("bdb_s", [128, 1024], BF16)
    ctab = sb("ctab", [128, ST], F32)
    stab = sb("stab", [128, ST], F32)
    rmask = sb("rmask_s", [128, 2], F32)
    onesf = sb("onesf_s", [1, 128], F32)
    epsc = sb("epsc_s", [1, 1], F32)
    nc.const_aps.register(F32, EPS, epsc[0:1, 0:1])
    onesb = sb("onesb_s", [1, 512], BF16)
    onescol = sb("onescol_s", [128, 1], BF16)
    rsum_q = sb("rsum_q", [1, 1024], BF16)
    rsum_k = sb("rsum_k", [1, 256], BF16)
    rsum_v = sb("rsum_v", [1, 256], BF16)
    rsum_u1 = sb("rsum_u1", [1, 256], BF16)

    xb = [sb(f"xb{i}", [128, 8, ST], F32) for i in range(2)]
    x_bf = sb("x_bf", [128, 8, ST], BF16)
    rows_cur = sb("rows_cur", [1, 1, 2, 512], F32)
    negmb_row = sb("negmb_row", [1, 512], BF16)
    srow = sb("srow", [1, 2, 512], F32)
    rs2_row = sb("rs2_row", [1, 512], F32)
    negm2_row = sb("negm2_row", [1, 512], F32)
    negm2b_row = sb("negm2b_row", [1, 512], BF16)
    rs_col = sb("rs_col", [128, 4], F32)

    kraw_bf = sb("kraw_bf", [128, 2, 512], BF16)
    tmp1 = sb("tmp1", [128, 512], F32)
    tmp2 = sb("tmp2", [128, 512], F32)
    rs2_sb = tmp2
    kT_sb = sb("kT_sb", [128, 2, ST], BF16)
    kq_scr = sb("kq_scr", [128, 1024], BF16)
    ks_scr = sb("ks_scr", [128, 1024], BF16)
    ktok = kq_scr
    k16 = ks_scr
    st_true = ks_scr
    qdq3 = None
    v_bf = sb("v_bf", [128, CPST, 256], BF16)
    state_r = sb("state_r", [128, 2, 1024], BF16)
    fbf = sb("fbf", [128, 1024], BF16)
    rrun = sb("rrun", [128, 1024], BF16)
    st_ld = sb("st_ld", [128, 2, 1024], BF16)
    qp = sb("qp", [128, 8, ST], BF16)
    sh_bf = sb("sh_bf", [128, 2048], BF16)
    S_bf = sh_bf
    H_bf = sh_bf
    attn_bf = sb("attn_bf", [128, 8, 256], BF16)
    pq_bf = sb("pq_bf", [128, 2, 512], BF16)
    P_bf = pq_bf
    Q_bf = pq_bf

    ps_a = psb("ps_a", [128, 512], F32)
    ps_k = [psb("ps_k0", [128, 512], F32), psb("ps_k1", [128, 512], F32)]
    ps_s = [psb(f"ps_s{i}", [128, 512], F32) for i in range(4)]
    ps_wt = psb("ps_wt", [128, 1024], BF16)

    sems = {}
    for s in ("s_const", "s_wkT", "s_wvT", "s_wqT", "s_woT", "s_x0", "s_x1",
              "s_tabc", "s_tabs", "s_rows", "s_out0", "s_out1",
              "s_w1", "s_w2", "s_w3", "s_kt", "s_vv", "s_ktl", "s_vvl", "s_st0", "s_st1",
              "s_sp0", "s_sp1", "s_g", "s_q", "s_t", "s_v", "s_s", "s_c"):
        sems[s] = sem(s)
    P = Prog(nc, sems)

    T, V, S = "s_t", "s_v", "s_s"

    tok_x = [None] * N_ST
    tok_x[0] = P.dma("sync", xb[0][:], xt[0], inc="s_x0")
    tok_state0 = P.op("vector", lambda e: e.memset(state_r[:], 0.0), inc=V)
    tok_const = None
    for dst, src in ((mt, mt_d), (identbf, identbf_d), (maskT, mask_d),
                     (qdec, qdec_d), (kdec, kdec_d), (bdb, bdb_d),
                     (rmask, rmask_d), (onesf, onesf_d), (epsc, epsc_d),
                     (onesb, onesb_d),
                     (onescol, onescol_d), (b2c, b2c_d)):
        tok_const = P.dma("sync", dst[:], src[:, :] if len(src.shape) == 2 else src[:],
                          inc="s_const")

    P.dma("gpsimd", wkbf_h[:, :], wk[:, :], inc="s_w1")
    tok_wkv_h = P.dma("gpsimd", wvbf_h[:, :], wv[:, :], inc="s_w1")
    P.dma("gpsimd", u1T[:], u1.rearrange("(k p) r -> p k r", p=128), inc="s_w2")
    P.dma("gpsimd", v1T[:], v1.rearrange("(k p) f -> p k f", p=128), inc="s_w2")
    for g in range(4):
        P.dma("gpsimd", u2T[:, 8 * g:8 * (g + 1), :],
              u2.rearrange("(k p) r -> p k r", p=128)[:, 8 * g:8 * (g + 1), :],
              inc="s_w2")
    P.dma("gpsimd", v2T[:], v2.rearrange("(k p) d -> p k d", p=128), inc="s_w2")
    tok_w2 = P.dma("gpsimd", b1c[:], b1c_d[:, :], inc="s_w2")
    tok_u1 = tok_v1 = tok_v2 = tok_b1 = tok_w2
    tok_u2 = [tok_w2] * 4
    P.dma("gpsimd", wqbf_h[:, :], wq[:, :], inc="s_w3")
    tok_w3 = P.dma("gpsimd", wobf_h[:, :], wo[:, :], inc="s_w3")
    tok_wq_h = tok_wo_h = tok_w3

    tok_wkT = tok_wvT = tok_wqT = tok_woT = None
    for k in range(8):
        tok_wkT = P.op("sync", lambda e, k=k: e.dma_start_transpose(
            wkT[:, k, :], wkbf_h[:, 128 * k:128 * (k + 1)]),
            waits=[tok_wkv_h], inc="s_wkT", inc_n=16)
    for k in range(8):
        tok_wvT = P.op("sync", lambda e, k=k: e.dma_start_transpose(
            wvT[:, k, :], wvbf_h[:, 128 * k:128 * (k + 1)]),
            waits=[tok_wkv_h], inc="s_wvT", inc_n=16)

    def rowsum(dst_row, srcT, nsl, waits):
        prev_evac = None
        for (a, b) in nsl:
            w0 = list(waits) + ([prev_evac] if prev_evac else [])
            for k in range(8):
                tk = P.op("tensor", lambda e, k=k, a=a, b=b: e.matmul(
                    ps_a[0:1, 0:b - a], onescol[:, 0:1], srcT[:, k, a:b],
                    start=(k == 0), stop=(k == 7)),
                    waits=w0 if k == 0 else (), inc=T if k == 7 else None)
            prev_evac = P.op("scalar", lambda e, a=a, b=b: e.copy(
                out=dst_row[0:1, a:b], in_=ps_a[0:1, 0:b - a]),
                waits=[tk], inc=S)
        return prev_evac

    tok_rsum_k = rowsum(rsum_k, wkT, [(0, 256)], [tok_wkT, tok_const])
    tok_rsum_v = rowsum(rsum_v, wvT, [(0, 256)], [tok_wvT, tok_rsum_k])

    tok_xbf_done = [None] * (N_ST + 1)
    tok_xb_free = [None] * (N_ST + 1)
    prev_sp2 = None
    tok_rows_free = [None] * (N_ST + 1)
    prev_ts_nm = None
    prev_ts_rc = None
    prev_tv_sin = None
    prev_kt_spill = None
    prev_v_spill = None
    tok_ctab_free = None
    prev_state_tt = tok_state0
    prev_ts_v_last = tok_rsum_v
    prev_ts_tok = None

    def pre_sq(stv, sq_waits):
        xv1 = xb[stv % 2][:, 0:4, :].rearrange("p a b -> p (a b)")
        xv2 = xb[stv % 2][:, 4:8, :].rearrange("p a b -> p (a b)")
        P.op("gpsimd", lambda e, xv1=xv1: e.tensor_mul(
            attn_bf.rearrange("p a b -> p (a b)"), xv1, xv1),
            waits=sq_waits, inc="s_q")
        return P.op("gpsimd", lambda e, xv2=xv2: e.tensor_mul(
            sh_bf[:, :], xv2, xv2), inc="s_q")

    def pre_mm(mm_waits):
        for k in range(8):
            srcb = attn_bf if k < 4 else sh_bf
            off = 512 * (k % 4)
            tt_stat = P.op("tensor", lambda e, srcb=srcb, off=off, k=k: e.matmul(
                ps_s[0][0:1, :], onescol[:, 0:1],
                bass.AP(srcb, off, [[2048, 128], [1, 512]]),
                start=(k == 0), stop=(k == 7)),
                waits=mm_waits if k == 0 else (),
                inc=T if k == 7 else None)
        return tt_stat

    tsq_cur = pre_sq(0, [tok_x[0]])
    prev_tt_stat = pre_mm([tsq_cur, tok_const])
    prev_kraw_rd = [None, None]
    prev_addsA = None
    prev_addsB = None
    prev_ttu = [None, None]
    pre_front = [None] * (N_ST + 1)

    def emit_front(stv):
        xc = xb[stv % 2]
        c1 = P.op("scalar", lambda e, xc=xc: e.activation(
            out=x_bf[:, 0:4, :].rearrange("p a b -> p (a b)"),
            in_=xc[:, 0:4, :].rearrange("p a b -> p (a b)"), func=AF.Copy),
            waits=[tok_x[stv], tok_xbf_done[stv - 1] if stv >= 1 else None],
            inc=S)
        c2 = P.op("scalar", lambda e, xc=xc: e.activation(
            out=x_bf[:, 4:8, :].rearrange("p a b -> p (a b)"),
            in_=xc[:, 4:8, :].rearrange("p a b -> p (a b)"), func=AF.Copy),
            waits=[c1], inc=S)
        sw = [c1, prev_ts_v_last]
        for k in range(8):
            tsum = P.op("tensor", lambda e, k=k: e.matmul(
                ps_a[0:1, :], onescol[:, 0:1], x_bf[:, k, :],
                start=(k == 0), stop=(k == 7)),
                waits=sw if k == 0 else ([c2] if k == 4 else ()),
                inc=T if k == 7 else None)
        return c1, c2, tsum

    for st in range(n_st1):
        if st + 1 < n_st1:
            w = [tok_xb_free[st - 1]] if st >= 1 else []
            tok_x[st + 1] = P.dma("sync", xb[(st + 1) % 2][:], xt[st + 1],
                                  waits=w, inc=f"s_x{(st + 1) % 2}")
            tsq_next = pre_sq(st + 1, [tok_x[st + 1], prev_tt_stat])
        if st == 3:
            for k in range(8):
                tok_wqT = P.op("sync", lambda e, k=k: e.dma_start_transpose(
                    wqT[:, k, :], wqbf_h[:, 128 * k:128 * (k + 1)]),
                    waits=[tok_w3], inc="s_wqT", inc_n=16)
            for k in range(8):
                tok_woT = P.op("sync", lambda e, k=k: e.dma_start_transpose(
                    woT[:, k, :], wobf_h[:, 128 * k:128 * (k + 1)]),
                    waits=[tok_w3], inc="s_woT", inc_n=16)
        xcur = xb[st % 2]
        if pre_front[st] is None:
            pre_front[st] = emit_front(st)
        tv_xbf1, tv_xbf, tt_sum = pre_front[st]
        tt_stat = prev_tt_stat
        s_ring = 0
        ts1 = P.op("scalar", lambda e, s_ring=s_ring: e.activation(
            out=rows_cur[0:1, s_ring, 0, :], in_=ps_a[0:1, :], func=AF.Copy,
            scale=-1.0 / D),
            waits=[tt_sum, tok_rows_free[st - 1] if st >= 1 else None,
                   ("s_rows", 16 * st) if st >= 1 else None,
                   prev_ts_nm], inc=S)
        tv_m2 = P.op("scalar", lambda e, s_ring=s_ring: e.activation(
            out=srow[0:1, 1, :], in_=rows_cur[0:1, s_ring, 0, :], func=AF.Square),
            waits=[ts1], inc=S)
        tv_var = P.op("vector", lambda e: e.scalar_tensor_tensor(
            srow[0:1, 0, :], ps_s[0][0:1, :], 1.0 / D, srow[0:1, 1, :],
            Alu.mult, Alu.subtract),
            waits=[tt_stat, tv_m2], inc=V)
        ts3 = P.op("scalar", lambda e: e.activation(
            out=srow[0:1, 1, :], in_=srow[0:1, 0, :], func=AF.Ln, bias=EPS),
            waits=[tv_var, tok_const], inc=S)
        ts_rs = P.op("scalar", lambda e, s_ring=s_ring: e.activation(
            out=rows_cur[0:1, s_ring, 1, :], in_=srow[0:1, 1, :], func=AF.Exp,
            scale=-0.5), waits=[ts3], inc=S)
        ts_nm = P.op("scalar", lambda e, s_ring=s_ring: e.copy(
            out=negmb_row[0:1, :], in_=rows_cur[0:1, s_ring, 0, :]),
            waits=[ts1], inc=S)
        prev_ts_nm = ts_nm
        tok_rows_sp = P.dma("sync",
                            rows_h[st].rearrange("a b -> (a b)").unsqueeze(0),
                            rows_cur[0:1, s_ring].rearrange("p a b -> p (a b)"),
                            waits=[ts_rs], inc="s_rows")
        tt_rsb = P.op("tensor", lambda e, s_ring=s_ring: e.matmul(
            ps_s[3][:, :], onesf[0:1, 0:128], rows_cur[0:1, s_ring, 1, :],
            start=True, stop=True),
            waits=[ts_rs, prev_tv_sin, prev_addsB], inc=T)
        for i in range(4):
            tt_rsc = P.op("tensor", lambda e, i=i, s_ring=s_ring: e.matmul(
                ps_s[2][:, i:i + 1], rows_cur[0:1, s_ring, 1, 128 * i:128 * i + 128],
                onesf[0:1, 0:1], start=True, stop=True),
                waits=[prev_ts_rc, prev_addsB] if i == 0 else (),
                inc=T if i == 3 else None)
        tok_tab = P.dma("sync", ctab[:], costab_d[st],
                        waits=[tok_ctab_free], inc="s_tabc")
        tok_tab2 = P.dma("sync", stab[:], sintab_d[st],
                         waits=[tok_ctab_free], inc="s_tabs")
        tv_cos = P.op("vector", lambda e: e.tensor_mul(ctab[:], ctab[:], ps_s[3][:, :]),
                      waits=[tt_rsb, tok_tab, tok_tab2], inc=V)
        tv_sin = P.op("vector", lambda e: e.tensor_mul(stab[:], stab[:], ps_s[3][:, :]),
                      waits=[tt_rsb, tok_tab2], inc=V)
        prev_tv_sin = tv_sin
        ts_rc = P.op("scalar", lambda e: e.copy(out=rs_col[:], in_=ps_s[2][:, 0:4]),
                     waits=[tt_rsc], inc=S)
        prev_ts_rc = ts_rc
        tv_k = [None, None]
        ts_k_prev = None
        tv1_prev = None
        tv2_prev = None
        tt_rot_prev = None
        for r in range(2):
            kps = ps_a if r == 0 else ps_s[0]
            kw = [ts1 if r == 0 else tv_var, tv_xbf, tok_wkT]
            for k in range(8):
                P.op("tensor", lambda e, k=k, r=r, kps=kps: e.matmul(
                    kps[:, :], wkT[:, k, 128 * r:128 * r + 128], x_bf[:, k, :],
                    start=(k == 0), stop=False), waits=kw if k == 0 else ())
            tt_k = P.op("tensor", lambda e, r=r, kps=kps: e.matmul(
                kps[:, :], rsum_k[0:1, 128 * r:128 * r + 128], negmb_row[0:1, :],
                start=False, stop=True), waits=[ts_nm], inc=T)
            skw = [tt_k]
            if prev_kraw_rd[r] is not None:
                skw += prev_kraw_rd[r]
            ts_k = P.op("scalar", lambda e, r=r, kps=kps: e.copy(
                out=kraw_bf[:, r, :], in_=kps[:, :]), waits=skw, inc=S)
            tt_rot = P.op("tensor", lambda e, r=r: e.matmul(
                ps_s[1][:, :], mt[:, :], kraw_bf[:, r, :], start=True, stop=True),
                waits=[ts_k, tv2_prev], inc=T)
            tv1 = P.op("vector", lambda e, r=r: e.tensor_mul(
                tmp1[:], kraw_bf[:, r, :], ctab[:]),
                waits=[ts_k, tv_cos, tv_k[0] if r == 1 else None], inc=V)
            tv2 = P.op("vector", lambda e: e.tensor_mul(tmp2[:], ps_s[1][:, :], stab[:]),
                       waits=[tt_rot, tv_sin], inc=V)
            tv_k[r] = P.op("vector", lambda e, r=r: e.tensor_add(
                kT_sb[:, r, :], tmp1[:], tmp2[:]),
                waits=[tv1, tv2, prev_kt_spill], inc=V)
            prev_kraw_rd[r] = [tv1, tt_rot]
            ts_k_prev, tv1_prev, tv2_prev, tt_rot_prev = ts_k, tv1, tv2, tt_rot
        prev_kt_spill = P.dma("gpsimd", kt_h[st], kT_sb[:],
                              waits=[tv_k[0], tv_k[1]], inc="s_kt")
        trw = [tv_k[0], tv_k[1], tok_const]
        if prev_ts_tok is not None:
            trw.append(prev_ts_tok)
        first_tr = True
        for r in range(2):
            for i in range(4):
                tt_tr = P.op("tensor", lambda e, r=r, i=i: e.transpose(
                    ps_wt[:, 256 * i + 128 * r: 256 * i + 128 * r + 128],
                    kT_sb[:, r, 128 * i:128 * i + 128], identbf[:, :]),
                    waits=trw if first_tr else (),
                    inc=T if (r == 1 and i == 3) else None)
                first_tr = False
        prev_ts_tok = P.op("scalar", lambda e: e.copy(
            out=ktok[:, :], in_=ps_wt[:, :]),
            waits=[tt_tr], inc=S)
        ts_tok = prev_ts_tok
        ts_v = [None] * CPST
        for i in range(CPST):
            vps = ps_a if i % 2 == 0 else ps_s[1]
            vw = [tok_wvT]
            if i > 1:
                vw.append(ts_v[i - 2])
            elif i == 0:
                vw.append(ts_k_prev)
            else:
                vw.append(tv2_prev)
            for k in range(8):
                P.op("tensor", lambda e, k=k, i=i, vps=vps: e.matmul(
                    vps[:, 0:256], x_bf[:, k, 128 * i:128 * i + 128], wvT[:, k, :],
                    start=(k == 0), stop=False), waits=vw if k == 0 else ())
            tt_v = P.op("tensor", lambda e, i=i, vps=vps: e.matmul(
                vps[:, 0:256], negmb_row[0:1, 128 * i:128 * i + 128],
                rsum_v[0:1, :], start=False, stop=True), inc=T)
            ts_v[i] = P.op("scalar", lambda e, i=i, vps=vps: e.activation(
                out=v_bf[:, i, :], in_=vps[:, 0:256], func=AF.Copy,
                scale=rs_col[:, i:i + 1]),
                waits=[tt_v, ts_rc, prev_v_spill], inc=S)
        prev_v_spill = P.dma("gpsimd", v_h[st], v_bf[:], waits=[ts_v[CPST - 1]],
                             inc="s_vv")
        tok_xbf_done[st] = tt_v
        prev_ts_v_last = ts_v[CPST - 1]
        if st + 1 < n_st1:
            prev_tt_stat = pre_mm([tsq_next, tv_var, ts_k_prev])
        if st == 0:
            P.op("vector", lambda e: e.memset(ps_k[0][:, :], 0.0),
                 waits=[prev_state_tt], inc=V)
            tv_z = P.op("vector", lambda e: e.memset(ps_k[1][:, :], 0.0), inc=V)
        def emit_k16(ci):
            kb = k16 if ci % 2 == 0 else pq_bf
            return P.op("vector", lambda e, ci=ci, kb=kb: e.tensor_tensor(
                bass.AP(kb, 0, [[1024, 128], [512, 2], [128, 4], [64, 2], [1, 64]]),
                bass.AP(ktok, 256 * ci,
                        [[1024, 128], [128, 2], [0, 4], [64, 2], [1, 64]]),
                bass.AP(kdec, 0,
                        [[1024, 128], [512, 2], [128, 4], [64, 2], [1, 64]]),
                Alu.mult),
                waits=[ts_tok, prev_ttu[ci % 2]], inc=V)

        k16tok = [None] * CPST
        k16tok[0] = emit_k16(0)
        k16tok[1] = emit_k16(1)
        for ci in range(CPST):
            c = CPST * st + ci
            pair = [ps_k[0], ps_k[1]] if ci % 2 == 0 else [ps_s[2], ps_s[3]]
            kb = k16 if ci % 2 == 0 else pq_bf
            tok_sp = P.dma("sync", st_h[c], state_r[:, c % 2, :],
                           waits=[prev_state_tt], inc=f"s_sp{c % 2}")
            if ci % 2 == 0:
                bank_free = [prev_addsA] if prev_addsA is not None else [tv_z]
            else:
                bank_free = ([prev_addsB] if prev_addsB is not None else []) + \
                            [ts_rc, tv_cos, tv_sin]
            uw = [k16tok[ci], ts_v[ci]] + bank_free
            for p in range(8):
                tt_u = P.op("tensor", lambda e, p=p, ci=ci, pair=pair, kb=kb: e.matmul(
                    pair[p // 4][:, 128 * (p % 4):128 * (p % 4) + 128],
                    kb[:, p // 4, 128 * (p % 4):128 * (p % 4) + 128]
                    if ci % 2 == 1 else kb[:, 128 * p:128 * p + 128],
                    v_bf[:, ci, 128 * (p // 4):128 * (p // 4) + 128],
                    start=(p % 4 == 0), stop=(p % 4 == 3)),
                    waits=uw if p == 0 else (), inc=T if p == 7 else None)
            prev_ttu[ci % 2] = tt_u
            if ci + 2 < CPST:
                k16tok[ci + 2] = emit_k16(ci + 2)
            if ci == 1 and st + 1 < n_st1:
                pre_front[st + 1] = emit_front(st + 1)
            tvm = P.op("vector", lambda e, c=c: e.tensor_mul(
                fbf[:], state_r[:, c % 2, :], bdb[:]),
                waits=[prev_state_tt], inc=V)
            ns = (c + 1) % 2
            prev_sp = tok_sp
            for g in range(2):
                for e_par in range(2):
                    sap = bass.AP(state_r, 2048 * 64 * e_par + 1024 * ns
                                  + 512 * g + 64 * e_par,
                                  [[2048, 64], [128, 4], [1, 64]])
                    fap = bass.AP(fbf, 64 * 1024 * e_par + 512 * g + 64 * e_par,
                                  [[1024, 64], [128, 4], [1, 64]])
                    pap = bass.AP(pair[g], 64 * 512 * e_par + 64 * e_par,
                                  [[512, 64], [128, 4], [1, 64]])
                    tok_last = P.op("vector",
                                    lambda e, sap=sap, fap=fap, pap=pap: e.tensor_add(
                        sap, fap, pap),
                        waits=[tt_u, tvm, prev_sp2] if (g == 0 and e_par == 0) else (),
                        inc=V)
            prev_sp2 = prev_sp
            prev_state_tt = tok_last
            if ci % 2 == 0:
                prev_addsA = tok_last
            else:
                prev_addsB = tok_last
        tok_ctab_free = tv_k[1]
        tok_xb_free[st] = tt_v
        tok_rows_free[st] = tt_rsc

    if not do_coll:
        P.op("gpsimd", lambda e: e.nop(),
             waits=[("s_sp0", P.count["s_sp0"]), ("s_sp1", P.count["s_sp1"]),
                    prev_state_tt])
        with nc.Block() as block:
            P.emit(block)
        stack.close()
        return nc
    tok_src = P.dma("gpsimd", src_h[:, :], state_r[:, N_CH % 2, :],
                    waits=[prev_state_tt], inc="s_g")
    P.op("gpsimd", lambda e: e.collective_compute(
        "AllGather", Alu.bypass,
        ins=[src_h[:, :]], outs=[dst_h[:, :]],
        replica_groups=[[0, 1], [2, 3], [4, 5], [6, 7]]),
        waits=[tok_src], inc="s_c")
    tok_coll = ("s_c", P.count["s_c"])
    tok_g0 = P.dma("gpsimd", fbf[:], dst_h[0:128, :], waits=[tok_coll], inc="s_g")
    tv_rrun = None

    tok_rsum_q = rowsum(rsum_q, wqT, [(0, 512), (512, 1024)],
                        [tok_wqT, prev_ts_v_last])
    tok_rsum_u1 = rowsum(rsum_u1, u1T, [(0, 256)], [tok_u1, tok_rsum_q])

    tok_x2 = [None] * N_ST
    tok_x2[0] = P.dma("sync", xb[0][:], xt[0], waits=[tok_xb_free[N_ST - 1]],
                      inc="s_x0")
    tok_kt_ld = [None] * N_ST
    tok_v_ld = [None] * N_ST
    tok_st_ld = [None] * N_ST
    tok_kt_ld[0] = P.dma("sync", kT_sb[:], kt_h[0],
                         waits=[("s_kt", 16 * N_ST), prev_state_tt], inc="s_ktl")
    tok_v_ld[0] = P.dma("sync", v_bf[:], v_h[0],
                        waits=[("s_vv", 16 * N_ST), prev_state_tt], inc="s_vvl")
    tok_rows_ld = [None] * N_ST
    tok_rows_ld[0] = P.dma("sync",
                           rows_cur[0:1, 0].rearrange("p a b -> p (a b)"),
                           rows_h[0].rearrange("a b -> (a b)").unsqueeze(0),
                           waits=[tok_rows_free[N_ST - 1], ("s_rows", 16 * N_ST)],
                           inc="s_rows")
    tok_stch = [None] * (N_CH + 2)
    all_sp = [("s_sp0", P.count["s_sp0"]), ("s_sp1", P.count["s_sp1"])]
    tok_stch[0] = P.dma("sync", st_ld[:, 0, :], st_h[0],
                        waits=all_sp, inc="s_st0")
    tok_stch[1] = P.dma("sync", st_ld[:, 1, :], st_h[1],
                        waits=all_sp, inc="s_st1")
    prev_tv_st = [None] * (N_CH + 2)

    tok_out = [None] * N_ST
    tv2_prev = [None]
    tv1_prev2 = [None, None]
    prev_tv_P1 = prev_tv_sin
    prev_S_evac_g = [None, None]
    prev_tt_o = None
    prev_ts_ot1 = None
    tok_rows_used = [None] * (N_ST + 1)
    prev_ts_attn = prev_ts_tok
    prev_negmb_user = None
    prev_tv_xbf2 = None
    prev_tv_rr = None
    tok_qp_free = None
    prev_o_evac = None
    prev_S_evac = None
    tok_xb2_free = [None] * (N_ST + 1)
    tok_ctab2_free = tok_ctab_free
    prev_psa_reader = tok_rsum_u1
    prev_tv_res2_last = None
    prev_tt_P = None
    prev_tsQ = None
    prev_ts_ot0 = None

    if n_st2 == 0:
        P.op("gpsimd", lambda e: e.nop(), waits=[tok_coll])
        with nc.Block() as block:
            P.emit(block)
        stack.close()
        return nc
    for st in range(n_st2):
        if st + 1 < n_st2:
            w = [tok_xb2_free[st - 1]] if st >= 1 else []
            tok_x2[st + 1] = P.dma("sync", xb[(st + 1) % 2][:], xt[st + 1], waits=w,
                                   inc=f"s_x{(st + 1) % 2}")
        xcur = xb[st % 2]
        s_ring = 0
        tv_xbf = P.op("vector", lambda e, xcur=xcur: e.tensor_copy(x_bf[:], xcur[:]),
                      waits=[tok_x2[st], prev_tt_P, prev_tv_xbf2], inc=V)
        ts_nm = P.op("scalar", lambda e, s_ring=s_ring: e.copy(
            out=negmb_row[0:1, :], in_=rows_cur[0:1, s_ring, 0, :]),
            waits=[tok_rows_ld[st], prev_negmb_user], inc=S)
        tt_rsb = P.op("tensor", lambda e, s_ring=s_ring: e.matmul(
            ps_a[:, :], onesf[0:1, 0:128], rows_cur[0:1, s_ring, 1, :],
            start=True, stop=True),
            waits=[prev_psa_reader, prev_tv_res2_last, tok_rows_ld[st]], inc=T)
        if st == 0:
            tok_tab_n = P.dma("sync", ctab[:], costab_d[0], waits=[tok_ctab2_free],
                              inc="s_tabc")
            tok_tab2_n = P.dma("sync", stab[:], sintab_d[0], waits=[tok_ctab2_free],
                               inc="s_tabs")
        tok_tab, tok_tab2 = tok_tab_n, tok_tab2_n
        tv_cos = P.op("vector", lambda e: e.tensor_mul(ctab[:], ctab[:], ps_a[:, :]),
                      waits=[tt_rsb, tok_tab, tok_tab2], inc=V)
        tv_sin = P.op("vector", lambda e: e.tensor_mul(stab[:], stab[:], ps_a[:, :]),
                      waits=[tt_rsb, tok_tab2], inc=V)
        tv_q = [None] * 8
        ts_q_prev = [None, None]
        pend = []
        def emit_rope(p_, ts_q_, rw_extra=()):
            rw = [ts_q_, tv2_prev[0]] + list(rw_extra)
            if prev_tsQ is not None:
                rw.append(prev_tsQ)
            tt_rot = P.op("tensor", lambda e: e.matmul(
                ps_k[0][:, :], mt[:, :], kraw_bf[:, p_ % 2, :], start=True, stop=True),
                waits=rw, inc=T)
            tv1 = P.op("vector", lambda e, p_=p_: e.tensor_mul(
                tmp1[:], kraw_bf[:, p_ % 2, :], ctab[:]),
                waits=[ts_q_, tv_cos, tv_q[p_ - 1] if p_ > 0 else None], inc=V)
            tv2 = P.op("vector", lambda e: e.tensor_mul(tmp2[:], ps_k[0][:, :], stab[:]),
                       waits=[tt_rot, tv_sin], inc=V)
            tv_q[p_] = P.op("vector", lambda e, p_=p_: e.tensor_add(
                qp[:, p_, :], tmp1[:], tmp2[:]), waits=[tv1, tv2], inc=V)
            tv2_prev[0] = tv2
        for p in range(8):
            qps = ps_a if p % 2 == 0 else ps_s[3]
            qw = [tv_cos, tv_sin, tv_xbf, tok_wqT, tok_rsum_q, ts_nm]
            if p == 0 and tok_qp_free is not None:
                qw.append(tok_qp_free)
            if p == 1:
                qw.append(prev_tv_P1)
            if ts_q_prev[p % 2] is not None:
                qw.append(ts_q_prev[p % 2])
            for k in range(8):
                P.op("tensor", lambda e, k=k, p=p, qps=qps: e.matmul(
                    qps[:, :], wqT[:, k, 128 * p:128 * p + 128], x_bf[:, k, :],
                    start=(k == 0), stop=False), waits=qw if k == 0 else ())
            tt_q = P.op("tensor", lambda e, p=p, qps=qps: e.matmul(
                qps[:, :], rsum_q[0:1, 128 * p:128 * p + 128], negmb_row[0:1, :],
                start=False, stop=True), inc=T)
            skw = [tt_q]
            if p >= 2:
                skw.append(tv1_prev2[p % 2])
            ts_q = P.op("scalar", lambda e, p=p, qps=qps: e.copy(
                out=kraw_bf[:, p % 2, :], in_=qps[:, :]), waits=skw, inc=S)
            ts_q_prev[p % 2] = ts_q
            pend.append((p, ts_q))
            if len(pend) >= 2:
                p_, t_ = pend.pop(0)
                emit_rope(p_, t_)
                tv1_prev2[p_ % 2] = tv_q[p_]
        while pend:
            p_, t_ = pend.pop(0)
            emit_rope(p_, t_)
            tv1_prev2[p_ % 2] = tv_q[p_]
        if st == 0:
            tv_sel0 = P.op("vector", lambda e: e.tensor_scalar_mul(
                rrun[:], fbf[:], rmask[:, 0:1]),
                waits=[tok_g0, tok_const, prev_state_tt], inc=V)
            tok_g1 = P.dma("gpsimd", fbf[:], dst_h[128:256, :], waits=[tv_sel0],
                           inc="s_g")
            tv_rrun = P.op("vector", lambda e: e.scalar_tensor_tensor(
                rrun[:], fbf[:], rmask[:, 1:2], rrun[:], Alu.mult, Alu.add),
                waits=[tok_g1], inc=V)
        if st + 1 < n_st2:
            tok_rows_ld[st + 1] = P.dma(
                "sync", rows_cur[0:1, 0].rearrange("p a b -> p (a b)"),
                rows_h[st + 1].rearrange("a b -> (a b)").unsqueeze(0),
                waits=[tt_rsb, ts_nm], inc="s_rows")
            tok_tab_n = P.dma("sync", ctab[:], costab_d[st + 1],
                              waits=[tv_q[7]], inc="s_tabc")
            tok_tab2_n = P.dma("sync", stab[:], sintab_d[st + 1],
                               waits=[tv_q[7]], inc="s_tabs")
        tv_res1 = [None] * 8
        for ci in range(CPST):
            c = CPST * st + ci
            dqw = [tv_q[7]]
            if prev_tt_o is not None:
                dqw.append(prev_tt_o)
            tv_qdq = P.op("vector", lambda e, ci=ci: e.tensor_mul(
                kq_scr[:].rearrange("p (a b) -> p a b", a=8),
                qp[:, :, 128 * ci:128 * ci + 128], qdec[:]),
                waits=dqw, inc=V)
            tv_st = P.op("vector", lambda e, c=c: e.tensor_add(
                st_true[:], st_ld[:, c % 2, :], rrun[:]),
                waits=[tv_rrun, tok_stch[c], prev_tv_rr], inc=V)
            prev_tv_st[c] = tv_st
            if c + 2 <= N_CH - 1:
                tok_stch[c + 2] = P.dma("sync", st_ld[:, c % 2, :], st_h[c + 2],
                                        waits=[tv_st], inc=f"s_st{c % 2}")
            prev_tv_rr = P.op("vector", lambda e: e.tensor_mul(
                rrun[:], rrun[:], bdb[:]), waits=[tv_st], inc=V)
            tv_mask = [None, None]
            for e_par in range(2):
                sw0 = [tv_q[7], tok_kt_ld[st], prev_S_evac_g[e_par]]
                if e_par == 1:
                    sw0.append(prev_tv_P1)
                    if ci >= 2:
                        sw0.append(tv_res1[7])
                if prev_tsQ is not None:
                    sw0.append(prev_tsQ)
                first_s = True
                for p in range(8):
                    h = PAIRS[p][e_par]
                    kv = h // 4
                    s = p + 8 * e_par
                    tt_sc = P.op("tensor", lambda e, p=p, e_par=e_par, kv=kv, s=s, ci=ci: e.matmul(
                        ps_s[s // 4][:, 128 * (s % 4):128 * (s % 4) + 128],
                        kT_sb[64 * (kv % 2):64 * (kv % 2) + 64, kv // 2,
                              128 * ci:128 * ci + 128],
                        qp[64 * e_par:64 * e_par + 64, p, 128 * ci:128 * ci + 128],
                        start=True, stop=True),
                        waits=sw0 if first_s else (), inc=T if p == 7 else None)
                    first_s = False
                P.op("vector", lambda e, e_par=e_par: e.tensor_mul(
                    S_bf[:, 1024 * e_par:1024 * e_par + 512], ps_s[2 * e_par][:, :],
                    maskT[:, 1024 * e_par:1024 * e_par + 512]), waits=[tt_sc], inc=V)
                tv_mask[e_par] = P.op("vector", lambda e, e_par=e_par: e.tensor_mul(
                    S_bf[:, 1024 * e_par + 512:1024 * (e_par + 1)],
                    ps_s[2 * e_par + 1][:, :],
                    maskT[:, 1024 * e_par + 512:1024 * (e_par + 1)]), inc=V)
                if e_par == 0:
                    o2w = [tv_qdq, tv_st]
                    if prev_ts_ot0 is not None:
                        o2w.append(prev_ts_ot0)
                    for p in range(8):
                        pw2 = o2w if p == 0 else (
                            [prev_ts_ot1] if (p == 4 and prev_ts_ot1 is not None)
                            else ())
                        P.op("tensor", lambda e, p=p: e.matmul(
                            ps_k[p // 4][:, 128 * (p % 4):128 * (p % 4) + 128],
                            kq_scr[:, 128 * p:128 * p + 128],
                            st_true[:, 128 * p:128 * p + 128],
                            start=(p % 4 == 0), stop=False),
                            waits=pw2)
            prev_S_evac = tv_mask[1]
            prev_S_evac_g = [tv_mask[0], tv_mask[1]]
            tt_o = None
            for e_par in range(2):
                ow = [tv_mask[e_par], tok_v_ld[st]]
                first_o = True
                for p in range(8):
                    h = PAIRS[p][e_par]
                    kv = h // 4
                    s = p + 8 * e_par
                    tt_o = P.op("tensor", lambda e, p=p, e_par=e_par, kv=kv, s=s, ci=ci: e.matmul(
                        ps_k[p // 4][:, 128 * (p % 4) + 64 * e_par:
                                     128 * (p % 4) + 64 * e_par + 64],
                        S_bf[:, 128 * s:128 * s + 128],
                        v_bf[:, ci, 64 * kv:64 * kv + 64],
                        start=False, stop=(e_par == 1 and p % 4 == 3)),
                        waits=ow if first_o else (),
                        inc=T if (e_par == 1 and p == 7) else None)
                    first_o = False
            ts_ot0 = P.op("scalar", lambda e: e.copy(
                out=S_bf[:, 0:512], in_=ps_k[0][:, :]),
                waits=[tt_o], inc=S)
            ts_ot1 = P.op("scalar", lambda e: e.copy(
                out=S_bf[:, 512:1024], in_=ps_k[1][:, :]), inc=S)
            prev_ts_ot0 = ts_ot0
            prev_ts_ot1 = ts_ot1
            prev_tt_o = tt_o
            for j in range(8):
                tt_tra = P.op("tensor", lambda e, j=j: e.transpose(
                    ps_wt[:, 128 * j:128 * j + 128],
                    S_bf[:, 128 * j:128 * j + 128], identbf[:, :]),
                    waits=([ts_ot0, ts_ot1, prev_ts_attn] if j == 0 else ()),
                    inc=T if j == 7 else None)
            prev_o_evac = P.op("scalar", lambda e, ci=ci: e.copy(
                out=attn_bf[:, :, 128 * (ci % 2):128 * (ci % 2) + 128],
                in_=ps_wt[:].rearrange("p (a b) -> p a b", a=8)),
                waits=[tt_tra], inc=S)
            prev_ts_attn = prev_o_evac
            if ci % 2 == 1:
                half = ci // 2
                for dch in range(8):
                    yps = ps_a if dch % 2 == 0 else ps_s[3]
                    ww = [prev_o_evac, tok_woT]
                    if half == 0:
                        if dch > 1:
                            ww.append(tv_res1[dch - 2])
                        elif dch == 0:
                            ww.append(tv_sin)
                        else:
                            ww.append(prev_S_evac)
                    else:
                        ww.append(prev_S_evac)
                        if dch > 1:
                            ww.append(tv_res1[dch - 2])
                        else:
                            ww.append(tv_res1[7])
                    for p in range(8):
                        tt_y = P.op("tensor", lambda e, p=p, dch=dch, yps=yps: e.matmul(
                            yps[:, 0:256], woT[:, p, 128 * dch:128 * dch + 128],
                            attn_bf[:, p, :],
                            start=(p == 0), stop=(p == 7)),
                            waits=ww if p == 0 else (), inc=T if p == 7 else None)
                    tv_res1[dch] = P.op("vector",
                                        lambda e, dch=dch, half=half, xcur=xcur,
                                        yps=yps: e.tensor_add(
                        xcur[:, dch, 256 * half:256 * half + 256],
                        xcur[:, dch, 256 * half:256 * half + 256],
                        yps[:, 0:256]), waits=[tt_y], inc=V)
        if st + 1 < N_ST:
            tok_kt_ld[st + 1] = P.dma("sync", kT_sb[:], kt_h[st + 1],
                                      waits=[prev_o_evac], inc="s_ktl")
            tok_v_ld[st + 1] = P.dma("sync", v_bf[:], v_h[st + 1],
                                     waits=[prev_o_evac], inc="s_vvl")
        if p2stage <= 3:
            P.op("gpsimd", lambda e: e.nop(), waits=[tv_res1[7]])
            with nc.Block() as block:
                P.emit(block)
            stack.close()
            return nc
        tv_c2 = [None, None]
        for h2 in range(2):
            tv_c2[h2] = P.op("vector", lambda e, h2=h2, xcur=xcur: e.tensor_copy(
                x_bf[:, 4 * h2:4 * h2 + 4, :].rearrange("p a b -> p (a b)"),
                xcur[:, 4 * h2:4 * h2 + 4, :].rearrange("p a b -> p (a b)")),
                waits=[tv_res1[4 * h2 + 3]], inc=V)
        gq = [None] * 8
        for k in range(8):
            gq[k] = P.op("scalar", lambda e, k=k, xcur=xcur: e.activation(
                out=qp[:, k, :], in_=xcur[:, k, :], func=AF.Square),
                waits=[tv_res1[k], tt_sc] if k == 0 else [tv_res1[k]], inc=S)
        tv_xbf2 = tv_c2[1]
        prev_tv_xbf2 = tv_xbf2
        for k in range(8):
            P.op("tensor", lambda e, k=k: e.matmul(
                ps_a[0:1, :], onescol[:, 0:1], x_bf[:, k, :],
                start=(k == 0), stop=(k == 7)),
                waits=[tv_c2[k // 4]] if k % 4 == 0 else ())
        for k in range(8):
            tt_stat = P.op("tensor", lambda e, k=k: e.matmul(
                ps_k[0][0:1, :], onescol[:, 0:1], qp[:, k, :],
                start=(k == 0), stop=(k == 7)),
                waits=[gq[k]],
                inc=T if k == 7 else None)
        tok_qp_free = tt_stat
        ts1 = P.op("scalar", lambda e: e.activation(
            out=negm2_row[0:1, :], in_=ps_a[0:1, :], func=AF.Copy, scale=-1.0 / D),
            waits=[tt_stat], inc=S)
        tv_m2 = P.op("scalar", lambda e: e.activation(
            out=srow[0:1, 1, :], in_=negm2_row[0:1, :], func=AF.Square),
            waits=[ts1], inc=S)
        tv_var = P.op("vector", lambda e: e.scalar_tensor_tensor(
            srow[0:1, 0, :], ps_k[0][0:1, :], 1.0 / D, srow[0:1, 1, :],
            Alu.mult, Alu.subtract),
            waits=[tt_stat, tv_m2], inc=V)
        ts3 = P.op("scalar", lambda e: e.activation(
            out=srow[0:1, 1, :], in_=srow[0:1, 0, :], func=AF.Ln, bias=EPS),
            waits=[tv_var, tok_const], inc=S)
        ts_rs2 = P.op("scalar", lambda e: e.activation(
            out=rs2_row[0:1, :], in_=srow[0:1, 1, :], func=AF.Exp, scale=-0.5),
            waits=[ts3], inc=S)
        ts_nm2 = P.op("scalar", lambda e: e.copy(
            out=negm2b_row[0:1, :], in_=negm2_row[0:1, :]), waits=[ts1], inc=S)
        tt_rsb2 = P.op("tensor", lambda e: e.matmul(
            ps_s[2][:, :], onesf[0:1, 0:128], rs2_row[0:1, :], start=True, stop=True),
            waits=[ts_rs2, prev_S_evac], inc=T)
        ts_rs2sb = P.op("scalar", lambda e: e.copy(out=rs2_sb[:], in_=ps_s[2][:, :]),
                        waits=[tt_rsb2, tv_q[7]], inc=S)
        if p2stage <= 4:
            P.op("gpsimd", lambda e: e.nop(), waits=[ts_rs2sb])
            with nc.Block() as block:
                P.emit(block)
            stack.close()
            return nc
        tv_P = [None, None]
        tt_P = [None, None]
        for r in range(2):
            pps = ps_a if r == 0 else ps_s[3]
            pw = [tv_xbf2, tok_rsum_u1]
            if r == 0:
                pw.append(tv_res1[6])
            else:
                pw.append(tv_res1[7])
            for k in range(8):
                P.op("tensor", lambda e, k=k, r=r, pps=pps: e.matmul(
                    pps[:, :], u1T[:, k, 128 * r:128 * r + 128], x_bf[:, k, :],
                    start=(k == 0), stop=False), waits=pw if k == 0 else ())
        for r in range(2):
            pps = ps_a if r == 0 else ps_s[3]
            tt_P[r] = P.op("tensor", lambda e, r=r, pps=pps: e.matmul(
                pps[:, :], rsum_u1[0:1, 128 * r:128 * r + 128], negm2b_row[0:1, :],
                start=False, stop=True), waits=[ts_nm2] if r == 0 else (), inc=T)
            tv_P[r] = P.op("vector", lambda e, r=r, pps=pps: e.tensor_mul(
                P_bf[:, r, :], pps[:, :], rs2_sb[:]),
                waits=[tt_P[r], ts_rs2sb], inc=V)
        prev_tt_P = tt_P[1]
        ts_H = [None] * 16
        first_q_bank = [True, True]

        def emit_v1(g):
            buf = g % 2
            gw = [tv_P[1], tok_v1]
            if g >= 2:
                gw.append(ts_H[g - 2])
            elif g < 2:
                gw.append(prev_S_evac)
            for j in range(2):
                ff = 2 * g + j
                for r in range(2):
                    tt = P.op("tensor", lambda e, ff=ff, r=r, buf=buf, j=j: e.matmul(
                        ps_s[2 * buf + j][:, :],
                        v1T[:, r, 128 * ff:128 * ff + 128], P_bf[:, r, :],
                        start=(r == 0), stop=(r == 1)),
                        waits=gw if (j == 0 and r == 0) else (),
                        inc=T if (j == 1 and r == 1) else None)
            return tt

        def emit_gelu(g):
            buf = g % 2
            P.op("scalar", lambda e, g=g, buf=buf: e.activation(
                out=H_bf[:, 1024 * buf:1024 * buf + 512],
                in_=ps_s[2 * buf][:, :],
                func=AF.Gelu_apprx_tanh, bias=b1c[:, 2 * g:2 * g + 1]),
                waits=[tt_H_tok[g], tok_b1], inc=S)
            ts_H[g] = P.op("scalar", lambda e, g=g, buf=buf: e.activation(
                out=H_bf[:, 1024 * buf + 512:1024 * (buf + 1)],
                in_=ps_s[2 * buf + 1][:, :],
                func=AF.Gelu_apprx_tanh, bias=b1c[:, 2 * g + 1:2 * g + 2]), inc=S)

        def emit_u2(g):
            nonlocal_last = None
            buf = g % 2
            u2w = [ts_H[g], tok_u2[g // 4]]
            if g == 0:
                u2w.append(prev_o_evac)
            for j in range(2):
                ff = 2 * g + j
                for m in range(2):
                    stt = first_q_bank[m]
                    first_q_bank[m] = False
                    nonlocal_last = P.op("tensor", lambda e, ff=ff, m=m, buf=buf, j=j,
                                         stt=stt, stp=(g == 15 and j == 1): e.matmul(
                        ps_k[m][:, :],
                        u2T[:, ff, 128 * m:128 * m + 128],
                        H_bf[:, 1024 * buf + 512 * j:1024 * buf + 512 * j + 512],
                        start=stt, stop=stp),
                        waits=u2w if (j == 0 and m == 0) else (),
                        inc=T if (g == 15 and j == 1 and m == 1) else None)
            return nonlocal_last

        tt_H_tok = [None] * 16
        tt_u2last = None
        for g in range(16):
            tt_H_tok[g] = emit_v1(g)
            emit_gelu(g)
            if g >= 1:
                tt_u2last = emit_u2(g - 1)
        tt_u2last = emit_u2(15)
        tt_Q = tt_u2last
        P.op("scalar", lambda e: e.copy(
            out=Q_bf[:, 0, :], in_=ps_k[0][:, :]), waits=[tt_Q], inc=S)
        ts_Q = P.op("scalar", lambda e: e.copy(
            out=Q_bf[:, 1, :], in_=ps_k[1][:, :]), inc=S)
        if p2stage <= 5:
            P.op("gpsimd", lambda e: e.nop(), waits=[ts_Q])
            with nc.Block() as block:
                P.emit(block)
            stack.close()
            return nc
        tv_res2 = [None] * 8
        for dch in range(8):
            vps = ps_a if dch % 2 == 0 else ps_s[3]
            vw = [ts_Q, tok_v2]
            if dch > 1:
                vw.append(tv_res2[dch - 2])
            else:
                vw.append(tv_P[1])
                vw.append(tv_res1[7])
            for r in range(2):
                tt_v2 = P.op("tensor", lambda e, r=r, dch=dch, vps=vps: e.matmul(
                    vps[:, :], v2T[:, r, 128 * dch:128 * dch + 128], Q_bf[:, r, :],
                    start=(r == 0), stop=(r == 1)),
                    waits=vw if r == 0 else (), inc=T if r == 1 else None)
            tv_res2[dch] = P.op("vector",
                                lambda e, dch=dch, xcur=xcur,
                                vps=vps: e.scalar_tensor_tensor(
                xcur[:, dch, :], vps[:, :], b2c[:, dch:dch + 1], xcur[:, dch, :],
                Alu.add, Alu.add), waits=[tt_v2], inc=V)
        tok_out[st] = P.dma("sync", out[st], xcur[:], waits=[tv_res2[7]],
                            inc=f"s_out{st % 2}")
        tok_xb2_free[st] = tok_out[st]
        tok_ctab2_free = tv_q[7]
        prev_psa_reader = ts_Q
        prev_tv_res2_last = tv_res2[7]
        prev_tsQ = ts_Q
        prev_tv_P1 = tv_P[1]
        tok_rows_used[st] = tt_rsb
        prev_negmb_user = tt_rsb

    P.op("sync", lambda e: e.nop(),
         waits=[("s_out0", P.count["s_out0"]), ("s_out1", P.count["s_out1"])])
    prev_kt_spill = prev_v_spill

    with nc.Block() as block:
        P.emit(block)
    stack.close()
    return nc


_NC_CACHE = None


def kernel(L0_attn_x, L0_attn_Wq, L0_attn_Wk, L0_attn_Wv, L0_attn_Wo,
           L0_ffn_U1, L0_ffn_V1, L0_ffn_b1, L0_ffn_U2, L0_ffn_V2, L0_ffn_b2):
    global _NC_CACHE
    x = np.asarray(L0_attn_x, np.float32)
    B = x.shape[0]
    perm = _perm_rows()
    wq_p = np.ascontiguousarray(np.asarray(L0_attn_Wq, np.float32)[perm, :])
    wo_p = np.ascontiguousarray(np.asarray(L0_attn_Wo, np.float32)[:, perm])
    wk = np.ascontiguousarray(np.asarray(L0_attn_Wk, np.float32))
    wv = np.ascontiguousarray(np.asarray(L0_attn_Wv, np.float32))
    u1 = np.ascontiguousarray(np.asarray(L0_ffn_U1, np.float32))
    v1 = np.ascontiguousarray(np.asarray(L0_ffn_V1, np.float32))
    u2 = np.ascontiguousarray(np.asarray(L0_ffn_U2, np.float32))
    v2 = np.ascontiguousarray(np.asarray(L0_ffn_V2, np.float32))
    b1c = np.ascontiguousarray(np.asarray(L0_ffn_b1, np.float32).reshape(32, 128).T)
    b2c = np.ascontiguousarray(np.asarray(L0_ffn_b2, np.float32).reshape(8, 128).T)

    consts = [_host_consts(0), _host_consts(1)]
    if _NC_CACHE is None:
        _NC_CACHE = build_nc()
    nc = _NC_CACHE

    in_maps = []
    for c in range(8):
        b, h = c // 2, c % 2
        xs = np.ascontiguousarray(x[b, h * S_CORE:(h + 1) * S_CORE, :].T)
        xtl = np.ascontiguousarray(
            xs.reshape(8, 128, N_ST, ST).transpose(2, 1, 0, 3))
        cc = consts[h]
        in_maps.append(dict(
            xt=xtl, wq=wq_p, wk=wk, wv=wv, wo=wo_p, u1=u1, v1=v1, u2=u2, v2=v2,
            b1c=b1c, b2c=b2c, mt=cc["mt"], identbf=cc["identbf"], mask=cc["mask"],
            qdec=cc["qdec"], kdec=cc["kdec"], bdb=cc["bdb"],
            costab=cc["costab"], sintab=cc["sintab"], rmask=cc["rmask"],
            onesf=cc["onesf"], onesb=cc["onesb"], onescol=cc["onescol"],
            epsc=cc["epsc"],
        ))

    res = run_bass_kernel_spmd(nc, in_maps, core_ids=list(range(8)))

    outp = np.empty((B, 8192, D), np.float32)
    for c in range(8):
        b, h = c // 2, c % 2
        arr = np.asarray(res.results[c]["out"])
        sl = arr.transpose(0, 3, 2, 1).reshape(S_CORE, D)
        outp[b, h * S_CORE:(h + 1) * S_CORE, :] = sl
    return outp
